# version 06
# baseline (speedup 1.0000x reference)
import math
from contextlib import ExitStack

import numpy as np
import ml_dtypes

import concourse.bass as bass
import concourse.mybir as mybir
from concourse.bass_utils import run_bass_kernel_spmd

F32 = mybir.dt.float32
BF16 = mybir.dt.bfloat16
I32 = mybir.dt.int32
AF = mybir.ActivationFunctionType
ALU = mybir.AluOpType
AX = mybir.AxisListType
BFNP = ml_dtypes.bfloat16

_NMC = [0]


def nm(base):
    _NMC[0] += 1
    return f"{base}_{_NMC[0]}"


S = 2048
D = 2048
NF = D // 128
NEG = -1.0e30



_ENG_OPS = {
    "tensor_tensor", "tensor_scalar", "tensor_copy", "tensor_add", "tensor_sub",
    "tensor_mul", "tensor_max", "tensor_scalar_add", "tensor_scalar_sub",
    "tensor_scalar_mul", "tensor_scalar_max", "tensor_scalar_min", "tensor_relu",
    "tensor_reduce", "reduce_max", "reduce_sum", "scalar_tensor_tensor",
    "tensor_tensor_scan", "memset", "memzero", "reciprocal", "select",
    "copy_predicated", "activation", "copy", "add", "mul", "sqrt", "square",
    "sign", "iota", "affine_select", "tensor_single_scalar",
}


class _EngWrap:

    def __init__(self, em, eng, sem_name):
        self._em = em
        self._eng = eng
        self._sem_name = sem_name

    def __getattr__(self, name):
        eng = self._eng
        attr = getattr(eng, name)
        if name not in _ENG_OPS:
            return attr
        em = self._em
        sem_name = self._sem_name

        def wrapped(*a, **k):
            h = em.sem(sem_name)
            eng.wait_ge(h, em.cnt(sem_name))
            inst = attr(*a, **k)
            inst.then_inc(h, 1)
            em.bump(sem_name)
            return inst
        return wrapped


class Em:

    def __init__(self, nc):
        self.nc = nc
        self.te = nc.tensor
        self.ve = _EngWrap(self, nc.vector, "q_ve")
        self.se = _EngWrap(self, nc.scalar, "q_se")
        self.gp = _EngWrap(self, nc.gpsimd, "q_gp")
        self.sy = nc.sync
        self._sems = {}
        self._cnt = {}
        self._ctx = []

    def sem(self, name):
        if name not in self._sems:
            h = self.nc.alloc_semaphore(name=name)
            self._sems[name] = h
            self._cnt[name] = 0
        return self._sems[name]

    def cnt(self, name):
        return self._cnt[name]

    def bump(self, name, by=1):
        self._cnt[name] += by
        return self._cnt[name]

    def _qsem(self, q, sem_name):
        return sem_name + ("_g" if q is self.gp else "")

    def dma(self, q, out, in_, sem_name):
        name = self._qsem(q, sem_name)
        h = self.sem(name)
        q.dma_start(out, in_).then_inc(h, 16)
        self.bump(name, 16)

    def wait(self, eng, sem_name, val=None):
        for name in (sem_name, sem_name + "_g"):
            if name in self._sems:
                eng.wait_ge(self._sems[name],
                            self.cnt(name) if val is None else val)

    def barrier(self):
        self.nc.all_engine_barrier()

    def hard(self):
        for name in list(self._sems):
            if name.startswith("d_"):
                for eng in (self.te, self.ve, self.se, self.gp, self.sy):
                    eng.wait_ge(self._sems[name], self._cnt[name])
        self.nc.all_engine_barrier()


def mm_kseries(em, ps_ap, pairs, then_sem=None, inc=1):
    n = len(pairs)
    last = None
    for i, (l, r) in enumerate(pairs):
        last = em.te.matmul(ps_ap, l, r, start=(i == 0), stop=(i == n - 1))
    if then_sem is not None:
        last.then_inc(then_sem, inc)
    return last




def _t(a):
    return np.ascontiguousarray(np.asarray(a).T)


def _pcol(b):
    b = np.asarray(b)
    return np.ascontiguousarray(b.reshape(-1, 128).T)


def _row(b):
    return np.ascontiguousarray(np.asarray(b).reshape(1, -1))




def build_consts():
    c = {}
    c["ident_bf"] = np.eye(128, dtype=BFNP)
    c["ident_f32"] = np.eye(128, dtype=np.float32)
    onescol = np.ones((128, 1), np.float32)
    c["onescol_f"] = onescol
    c["onescol_b"] = onescol.astype(BFNP)
    onesrow = np.ones((1, 128), np.float32)
    c["onesrow_f"] = onesrow
    c["onesrow_b"] = onesrow.astype(BFNP)
    for r in range(2):
        p = np.arange(128)[:, None]
        j = np.arange(256)[None, :]
        c[f"mask256_r{r}"] = np.where(p <= j - 128 * r, 0.0, NEG).astype(np.float32)
    for r in range(4):
        p = np.arange(128)[:, None]
        j = np.arange(512)[None, :]
        c[f"mask512_r{r}"] = np.where(p <= j - 128 * r, 0.0, NEG).astype(np.float32)
    Dh = 128
    inv = 1.0 / (1e6 ** (np.arange(0, Dh, 2) / Dh))
    fr = np.arange(S)[:, None] * inv[None, :]
    cos = np.concatenate([np.cos(fr), np.cos(fr)], 1)
    sin = np.concatenate([np.sin(fr), np.sin(fr)], 1)
    c["ropecos"] = np.ascontiguousarray(cos.T).astype(np.float32)
    c["ropesin"] = np.ascontiguousarray(sin.T).astype(np.float32)
    R = np.zeros((128, 128), np.float32)
    for m in range(64):
        R[m + 64, m] = -1.0
    for m in range(64, 128):
        R[m - 64, m] = 1.0
    c["rope_R"] = R.astype(BFNP)
    E = np.zeros((128, 16 * 16), np.float32)
    for h in range(16):
        E[:, 16 * h + h] = 1.0
    c["selE"] = E.astype(BFNP)
    B = np.zeros((16, 16 * 128), np.float32)
    for h in range(16):
        B[h, 128 * h:128 * (h + 1)] = 1.0
    c["selB"] = B.astype(BFNP)
    B8 = np.zeros((8, 8 * 128), np.float32)
    for e in range(8):
        B8[e, 128 * e:128 * (e + 1)] = 1.0
    c["sel8"] = B8.astype(BFNP)
    return c


class Consts:

    def __init__(self, em, names, consts):
        self.handles = {}
        self.sb = {}
        nc = em.nc
        self._stack = []
        for n in names:
            a = consts[n]
            h = nc.inline_tensor(np.ascontiguousarray(a), f"c_{n}")
            dt = BF16 if a.dtype == BFNP else (F32 if a.dtype == np.float32 else I32)
            t = nc.alloc_sbuf_tensor(f"cs_{n}", list(a.shape), dt)
            em.dma(em.sy, t[tuple(slice(None) for _ in a.shape)], h.ap()[tuple(slice(None) for _ in a.shape)], "d_const")
            self.sb[n] = t
        em.hard()

    def __getitem__(self, n):
        return self.sb[n]




def emit_phase0(em, C, io, xn_sb=None, adds=(), out_x=None, router=None,
                l2fix=None, save_rows=None, combine_only=False, ct=512):
    nc = em.nc
    x_in = io["x_in"]
    with (
        nc.sbuf_tensor([128, NF * ct], F32) as xch,
        nc.sbuf_tensor([128, NF * ct], F32) as ach,
        nc.sbuf_tensor([1, ct * 8], F32) as rows,
        nc.psum_tensor([1, ct], F32) as ps_sum,
        nc.psum_tensor([1, ct], F32) as ps_sq,
        nc.psum_tensor([128, ct], F32) as ps_b1,
        nc.psum_tensor([128, ct], F32) as ps_b2,
    ):
        sqch = ach
        r_mean = rows[:, 0:ct]
        r_msq = rows[:, ct:2 * ct]
        r_var = rows[:, 2 * ct:3 * ct]
        r_rstd = rows[:, 3 * ct:4 * ct]
        r_nm = rows[:, 4 * ct:5 * ct]
        r_t = rows[:, 5 * ct:6 * ct]
        for c in range(S // ct):
            tsl = slice(c * ct, (c + 1) * ct)
            if "d_xout" in em._sems:
                em.wait(em.sy, "d_xout")
            em.dma(em.sy, xch[:].rearrange("p (f t) -> p f t", f=NF),
                   x_in[:, tsl].rearrange("(f p) t -> p f t", p=128), "d_act")
            em.wait(em.ve, "d_act")
            em.barrier()
            if l2fix is not None:
                l2fix(em, C, c, xch)
            for a_ap in adds:
                em.dma(em.sy, ach[:].rearrange("p (f t) -> p f t", f=NF),
                       a_ap[:, tsl].rearrange("(f p) t -> p f t", p=128), "d_act")
                em.wait(em.ve, "d_act")
                for f in range(NF):
                    em.ve.tensor_tensor(xch[:, f * ct:(f + 1) * ct],
                                        xch[:, f * ct:(f + 1) * ct],
                                        ach[:, f * ct:(f + 1) * ct], ALU.add)
                em.barrier()
            if out_x is not None:
                em.dma(em.sy, out_x[:, tsl].rearrange("(f p) t -> p f t", p=128),
                       xch[:].rearrange("p (f t) -> p f t", f=NF), "d_xout")
            if combine_only:
                em.barrier()
                continue
            for f in range(NF):
                em.se.activation(sqch[:, f * ct:(f + 1) * ct],
                                 xch[:, f * ct:(f + 1) * ct], AF.Square)
            em.barrier()
            mm_kseries(em, ps_sum[:, :],
                       [(C["onescol_f"][:, :], xch[:, f * ct:(f + 1) * ct]) for f in range(NF)])
            mm_kseries(em, ps_sq[:, :],
                       [(C["onescol_f"][:, :], sqch[:, f * ct:(f + 1) * ct]) for f in range(NF)])
            if router is not None:
                router(em, C, c, xch)
            em.barrier()
            em.ve.tensor_scalar(r_mean, ps_sum[:, :], 1.0 / D, None, ALU.mult)
            em.ve.tensor_scalar(r_msq, ps_sq[:, :], 1.0 / D, None, ALU.mult)
            em.barrier()
            em.se.activation(r_t, r_mean, AF.Square)
            em.barrier()
            em.ve.tensor_tensor(r_var, r_msq, r_t, ALU.subtract)
            em.barrier()
            em.ve.tensor_scalar(r_var, r_var, 1e-5, None, ALU.add)
            em.barrier()
            em.se.activation(r_t, r_var, AF.Sqrt)
            em.barrier()
            em.ve.reciprocal(r_rstd, r_t)
            em.barrier()
            em.ve.tensor_tensor(r_nm, r_mean, r_rstd, ALU.mult)
            em.barrier()
            em.ve.tensor_scalar(r_nm, r_nm, -1.0, None, ALU.mult)
            em.barrier()
            if save_rows is not None:
                em.ve.tensor_copy(save_rows["mean"][:, tsl], r_mean)
                em.ve.tensor_copy(save_rows["rstd"][:, tsl], r_rstd)
            em.te.matmul(ps_b1[:, :], C["onesrow_f"][:, :], r_rstd, start=True, stop=True)
            em.te.matmul(ps_b2[:, :], C["onesrow_f"][:, :], r_nm, start=True, stop=True)
            em.barrier()
            for f in range(NF):
                em.ve.tensor_tensor(sqch[:, f * ct:(f + 1) * ct],
                                    xch[:, f * ct:(f + 1) * ct], ps_b1[:, :], ALU.mult)
            em.barrier()
            for f in range(NF):
                em.ve.tensor_tensor(xn_sb[:, f * S + c * ct: f * S + (c + 1) * ct],
                                    sqch[:, f * ct:(f + 1) * ct], ps_b2[:, :], ALU.add)
            em.barrier()
            if router is not None:
                router(em, C, c, xch, post=True)
        em.hard()




def attn_alloc(em, cfg):
    nc = em.nc
    st = {}
    st["bias_sb"] = nc.alloc_sbuf_tensor(nm("at_bias"), [128, 3 * NF + 1], F32)
    Dkv = cfg["kvH"] * cfg["dh"]
    st["bvrow_sb"] = nc.alloc_sbuf_tensor(nm("at_bvrow"), [1, Dkv], BF16)
    return st


def emit_attn_proj(em, C, st, xn_sb, cfg, wd, q_hbm, v_hbm, k_hbm):
    nc = em.nc
    H, dh, kvH = cfg["H"], cfg["dh"], cfg["kvH"]
    Dq = H * dh
    Dkv = kvH * dh
    bias_sb = st["bias_sb"]
    bvrow_sb = st["bvrow_sb"]
    if cfg.get("has_bias", True):
        em.dma(em.sy, bias_sb[:, 0:Dq // 128], wd["bq"][:, :], "d_w")
        em.dma(em.sy, bias_sb[:, NF:NF + Dkv // 128], wd["bk"][:, :], "d_w")
        em.dma(em.gp, bvrow_sb[:, :], wd["bv"][:, :], "d_w")
    em.dma(em.sy, bias_sb[:, 2 * NF:3 * NF], wd["bo"][:, :], "d_w")
    em.hard()
    em.barrier()

    def proj_featmajor(w_ap, ncols, bias_col, dest_cb):
        with (
            nc.sbuf_tensor([128, 2 * NF * 128], BF16) as wc,
            nc.sbuf_tensor([128, 2 * 512], BF16) as st,
            nc.psum_tensor([128, 512], F32) as ps,
            nc.psum_tensor([128, 512], F32) as ps2,
        ):
            pss = [ps, ps2]
            for o in range(ncols // 128):
                b = o % 2
                wcb = wc[:, b * NF * 128:(b + 1) * NF * 128]
                em.dma(em.gp, wcb.rearrange("p (f c) -> p f c", f=NF),
                       w_ap[:, o * 128:(o + 1) * 128].rearrange("(f p) c -> p f c", p=128),
                       "d_w")
                em.hard()
                for tq in range(4):
                    mm_kseries(em, pss[tq % 2][:, :],
                               [(wcb[:, f * 128:(f + 1) * 128],
                                 xn_sb[:, f * S + tq * 512:f * S + (tq + 1) * 512])
                                for f in range(NF)])
                    em.barrier()
                    stg = st[:, (tq % 2) * 512:(tq % 2 + 1) * 512]
                    em.wait(em.se, "d_sc")
                    if bias_col is not None:
                        em.se.activation(stg, pss[tq % 2][:, :], AF.Identity,
                                         bias=bias_col(o))
                    else:
                        em.se.activation(stg, pss[tq % 2][:, :], AF.Identity)
                    em.barrier()
                    dest_cb(o, tq, stg)
                    em.barrier()
            em.hard()

    qb = (lambda o: bias_sb[:, o:o + 1]) if cfg.get("has_bias", True) else None
    proj_featmajor(wd["wq"], Dq, qb,
                   lambda o, tq, stg: em.dma(em.sy, q_hbm[o * 128:(o + 1) * 128, tq * 512:(tq + 1) * 512], stg, "d_sc"))
    kb = (lambda o: bias_sb[:, NF + o:NF + o + 1]) if cfg.get("has_bias", True) else None
    proj_featmajor(wd["wk"], Dkv, kb,
                   lambda o, tq, stg: em.dma(em.sy, k_hbm[o * 128:(o + 1) * 128, tq * 512:(tq + 1) * 512], stg, "d_sc"))

    with (
        nc.sbuf_tensor([128, NF * 512], BF16) as wvc,
        nc.sbuf_tensor([128, 2 * 512], BF16) as st,
        nc.psum_tensor([128, 512], F32) as ps,
        nc.psum_tensor([128, 512], F32) as ps2,
    ):
        pss = [ps, ps2]
        for oc in range(max(1, Dkv // 512)):
            osl = slice(oc * 512, (oc + 1) * 512)
            em.dma(em.gp, wvc[:].rearrange("p (f c) -> p f c", f=NF),
                   wd["wv"][:, osl].rearrange("(f p) c -> p f c", p=128), "d_w")
            em.hard()
            for t in range(16):
                p = pss[t % 2]
                pairs = [(xn_sb[:, f * S + t * 128: f * S + (t + 1) * 128],
                          wvc[:, f * 512:(f + 1) * 512]) for f in range(NF)]
                if cfg.get("has_bias", True):
                    pairs.append((C["onesrow_b"][:, :], bvrow_sb[:, osl]))
                mm_kseries(em, p[:, :], pairs)
                em.barrier()
                stg = st[:, (t % 2) * 512:(t % 2 + 1) * 512]
                em.wait(em.se, "d_sc")
                em.se.activation(stg, p[:, :], AF.Identity)
                em.barrier()
                em.dma(em.sy, v_hbm[t * 128:(t + 1) * 128, osl], stg, "d_sc")
                em.barrier()
        em.hard()



def emit_attn_core(em, C, st, cfg, q_hbm, v_hbm, k_hbm, feats_hbm, acc_out, wd):
    nc = em.nc
    H, dh, kvH = cfg["H"], cfg["dh"], cfg["kvH"]
    Dq = H * dh
    causal = cfg["causal"]
    escale = cfg["escale"]
    TQ = 512
    bias_sb = st["bias_sb"]
    NTQ = S // TQ
    smm = em.sem("a_mm")
    with ExitStack() as _es:
        sraw = _es.enter_context(nc.sbuf_tensor(nm("sraw"), [128, 16 * TQ], F32))
        expP = _es.enter_context(nc.sbuf_tensor(nm("expP"), [128, 16 * TQ], BF16))
        vh = _es.enter_context(nc.sbuf_tensor(nm("vh"), [128, 16 * dh], BF16))
        kh = _es.enter_context(nc.sbuf_tensor(nm("kh"), [128, S], BF16))
        qh = _es.enter_context(nc.sbuf_tensor(nm("qh"), [128, TQ], BF16))
        mrun = _es.enter_context(nc.sbuf_tensor(nm("mrun"), [128, TQ], F32))
        tmp2 = _es.enter_context(nc.sbuf_tensor(nm("tmp2"), [128, 2 * TQ], F32))
        esum = _es.enter_context(nc.sbuf_tensor(nm("esum"), [128, TQ], F32))
        rows = _es.enter_context(nc.sbuf_tensor(nm("rows"), [1, 2 * TQ], F32))
        cm = _es.enter_context(nc.sbuf_tensor(nm("cm"), [128, 4], F32))
        ftmp = _es.enter_context(nc.sbuf_tensor(nm("ftmp"), [128, TQ], F32))
        fbf = _es.enter_context(nc.sbuf_tensor(nm("fbf"), [128, TQ], BF16))
        ps_s0 = _es.enter_context(nc.psum_tensor(nm("ps_s0"), [128, TQ], F32))
        ps_s1 = _es.enter_context(nc.psum_tensor(nm("ps_s1"), [128, TQ], F32))
        ps_t = _es.enter_context(nc.psum_tensor(nm("ps_t"), [128, 128], F32))
        ps_r = _es.enter_context(nc.psum_tensor(nm("ps_r"), [1, TQ], F32))
        ps_f = _es.enter_context(nc.psum_tensor(nm("ps_f"), [128, TQ], F32))
        ps_b = _es.enter_context(nc.psum_tensor(nm("ps_b"), [128, TQ], F32))
        pss = [ps_s0, ps_s1]
        mrow = rows[:, 0:TQ]
        rrow = rows[:, TQ:2 * TQ]
        for g in range(kvH):
            em.dma(em.gp, vh[:, 0:16 * dh].rearrange("p (f c) -> p f c", f=16),
                   v_hbm[:, g * dh:(g + 1) * dh].rearrange("(f p) c -> p f c", p=128),
                   "d_sc")
            em.dma(em.sy, kh[0:dh, :], k_hbm[g * dh:(g + 1) * dh, :], "d_sc")
            em.hard()
            for h in range(g * (H // kvH), (g + 1) * (H // kvH)):
                for tq in range(NTQ):
                    ntk = min(16, (tq + 1) * (TQ // 128)) if causal else 16
                    em.wait(em.sy, "d_sc")
                    em.dma(em.sy, qh[0:dh, :],
                           q_hbm[h * dh:(h + 1) * dh, tq * TQ:(tq + 1) * TQ], "d_sc")
                    em.wait(em.te, "d_sc")
                    evdone = []
                    for i in range(ntk):
                        ps = pss[i % 2]
                        lhsT = kh[0:dh, i * 128:(i + 1) * 128]
                        if i >= 2:
                            em.wait(em.te, "q_ve", evdone[i - 2])
                        em.te.matmul(ps[:, :], lhsT, qh[0:dh, :],
                                     start=True, stop=True).then_inc(smm, 1)
                        em.bump("a_mm")
                        em.wait(em.ve, "a_mm")
                        ssl = sraw[:, i * TQ:(i + 1) * TQ]
                        r = i - tq * (TQ // 128)
                        if causal and r >= 0:
                            em.ve.tensor_tensor(ssl, ps[:, :],
                                                C[f"mask{TQ}_r{r}"][:, :], ALU.add)
                        else:
                            em.ve.tensor_copy(ssl, ps[:, :])
                        evdone.append(em.cnt("q_ve"))
                        if i == 0:
                            em.ve.tensor_copy(mrun[:, :], ssl)
                        else:
                            em.ve.tensor_tensor(mrun[:, :], mrun[:, :], ssl, ALU.max)
                    em.wait(em.te, "q_ve")
                    for j in range(TQ // 128):
                        em.te.transpose(ps_t[:, :], mrun[:, j * 128:(j + 1) * 128],
                                        C["ident_f32"][:, :]).then_inc(smm, 1)
                        em.bump("a_mm")
                        em.wait(em.ve, "a_mm")
                        em.ve.reduce_max(cm[:, j:j + 1], ps_t[:, :], axis=AX.X)
                        em.wait(em.te, "q_ve")
                    for j in range(TQ // 128):
                        em.te.transpose(ps_t[0:1, 0:128], cm[:, j:j + 1],
                                        C["ident_f32"][:, :]).then_inc(smm, 1)
                        em.bump("a_mm")
                        em.wait(em.ve, "a_mm")
                        em.ve.tensor_copy(mrow[:, j * 128:(j + 1) * 128],
                                          ps_t[0:1, 0:128])
                        em.wait(em.te, "q_ve")
                    em.te.matmul(ps_b[:, :], C["onesrow_f"][:, :], mrow,
                                 start=True, stop=True).then_inc(smm, 1)
                    em.bump("a_mm")
                    em.wait(em.ve, "a_mm")
                    exdone = []
                    for i in range(ntk):
                        ssl = sraw[:, i * TQ:(i + 1) * TQ]
                        esl = expP[:, i * TQ:(i + 1) * TQ]
                        tb = tmp2[:, (i % 2) * TQ:(i % 2 + 1) * TQ]
                        if i >= 2:
                            em.wait(em.ve, "q_se", exdone[i - 2])
                        em.ve.tensor_tensor(tb, ssl, ps_b[:, :], ALU.subtract)
                        em.wait(em.se, "q_ve")
                        em.se.activation(esl, tb, AF.Exp, scale=escale)
                        exdone.append(em.cnt("q_se"))
                        em.wait(em.ve, "q_se")
                        if i == 0:
                            em.ve.tensor_copy(esum[:, :], esl)
                        else:
                            em.ve.tensor_tensor(esum[:, :], esum[:, :], esl, ALU.add)
                        em.wait(em.te, "q_se")
                        em.te.matmul(ps_f[0:dh, :], vh[:, i * dh:(i + 1) * dh],
                                     esl, start=(i == 0), stop=(i == ntk - 1))
                    em.wait(em.te, "q_ve")
                    em.te.matmul(ps_r[:, :], C["onescol_f"][:, :], esum[:, :],
                                 start=True, stop=True).then_inc(smm, 1)
                    em.bump("a_mm")
                    em.wait(em.ve, "a_mm")
                    em.ve.reciprocal(rrow, ps_r[:, :])
                    em.wait(em.te, "q_ve")
                    em.te.matmul(ps_b[:, :], C["onesrow_f"][:, :], rrow,
                                 start=True, stop=True).then_inc(smm, 1)
                    em.bump("a_mm")
                    em.wait(em.se, "a_mm")
                    em.se.activation(ftmp[0:dh, :], ps_f[0:dh, :], AF.Identity)
                    em.wait(em.ve, "q_se")
                    em.wait(em.ve, "d_sc")
                    em.ve.tensor_tensor(fbf[0:dh, :], ftmp[0:dh, :], ps_b[0:dh, :],
                                        ALU.mult)
                    em.wait(em.sy, "q_ve")
                    em.dma(em.sy, feats_hbm[h * dh:(h + 1) * dh,
                                            tq * TQ:(tq + 1) * TQ], fbf[0:dh, :],
                           "d_sc")
                    em.barrier()
        em.hard()
    with (
        nc.sbuf_tensor([128, NF * 128], BF16) as wc,
        nc.sbuf_tensor([128, NF * 512], BF16) as rhs,
        nc.sbuf_tensor([128, 512], F32) as st,
        nc.psum_tensor([128, 512], F32) as ps,
    ):
        for tq in range(4):
            em.dma(em.gp, rhs[:].rearrange("p (f c) -> p f c", f=NF),
                   feats_hbm[:, tq * 512:(tq + 1) * 512].rearrange("(f p) c -> p f c", p=128),
                   "d_sc")
            em.hard()
            for o in range(NF):
                em.dma(em.gp, wc[:].rearrange("p (f c) -> p f c", f=Dq // 128),
                       wd["wo"][:, o * 128:(o + 1) * 128].rearrange("(f p) c -> p f c", p=128),
                       "d_w")
                em.hard()
                mm_kseries(em, ps[:, :],
                           [(wc[:, f * 128:(f + 1) * 128], rhs[:, f * 512:(f + 1) * 512])
                            for f in range(Dq // 128)])
                em.barrier()
                em.se.activation(st[:, :], ps[:, :], AF.Identity,
                                 bias=bias_sb[:, 2 * NF + o:2 * NF + o + 1])
                em.barrier()
                em.dma(em.sy, acc_out[o * 128:(o + 1) * 128, tq * 512:(tq + 1) * 512],
                       st[:, :], "d_sc")
                em.hard()




def emit_ffn_swiglu(em, C, xn_sb, cfg, wd, acc_out):
    nc = em.nc
    NI = cfg["NI"]
    NO = NI // 128
    has_s = cfg.get("scales", False)
    scal_sb = cfg.get("scal_sb")
    if has_s:
        em.dma(em.sy, scal_sb[:, 0:NO], wd["gate_s"][:, :], "d_w")
        em.dma(em.sy, scal_sb[:, NO:2 * NO], wd["up_s"][:, :], "d_w")
        em.dma(em.sy, scal_sb[:, 2 * NO:2 * NO + NF], wd["down_s"][:, :], "d_w")
        em.hard()
    with (
        nc.sbuf_tensor([128, NF * 128], BF16) as wc,
        nc.sbuf_tensor([128, NF * 128], BF16) as wc2,
        nc.sbuf_tensor([128, 512], BF16) as gsb,
        nc.sbuf_tensor([128, 512], BF16) as usb,
        nc.sbuf_tensor([128, NO * 512], BF16) as hch,
        nc.sbuf_tensor([128, NO * 128], BF16) as dc,
        nc.sbuf_tensor([128, 512], F32) as st,
        nc.psum_tensor([128, 512], F32) as ps,
        nc.psum_tensor([128, 512], F32) as ps2,
    ):
        for tq in range(4):
            tsl = slice(tq * 512, (tq + 1) * 512)
            for o in range(NO):
                em.dma(em.gp, wc[:].rearrange("p (f c) -> p f c", f=NF),
                       wd["gate"][:, o * 128:(o + 1) * 128].rearrange("(f p) c -> p f c", p=128),
                       "d_w")
                em.dma(em.gp, wc2[:].rearrange("p (f c) -> p f c", f=NF),
                       wd["up"][:, o * 128:(o + 1) * 128].rearrange("(f p) c -> p f c", p=128),
                       "d_w")
                em.hard()
                mm_kseries(em, ps[:, :],
                           [(wc[:, f * 128:(f + 1) * 128],
                             xn_sb[:, f * S + tq * 512: f * S + (tq + 1) * 512])
                            for f in range(NF)])
                mm_kseries(em, ps2[:, :],
                           [(wc2[:, f * 128:(f + 1) * 128],
                             xn_sb[:, f * S + tq * 512: f * S + (tq + 1) * 512])
                            for f in range(NF)])
                em.barrier()
                if has_s:
                    em.se.activation(gsb[:, :], ps[:, :], AF.Silu,
                                     scale=scal_sb[:, o:o + 1])
                    em.se.activation(usb[:, :], ps2[:, :], AF.Identity,
                                     scale=scal_sb[:, NO + o:NO + o + 1])
                else:
                    em.se.activation(gsb[:, :], ps[:, :], AF.Silu)
                    em.se.activation(usb[:, :], ps2[:, :], AF.Identity)
                em.barrier()
                em.ve.tensor_tensor(hch[:, o * 512:(o + 1) * 512], gsb[:, :], usb[:, :],
                                    ALU.mult)
                em.barrier()
            for o2 in range(NF):
                em.dma(em.gp, dc[:].rearrange("p (f c) -> p f c", f=NO),
                       wd["down"][:, o2 * 128:(o2 + 1) * 128].rearrange("(f p) c -> p f c", p=128),
                       "d_w")
                em.hard()
                mm_kseries(em, ps[:, :],
                           [(dc[:, f * 128:(f + 1) * 128], hch[:, f * 512:(f + 1) * 512])
                            for f in range(NO)])
                em.barrier()
                if has_s:
                    em.se.activation(st[:, :], ps[:, :], AF.Identity,
                                     scale=scal_sb[:, 2 * NO + o2:2 * NO + o2 + 1])
                else:
                    em.se.activation(st[:, :], ps[:, :], AF.Identity)
                em.barrier()
                em.dma(em.sy, acc_out[o2 * 128:(o2 + 1) * 128, tsl], st[:, :], "d_sc")
                em.hard()




def _io_tensors(nc, names_shapes_dtypes):
    out = {}
    for name, shape, dt, kind in names_shapes_dtypes:
        out[name] = nc.dram_tensor(name, list(shape), dt, kind=kind).ap()
    return out


def build_prep(consts):
    nc = bass.Bass("TRN2", target_bir_lowering=False)
    em = Em(nc)
    specs = [
        ("lut", (1, 16), F32, "ExternalInput"),
        ("gidxT", (256, 5504), I32, "ExternalInput"),
        ("uidxT", (256, 5504), I32, "ExternalInput"),
        ("didxT", (688, 2048), I32, "ExternalInput"),
        ("gw_bf", (256, 5504), BF16, "ExternalOutput"),
        ("uw_bf", (256, 5504), BF16, "ExternalOutput"),
        ("dw_bf", (688, 2048), BF16, "ExternalOutput"),
    ]
    casts = [
        ("l1g", (256, 5632)), ("l1u", (256, 5632)), ("l1d", (704, 2048)),
        ("l2w1", (256, 8192)), ("l2w2", (1024, 2048)),
        ("l3w1", (256, 5632)), ("l3w3", (256, 5632)), ("l3w2", (704, 2048)),
    ]
    for n, shp in casts:
        specs.append((n, shp, F32, "ExternalInput"))
        specs.append((n + "_bf", shp, BF16, "ExternalOutput"))
    io = _io_tensors(nc, specs)
    C = Consts(em, ["onesrow_f"], consts)
    lut_sb = nc.alloc_sbuf_tensor([1, 16], F32)
    lutb_sb = nc.alloc_sbuf_tensor([128, 16], F32)
    em.dma(em.sy, lut_sb[:, :], io["lut"][:, :], "d_w")
    em.hard()
    with nc.psum_tensor([128, 16], F32) as psl:
        em.te.matmul(psl[:, :], C["onesrow_f"][:, :], lut_sb[:, :], start=True, stop=True)
        em.barrier()
        em.ve.tensor_copy(lutb_sb[:, :], psl[:, :])
        em.barrier()
    with (
        nc.sbuf_tensor([128, 5504], I32) as idxi,
        nc.sbuf_tensor([128, 5504], BF16) as idxb,
        nc.sbuf_tensor([128, 5504], BF16) as tmp,
        nc.sbuf_tensor([128, 5504], BF16) as acc,
    ):
        for mat, cols, rows in (("gidxT", 5504, 256), ("uidxT", 5504, 256),
                                ("didxT", 2048, 688)):
            for r0 in range(0, rows, 128):
                rr = min(128, rows - r0)
                em.dma(em.sy, idxi[0:rr, 0:cols], io[mat][r0:r0 + rr, :], "d_w")
                em.hard()
                em.ve.tensor_copy(idxb[0:rr, 0:cols], idxi[0:rr, 0:cols])
                em.barrier()
                for v in range(16):
                    em.ve.tensor_scalar(tmp[0:rr, 0:cols], idxb[0:rr, 0:cols], float(v),
                                        lutb_sb[0:rr, v:v + 1], ALU.is_equal, ALU.mult)
                    em.barrier()
                    if v == 0:
                        em.ve.tensor_copy(acc[0:rr, 0:cols], tmp[0:rr, 0:cols])
                    else:
                        em.ve.tensor_tensor(acc[0:rr, 0:cols], acc[0:rr, 0:cols],
                                            tmp[0:rr, 0:cols], ALU.add)
                    em.barrier()
                out = {"gidxT": "gw_bf", "uidxT": "uw_bf", "didxT": "dw_bf"}[mat]
                em.dma(em.sy, io[out][r0:r0 + rr, :], acc[0:rr, 0:cols], "d_sc")
                em.hard()
    with nc.sbuf_tensor([128, 8192], BF16) as cst:
        for n, shp in casts:
            rows, cols = shp
            for r0 in range(0, rows, 128):
                rr = min(128, rows - r0)
                em.dma(em.gp, cst[0:rr, 0:cols], io[n][r0:r0 + rr, :], "d_w")
                em.hard()
                em.dma(em.sy, io[n + "_bf"][r0:r0 + rr, :], cst[0:rr, 0:cols], "d_sc")
                em.hard()
    em.hard()
    return nc


ATTN_CFG = {
    0: dict(H=32, dh=64, kvH=32, causal=False, escale=1.0 / math.sqrt(32.0),
            has_bias=True, Dq=2048, Dkv=2048),
    1: dict(H=16, dh=128, kvH=16, causal=True, escale=1.0 / math.sqrt(128.0),
            has_bias=True, Dq=2048, Dkv=2048),
    3: dict(H=16, dh=128, kvH=4, causal=True, escale=1.0,
            has_bias=False, Dq=2048, Dkv=512),
}


def build_layer01(layer, consts, debug=False):
    nc = bass.Bass("TRN2", target_bir_lowering=False)
    em = Em(nc)
    NI = 5504 if layer == 0 else 5632
    dk = "ExternalOutput" if debug else "Internal"
    specs = [
        ("xin", (D, S), F32, "ExternalInput"),
        ("xout", (D, S), F32, "ExternalOutput"),
        ("wq", (D, D), F32, "ExternalInput"),
        ("wk", (D, D), F32, "ExternalInput"),
        ("wv", (D, D), F32, "ExternalInput"),
        ("wo", (D, D), F32, "ExternalInput"),
        ("bq", (128, NF), F32, "ExternalInput"),
        ("bk", (128, NF), F32, "ExternalInput"),
        ("bv", (1, D), F32, "ExternalInput"),
        ("bo", (128, NF), F32, "ExternalInput"),
        ("gate", (D, NI), BF16, "ExternalInput"),
        ("up", (D, NI), BF16, "ExternalInput"),
        ("down", (NI, D), BF16, "ExternalInput"),
        ("accA", (D, S), F32, dk),
        ("accF", (D, S), F32, dk),
        ("q_hbm", (D, S), BF16, dk),
        ("k_hbm", (D, S), BF16, dk),
        ("v_hbm", (S, D), BF16, dk),
        ("f_hbm", (D, S), BF16, dk),
    ]
    if layer == 1:
        specs += [("gate_s", (128, NI // 128), F32, "ExternalInput"),
                  ("up_s", (128, NI // 128), F32, "ExternalInput"),
                  ("down_s", (128, NF), F32, "ExternalInput")]
    io = _io_tensors(nc, specs)
    cfg = ATTN_CFG[layer]
    mask_names = [f"mask512_r{r}" for r in range(4)] if cfg["causal"] else []
    C = Consts(em, ["onescol_f", "onesrow_f", "onesrow_b", "ident_f32"] + mask_names,
               consts)
    ffn_cfg = dict(NI=NI, scales=(layer == 1))
    if layer == 1:
        ffn_cfg["scal_sb"] = nc.alloc_sbuf_tensor(nm("ff_scal"),
                                                  [128, 2 * (NI // 128) + NF], F32)
    wd_attn = dict(wq=io["wq"], wk=io["wk"], wv=io["wv"], wo=io["wo"],
                   bq=io["bq"], bk=io["bk"], bv=io["bv"], bo=io["bo"])
    wd_ffn = dict(gate=io["gate"], up=io["up"], down=io["down"])
    if layer == 1:
        wd_ffn.update(gate_s=io["gate_s"], up_s=io["up_s"], down_s=io["down_s"])

    ast = attn_alloc(em, cfg)
    with nc.sbuf_tensor([128, NF * S], BF16) as xn_sb:
        emit_phase0(em, C, {"x_in": io["xin"]}, xn_sb=xn_sb)
        emit_ffn_swiglu(em, C, xn_sb, ffn_cfg, wd_ffn, io["accF"])
        emit_attn_proj(em, C, ast, xn_sb, cfg, wd_attn, io["q_hbm"], io["v_hbm"],
                       io["k_hbm"])
    emit_attn_core(em, C, ast, cfg, io["q_hbm"], io["v_hbm"], io["k_hbm"],
                   io["f_hbm"], io["accA"], wd_attn)
    emit_phase0(em, C, {"x_in": io["xin"]}, adds=(io["accA"], io["accF"]),
                out_x=io["xout"], combine_only=True)
    em.hard()
    return nc



_CONSTS = None


def _consts():
    global _CONSTS
    if _CONSTS is None:
        _CONSTS = build_consts()
    return _CONSTS


def _bf(a):
    return np.ascontiguousarray(np.asarray(a)).astype(BFNP)


def run_prep(inp):
    consts = _consts()
    nc = build_prep(consts)
    g_idxT = _t(inp["L0_ffn_gate_idx"])
    u_idxT = _t(inp["L0_ffn_up_idx"])
    d_idxT = _t(inp["L0_ffn_down_idx"])
    casts_full = {
        "l1g": _t(inp["L1_ffn_gate_w"]), "l1u": _t(inp["L1_ffn_up_w"]),
        "l1d": _t(inp["L1_ffn_down_w"]),
        "l2w1": np.ascontiguousarray(inp["L2_ffn_W1"]),
        "l2w2": np.ascontiguousarray(inp["L2_ffn_W2"]),
        "l3w1": _t(inp["L3_ffn_w1"]), "l3w3": _t(inp["L3_ffn_w3"]),
        "l3w2": _t(inp["L3_ffn_w2"]),
    }
    in_maps = []
    for c in range(8):
        m = {"lut": _row(inp["L0_ffn_lut"]).astype(np.float32),
             "gidxT": np.ascontiguousarray(g_idxT[c * 256:(c + 1) * 256]),
             "uidxT": np.ascontiguousarray(u_idxT[c * 256:(c + 1) * 256]),
             "didxT": np.ascontiguousarray(d_idxT[c * 688:(c + 1) * 688])}
        for n, a in casts_full.items():
            rows = a.shape[0] // 8
            m[n] = np.ascontiguousarray(a[c * rows:(c + 1) * rows]).astype(np.float32)
        in_maps.append(m)
    res = run_bass_kernel_spmd(nc, in_maps, core_ids=list(range(8)))
    out = {}
    for n in ["gw_bf", "uw_bf", "dw_bf"] + [k + "_bf" for k in casts_full]:
        out[n] = np.concatenate([res.results[c][n] for c in range(8)], axis=0)
    return out


def layer01_inmap(layer, inp, prep, xT_c):
    L = f"L{layer}_attn_"
    if layer == 0:
        m = dict(wq=inp[L + "Wq"], wk=inp[L + "Wk"], wv=inp[L + "Wv"],
                 wo=inp[L + "Wo"])
        m["gate"], m["up"], m["down"] = prep["gw_bf"], prep["uw_bf"], prep["dw_bf"]
    else:
        m = dict(wq=_t(inp[L + "Wq"]), wk=_t(inp[L + "Wk"]), wv=_t(inp[L + "Wv"]),
                 wo=_t(inp[L + "Wo"]))
        m["gate"], m["up"], m["down"] = prep["l1g_bf"], prep["l1u_bf"], prep["l1d_bf"]
        m["gate_s"] = _pcol(inp["L1_ffn_gate_s"]).astype(np.float32)
        m["up_s"] = _pcol(inp["L1_ffn_up_s"]).astype(np.float32)
        m["down_s"] = _pcol(inp["L1_ffn_down_s"]).astype(np.float32)
    m["bq"] = _pcol(inp[L + "bq"]).astype(np.float32)
    m["bk"] = _pcol(inp[L + "bk"]).astype(np.float32)
    m["bv"] = _row(inp[L + "bv"]).astype(np.float32)
    m["bo"] = _pcol(inp[L + "bo"]).astype(np.float32)
    m = {k: (np.ascontiguousarray(v) if v.dtype != np.float64 else
             np.ascontiguousarray(v).astype(np.float32)) for k, v in m.items()}
    m["xin"] = xT_c
    return m


def run_layer01(layer, inp, prep, xT, debug=False):
    nc = build_layer01(layer, _consts(), debug=debug)
    in_maps = [layer01_inmap(layer, inp, prep, xT[c]) for c in range(8)]
    res = run_bass_kernel_spmd(nc, in_maps, core_ids=list(range(8)))
    if debug:
        return res.results
    return [res.results[c]["xout"] for c in range(8)]


def kernel(**inputs):
    inp = {k: np.asarray(v) for k, v in inputs.items()}
    prep = run_prep(inp)
    x = inp["L0_attn_x"]
    xT = [np.ascontiguousarray(x[c].T).astype(np.float32) for c in range(8)]
    xT = run_layer01(0, inp, prep, xT)
    xT = run_layer01(1, inp, prep, xT)
    xT = run_layer2(inp, prep, xT)
    xT = run_layer3(inp, prep, xT)
    out = np.stack([np.ascontiguousarray(np.asarray(t).T) for t in xT])
    return out.astype(np.float32)




def l3_router_hook(em, C, st, io):
    def router(em, C, c, xch, post=False):
        nc = em.nc
        if not post:
            mm_kseries(em, st["ps_l"][:, :],
                       [(st["gwT_sb"][:, f * 8:(f + 1) * 8],
                         xch[:, f * 512:(f + 1) * 512]) for f in range(NF)])
            return
        r_mean = st["r_mean"]
        r_rstd = st["r_rstd"]
        em.te.matmul(st["ps_b8"][0:8, :], C["onesrow_f"][0:1, 0:8], r_mean,
                     start=True, stop=True)
        em.barrier()
        em.ve.tensor_scalar(st["t8"][0:8, :], st["ps_b8"][0:8, :],
                            st["gsum_sb"][0:8, 0:1], None, ALU.mult)
        em.barrier()
        em.ve.tensor_tensor(st["t8b"][0:8, :], st["ps_l"][0:8, :], st["t8"][0:8, :],
                            ALU.subtract)
        em.barrier()
        em.te.matmul(st["ps_b8"][0:8, :], C["onesrow_f"][0:1, 0:8], r_rstd,
                     start=True, stop=True)
        em.barrier()
        em.ve.tensor_tensor(st["logits_sb"][0:8, c * 512:(c + 1) * 512],
                            st["t8b"][0:8, :], st["ps_b8"][0:8, :], ALU.mult)
        em.barrier()
    return router


def emit_l3_top2(em, C, st):
    nc = em.nc
    with ExitStack() as _es:
        lgT = _es.enter_context(nc.sbuf_tensor(nm("lgT"), [128, 8], F32))
        msk = _es.enter_context(nc.sbuf_tensor(nm("msk"), [128, 8], F32))
        wfT = _es.enter_context(nc.sbuf_tensor(nm("wfT"), [128, 8], F32))
        aa = _es.enter_context(nc.sbuf_tensor(nm("aa"), [128, 8], F32))
        m1 = _es.enter_context(nc.sbuf_tensor(nm("m1"), [128, 8], F32))
        ps_t = _es.enter_context(nc.psum_tensor(nm("ps_t"), [128, 128], F32))
        for t in range(16):
            em.te.transpose(ps_t[0:128, 0:8],
                            st["logits_sb"][0:8, t * 128:(t + 1) * 128],
                            C["ident_f32"][0:8, 0:8])
            em.barrier()
            em.ve.tensor_copy(lgT[:, 0:8], ps_t[0:128, 0:8])
            em.barrier()
            em.ve.reduce_max(m1[:, 0:1], lgT[:, 0:8], axis=AX.X)
            em.barrier()
            em.ve.tensor_scalar(msk[:, 0:8], lgT[:, 0:8], m1[:, 0:1], None,
                                ALU.is_equal)
            em.barrier()
            em.ve.tensor_scalar(aa[:, 0:8], msk[:, 0:8], 2.0e30, None, ALU.mult)
            em.barrier()
            em.ve.tensor_tensor(aa[:, 0:8], lgT[:, 0:8], aa[:, 0:8], ALU.subtract)
            em.barrier()
            em.ve.reduce_max(m1[:, 3:4], aa[:, 0:8], axis=AX.X)
            em.barrier()
            em.ve.tensor_scalar(m1[:, 4:5], m1[:, 0:1], -1.0, None, ALU.mult)
            em.barrier()
            em.se.activation(m1[:, 5:6], m1[:, 3:4], AF.Exp, bias=m1[:, 4:5])
            em.barrier()
            em.ve.tensor_scalar(m1[:, 6:7], m1[:, 5:6], 1.0, None, ALU.add)
            em.barrier()
            em.ve.reciprocal(m1[:, 6:7], m1[:, 6:7])
            em.barrier()
            em.ve.tensor_tensor(m1[:, 7:8], m1[:, 5:6], m1[:, 6:7], ALU.mult)
            em.barrier()
            em.ve.tensor_scalar(wfT[:, 0:8], msk[:, 0:8], m1[:, 6:7], None,
                                ALU.mult)
            em.barrier()
            em.ve.tensor_scalar(msk[:, 0:8], aa[:, 0:8], m1[:, 3:4], None,
                                ALU.is_equal, )
            em.barrier()
            em.ve.tensor_scalar(msk[:, 0:8], msk[:, 0:8], m1[:, 7:8], None,
                                ALU.mult)
            em.barrier()
            em.ve.tensor_tensor(wfT[:, 0:8], wfT[:, 0:8], msk[:, 0:8], ALU.add)
            em.barrier()
            em.te.transpose(ps_t[0:8, 0:128], wfT[:, 0:8], C["ident_f32"][:, :])
            em.barrier()
            em.ve.tensor_copy(st["wfull_bf"][0:8, t * 128:(t + 1) * 128],
                              ps_t[0:8, 0:128])
            em.barrier()


def emit_ffn_l3(em, C, st, xn_sb, io, acc_out):
    nc = em.nc
    NO = 44
    with ExitStack() as _es:
        a1T = _es.enter_context(nc.sbuf_tensor(nm("a1T"), [128, NF * 128], BF16))
        a3T = _es.enter_context(nc.sbuf_tensor(nm("a3T"), [128, NF * 128], BF16))
        ust = _es.enter_context(nc.sbuf_tensor(nm("ust"), [16, 512], BF16))
        ps_u = _es.enter_context(nc.psum_tensor(nm("ps_u"), [16, 512], F32))
        em.dma(em.gp, a1T[:].rearrange("p (f c) -> p f c", f=NF),
               io["a1T"].rearrange("(f p) c -> p f c", p=128), "d_w")
        em.dma(em.gp, a3T[:].rearrange("p (f c) -> p f c", f=NF),
               io["a3T"].rearrange("(f p) c -> p f c", p=128), "d_w")
        em.hard()
        for e in range(8):
            for src, dst in ((a1T, "u1_hbm"), (a3T, "u3_hbm")):
                for c in range(4):
                    mm_kseries(em, ps_u[0:16, :],
                               [(src[:, f * 128 + 16 * e: f * 128 + 16 * e + 16],
                                 xn_sb[:, f * S + c * 512: f * S + (c + 1) * 512])
                                for f in range(NF)])
                    em.barrier()
                    em.se.activation(ust[0:16, :], ps_u[0:16, :], AF.Identity,
                                     scale=2.0)
                    em.barrier()
                    em.dma(em.sy, io[dst][e * 16:(e + 1) * 16, c * 512:(c + 1) * 512],
                           ust[0:16, :], "d_sc")
                    em.hard()
    with ExitStack() as _es:
        w1c = _es.enter_context(nc.sbuf_tensor(nm("w1c"), [128, NF * 128], BF16))
        w3c = _es.enter_context(nc.sbuf_tensor(nm("w3c"), [128, NF * 128], BF16))
        stg = _es.enter_context(nc.sbuf_tensor(nm("stg"), [128, 512], BF16))
        ps = _es.enter_context(nc.psum_tensor(nm("ps"), [128, 512], F32))
        for f in range(NO):
            em.dma(em.gp, w1c[:].rearrange("p (k c) -> p k c", k=NF),
                   io["w1T"][:, f * 128:(f + 1) * 128].rearrange("(k p) c -> p k c", p=128),
                   "d_w")
            em.dma(em.gp, w3c[:].rearrange("p (k c) -> p k c", k=NF),
                   io["w3T"][:, f * 128:(f + 1) * 128].rearrange("(k p) c -> p k c", p=128),
                   "d_w")
            em.hard()
            for src, dst in ((w1c, "base1_hbm"), (w3c, "base3_hbm")):
                for c in range(4):
                    mm_kseries(em, ps[:, :],
                               [(src[:, k * 128:(k + 1) * 128],
                                 xn_sb[:, k * S + c * 512: k * S + (c + 1) * 512])
                                for k in range(NF)])
                    em.barrier()
                    em.se.activation(stg[:, :], ps[:, :], AF.Identity)
                    em.barrier()
                    em.dma(em.sy, io[dst][f * 128:(f + 1) * 128, c * 512:(c + 1) * 512],
                           stg[:, :], "d_sc")
                    em.hard()
    smm = em.sem("f3_mm")
    with ExitStack() as _es:
        b1e = _es.enter_context(nc.sbuf_tensor(nm("b1e"), [16, 5632], BF16))
        b3e = _es.enter_context(nc.sbuf_tensor(nm("b3e"), [16, 5632], BF16))
        u1e = _es.enter_context(nc.sbuf_tensor(nm("u1e"), [16, S], BF16))
        u3e = _es.enter_context(nc.sbuf_tensor(nm("u3e"), [16, S], BF16))
        a2T = _es.enter_context(nc.sbuf_tensor(nm("a2T"), [128, NO * 128], BF16))
        base1 = _es.enter_context(nc.sbuf_tensor(nm("base1"), [128, S], BF16))
        base3 = _es.enter_context(nc.sbuf_tensor(nm("base3"), [128, S], BF16))
        web = _es.enter_context(nc.sbuf_tensor(nm("web"), [128, S], F32))
        w16 = _es.enter_context(nc.sbuf_tensor(nm("w16"), [16, S], F32))
        t1 = _es.enter_context(nc.sbuf_tensor(nm("t1"), [128, 2 * 512], F32))
        s1 = _es.enter_context(nc.sbuf_tensor(nm("s1"), [128, 2 * 512], BF16))
        t3 = _es.enter_context(nc.sbuf_tensor(nm("t3"), [128, 2 * 512], BF16))
        actb = _es.enter_context(nc.sbuf_tensor(nm("actb"), [128, 2 * 512], BF16))
        actw = _es.enter_context(nc.sbuf_tensor(nm("actw"), [128, S], BF16))
        u2st = _es.enter_context(nc.sbuf_tensor(nm("u2st"), [16, 512], BF16))
        ps_d1 = _es.enter_context(nc.psum_tensor(nm("ps_d1"), [128, 512], F32))
        ps_d3 = _es.enter_context(nc.psum_tensor(nm("ps_d3"), [128, 512], F32))
        ps_u2 = [
            _es.enter_context(nc.psum_tensor(nm("ps_u2"), [16, 512], F32))
            for _ in range(4)]
        ps_w = _es.enter_context(nc.psum_tensor(nm("ps_w"), [128, 512], F32))
        em.dma(em.gp, a2T[:].rearrange("p (f c) -> p f c", f=NO),
               io["a2T"].rearrange("(f p) c -> p f c", p=128), "d_w")
        em.hard()
        for e in range(8):
            em.dma(em.gp, b1e[0:16, :], io["b1T"][e, :, :], "d_w")
            em.dma(em.gp, b3e[0:16, :], io["b3T"][e, :, :], "d_w")
            em.dma(em.sy, u1e[0:16, :], io["u1_hbm"][e * 16:(e + 1) * 16, :], "d_w")
            em.dma(em.sy, u3e[0:16, :], io["u3_hbm"][e * 16:(e + 1) * 16, :], "d_w")
            em.hard()
            for c in range(4):
                em.te.matmul(ps_w[:, :], st["sel8_sb"][:, e * 128:(e + 1) * 128],
                             st["wfull_bf"][0:8, c * 512:(c + 1) * 512],
                             start=True, stop=True)
                em.barrier()
                em.ve.tensor_copy(web[:, c * 512:(c + 1) * 512], ps_w[:, :])
                em.barrier()
                em.te.matmul(ps_w[0:16, :],
                             st["sel8_sb"][:, e * 128:e * 128 + 16],
                             st["wfull_bf"][0:8, c * 512:(c + 1) * 512],
                             start=True, stop=True)
                em.barrier()
                em.ve.tensor_copy(w16[0:16, c * 512:(c + 1) * 512], ps_w[0:16, :])
                em.barrier()
            gpdone = []
            for f in range(NO):
                em.wait(em.sy, "d_act")
                em.wait(em.sy, "q_ve")
                em.dma(em.sy, base1[:].rearrange("p (k c) -> p k c", k=4),
                       io["base1_hbm"][f * 128:(f + 1) * 128, :].rearrange("p (k c) -> p k c", k=4),
                       "d_act")
                em.dma(em.sy, base3[:].rearrange("p (k c) -> p k c", k=4),
                       io["base3_hbm"][f * 128:(f + 1) * 128, :].rearrange("p (k c) -> p k c", k=4),
                       "d_act")
                for eng in (em.te, em.ve, em.gp):
                    em.wait(eng, "d_act")
                for c in range(4):
                    csl = slice(c * 512, (c + 1) * 512)
                    pd1, pd3 = (ps_d1, ps_d3)
                    em.te.matmul(pd1[:, :], b1e[0:16, f * 128:(f + 1) * 128],
                                 u1e[0:16, csl], start=True, stop=True).then_inc(smm, 1)
                    em.bump("f3_mm")
                    em.te.matmul(pd3[:, :], b3e[0:16, f * 128:(f + 1) * 128],
                                 u3e[0:16, csl], start=True,
                                 stop=True).then_inc(smm, 1)
                    em.bump("f3_mm")
                    tb = t1[:, (c % 2) * 512:(c % 2 + 1) * 512]
                    sb1 = s1[:, (c % 2) * 512:(c % 2 + 1) * 512]
                    tb3 = t3[:, (c % 2) * 512:(c % 2 + 1) * 512]
                    ab = actb[:, (c % 2) * 512:(c % 2 + 1) * 512]
                    if gpdone:
                        em.wait(em.ve, "q_gp", gpdone[-1])
                    em.wait(em.ve, "f3_mm")
                    em.ve.tensor_tensor(tb, pd1[:, :], base1[:, csl], ALU.add)
                    em.ve.tensor_tensor(tb3, pd3[:, :], base3[:, csl], ALU.add)
                    em.wait(em.se, "q_ve")
                    em.se.activation(sb1, tb, AF.Silu)
                    em.wait(em.ve, "q_se")
                    em.ve.tensor_tensor(ab, sb1, tb3, ALU.mult)
                    em.wait(em.gp, "q_ve")
                    em.gp.tensor_tensor(actw[:, csl], ab, web[:, csl], ALU.mult)
                    gpdone.append(em.cnt("q_gp"))
                    em.wait(em.te, "q_gp")
                    em.te.matmul(ps_u2[c][0:16, :],
                                 a2T[:, f * 128 + 16 * e: f * 128 + 16 * e + 16],
                                 actw[:, csl], start=(f == 0), stop=(f == NO - 1))
                em.wait(em.gp, "q_gp")
                if e == 0:
                    em.dma(em.gp, io["ACT_hbm"][f * 128:(f + 1) * 128, :],
                           actw[:, :], "d_act")
                else:
                    h = em.sem("d_act_g")
                    em.gp._eng.dma_start(io["ACT_hbm"][f * 128:(f + 1) * 128, :],
                                         actw[:, :], accum_op=ALU.add).then_inc(h, 16)
                    em.bump("d_act_g", 16)
            em.barrier()
            em.barrier()
            for c in range(4):
                em.ve.scalar_tensor_tensor(u2st[0:16, :], ps_u2[c][0:16, :], 2.0,
                                           w16[0:16, c * 512:(c + 1) * 512],
                                           ALU.mult, ALU.mult)
                em.barrier()
                em.dma(em.sy, io["u2w_hbm"][e * 16:(e + 1) * 16,
                                            c * 512:(c + 1) * 512], u2st[0:16, :],
                       "d_sc")
                em.hard()
    with ExitStack() as _es:
        rhs = _es.enter_context(nc.sbuf_tensor(nm("rhs"), [128, NO * 512], BF16))
        dc = _es.enter_context(nc.sbuf_tensor(nm("dc"), [128, NO * 128], BF16))
        u2w = _es.enter_context(nc.sbuf_tensor(nm("u2w"), [16, 8 * 512], BF16))
        b2o = _es.enter_context(nc.sbuf_tensor(nm("b2o"), [16, 8 * 128], BF16))
        stg = _es.enter_context(nc.sbuf_tensor(nm("stg"), [128, 512], F32))
        ps = _es.enter_context(nc.psum_tensor(nm("ps"), [128, 512], F32))
        for c in range(4):
            em.dma(em.gp, rhs[:].rearrange("p (f c) -> p f c", f=NO),
                   io["ACT_hbm"][:, c * 512:(c + 1) * 512].rearrange("(f p) c -> p f c", p=128),
                   "d_w")
            em.dma(em.sy, u2w[0:16, :].rearrange("p (e c) -> p e c", e=8),
                   io["u2w_hbm"][:, c * 512:(c + 1) * 512].rearrange("(e p) c -> p e c", p=16),
                   "d_w")
            em.hard()
            for o in range(NF):
                em.dma(em.gp, dc[:].rearrange("p (f c) -> p f c", f=NO),
                       io["w2T"][:, o * 128:(o + 1) * 128].rearrange("(f p) c -> p f c", p=128),
                       "d_w")
                em.dma(em.gp, b2o[0:16, :].rearrange("p (e c) -> p e c", e=8),
                       io["b2T"].rearrange("r (e n) -> r e n", e=8)[:, :, o * 128:(o + 1) * 128],
                       "d_w")
                em.hard()
                pairs = [(dc[:, f * 128:(f + 1) * 128], rhs[:, f * 512:(f + 1) * 512])
                         for f in range(NO)]
                pairs += [(b2o[0:16, e * 128:(e + 1) * 128],
                           u2w[0:16, e * 512:(e + 1) * 512]) for e in range(8)]
                mm_kseries(em, ps[:, :], pairs)
                em.barrier()
                em.se.activation(stg[:, :], ps[:, :], AF.Identity)
                em.barrier()
                em.dma(em.sy, acc_out[o * 128:(o + 1) * 128, c * 512:(c + 1) * 512],
                       stg[:, :], "d_sc")
                em.hard()


def build_layer3(consts, debug=False):
    nc = bass.Bass("TRN2", target_bir_lowering=False)
    em = Em(nc)
    dk = "ExternalOutput" if debug else "Internal"
    specs = [
        ("xin", (D, S), F32, "ExternalInput"),
        ("xout", (D, S), F32, "ExternalOutput"),
        ("wq", (D, D), F32, "ExternalInput"),
        ("wk", (D, 512), F32, "ExternalInput"),
        ("wv", (D, 512), F32, "ExternalInput"),
        ("wo", (D, D), F32, "ExternalInput"),
        ("bo", (128, NF), F32, "ExternalInput"),
        ("gwT", (D, 8), F32, "ExternalInput"),
        ("w1T", (D, 5632), BF16, "ExternalInput"),
        ("w3T", (D, 5632), BF16, "ExternalInput"),
        ("w2T", (5632, D), BF16, "ExternalInput"),
        ("a1T", (D, 128), F32, "ExternalInput"),
        ("a3T", (D, 128), F32, "ExternalInput"),
        ("a2T", (5632, 128), F32, "ExternalInput"),
        ("b1T", (8, 16, 5632), F32, "ExternalInput"),
        ("b3T", (8, 16, 5632), F32, "ExternalInput"),
        ("b2T", (16, 8 * D), F32, "ExternalInput"),
        ("accA", (D, S), F32, dk),
        ("accF", (D, S), F32, dk),
        ("q_hbm", (D, S), BF16, "Internal"),
        ("k_hbm", (512, S), BF16, "Internal"),
        ("v_hbm", (S, 512), BF16, "Internal"),
        ("f_hbm", (D, S), BF16, "Internal"),
        ("base1_hbm", (5632, S), BF16, "Internal"),
        ("base3_hbm", (5632, S), BF16, "Internal"),
        ("ACT_hbm", (5632, S), BF16, dk),
        ("u1_hbm", (128, S), BF16, "Internal"),
        ("u3_hbm", (128, S), BF16, "Internal"),
        ("u2w_hbm", (128, S), BF16, dk),
    ]
    io = _io_tensors(nc, specs)
    cfg = ATTN_CFG[3]
    C = Consts(em, ["onescol_f", "onesrow_f", "onesrow_b", "ident_f32", "sel8",
                    "mask512_r0", "mask512_r1", "mask512_r2", "mask512_r3"], consts)
    wd_attn = dict(wq=io["wq"], wk=io["wk"], wv=io["wv"], wo=io["wo"], bo=io["bo"])
    ast = attn_alloc(em, cfg)
    st = {}
    st["gwT_sb"] = nc.alloc_sbuf_tensor(nm("gwT"), [128, NF * 8], F32)
    st["gsum_sb"] = nc.alloc_sbuf_tensor(nm("gsum"), [8, 1], F32)
    st["logits_sb"] = nc.alloc_sbuf_tensor(nm("logits"), [8, S], F32)
    st["wfull_bf"] = nc.alloc_sbuf_tensor(nm("wfull"), [8, S], BF16)
    st["t8"] = nc.alloc_sbuf_tensor(nm("t8"), [8, 512], F32)
    st["t8b"] = nc.alloc_sbuf_tensor(nm("t8b"), [8, 512], F32)
    st["sel8_sb"] = C["sel8"]
    em.dma(em.sy, st["gwT_sb"][:].rearrange("p (f c) -> p f c", f=NF),
           io["gwT"].rearrange("(f p) c -> p f c", p=128), "d_w")
    em.hard()
    em.barrier()
    mean_sb = nc.alloc_sbuf_tensor(nm("p3mean"), [1, S], F32)
    rstd_sb = nc.alloc_sbuf_tensor(nm("p3rstd"), [1, S], F32)
    with nc.sbuf_tensor(nm("xn"), [128, NF * S], BF16) as xn_sb:
        with ExitStack() as _es:
            ps_l = _es.enter_context(nc.psum_tensor(nm("ps_l"), [8, 512], F32))
            ps_b8 = _es.enter_context(nc.psum_tensor(nm("ps_b8"), [8, 512], F32))
            st["ps_l"] = ps_l
            st["ps_b8"] = ps_b8
            mm_kseries(em, ps_b8[0:8, 0:1],
                       [(st["gwT_sb"][:, f * 8:(f + 1) * 8], C["onescol_f"][:, :])
                        for f in range(NF)])
            em.barrier()
            em.ve.tensor_copy(st["gsum_sb"][0:8, :], ps_b8[0:8, 0:1])
            em.barrier()
            rows_holder = {}

            def router(em_, C_, c, xch, post=False):
                if not post:
                    mm_kseries(em_, ps_l[0:8, :],
                               [(st["gwT_sb"][:, f * 8:(f + 1) * 8],
                                 xch[:, f * 512:(f + 1) * 512]) for f in range(NF)])
                    return
                r_mean, r_rstd = rows_holder["mean"], rows_holder["rstd"]
                em_.te.matmul(ps_b8[0:8, :], C_["onesrow_f"][0:1, 0:8], r_mean,
                              start=True, stop=True)
                em_.barrier()
                em_.ve.tensor_scalar(st["t8"][0:8, :], ps_b8[0:8, :],
                                     st["gsum_sb"][0:8, 0:1], None, ALU.mult)
                em_.barrier()
                em_.ve.tensor_tensor(st["t8b"][0:8, :], ps_l[0:8, :],
                                     st["t8"][0:8, :], ALU.subtract)
                em_.barrier()
                em_.te.matmul(ps_b8[0:8, :], C_["onesrow_f"][0:1, 0:8], r_rstd,
                              start=True, stop=True)
                em_.barrier()
                em_.ve.tensor_tensor(st["logits_sb"][0:8, c * 512:(c + 1) * 512],
                                     st["t8b"][0:8, :], ps_b8[0:8, :], ALU.mult)
                em_.barrier()

            def router3(em_, C_, c, xch, post=False):
                if post:
                    rows_holder["mean"] = mean_sb[:, c * 512:(c + 1) * 512]
                    rows_holder["rstd"] = rstd_sb[:, c * 512:(c + 1) * 512]
                router(em_, C_, c, xch, post=post)

            emit_phase0_l3(em, C, {"x_in": io["xin"]}, xn_sb, router3,
                           mean_sb, rstd_sb)
        emit_l3_top2(em, C, st)
        emit_ffn_l3(em, C, st, xn_sb, io, io["accF"])
        emit_attn_proj(em, C, ast, xn_sb, cfg, wd_attn, io["q_hbm"],
                       io["v_hbm"], io["k_hbm"])
    emit_attn_core(em, C, ast, cfg, io["q_hbm"], io["v_hbm"], io["k_hbm"],
                   io["f_hbm"], io["accA"], wd_attn)
    emit_phase0(em, C, {"x_in": io["xin"]}, adds=(io["accA"], io["accF"]),
                out_x=io["xout"], combine_only=True)
    em.hard()
    return nc


def emit_phase0_l3(em, C, io, xn_sb, router3, mean_sb, rstd_sb):
    emit_phase0(em, C, io, xn_sb=xn_sb, router=router3,
                save_rows={"mean": mean_sb, "rstd": rstd_sb})


def run_layer3(inp, prep, xT, debug=False):
    nc = build_layer3(_consts(), debug=debug)
    m = dict(
        wq=_t(inp["L3_attn_Wq"]), wk=_t(inp["L3_attn_Wk"]),
        wv=_t(inp["L3_attn_Wv"]), wo=_t(inp["L3_attn_Wo"]),
        bo=_pcol(inp["L3_attn_bo"]),
        gwT=_t(inp["L3_ffn_gate_w"]),
        w1T=prep["l3w1_bf"], w3T=prep["l3w3_bf"], w2T=prep["l3w2_bf"],
        a1T=np.ascontiguousarray(np.transpose(inp["L3_ffn_a1"], (2, 0, 1)).reshape(D, 128)),
        a3T=np.ascontiguousarray(np.transpose(inp["L3_ffn_a3"], (2, 0, 1)).reshape(D, 128)),
        a2T=np.ascontiguousarray(np.transpose(inp["L3_ffn_a2"], (2, 0, 1)).reshape(5632, 128)),
        b1T=np.ascontiguousarray(np.transpose(inp["L3_ffn_b1"], (0, 2, 1))),
        b3T=np.ascontiguousarray(np.transpose(inp["L3_ffn_b3"], (0, 2, 1))),
        b2T=np.ascontiguousarray(np.transpose(inp["L3_ffn_b2"], (2, 0, 1)).reshape(16, 8 * D)),
    )
    m = {k: np.ascontiguousarray(v).astype(np.float32)
         if np.asarray(v).dtype == np.float64 or np.asarray(v).dtype == np.float32
         else np.ascontiguousarray(v) for k, v in m.items()}
    in_maps = []
    for c in range(8):
        mc = dict(m)
        mc["xin"] = xT[c]
        in_maps.append(mc)
    res = run_bass_kernel_spmd(nc, in_maps, core_ids=list(range(8)))
    if debug:
        return res.results
    return [res.results[c]["xout"] for c in range(8)]




def build_layer2(consts, debug=False):
    nc = bass.Bass("TRN2", target_bir_lowering=False)
    em = Em(nc)
    dk = "ExternalOutput" if debug else "Internal"
    specs = [
        ("xin", (D, S), F32, "ExternalInput"),
        ("xout", (D, S), F32, "ExternalOutput"),
        ("wq", (D, D), F32, "ExternalInput"),
        ("wk", (D, D), F32, "ExternalInput"),
        ("wv", (D, D), F32, "ExternalInput"),
        ("wz", (D, D), F32, "ExternalInput"),
        ("wab", (D, 32), F32, "ExternalInput"),
        ("convw", (D, 12), F32, "ExternalInput"),
        ("normw", (128, NF), F32, "ExternalInput"),
        ("wout", (D, D), F32, "ExternalInput"),
        ("wr", (128, NF), F32, "ExternalInput"),
        ("br", (1, 1), F32, "ExternalInput"),
        ("w1", (D, 8192), BF16, "ExternalInput"),
        ("b1", (128, 64), F32, "ExternalInput"),
        ("w2", (8192, D), BF16, "ExternalInput"),
        ("b2", (128, NF), F32, "ExternalInput"),
        ("accA", (D, S), F32, dk),
        ("accF", (D, S), F32, dk),
        ("qn_hbm", (D, S), BF16, dk),
        ("kn_hbm", (D, S), BF16, dk),
        ("v_hbm", (D, S), BF16, dk),
        ("z_hbm", (D, S), BF16, dk),
        ("wrow_dbg", (1, S), F32, dk),
        ("sel_dbg", (1, S), F32, dk),
    ]
    io = _io_tensors(nc, specs)
    C = Consts(em, ["onescol_f", "onesrow_f", "onesrow_b", "ident_f32", "selE",
                    "selB", "ropecos", "ropesin", "rope_R"], consts)
    ab_sb = nc.alloc_sbuf_tensor(nm("ab"), [16, 2 * (S + 3)], F32)
    wrow_sb = nc.alloc_sbuf_tensor(nm("wrow"), [1, S], F32)
    selw_sb = nc.alloc_sbuf_tensor(nm("selw"), [1, S], F32)
    comb_sb = nc.alloc_sbuf_tensor(nm("comb"), [1, 2 * S], F32)
    mean_sb = nc.alloc_sbuf_tensor(nm("mean2"), [1, S], F32)
    rstd_sb = nc.alloc_sbuf_tensor(nm("rstd2"), [1, S], F32)
    misc_sb = nc.alloc_sbuf_tensor(nm("misc2"), [128, 8], F32)
    wsum_sb = nc.alloc_sbuf_tensor(nm("wsum2"), [1, 1], F32)
    wr_sb = nc.alloc_sbuf_tensor(nm("wr2"), [128, NF], F32)
    normw_sb = nc.alloc_sbuf_tensor(nm("nw2"), [128, NF], F32)
    wab_sb = nc.alloc_sbuf_tensor(nm("wab2"), [128, NF * 32], BF16)
    convw_sb = nc.alloc_sbuf_tensor(nm("cw2"), [128, NF * 12], F32)
    em.dma(em.sy, wr_sb[:, :], io["wr"][:, :], "d_w")
    em.dma(em.sy, normw_sb[:, :], io["normw"][:, :], "d_w")
    em.dma(em.gp, wab_sb[:].rearrange("p (f c) -> p f c", f=NF),
           io["wab"].rearrange("(f p) c -> p f c", p=128), "d_w")
    em.dma(em.sy, convw_sb[:].rearrange("p (f c) -> p f c", f=NF),
           io["convw"].rearrange("(f p) c -> p f c", p=128), "d_w")
    em.dma(em.sy, wsum_sb[0:1, 0:1], io["br"][:, :], "d_w")
    em.hard()
    with nc.psum_tensor(nm("psg"), [1, 512], F32) as psg:
        mm_kseries(em, psg[0:1, 0:1],
                   [(wr_sb[:, f:f + 1], C["onescol_f"][:, :]) for f in range(NF)])
        em.barrier()
        em.ve.tensor_copy(misc_sb[0:1, 0:1], psg[0:1, 0:1])
        em.barrier()
    br_sb = wsum_sb
    rows_holder = {}

    def router(em_, C_, c, xch, post=False):
        if not post:
            mm_kseries(em_, rt["ps_w"][0:1, :],
                       [(wr_sb[:, f:f + 1], xch[:, f * 256:(f + 1) * 256])
                        for f in range(NF)])
            return
        tsl = slice(c * 256, (c + 1) * 256)
        r_mean, r_rstd = rows_holder["mean"], rows_holder["rstd"]
        em_.ve.tensor_scalar(rt["t1"][0:1, :], r_mean, misc_sb[0:1, 0:1], None,
                             ALU.mult)
        em_.barrier()
        em_.ve.tensor_tensor(rt["t1"][0:1, :], rt["ps_w"][0:1, :], rt["t1"][0:1, :],
                             ALU.subtract)
        em_.barrier()
        em_.ve.tensor_tensor(rt["t1"][0:1, :], rt["t1"][0:1, :], r_rstd, ALU.mult)
        em_.barrier()
        em_.ve.tensor_scalar(wrow_sb[0:1, tsl], rt["t1"][0:1, :],
                             br_sb[0:1, 0:1], None, ALU.add)
        em_.barrier()

    def router3(em_, C_, c, xch, post=False):
        if post:
            rows_holder["mean"] = mean_sb[:, c * 256:(c + 1) * 256]
            rows_holder["rstd"] = rstd_sb[:, c * 256:(c + 1) * 256]
        router(em_, C_, c, xch, post=post)

    rt = {}
    with nc.sbuf_tensor(nm("xn"), [128, NF * S], BF16) as xn_sb:
        with ExitStack() as _es:
            rt["ps_w"] = _es.enter_context(nc.psum_tensor(nm("ps_w"), [1, 256], F32))
            rt["t1"] = _es.enter_context(nc.sbuf_tensor(nm("rt_t1"), [1, 256], F32))
            emit_phase0(em, C, {"x_in": io["xin"]}, xn_sb=xn_sb, router=router3,
                        save_rows={"mean": mean_sb, "rstd": rstd_sb}, ct=256)
        emit_l2_select(em, C, io, wrow_sb, selw_sb, comb_sb, mean_sb, rstd_sb)
        emit_l2_ffn(em, C, io, xn_sb, selw_sb)
        emit_l2_attnproj(em, C, io, xn_sb, ab_sb, wab_sb, convw_sb)
    emit_l2_attncore(em, C, io, ab_sb, normw_sb)
    def l2fix(em_, C_, c, xch):
        with ExitStack() as _es:
            ps_a = _es.enter_context(nc.psum_tensor(nm("psfa"), [128, 256], F32))
            ps_b = _es.enter_context(nc.psum_tensor(nm("psfb"), [128, 256], F32))
            em_.te.matmul(ps_a[:, :], C_["onesrow_f"][:, :], comb_sb[0:1, c * 256:(c + 1) * 256],
                          start=True, stop=True)
            em_.te.matmul(ps_b[:, :], C_["onesrow_f"][:, :], comb_sb[0:1, S + c * 256:S + (c + 1) * 256],
                          start=True, stop=True)
            em_.barrier()
            for f in range(NF):
                em_.ve.tensor_tensor(xch[:, f * 256:(f + 1) * 256],
                                     xch[:, f * 256:(f + 1) * 256], ps_a[:, :],
                                     ALU.mult)
            em_.barrier()
            for f in range(NF):
                em_.ve.tensor_tensor(xch[:, f * 256:(f + 1) * 256],
                                     xch[:, f * 256:(f + 1) * 256], ps_b[:, :],
                                     ALU.add)
            em_.barrier()
    emit_phase0(em, C, {"x_in": io["xin"]}, adds=(io["accA"], io["accF"]),
                out_x=io["xout"], combine_only=True, l2fix=l2fix, ct=256)
    if debug:
        em.ve.tensor_copy(wrow_sb[0:1, 0:1], wrow_sb[0:1, 0:1])
        em.dma(em.sy, io["wrow_dbg"][:, :], wrow_sb[0:1, :], "d_sc")
        em.dma(em.sy, io["sel_dbg"][:, :], selw_sb[0:1, :], "d_sc")
    em.hard()
    return nc


def emit_l2_select(em, C, io, wrow_sb, selw_sb, comb_sb, mean_sb, rstd_sb):
    nc = em.nc
    with ExitStack() as _es:
        wcol = _es.enter_context(nc.sbuf_tensor(nm("wcol"), [128, 16], F32))
        wb = _es.enter_context(nc.sbuf_tensor(nm("wb"), [128, S], F32))
        cmp = _es.enter_context(nc.sbuf_tensor(nm("cmp"), [128, 512], F32))
        rowt = _es.enter_context(nc.sbuf_tensor(nm("rowt"), [1, 2 * S], F32))
        ps_t = _es.enter_context(nc.psum_tensor(nm("ps_t2"), [128, 128], F32))
        ps_c = _es.enter_context(nc.psum_tensor(nm("ps_c2"), [1, 512], F32))
        ps_bx = _es.enter_context(nc.psum_tensor(nm("ps_bx"), [128, 512], F32))
        for i in range(16):
            em.te.transpose(ps_t[0:128, 0:1], wrow_sb[0:1, i * 128:(i + 1) * 128],
                            C["ident_f32"][0:1, 0:1])
            em.barrier()
            em.ve.tensor_copy(wcol[:, i:i + 1], ps_t[0:128, 0:1])
            em.barrier()
        for c in range(4):
            em.te.matmul(ps_bx[:, :], C["onesrow_f"][:, :],
                         wrow_sb[0:1, c * 512:(c + 1) * 512], start=True, stop=True)
            em.barrier()
            em.ve.tensor_copy(wb[:, c * 512:(c + 1) * 512], ps_bx[:, :])
            em.barrier()
        for c in range(4):
            tsl = slice(c * 512, (c + 1) * 512)
            for i in range(16):
                em.ve.tensor_scalar(cmp[:, :], wb[:, tsl], wcol[:, i:i + 1], None,
                                    ALU.is_lt)
                em.barrier()
                em.te.matmul(ps_c[0:1, :], C["onescol_f"][:, :], cmp[:, :],
                             start=(i == 0), stop=(i == 15))
                em.barrier()
            em.ve.tensor_scalar(rowt[0:1, 0:512], ps_c[0:1, :], 510.5, None,
                                ALU.is_lt)
            em.barrier()
            em.ve.tensor_tensor(selw_sb[0:1, tsl], rowt[0:1, 0:512],
                                wrow_sb[0:1, tsl], ALU.mult)
            em.ve.tensor_scalar(rowt[0:1, 512:1024], rowt[0:1, 0:512], -1.0, 1.0,
                                ALU.mult, ALU.add)
            em.barrier()
            em.ve.tensor_tensor(rowt[0:1, 1024:1536], rowt[0:1, 512:1024],
                                rstd_sb[0:1, tsl], ALU.mult)
            em.barrier()
            em.ve.tensor_scalar(comb_sb[0:1, tsl], rowt[0:1, 1024:1536], 1.0, None,
                                ALU.add)
            em.ve.tensor_tensor(rowt[0:1, 1536:2048], rowt[0:1, 1024:1536],
                                mean_sb[0:1, tsl], ALU.mult)
            em.barrier()
            em.ve.tensor_scalar(comb_sb[0:1, S + c * 512:S + (c + 1) * 512],
                                rowt[0:1, 1536:2048], -1.0, None, ALU.mult)
            em.barrier()


def emit_l2_ffn(em, C, io, xn_sb, selw_sb):
    nc = em.nc
    GA = 0.7978845608028654
    GB = 0.044715
    CT = 256
    with ExitStack() as _es:
        wc = _es.enter_context(nc.sbuf_tensor(nm("wc"), [128, NF * 128], BF16))
        b1_sb = _es.enter_context(nc.sbuf_tensor(nm("b1s"), [128, 64], F32))
        b2_sb = _es.enter_context(nc.sbuf_tensor(nm("b2s"), [128, NF], F32))
        hch = _es.enter_context(nc.sbuf_tensor(nm("hch"), [128, 64 * CT], BF16))
        dc = _es.enter_context(nc.sbuf_tensor(nm("dc"), [128, 64 * 128], BF16))
        ug = _es.enter_context(nc.sbuf_tensor(nm("ug"), [128, CT], F32))
        t_ = _es.enter_context(nc.sbuf_tensor(nm("t_"), [128, CT], F32))
        stg = _es.enter_context(nc.sbuf_tensor(nm("stg"), [128, CT], F32))
        ps = _es.enter_context(nc.psum_tensor(nm("ps"), [128, CT], F32))
        ps_b = _es.enter_context(nc.psum_tensor(nm("ps_b2b"), [128, CT], F32))
        em.dma(em.sy, b1_sb[:, :], io["b1"][:, :], "d_w")
        em.dma(em.sy, b2_sb[:, :], io["b2"][:, :], "d_w")
        em.hard()
        for c in range(S // CT):
            tsl = slice(c * CT, (c + 1) * CT)
            for o in range(64):
                em.dma(em.gp, wc[:].rearrange("p (f x) -> p f x", f=NF),
                       io["w1"][:, o * 128:(o + 1) * 128].rearrange("(f p) x -> p f x", p=128),
                       "d_w")
                em.hard()
                mm_kseries(em, ps[:, :],
                           [(wc[:, f * 128:(f + 1) * 128],
                             xn_sb[:, f * S + c * CT: f * S + (c + 1) * CT])
                            for f in range(NF)])
                em.barrier()
                em.se.activation(ug[:, :], ps[:, :], AF.Identity,
                                 bias=b1_sb[:, o:o + 1])
                em.barrier()
                em.se.activation(t_[:, :], ug[:, :], AF.Square)
                em.barrier()
                em.ve.tensor_scalar(t_[:, :], t_[:, :], GA * GB, GA, ALU.mult,
                                    ALU.add)
                em.barrier()
                em.ve.tensor_tensor(t_[:, :], ug[:, :], t_[:, :], ALU.mult)
                em.barrier()
                em.se.activation(t_[:, :], t_[:, :], AF.Tanh)
                em.barrier()
                em.ve.tensor_scalar(t_[:, :], t_[:, :], 0.5, 0.5, ALU.mult, ALU.add)
                em.barrier()
                em.ve.tensor_tensor(hch[:, o * CT:(o + 1) * CT], ug[:, :], t_[:, :],
                                    ALU.mult)
                em.barrier()
            em.te.matmul(ps_b[:, :], C["onesrow_f"][:, :], selw_sb[0:1, tsl],
                         start=True, stop=True)
            em.barrier()
            for o2 in range(NF):
                em.dma(em.gp, dc[:].rearrange("p (f x) -> p f x", f=64),
                       io["w2"][:, o2 * 128:(o2 + 1) * 128].rearrange("(f p) x -> p f x", p=128),
                       "d_w")
                em.hard()
                mm_kseries(em, ps[:, :],
                           [(dc[:, f * 128:(f + 1) * 128],
                             hch[:, f * CT:(f + 1) * CT]) for f in range(64)])
                em.barrier()
                em.se.activation(stg[:, :], ps[:, :], AF.Identity,
                                 bias=b2_sb[:, o2:o2 + 1])
                em.barrier()
                em.ve.tensor_tensor(stg[:, :], stg[:, :], ps_b[:, :], ALU.mult)
                em.barrier()
                em.dma(em.sy, io["accF"][o2 * 128:(o2 + 1) * 128, tsl], stg[:, :],
                       "d_sc")
                em.hard()


def emit_l2_attnproj(em, C, io, xn_sb, ab_sb, wab_sb, convw_sb):
    nc = em.nc
    with ExitStack() as _es:
        wc = _es.enter_context(nc.sbuf_tensor(nm("wc"), [128, NF * 128], BF16))
        raw = _es.enter_context(nc.sbuf_tensor(nm("raw"), [128, S + 3], F32))
        cvb = _es.enter_context(nc.sbuf_tensor(nm("cvb"), [128, S], BF16))
        qr = _es.enter_context(nc.sbuf_tensor(nm("qr"), [128, 512], F32))
        t_ = _es.enter_context(nc.sbuf_tensor(nm("t_2"), [128, 512], F32))
        rows = _es.enter_context(nc.sbuf_tensor(nm("rows2"), [1, 512], F32))
        stg = _es.enter_context(nc.sbuf_tensor(nm("stg2"), [128, 512], BF16))
        ps = _es.enter_context(nc.psum_tensor(nm("ps2"), [128, 512], F32))
        ps_r = _es.enter_context(nc.psum_tensor(nm("ps_r2"), [128, 512], F32))
        ps_s = _es.enter_context(nc.psum_tensor(nm("ps_s2"), [1, 512], F32))
        ps_bx = _es.enter_context(nc.psum_tensor(nm("ps_bx2"), [128, 512], F32))
        em.ve.memset(raw[:, 0:3], 0.0)
        em.barrier()
        for name, w_ap, dst, do_rope in (
                ("q", io["wq"], io["qn_hbm"], True),
                ("k", io["wk"], io["kn_hbm"], True),
                ("v", io["wv"], io["v_hbm"], False)):
            cvcol = {"q": 0, "k": 4, "v": 8}[name]
            for h in range(16):
                em.dma(em.gp, wc[:].rearrange("p (f x) -> p f x", f=NF),
                       w_ap[:, h * 128:(h + 1) * 128].rearrange("(f p) x -> p f x", p=128),
                       "d_w")
                em.hard()
                for c in range(4):
                    mm_kseries(em, ps[:, :],
                               [(wc[:, f * 128:(f + 1) * 128],
                                 xn_sb[:, f * S + c * 512: f * S + (c + 1) * 512])
                                for f in range(NF)])
                    em.barrier()
                    em.se.activation(raw[:, 3 + c * 512:3 + (c + 1) * 512], ps[:, :],
                                     AF.Silu)
                    em.barrier()
                cw = convw_sb
                for c in range(4):
                    em.ve.tensor_scalar(t_[:, :], raw[:, 3 + c * 512 - 3: 3 + (c + 1) * 512 - 3],
                                        cw[:, h * 12 + cvcol: h * 12 + cvcol + 1],
                                        None, ALU.mult)
                    em.barrier()
                    for l in (2, 1, 0):
                        j = 3 - l
                        em.ve.scalar_tensor_tensor(
                            t_[:, :],
                            raw[:, 3 + c * 512 - l: 3 + (c + 1) * 512 - l],
                            cw[:, h * 12 + cvcol + j: h * 12 + cvcol + j + 1],
                            t_[:, :], ALU.mult, ALU.add)
                        em.barrier()
                    em.ve.tensor_copy(cvb[:, c * 512:(c + 1) * 512], t_[:, :])
                    em.barrier()
                if not do_rope:
                    em.dma(em.sy, dst[h * 128:(h + 1) * 128, :], cvb[:, :], "d_sc")
                    em.hard()
                    continue
                for c in range(4):
                    csl = slice(c * 512, (c + 1) * 512)
                    em.te.matmul(ps_r[:, :], C["rope_R"][:, :], cvb[:, csl],
                                 start=True, stop=True)
                    em.barrier()
                    em.ve.tensor_tensor(qr[:, :], cvb[:, csl],
                                        C["ropecos"][:, csl], ALU.mult)
                    em.barrier()
                    em.ve.tensor_tensor(t_[:, :], ps_r[:, :], C["ropesin"][:, csl],
                                        ALU.mult)
                    em.barrier()
                    em.ve.tensor_tensor(qr[:, :], qr[:, :], t_[:, :], ALU.add)
                    em.barrier()
                    em.se.activation(t_[:, :], qr[:, :], AF.Square)
                    em.barrier()
                    em.te.matmul(ps_s[0:1, :], C["onescol_f"][:, :], t_[:, :],
                                 start=True, stop=True)
                    em.barrier()
                    em.se.activation(rows[0:1, :], ps_s[0:1, :], AF.Sqrt)
                    em.barrier()
                    em.ve.tensor_scalar(rows[0:1, :], rows[0:1, :], 1e-12, None,
                                        ALU.max)
                    em.barrier()
                    em.ve.reciprocal(rows[0:1, :], rows[0:1, :])
                    em.barrier()
                    em.te.matmul(ps_bx[:, :], C["onesrow_f"][:, :], rows[0:1, :],
                                 start=True, stop=True)
                    em.barrier()
                    em.ve.tensor_tensor(stg[:, :], qr[:, :], ps_bx[:, :], ALU.mult)
                    em.barrier()
                    em.dma(em.sy, dst[h * 128:(h + 1) * 128, csl], stg[:, :], "d_sc")
                    em.hard()
        for h in range(16):
            em.dma(em.gp, wc[:].rearrange("p (f x) -> p f x", f=NF),
                   io["wz"][:, h * 128:(h + 1) * 128].rearrange("(f p) x -> p f x", p=128),
                   "d_w")
            em.hard()
            for c in range(4):
                mm_kseries(em, ps[:, :],
                           [(wc[:, f * 128:(f + 1) * 128],
                             xn_sb[:, f * S + c * 512: f * S + (c + 1) * 512])
                            for f in range(NF)])
                em.barrier()
                em.se.activation(stg[:, :], ps[:, :], AF.Silu)
                em.barrier()
                em.dma(em.sy, io["z_hbm"][h * 128:(h + 1) * 128, c * 512:(c + 1) * 512],
                       stg[:, :], "d_sc")
                em.hard()
        em.ve.memset(ab_sb[0:16, :], 0.0)
        em.barrier()
        with nc.psum_tensor(nm("ps_ab"), [16, 512], F32) as ps_ab:
            for c in range(4):
                mm_kseries(em, ps_ab[0:16, :],
                           [(wab_sb[:, f * 32:f * 32 + 16],
                             xn_sb[:, f * S + c * 512: f * S + (c + 1) * 512])
                            for f in range(NF)])
                em.barrier()
                em.ve.tensor_copy(ab_sb[0:16, 3 + c * 512: 3 + (c + 1) * 512],
                                  ps_ab[0:16, :])
                em.barrier()
                mm_kseries(em, ps_ab[0:16, :],
                           [(wab_sb[:, f * 32 + 16:f * 32 + 32],
                             xn_sb[:, f * S + c * 512: f * S + (c + 1) * 512])
                            for f in range(NF)])
                em.barrier()
                em.ve.tensor_copy(ab_sb[0:16, (S + 3) + 3 + c * 512: (S + 3) + 3 + (c + 1) * 512],
                                  ps_ab[0:16, :])
                em.barrier()
        em.hard()


def emit_l2_attncore(em, C, io, ab_sb, normw_sb):
    nc = em.nc
    ESC = 1.0 / math.sqrt(128.0)
    with ExitStack() as _es:
        qnb = _es.enter_context(nc.sbuf_tensor(nm("qnb"), [128, S], BF16))
        knb = _es.enter_context(nc.sbuf_tensor(nm("knb"), [128, S + 3], BF16))
        vb = _es.enter_context(nc.sbuf_tensor(nm("vb"), [128, S + 3], BF16))
        m_ = _es.enter_context(nc.sbuf_tensor(nm("m_"), [128, 512], BF16))
        gt = _es.enter_context(nc.sbuf_tensor(nm("gt"), [16, 512], F32))
        pb = _es.enter_context(nc.sbuf_tensor(nm("pb"), [16, 4 * 512], BF16))
        oacc = _es.enter_context(nc.sbuf_tensor(nm("oacc"), [128, NF * 512], F32))
        t_ = _es.enter_context(nc.sbuf_tensor(nm("t_3"), [128, 512], F32))
        zb = _es.enter_context(nc.sbuf_tensor(nm("zb"), [128, 512], BF16))
        wor = _es.enter_context(nc.sbuf_tensor(nm("wor"), [128, NF * 512], BF16))
        rows = _es.enter_context(nc.sbuf_tensor(nm("rows3"), [1, 512], F32))
        wc = _es.enter_context(nc.sbuf_tensor(nm("wc3"), [128, NF * 128], BF16))
        stg = _es.enter_context(nc.sbuf_tensor(nm("stg3"), [128, 512], F32))
        ps_at = [_es.enter_context(nc.psum_tensor(nm("ps_at"), [16, 512], F32))
                 for _ in range(4)]
        ps_w = _es.enter_context(nc.psum_tensor(nm("ps_w3"), [128, 512], F32))
        ps_s = _es.enter_context(nc.psum_tensor(nm("ps_s3"), [1, 512], F32))
        ps_bx = _es.enter_context(nc.psum_tensor(nm("ps_bx3"), [128, 512], F32))
        em.ve.memset(knb[:, 0:3], 0.0)
        em.ve.memset(vb[:, 0:3], 0.0)
        em.barrier()
        for c in range(4):
            tsl = slice(c * 512, (c + 1) * 512)
            for h in range(16):
                em.dma(em.sy, qnb[:, tsl], io["qn_hbm"][h * 128:(h + 1) * 128, tsl],
                       "d_sc")
                em.dma(em.sy, knb[:, 3:3 + S], io["kn_hbm"][h * 128:(h + 1) * 128, :],
                       "d_sc")
                em.hard()
                for s_ in range(4):
                    em.ve.tensor_tensor(
                        m_[:, :], qnb[:, tsl],
                        knb[:, 3 + c * 512 - s_: 3 + (c + 1) * 512 - s_], ALU.mult)
                    em.barrier()
                    em.te.matmul(ps_at[s_][0:16, :], C["selE"][:, h * 16:(h + 1) * 16],
                                 m_[:, :], start=(h == 0), stop=(h == 15))
                    em.barrier()
            for s_ in range(4):
                a_sh = ab_sb[0:16, 3 + c * 512 - s_: 3 + (c + 1) * 512 - s_]
                b_r = ab_sb[0:16, (S + 3) + 3 + c * 512: (S + 3) + 3 + (c + 1) * 512]
                em.ve.tensor_tensor(gt[0:16, :], a_sh, b_r, ALU.mult)
                em.barrier()
                em.se.activation(gt[0:16, :], gt[0:16, :], AF.Sigmoid)
                em.barrier()
                em.ve.scalar_tensor_tensor(pb[0:16, s_ * 512:(s_ + 1) * 512],
                                           ps_at[s_][0:16, :], ESC, gt[0:16, :],
                                           ALU.mult, ALU.mult)
                em.barrier()
            em.ve.memset(oacc[:, :], 0.0)
            em.barrier()
            for h in range(16):
                em.dma(em.sy, vb[:, 3:3 + S], io["v_hbm"][h * 128:(h + 1) * 128, :],
                       "d_sc")
                em.hard()
                for s_ in range(4):
                    em.te.matmul(ps_w[:, :], C["selB"][0:16, h * 128:(h + 1) * 128],
                                 pb[0:16, s_ * 512:(s_ + 1) * 512], start=True,
                                 stop=True)
                    em.barrier()
                    em.ve.tensor_tensor(t_[:, :], ps_w[:, :],
                                        vb[:, 3 + c * 512 - s_: 3 + (c + 1) * 512 - s_],
                                        ALU.mult)
                    em.barrier()
                    em.ve.tensor_tensor(oacc[:, h * 512:(h + 1) * 512],
                                        oacc[:, h * 512:(h + 1) * 512], t_[:, :],
                                        ALU.add)
                    em.barrier()
            for h in range(16):
                em.se.activation(t_[:, :], oacc[:, h * 512:(h + 1) * 512], AF.Square)
                em.barrier()
                em.te.matmul(ps_s[0:1, :], C["onescol_f"][:, :], t_[:, :],
                             start=(h == 0), stop=(h == 15))
                em.barrier()
            em.ve.tensor_scalar(rows[0:1, :], ps_s[0:1, :], 1.0 / D, None, ALU.mult)
            em.barrier()
            em.ve.tensor_scalar(rows[0:1, :], rows[0:1, :], 1e-6, None, ALU.add)
            em.barrier()
            em.se.activation(rows[0:1, :], rows[0:1, :], AF.Sqrt)
            em.barrier()
            em.ve.reciprocal(rows[0:1, :], rows[0:1, :])
            em.barrier()
            em.te.matmul(ps_bx[:, :], C["onesrow_f"][:, :], rows[0:1, :],
                         start=True, stop=True)
            em.barrier()
            for h in range(16):
                em.wait(em.ve, "d_sc")
                em.dma(em.sy, zb[:, :], io["z_hbm"][h * 128:(h + 1) * 128, tsl],
                       "d_sc")
                em.wait(em.ve, "d_sc")
                em.ve.tensor_tensor(t_[:, :], oacc[:, h * 512:(h + 1) * 512],
                                    ps_bx[:, :], ALU.mult)
                em.barrier()
                em.ve.tensor_scalar(t_[:, :], t_[:, :], normw_sb[:, h:h + 1], None,
                                    ALU.mult)
                em.barrier()
                em.ve.tensor_tensor(wor[:, h * 512:(h + 1) * 512], t_[:, :], zb[:, :],
                                    ALU.mult)
                em.barrier()
            for o in range(NF):
                em.dma(em.gp, wc[:].rearrange("p (f x) -> p f x", f=NF),
                       io["wout"][:, o * 128:(o + 1) * 128].rearrange("(f p) x -> p f x", p=128),
                       "d_w")
                em.hard()
                mm_kseries(em, ps_w[:, :],
                           [(wc[:, f * 128:(f + 1) * 128],
                             wor[:, f * 512:(f + 1) * 512]) for f in range(NF)])
                em.barrier()
                em.se.activation(stg[:, :], ps_w[:, :], AF.Identity)
                em.barrier()
                em.dma(em.sy, io["accA"][o * 128:(o + 1) * 128, tsl], stg[:, :],
                       "d_sc")
                em.hard()


def run_layer2(inp, prep, xT, debug=False):
    nc = build_layer2(_consts(), debug=debug)
    cw = np.concatenate([inp["L2_attn_conv_q_w"], inp["L2_attn_conv_k_w"],
                         inp["L2_attn_conv_v_w"]], axis=1)
    m = dict(
        wq=inp["L2_attn_Wq"], wk=inp["L2_attn_Wk"], wv=inp["L2_attn_Wv"],
        wz=inp["L2_attn_Wz"],
        wab=np.concatenate([inp["L2_attn_Wa"], inp["L2_attn_Wb"]], axis=1),
        convw=cw, normw=_pcol(inp["L2_attn_norm_w"]), wout=inp["L2_attn_Wout"],
        wr=_pcol(inp["L2_ffn_W_r"][:, 0]), br=inp["L2_ffn_b_r"].reshape(1, 1),
        w1=prep["l2w1_bf"], b1=_pcol(inp["L2_ffn_b1"]), w2=prep["l2w2_bf"],
        b2=_pcol(inp["L2_ffn_b2"]),
    )
    m = {k: np.ascontiguousarray(v).astype(np.float32)
         if np.asarray(v).dtype in (np.float64, np.float32) else np.ascontiguousarray(v)
         for k, v in m.items()}
    in_maps = []
    for c in range(8):
        mc = dict(m)
        mc["xin"] = xT[c]
        in_maps.append(mc)
    res = run_bass_kernel_spmd(nc, in_maps, core_ids=list(range(8)))
    if debug:
        return res.results
    return [res.results[c]["xout"] for c in range(8)]


# version 08
# speedup vs baseline: 1.1422x; 1.1422x over previous
import math
from contextlib import ExitStack

import numpy as np
import ml_dtypes

import concourse.bass as bass
import concourse.mybir as mybir
from concourse.bass_utils import run_bass_kernel_spmd

F32 = mybir.dt.float32
BF16 = mybir.dt.bfloat16
I32 = mybir.dt.int32
AF = mybir.ActivationFunctionType
ALU = mybir.AluOpType
AX = mybir.AxisListType
BFNP = ml_dtypes.bfloat16

_NMC = [0]


def nm(base):
    _NMC[0] += 1
    return f"{base}_{_NMC[0]}"


S = 2048
D = 2048
NF = D // 128
NEG = -1.0e30



_ENG_OPS = {
    "tensor_tensor", "tensor_scalar", "tensor_copy", "tensor_add", "tensor_sub",
    "tensor_mul", "tensor_max", "tensor_scalar_add", "tensor_scalar_sub",
    "tensor_scalar_mul", "tensor_scalar_max", "tensor_scalar_min", "tensor_relu",
    "tensor_reduce", "reduce_max", "reduce_sum", "scalar_tensor_tensor",
    "tensor_tensor_scan", "memset", "memzero", "reciprocal", "select",
    "copy_predicated", "activation", "copy", "add", "mul", "sqrt", "square",
    "sign", "iota", "affine_select", "tensor_single_scalar",
}


class _EngWrap:

    def __init__(self, em, eng, sem_name):
        self._em = em
        self._eng = eng
        self._sem_name = sem_name

    def __getattr__(self, name):
        eng = self._eng
        attr = getattr(eng, name)
        if name not in _ENG_OPS:
            return attr
        em = self._em
        sem_name = self._sem_name

        def wrapped(*a, **k):
            h = em.sem(sem_name)
            eng.wait_ge(h, em.cnt(sem_name))
            inst = attr(*a, **k)
            inst.then_inc(h, 1)
            em.bump(sem_name)
            return inst
        return wrapped


class Em:

    def __init__(self, nc):
        self.nc = nc
        self.te = nc.tensor
        self.ve = _EngWrap(self, nc.vector, "q_ve")
        self.se = _EngWrap(self, nc.scalar, "q_se")
        self.gp = _EngWrap(self, nc.gpsimd, "q_gp")
        self.sy = nc.sync
        self._sems = {}
        self._cnt = {}
        self._ctx = []

    def sem(self, name):
        if name not in self._sems:
            h = self.nc.alloc_semaphore(name=name)
            self._sems[name] = h
            self._cnt[name] = 0
        return self._sems[name]

    def cnt(self, name):
        return self._cnt[name]

    def bump(self, name, by=1):
        self._cnt[name] += by
        return self._cnt[name]

    def _qsem(self, q, sem_name):
        return sem_name + ("_g" if q is self.gp else "")

    def dma(self, q, out, in_, sem_name):
        name = self._qsem(q, sem_name)
        h = self.sem(name)
        q.dma_start(out, in_).then_inc(h, 16)
        self.bump(name, 16)

    def wait(self, eng, sem_name, val=None):
        for name in (sem_name, sem_name + "_g"):
            if name in self._sems:
                eng.wait_ge(self._sems[name],
                            self.cnt(name) if val is None else val)

    def barrier(self):
        self.nc.all_engine_barrier()

    def hard(self):
        if not hasattr(self, "_hard_seen"):
            self._hard_seen = {}
        for name in list(self._sems):
            if name.startswith("d_") and self._hard_seen.get(name) != self._cnt[name]:
                self._hard_seen[name] = self._cnt[name]
                for eng in (self.te, self.ve, self.se, self.gp, self.sy):
                    eng.wait_ge(self._sems[name], self._cnt[name])
        self.nc.all_engine_barrier()


class ColStream:

    def __init__(self, em, streams, n_o, sem="d_w"):
        self.em = em
        self.streams = streams
        self.n_o = n_o
        self.sem = sem
        self.loadc = []
        self.consumed = []
        self.tsem = em.sem("t_mm")

    def _load(self, o):
        em = self.em
        if o >= 2:
            em.gp.wait_ge(self.tsem, self.consumed[o - 2])
        name = self.sem + "AB"[o % 2]
        for (w_ap, cols, kt, bufs) in self.streams:
            em.dma(em.gp, bufs[o % 2][:].rearrange("p (k c) -> p k c", k=kt),
                   w_ap[:, o * cols:(o + 1) * cols].rearrange(
                       "(k p) c -> p k c", p=128), name)
        self.loadc.append((name, em.cnt(em._qsem(em.gp, name))))

    def tag(self, inst):
        inst.then_inc(self.tsem, 1)
        self.em.bump("t_mm")

    def run(self, body):
        em = self.em
        self._load(0)
        for o in range(self.n_o):
            if o + 1 < self.n_o:
                self._load(o + 1)
            nmo, cnto = self.loadc[o]
            em.wait(em.te, em._qsem(em.gp, nmo), cnto)
            n0 = self.em.cnt("t_mm")
            body(o, [bufs[o % 2] for (_, _, _, bufs) in self.streams])
            assert self.em.cnt("t_mm") == n0 + 1, "body must cs.tag one matmul"
            self.consumed.append(self.em.cnt("t_mm"))
        em.barrier()


def mm_kseries(em, ps_ap, pairs, then_sem=None, inc=1):
    n = len(pairs)
    last = None
    for i, (l, r) in enumerate(pairs):
        last = em.te.matmul(ps_ap, l, r, start=(i == 0), stop=(i == n - 1))
    if then_sem is not None:
        last.then_inc(then_sem, inc)
    return last




def _t(a):
    return np.ascontiguousarray(np.asarray(a).T)


def _pcol(b):
    b = np.asarray(b)
    return np.ascontiguousarray(b.reshape(-1, 128).T)


def _row(b):
    return np.ascontiguousarray(np.asarray(b).reshape(1, -1))




def build_consts():
    c = {}
    c["ident_bf"] = np.eye(128, dtype=BFNP)
    c["ident_f32"] = np.eye(128, dtype=np.float32)
    onescol = np.ones((128, 1), np.float32)
    c["onescol_f"] = onescol
    c["onescol_b"] = onescol.astype(BFNP)
    onesrow = np.ones((1, 128), np.float32)
    c["onesrow_f"] = onesrow
    c["onesrow_b"] = onesrow.astype(BFNP)
    for r in range(2):
        p = np.arange(128)[:, None]
        j = np.arange(256)[None, :]
        c[f"mask256_r{r}"] = np.where(p <= j - 128 * r, 0.0, NEG).astype(np.float32)
    for r in range(4):
        p = np.arange(128)[:, None]
        j = np.arange(512)[None, :]
        c[f"mask512_r{r}"] = np.where(p <= j - 128 * r, 0.0, NEG).astype(np.float32)
    Dh = 128
    inv = 1.0 / (1e6 ** (np.arange(0, Dh, 2) / Dh))
    fr = np.arange(S)[:, None] * inv[None, :]
    cos = np.concatenate([np.cos(fr), np.cos(fr)], 1)
    sin = np.concatenate([np.sin(fr), np.sin(fr)], 1)
    c["ropecos"] = np.ascontiguousarray(cos.T).astype(np.float32)
    c["ropesin"] = np.ascontiguousarray(sin.T).astype(np.float32)
    R = np.zeros((128, 128), np.float32)
    for m in range(64):
        R[m + 64, m] = -1.0
    for m in range(64, 128):
        R[m - 64, m] = 1.0
    c["rope_R"] = R.astype(BFNP)
    E = np.zeros((128, 16 * 16), np.float32)
    for h in range(16):
        E[:, 16 * h + h] = 1.0
    c["selE"] = E.astype(BFNP)
    B = np.zeros((16, 16 * 128), np.float32)
    for h in range(16):
        B[h, 128 * h:128 * (h + 1)] = 1.0
    c["selB"] = B.astype(BFNP)
    B8 = np.zeros((8, 8 * 128), np.float32)
    for e in range(8):
        B8[e, 128 * e:128 * (e + 1)] = 1.0
    c["sel8"] = B8.astype(BFNP)
    return c


class Consts:

    def __init__(self, em, names, consts):
        self.handles = {}
        self.sb = {}
        nc = em.nc
        self._stack = []
        for n in names:
            a = consts[n]
            h = nc.inline_tensor(np.ascontiguousarray(a), f"c_{n}")
            dt = BF16 if a.dtype == BFNP else (F32 if a.dtype == np.float32 else I32)
            t = nc.alloc_sbuf_tensor(f"cs_{n}", list(a.shape), dt)
            em.dma(em.sy, t[tuple(slice(None) for _ in a.shape)], h.ap()[tuple(slice(None) for _ in a.shape)], "d_const")
            self.sb[n] = t
        em.hard()

    def __getitem__(self, n):
        return self.sb[n]




def emit_phase0(em, C, io, xn_sb=None, adds=(), out_x=None, router=None,
                l2fix=None, save_rows=None, combine_only=False, ct=512):
    nc = em.nc
    x_in = io["x_in"]
    with (
        nc.sbuf_tensor([128, NF * ct], F32) as xch,
        nc.sbuf_tensor([128, NF * ct], F32) as ach,
        nc.sbuf_tensor([1, ct * 8], F32) as rows,
        nc.psum_tensor([1, ct], F32) as ps_sum,
        nc.psum_tensor([1, ct], F32) as ps_sq,
        nc.psum_tensor([128, ct], F32) as ps_b1,
        nc.psum_tensor([128, ct], F32) as ps_b2,
    ):
        sqch = ach
        r_mean = rows[:, 0:ct]
        r_msq = rows[:, ct:2 * ct]
        r_var = rows[:, 2 * ct:3 * ct]
        r_rstd = rows[:, 3 * ct:4 * ct]
        r_nm = rows[:, 4 * ct:5 * ct]
        r_t = rows[:, 5 * ct:6 * ct]
        for c in range(S // ct):
            tsl = slice(c * ct, (c + 1) * ct)
            if "d_xout" in em._sems:
                em.wait(em.sy, "d_xout")
            em.dma(em.sy, xch[:].rearrange("p (f t) -> p f t", f=NF),
                   x_in[:, tsl].rearrange("(f p) t -> p f t", p=128), "d_act")
            em.wait(em.ve, "d_act")
            em.barrier()
            if l2fix is not None:
                l2fix(em, C, c, xch)
            for a_ap in adds:
                em.dma(em.sy, ach[:].rearrange("p (f t) -> p f t", f=NF),
                       a_ap[:, tsl].rearrange("(f p) t -> p f t", p=128), "d_act")
                em.wait(em.ve, "d_act")
                for f in range(NF):
                    em.ve.tensor_tensor(xch[:, f * ct:(f + 1) * ct],
                                        xch[:, f * ct:(f + 1) * ct],
                                        ach[:, f * ct:(f + 1) * ct], ALU.add)
                em.barrier()
            if out_x is not None:
                em.dma(em.sy, out_x[:, tsl].rearrange("(f p) t -> p f t", p=128),
                       xch[:].rearrange("p (f t) -> p f t", f=NF), "d_xout")
            if combine_only:
                em.barrier()
                continue
            for f in range(NF):
                em.se.activation(sqch[:, f * ct:(f + 1) * ct],
                                 xch[:, f * ct:(f + 1) * ct], AF.Square)
            em.barrier()
            mm_kseries(em, ps_sum[:, :],
                       [(C["onescol_f"][:, :], xch[:, f * ct:(f + 1) * ct]) for f in range(NF)])
            mm_kseries(em, ps_sq[:, :],
                       [(C["onescol_f"][:, :], sqch[:, f * ct:(f + 1) * ct]) for f in range(NF)])
            if router is not None:
                router(em, C, c, xch)
            em.barrier()
            em.ve.tensor_scalar(r_mean, ps_sum[:, :], 1.0 / D, None, ALU.mult)
            em.ve.tensor_scalar(r_msq, ps_sq[:, :], 1.0 / D, None, ALU.mult)
            em.barrier()
            em.se.activation(r_t, r_mean, AF.Square)
            em.barrier()
            em.ve.tensor_tensor(r_var, r_msq, r_t, ALU.subtract)
            em.barrier()
            em.ve.tensor_scalar(r_var, r_var, 1e-5, None, ALU.add)
            em.barrier()
            em.se.activation(r_t, r_var, AF.Sqrt)
            em.barrier()
            em.ve.reciprocal(r_rstd, r_t)
            em.barrier()
            em.ve.tensor_tensor(r_nm, r_mean, r_rstd, ALU.mult)
            em.barrier()
            em.ve.tensor_scalar(r_nm, r_nm, -1.0, None, ALU.mult)
            em.barrier()
            if save_rows is not None:
                em.ve.tensor_copy(save_rows["mean"][:, tsl], r_mean)
                em.ve.tensor_copy(save_rows["rstd"][:, tsl], r_rstd)
            em.te.matmul(ps_b1[:, :], C["onesrow_f"][:, :], r_rstd, start=True, stop=True)
            em.te.matmul(ps_b2[:, :], C["onesrow_f"][:, :], r_nm, start=True, stop=True)
            em.barrier()
            for f in range(NF):
                em.ve.tensor_tensor(sqch[:, f * ct:(f + 1) * ct],
                                    xch[:, f * ct:(f + 1) * ct], ps_b1[:, :], ALU.mult)
            em.barrier()
            for f in range(NF):
                em.ve.tensor_tensor(xn_sb[:, f * S + c * ct: f * S + (c + 1) * ct],
                                    sqch[:, f * ct:(f + 1) * ct], ps_b2[:, :], ALU.add)
            em.barrier()
            if router is not None:
                router(em, C, c, xch, post=True)
        em.hard()




def attn_alloc(em, cfg):
    nc = em.nc
    st = {}
    st["bias_sb"] = nc.alloc_sbuf_tensor(nm("at_bias"), [128, 3 * NF + 1], F32)
    Dkv = cfg["kvH"] * cfg["dh"]
    st["bvrow_sb"] = nc.alloc_sbuf_tensor(nm("at_bvrow"), [1, Dkv], BF16)
    return st


def emit_attn_proj(em, C, st, xn_sb, cfg, wd, q_hbm, v_hbm, k_hbm):
    nc = em.nc
    H, dh, kvH = cfg["H"], cfg["dh"], cfg["kvH"]
    Dq = H * dh
    Dkv = kvH * dh
    bias_sb = st["bias_sb"]
    bvrow_sb = st["bvrow_sb"]
    if cfg.get("has_bias", True):
        em.dma(em.sy, bias_sb[:, 0:Dq // 128], wd["bq"][:, :], "d_w")
        em.dma(em.sy, bias_sb[:, NF:NF + Dkv // 128], wd["bk"][:, :], "d_w")
        em.dma(em.gp, bvrow_sb[:, :], wd["bv"][:, :], "d_w")
    em.dma(em.sy, bias_sb[:, 2 * NF:3 * NF], wd["bo"][:, :], "d_w")
    em.hard()
    em.barrier()

    def proj_featmajor(w_ap, ncols, bias_col, dest_cb):
        smm = em.sem("a_mm")
        with ExitStack() as _es:
            wb2 = [_es.enter_context(nc.sbuf_tensor(nm("pjw"), [128, NF * 128], BF16))
                   for _ in range(2)]
            st4 = _es.enter_context(nc.sbuf_tensor(nm("pjst"), [128, 4 * 512], BF16))
            pss = [_es.enter_context(nc.psum_tensor(nm("pjps"), [128, 512], F32))
                   for _ in range(2)]
            psfree = []
            stgd = []

            def body(o, bufs):
                wcb = bufs[0]
                for tq in range(4):
                    gi = o * 4 + tq
                    p = pss[gi % 2]
                    if gi >= 2:
                        em.wait(em.te, "q_se", psfree[gi - 2])
                    last = mm_kseries(em, p[:, :],
                                      [(wcb[:, f * 128:(f + 1) * 128],
                                        xn_sb[:, f * S + tq * 512: f * S + (tq + 1) * 512])
                                       for f in range(NF)])
                    if tq == 3:
                        cs.tag(last)
                        em.wait(em.se, "t_mm")
                    else:
                        last.then_inc(smm, 1)
                        em.bump("a_mm")
                        em.wait(em.se, "a_mm")
                    em.wait(em.se, "d_sc")
                    stg = st4[:, (gi % 4) * 512:(gi % 4 + 1) * 512]
                    if bias_col is not None:
                        em.se.activation(stg, p[:, :], AF.Identity,
                                         bias=bias_col(o))
                    else:
                        em.se.activation(stg, p[:, :], AF.Identity)
                    psfree.append(em.cnt("q_se"))
                    em.wait(em.sy, "q_se")
                    dest_cb(o, tq, stg)
                    stgd.append(em.cnt("d_sc"))

            cs = ColStream(em, [(w_ap, 128, NF, (wb2[0], wb2[1]))], ncols // 128)
            cs.run(body)
            em.hard()

    qb = (lambda o: bias_sb[:, o:o + 1]) if cfg.get("has_bias", True) else None
    proj_featmajor(wd["wq"], Dq, qb,
                   lambda o, tq, stg: em.dma(em.sy, q_hbm[o * 128:(o + 1) * 128, tq * 512:(tq + 1) * 512], stg, "d_sc"))
    kb = (lambda o: bias_sb[:, NF + o:NF + o + 1]) if cfg.get("has_bias", True) else None
    proj_featmajor(wd["wk"], Dkv, kb,
                   lambda o, tq, stg: em.dma(em.sy, k_hbm[o * 128:(o + 1) * 128, tq * 512:(tq + 1) * 512], stg, "d_sc"))

    smmv = em.sem("a_mm")
    with ExitStack() as _es:
        wv2 = [_es.enter_context(nc.sbuf_tensor(nm("pvw"), [128, NF * 512], BF16))
               for _ in range(2)]
        st4 = _es.enter_context(nc.sbuf_tensor(nm("pvst"), [128, 4 * 512], BF16))
        pss = [_es.enter_context(nc.psum_tensor(nm("pvps"), [128, 512], F32))
               for _ in range(2)]
        psfree = []
        stgd = []

        def vbody(oc, bufs):
            wvc = bufs[0]
            osl = slice(oc * 512, (oc + 1) * 512)
            for t in range(16):
                gi = oc * 16 + t
                p = pss[gi % 2]
                if gi >= 2:
                    em.wait(em.te, "q_se", psfree[gi - 2])
                pairs = [(xn_sb[:, f * S + t * 128: f * S + (t + 1) * 128],
                          wvc[:, f * 512:(f + 1) * 512]) for f in range(NF)]
                if cfg.get("has_bias", True):
                    pairs.append((C["onesrow_b"][:, :], bvrow_sb[:, osl]))
                last = mm_kseries(em, p[:, :], pairs)
                if t == 15:
                    cs.tag(last)
                    em.wait(em.se, "t_mm")
                else:
                    last.then_inc(smmv, 1)
                    em.bump("a_mm")
                    em.wait(em.se, "a_mm")
                em.wait(em.se, "d_sc")
                stg = st4[:, (gi % 4) * 512:(gi % 4 + 1) * 512]
                em.se.activation(stg, p[:, :], AF.Identity)
                psfree.append(em.cnt("q_se"))
                em.wait(em.sy, "q_se")
                em.dma(em.sy, v_hbm[t * 128:(t + 1) * 128, osl], stg, "d_sc")
                stgd.append(em.cnt("d_sc"))

        cs = ColStream(em, [(wd["wv"], 512, NF, (wv2[0], wv2[1]))],
                       max(1, Dkv // 512))
        cs.run(vbody)
        em.hard()


def emit_attn_core(em, C, st, cfg, q_hbm, v_hbm, k_hbm, feats_hbm, acc_out, wd):
    nc = em.nc
    H, dh, kvH = cfg["H"], cfg["dh"], cfg["kvH"]
    Dq = H * dh
    causal = cfg["causal"]
    escale = cfg["escale"]
    TQ = 512
    bias_sb = st["bias_sb"]
    NTQ = S // TQ
    smm = em.sem("a_mm")
    with ExitStack() as _es:
        sraw = _es.enter_context(nc.sbuf_tensor(nm("sraw"), [128, 16 * TQ], F32))
        expP = _es.enter_context(nc.sbuf_tensor(nm("expP"), [128, 16 * TQ], BF16))
        vh = _es.enter_context(nc.sbuf_tensor(nm("vh"), [128, 16 * dh], BF16))
        kh = _es.enter_context(nc.sbuf_tensor(nm("kh"), [128, S], BF16))
        qh = _es.enter_context(nc.sbuf_tensor(nm("qh"), [128, TQ], BF16))
        mrun = _es.enter_context(nc.sbuf_tensor(nm("mrun"), [128, TQ], F32))
        tmp2 = _es.enter_context(nc.sbuf_tensor(nm("tmp2"), [128, 2 * TQ], F32))
        esum = _es.enter_context(nc.sbuf_tensor(nm("esum"), [128, TQ], F32))
        rows = _es.enter_context(nc.sbuf_tensor(nm("rows"), [1, 2 * TQ], F32))
        cm = _es.enter_context(nc.sbuf_tensor(nm("cm"), [128, 4], F32))
        ftmp = _es.enter_context(nc.sbuf_tensor(nm("ftmp"), [128, TQ], F32))
        fbf = _es.enter_context(nc.sbuf_tensor(nm("fbf"), [128, TQ], BF16))
        ps_s0 = _es.enter_context(nc.psum_tensor(nm("ps_s0"), [128, TQ], F32))
        ps_s1 = _es.enter_context(nc.psum_tensor(nm("ps_s1"), [128, TQ], F32))
        ps_t = _es.enter_context(nc.psum_tensor(nm("ps_t"), [128, 128], F32))
        ps_r = _es.enter_context(nc.psum_tensor(nm("ps_r"), [1, TQ], F32))
        ps_f = _es.enter_context(nc.psum_tensor(nm("ps_f"), [128, TQ], F32))
        ps_b = _es.enter_context(nc.psum_tensor(nm("ps_b"), [128, TQ], F32))
        pss = [ps_s0, ps_s1]
        mrow = rows[:, 0:TQ]
        rrow = rows[:, TQ:2 * TQ]
        for g in range(kvH):
            em.dma(em.gp, vh[:, 0:16 * dh].rearrange("p (f c) -> p f c", f=16),
                   v_hbm[:, g * dh:(g + 1) * dh].rearrange("(f p) c -> p f c", p=128),
                   "d_sc")
            em.dma(em.sy, kh[0:dh, :], k_hbm[g * dh:(g + 1) * dh, :], "d_sc")
            em.hard()
            for h in range(g * (H // kvH), (g + 1) * (H // kvH)):
                for tq in range(NTQ):
                    ntk = min(16, (tq + 1) * (TQ // 128)) if causal else 16
                    em.wait(em.sy, "d_sc")
                    em.dma(em.sy, qh[0:dh, :],
                           q_hbm[h * dh:(h + 1) * dh, tq * TQ:(tq + 1) * TQ], "d_sc")
                    em.wait(em.te, "d_sc")
                    evdone = []
                    for i in range(ntk):
                        ps = pss[i % 2]
                        lhsT = kh[0:dh, i * 128:(i + 1) * 128]
                        if i >= 2:
                            em.wait(em.te, "q_ve", evdone[i - 2])
                        em.te.matmul(ps[:, :], lhsT, qh[0:dh, :],
                                     start=True, stop=True).then_inc(smm, 1)
                        em.bump("a_mm")
                        em.wait(em.ve, "a_mm")
                        ssl = sraw[:, i * TQ:(i + 1) * TQ]
                        r = i - tq * (TQ // 128)
                        if causal and r >= 0:
                            em.ve.tensor_tensor(ssl, ps[:, :],
                                                C[f"mask{TQ}_r{r}"][:, :], ALU.add)
                        else:
                            em.ve.tensor_copy(ssl, ps[:, :])
                        evdone.append(em.cnt("q_ve"))
                        if i == 0:
                            em.ve.tensor_copy(mrun[:, :], ssl)
                        else:
                            em.ve.tensor_tensor(mrun[:, :], mrun[:, :], ssl, ALU.max)
                    em.wait(em.te, "q_ve")
                    for j in range(TQ // 128):
                        em.te.transpose(ps_t[:, :], mrun[:, j * 128:(j + 1) * 128],
                                        C["ident_f32"][:, :]).then_inc(smm, 1)
                        em.bump("a_mm")
                        em.wait(em.ve, "a_mm")
                        em.ve.reduce_max(cm[:, j:j + 1], ps_t[:, :], axis=AX.X)
                        em.wait(em.te, "q_ve")
                    for j in range(TQ // 128):
                        em.te.transpose(ps_t[0:1, 0:128], cm[:, j:j + 1],
                                        C["ident_f32"][:, :]).then_inc(smm, 1)
                        em.bump("a_mm")
                        em.wait(em.ve, "a_mm")
                        em.ve.tensor_copy(mrow[:, j * 128:(j + 1) * 128],
                                          ps_t[0:1, 0:128])
                        em.wait(em.te, "q_ve")
                    em.te.matmul(ps_b[:, :], C["onesrow_f"][:, :], mrow,
                                 start=True, stop=True).then_inc(smm, 1)
                    em.bump("a_mm")
                    em.wait(em.ve, "a_mm")
                    exdone = []
                    for i in range(ntk):
                        ssl = sraw[:, i * TQ:(i + 1) * TQ]
                        esl = expP[:, i * TQ:(i + 1) * TQ]
                        tb = tmp2[:, (i % 2) * TQ:(i % 2 + 1) * TQ]
                        if i >= 2:
                            em.wait(em.ve, "q_se", exdone[i - 2])
                        em.ve.tensor_tensor(tb, ssl, ps_b[:, :], ALU.subtract)
                        em.wait(em.se, "q_ve")
                        em.se.activation(esl, tb, AF.Exp, scale=escale)
                        exdone.append(em.cnt("q_se"))
                        em.wait(em.ve, "q_se")
                        if i == 0:
                            em.ve.tensor_copy(esum[:, :], esl)
                        else:
                            em.ve.tensor_tensor(esum[:, :], esum[:, :], esl, ALU.add)
                        em.wait(em.te, "q_se")
                        em.te.matmul(ps_f[0:dh, :], vh[:, i * dh:(i + 1) * dh],
                                     esl, start=(i == 0), stop=(i == ntk - 1))
                    em.wait(em.te, "q_ve")
                    em.te.matmul(ps_r[:, :], C["onescol_f"][:, :], esum[:, :],
                                 start=True, stop=True).then_inc(smm, 1)
                    em.bump("a_mm")
                    em.wait(em.ve, "a_mm")
                    em.ve.reciprocal(rrow, ps_r[:, :])
                    em.wait(em.te, "q_ve")
                    em.te.matmul(ps_b[:, :], C["onesrow_f"][:, :], rrow,
                                 start=True, stop=True).then_inc(smm, 1)
                    em.bump("a_mm")
                    em.wait(em.se, "a_mm")
                    em.se.activation(ftmp[0:dh, :], ps_f[0:dh, :], AF.Identity)
                    em.wait(em.ve, "q_se")
                    em.wait(em.ve, "d_sc")
                    em.ve.tensor_tensor(fbf[0:dh, :], ftmp[0:dh, :], ps_b[0:dh, :],
                                        ALU.mult)
                    em.wait(em.sy, "q_ve")
                    em.dma(em.sy, feats_hbm[h * dh:(h + 1) * dh,
                                            tq * TQ:(tq + 1) * TQ], fbf[0:dh, :],
                           "d_sc")
                    em.barrier()
        em.hard()
    smm2 = em.sem("a_mm")
    with ExitStack() as _es:
        wo2 = [_es.enter_context(nc.sbuf_tensor(nm("opw"), [128, (Dq // 128) * 128], BF16))
               for _ in range(2)]
        rhs = _es.enter_context(nc.sbuf_tensor(nm("oprhs"), [128, (Dq // 128) * 512], BF16))
        st4 = _es.enter_context(nc.sbuf_tensor(nm("opst"), [128, 4 * 512], F32))
        pss = [_es.enter_context(nc.psum_tensor(nm("opps"), [128, 512], F32))
               for _ in range(2)]
        for tq in range(4):
            em.dma(em.gp, rhs[:].rearrange("p (f c) -> p f c", f=Dq // 128),
                   feats_hbm[:, tq * 512:(tq + 1) * 512].rearrange("(f p) c -> p f c", p=128),
                   "d_sc")
            em.hard()
            psfree = []
            stgd = []

            def obody(o, bufs):
                wcb = bufs[0]
                p = pss[o % 2]
                if o >= 2:
                    em.wait(em.te, "q_se", psfree[o - 2])
                cs.tag(mm_kseries(em, p[:, :],
                                  [(wcb[:, f * 128:(f + 1) * 128],
                                    rhs[:, f * 512:(f + 1) * 512])
                                   for f in range(Dq // 128)]))
                em.wait(em.se, "t_mm")
                em.wait(em.se, "d_sc")
                stg = st4[:, (o % 4) * 512:(o % 4 + 1) * 512]
                em.se.activation(stg, p[:, :], AF.Identity,
                                 bias=bias_sb[:, 2 * NF + o:2 * NF + o + 1])
                psfree.append(em.cnt("q_se"))
                em.wait(em.sy, "q_se")
                em.dma(em.sy, acc_out[o * 128:(o + 1) * 128, tq * 512:(tq + 1) * 512],
                       stg, "d_sc")
                stgd.append(em.cnt("d_sc"))

            cs = ColStream(em, [(wd["wo"], 128, Dq // 128, (wo2[0], wo2[1]))], NF)
            cs.run(obody)
        em.hard()





def emit_ffn_swiglu(em, C, xn_sb, cfg, wd, acc_out):
    nc = em.nc
    NI = cfg["NI"]
    NO = NI // 128
    has_s = cfg.get("scales", False)
    scal_sb = cfg.get("scal_sb")
    if has_s:
        em.dma(em.sy, scal_sb[:, 0:NO], wd["gate_s"][:, :], "d_w")
        em.dma(em.sy, scal_sb[:, NO:2 * NO], wd["up_s"][:, :], "d_w")
        em.dma(em.sy, scal_sb[:, 2 * NO:2 * NO + NF], wd["down_s"][:, :], "d_w")
        em.hard()
    with ExitStack() as _es:
        wg = [_es.enter_context(nc.sbuf_tensor(nm("wg"), [128, NF * 128], BF16))
              for _ in range(2)]
        wu = [_es.enter_context(nc.sbuf_tensor(nm("wu"), [128, NF * 128], BF16))
              for _ in range(2)]
        dcb = [_es.enter_context(nc.sbuf_tensor(nm("dcb"), [128, NO * 128], BF16))
               for _ in range(2)]
        gsb = _es.enter_context(nc.sbuf_tensor(nm("gsb"), [128, 2 * 512], BF16))
        usb = _es.enter_context(nc.sbuf_tensor(nm("usb"), [128, 2 * 512], BF16))
        hch = _es.enter_context(nc.sbuf_tensor(nm("hch"), [128, NO * 512], BF16))
        stg = _es.enter_context(nc.sbuf_tensor(nm("stg"), [128, 2 * 512], F32))
        psg = [_es.enter_context(nc.psum_tensor(nm("psg"), [128, 512], F32))
               for _ in range(2)]
        psu = [_es.enter_context(nc.psum_tensor(nm("psu"), [128, 512], F32))
               for _ in range(2)]
        for tq in range(4):
            tsl = slice(tq * 512, (tq + 1) * 512)
            psfree = []
            mufree = []

            def body(o, bufs):
                wgb, wub = bufs
                if o >= 2:
                    em.wait(em.te, "q_se", psfree[o - 2])
                mm_kseries(em, psg[o % 2][:, :],
                           [(wgb[:, f * 128:(f + 1) * 128],
                             xn_sb[:, f * S + tq * 512: f * S + (tq + 1) * 512])
                            for f in range(NF)])
                cs.tag(mm_kseries(em, psu[o % 2][:, :],
                                  [(wub[:, f * 128:(f + 1) * 128],
                                    xn_sb[:, f * S + tq * 512: f * S + (tq + 1) * 512])
                                   for f in range(NF)]))
                em.wait(em.se, "t_mm")
                if o >= 2:
                    em.wait(em.se, "q_ve", mufree[o - 2])
                gs = gsb[:, (o % 2) * 512:(o % 2 + 1) * 512]
                us = usb[:, (o % 2) * 512:(o % 2 + 1) * 512]
                if has_s:
                    em.se.activation(gs, psg[o % 2][:, :], AF.Silu,
                                     scale=scal_sb[:, o:o + 1])
                    em.se.activation(us, psu[o % 2][:, :], AF.Identity,
                                     scale=scal_sb[:, NO + o:NO + o + 1])
                else:
                    em.se.activation(gs, psg[o % 2][:, :], AF.Silu)
                    em.se.activation(us, psu[o % 2][:, :], AF.Identity)
                psfree.append(em.cnt("q_se"))
                em.wait(em.ve, "q_se")
                em.ve.tensor_tensor(hch[:, o * 512:(o + 1) * 512], gs, us, ALU.mult)
                mufree.append(em.cnt("q_ve"))

            cs = ColStream(em, [(wd["gate"], 128, NF, (wg[0], wg[1])),
                                (wd["up"], 128, NF, (wu[0], wu[1]))], NO)
            cs.run(body)
            dfree = []
            stgd = []

            def dbody(o2, bufs):
                dcb_ = bufs[0]
                if o2 >= 2:
                    em.wait(em.te, "q_se", dfree[o2 - 2])
                cs2.tag(mm_kseries(em, psg[o2 % 2][:, :],
                                   [(dcb_[:, f * 128:(f + 1) * 128],
                                     hch[:, f * 512:(f + 1) * 512])
                                    for f in range(NO)]))
                em.wait(em.se, "t_mm")
                em.wait(em.se, "d_sc")
                sg = stg[:, (o2 % 2) * 512:(o2 % 2 + 1) * 512]
                if has_s:
                    em.se.activation(sg, psg[o2 % 2][:, :], AF.Identity,
                                     scale=scal_sb[:, 2 * NO + o2:2 * NO + o2 + 1])
                else:
                    em.se.activation(sg, psg[o2 % 2][:, :], AF.Identity)
                dfree.append(em.cnt("q_se"))
                em.wait(em.sy, "q_se")
                em.dma(em.sy, acc_out[o2 * 128:(o2 + 1) * 128, tsl], sg, "d_sc")
                stgd.append(em.cnt("d_sc"))

            cs2 = ColStream(em, [(wd["down"], 128, NO, (dcb[0], dcb[1]))], NF)
            cs2.run(dbody)
        em.hard()




def _io_tensors(nc, names_shapes_dtypes):
    out = {}
    for name, shape, dt, kind in names_shapes_dtypes:
        out[name] = nc.dram_tensor(name, list(shape), dt, kind=kind).ap()
    return out


def build_prep(consts):
    nc = bass.Bass("TRN2", target_bir_lowering=False)
    em = Em(nc)
    specs = [
        ("lut", (1, 16), F32, "ExternalInput"),
        ("gidxT", (256, 5504), I32, "ExternalInput"),
        ("uidxT", (256, 5504), I32, "ExternalInput"),
        ("didxT", (688, 2048), I32, "ExternalInput"),
        ("gw_bf", (256, 5504), BF16, "ExternalOutput"),
        ("uw_bf", (256, 5504), BF16, "ExternalOutput"),
        ("dw_bf", (688, 2048), BF16, "ExternalOutput"),
    ]
    casts = [
        ("l1g", (256, 5632)), ("l1u", (256, 5632)), ("l1d", (704, 2048)),
        ("l2w1", (256, 8192)), ("l2w2", (1024, 2048)),
        ("l3w1", (256, 5632)), ("l3w3", (256, 5632)), ("l3w2", (704, 2048)),
    ]
    for n, shp in casts:
        specs.append((n, shp, F32, "ExternalInput"))
        specs.append((n + "_bf", shp, BF16, "ExternalOutput"))
    io = _io_tensors(nc, specs)
    C = Consts(em, ["onesrow_f"], consts)
    lut_sb = nc.alloc_sbuf_tensor([1, 16], F32)
    lutb_sb = nc.alloc_sbuf_tensor([128, 16], F32)
    em.dma(em.sy, lut_sb[:, :], io["lut"][:, :], "d_w")
    em.hard()
    with nc.psum_tensor([128, 16], F32) as psl:
        em.te.matmul(psl[:, :], C["onesrow_f"][:, :], lut_sb[:, :], start=True, stop=True)
        em.barrier()
        em.ve.tensor_copy(lutb_sb[:, :], psl[:, :])
        em.barrier()
    with (
        nc.sbuf_tensor([128, 5504], I32) as idxi,
        nc.sbuf_tensor([128, 5504], BF16) as idxb,
        nc.sbuf_tensor([128, 5504], BF16) as tmp,
        nc.sbuf_tensor([128, 5504], BF16) as acc,
    ):
        for mat, cols, rows in (("gidxT", 5504, 256), ("uidxT", 5504, 256),
                                ("didxT", 2048, 688)):
            for r0 in range(0, rows, 128):
                rr = min(128, rows - r0)
                em.dma(em.sy, idxi[0:rr, 0:cols], io[mat][r0:r0 + rr, :], "d_w")
                em.hard()
                em.ve.tensor_copy(idxb[0:rr, 0:cols], idxi[0:rr, 0:cols])
                em.barrier()
                for v in range(16):
                    em.ve.tensor_scalar(tmp[0:rr, 0:cols], idxb[0:rr, 0:cols], float(v),
                                        lutb_sb[0:rr, v:v + 1], ALU.is_equal, ALU.mult)
                    em.barrier()
                    if v == 0:
                        em.ve.tensor_copy(acc[0:rr, 0:cols], tmp[0:rr, 0:cols])
                    else:
                        em.ve.tensor_tensor(acc[0:rr, 0:cols], acc[0:rr, 0:cols],
                                            tmp[0:rr, 0:cols], ALU.add)
                    em.barrier()
                out = {"gidxT": "gw_bf", "uidxT": "uw_bf", "didxT": "dw_bf"}[mat]
                em.dma(em.sy, io[out][r0:r0 + rr, :], acc[0:rr, 0:cols], "d_sc")
                em.hard()
    with nc.sbuf_tensor([128, 8192], BF16) as cst:
        for n, shp in casts:
            rows, cols = shp
            for r0 in range(0, rows, 128):
                rr = min(128, rows - r0)
                em.dma(em.gp, cst[0:rr, 0:cols], io[n][r0:r0 + rr, :], "d_w")
                em.hard()
                em.dma(em.sy, io[n + "_bf"][r0:r0 + rr, :], cst[0:rr, 0:cols], "d_sc")
                em.hard()
    em.hard()
    return nc


ATTN_CFG = {
    0: dict(H=32, dh=64, kvH=32, causal=False, escale=1.0 / math.sqrt(32.0),
            has_bias=True, Dq=2048, Dkv=2048),
    1: dict(H=16, dh=128, kvH=16, causal=True, escale=1.0 / math.sqrt(128.0),
            has_bias=True, Dq=2048, Dkv=2048),
    3: dict(H=16, dh=128, kvH=4, causal=True, escale=1.0,
            has_bias=False, Dq=2048, Dkv=512),
}


def build_layer01(layer, consts, debug=False):
    nc = bass.Bass("TRN2", target_bir_lowering=False)
    em = Em(nc)
    NI = 5504 if layer == 0 else 5632
    dk = "ExternalOutput" if debug else "Internal"
    specs = [
        ("xin", (D, S), F32, "ExternalInput"),
        ("xout", (D, S), F32, "ExternalOutput"),
        ("wq", (D, D), F32, "ExternalInput"),
        ("wk", (D, D), F32, "ExternalInput"),
        ("wv", (D, D), F32, "ExternalInput"),
        ("wo", (D, D), F32, "ExternalInput"),
        ("bq", (128, NF), F32, "ExternalInput"),
        ("bk", (128, NF), F32, "ExternalInput"),
        ("bv", (1, D), F32, "ExternalInput"),
        ("bo", (128, NF), F32, "ExternalInput"),
        ("gate", (D, NI), BF16, "ExternalInput"),
        ("up", (D, NI), BF16, "ExternalInput"),
        ("down", (NI, D), BF16, "ExternalInput"),
        ("accA", (D, S), F32, dk),
        ("accF", (D, S), F32, dk),
        ("q_hbm", (D, S), BF16, dk),
        ("k_hbm", (D, S), BF16, dk),
        ("v_hbm", (S, D), BF16, dk),
        ("f_hbm", (D, S), BF16, dk),
    ]
    if layer == 1:
        specs += [("gate_s", (128, NI // 128), F32, "ExternalInput"),
                  ("up_s", (128, NI // 128), F32, "ExternalInput"),
                  ("down_s", (128, NF), F32, "ExternalInput")]
    io = _io_tensors(nc, specs)
    cfg = ATTN_CFG[layer]
    mask_names = [f"mask512_r{r}" for r in range(4)] if cfg["causal"] else []
    C = Consts(em, ["onescol_f", "onesrow_f", "onesrow_b", "ident_f32"] + mask_names,
               consts)
    ffn_cfg = dict(NI=NI, scales=(layer == 1))
    if layer == 1:
        ffn_cfg["scal_sb"] = nc.alloc_sbuf_tensor(nm("ff_scal"),
                                                  [128, 2 * (NI // 128) + NF], F32)
    wd_attn = dict(wq=io["wq"], wk=io["wk"], wv=io["wv"], wo=io["wo"],
                   bq=io["bq"], bk=io["bk"], bv=io["bv"], bo=io["bo"])
    wd_ffn = dict(gate=io["gate"], up=io["up"], down=io["down"])
    if layer == 1:
        wd_ffn.update(gate_s=io["gate_s"], up_s=io["up_s"], down_s=io["down_s"])

    ast = attn_alloc(em, cfg)
    with nc.sbuf_tensor([128, NF * S], BF16) as xn_sb:
        emit_phase0(em, C, {"x_in": io["xin"]}, xn_sb=xn_sb)
        emit_ffn_swiglu(em, C, xn_sb, ffn_cfg, wd_ffn, io["accF"])
        emit_attn_proj(em, C, ast, xn_sb, cfg, wd_attn, io["q_hbm"], io["v_hbm"],
                       io["k_hbm"])
    emit_attn_core(em, C, ast, cfg, io["q_hbm"], io["v_hbm"], io["k_hbm"],
                   io["f_hbm"], io["accA"], wd_attn)
    emit_phase0(em, C, {"x_in": io["xin"]}, adds=(io["accA"], io["accF"]),
                out_x=io["xout"], combine_only=True)
    em.hard()
    return nc



_CONSTS = None


def _consts():
    global _CONSTS
    if _CONSTS is None:
        _CONSTS = build_consts()
    return _CONSTS


def _bf(a):
    return np.ascontiguousarray(np.asarray(a)).astype(BFNP)


def run_prep(inp):
    consts = _consts()
    nc = build_prep(consts)
    g_idxT = _t(inp["L0_ffn_gate_idx"])
    u_idxT = _t(inp["L0_ffn_up_idx"])
    d_idxT = _t(inp["L0_ffn_down_idx"])
    casts_full = {
        "l1g": _t(inp["L1_ffn_gate_w"]), "l1u": _t(inp["L1_ffn_up_w"]),
        "l1d": _t(inp["L1_ffn_down_w"]),
        "l2w1": np.ascontiguousarray(inp["L2_ffn_W1"]),
        "l2w2": np.ascontiguousarray(inp["L2_ffn_W2"]),
        "l3w1": _t(inp["L3_ffn_w1"]), "l3w3": _t(inp["L3_ffn_w3"]),
        "l3w2": _t(inp["L3_ffn_w2"]),
    }
    in_maps = []
    for c in range(8):
        m = {"lut": _row(inp["L0_ffn_lut"]).astype(np.float32),
             "gidxT": np.ascontiguousarray(g_idxT[c * 256:(c + 1) * 256]),
             "uidxT": np.ascontiguousarray(u_idxT[c * 256:(c + 1) * 256]),
             "didxT": np.ascontiguousarray(d_idxT[c * 688:(c + 1) * 688])}
        for n, a in casts_full.items():
            rows = a.shape[0] // 8
            m[n] = np.ascontiguousarray(a[c * rows:(c + 1) * rows]).astype(np.float32)
        in_maps.append(m)
    res = run_bass_kernel_spmd(nc, in_maps, core_ids=list(range(8)))
    out = {}
    for n in ["gw_bf", "uw_bf", "dw_bf"] + [k + "_bf" for k in casts_full]:
        out[n] = np.concatenate([res.results[c][n] for c in range(8)], axis=0)
    return out


def layer01_inmap(layer, inp, prep, xT_c):
    L = f"L{layer}_attn_"
    if layer == 0:
        m = dict(wq=inp[L + "Wq"], wk=inp[L + "Wk"], wv=inp[L + "Wv"],
                 wo=inp[L + "Wo"])
        m["gate"], m["up"], m["down"] = prep["gw_bf"], prep["uw_bf"], prep["dw_bf"]
    else:
        m = dict(wq=_t(inp[L + "Wq"]), wk=_t(inp[L + "Wk"]), wv=_t(inp[L + "Wv"]),
                 wo=_t(inp[L + "Wo"]))
        m["gate"], m["up"], m["down"] = prep["l1g_bf"], prep["l1u_bf"], prep["l1d_bf"]
        m["gate_s"] = _pcol(inp["L1_ffn_gate_s"]).astype(np.float32)
        m["up_s"] = _pcol(inp["L1_ffn_up_s"]).astype(np.float32)
        m["down_s"] = _pcol(inp["L1_ffn_down_s"]).astype(np.float32)
    m["bq"] = _pcol(inp[L + "bq"]).astype(np.float32)
    m["bk"] = _pcol(inp[L + "bk"]).astype(np.float32)
    m["bv"] = _row(inp[L + "bv"]).astype(np.float32)
    m["bo"] = _pcol(inp[L + "bo"]).astype(np.float32)
    m = {k: (np.ascontiguousarray(v) if v.dtype != np.float64 else
             np.ascontiguousarray(v).astype(np.float32)) for k, v in m.items()}
    m["xin"] = xT_c
    return m


def run_layer01(layer, inp, prep, xT, debug=False):
    nc = build_layer01(layer, _consts(), debug=debug)
    in_maps = [layer01_inmap(layer, inp, prep, xT[c]) for c in range(8)]
    res = run_bass_kernel_spmd(nc, in_maps, core_ids=list(range(8)))
    if debug:
        return res.results
    return [res.results[c]["xout"] for c in range(8)]


def kernel(**inputs):
    inp = {k: np.asarray(v) for k, v in inputs.items()}
    prep = run_prep(inp)
    x = inp["L0_attn_x"]
    xT = [np.ascontiguousarray(x[c].T).astype(np.float32) for c in range(8)]
    xT = run_layer01(0, inp, prep, xT)
    xT = run_layer01(1, inp, prep, xT)
    xT = run_layer2(inp, prep, xT)
    xT = run_layer3(inp, prep, xT)
    out = np.stack([np.ascontiguousarray(np.asarray(t).T) for t in xT])
    return out.astype(np.float32)




def l3_router_hook(em, C, st, io):
    def router(em, C, c, xch, post=False):
        nc = em.nc
        if not post:
            mm_kseries(em, st["ps_l"][:, :],
                       [(st["gwT_sb"][:, f * 8:(f + 1) * 8],
                         xch[:, f * 512:(f + 1) * 512]) for f in range(NF)])
            return
        r_mean = st["r_mean"]
        r_rstd = st["r_rstd"]
        em.te.matmul(st["ps_b8"][0:8, :], C["onesrow_f"][0:1, 0:8], r_mean,
                     start=True, stop=True)
        em.barrier()
        em.ve.tensor_scalar(st["t8"][0:8, :], st["ps_b8"][0:8, :],
                            st["gsum_sb"][0:8, 0:1], None, ALU.mult)
        em.barrier()
        em.ve.tensor_tensor(st["t8b"][0:8, :], st["ps_l"][0:8, :], st["t8"][0:8, :],
                            ALU.subtract)
        em.barrier()
        em.te.matmul(st["ps_b8"][0:8, :], C["onesrow_f"][0:1, 0:8], r_rstd,
                     start=True, stop=True)
        em.barrier()
        em.ve.tensor_tensor(st["logits_sb"][0:8, c * 512:(c + 1) * 512],
                            st["t8b"][0:8, :], st["ps_b8"][0:8, :], ALU.mult)
        em.barrier()
    return router


def emit_l3_top2(em, C, st):
    nc = em.nc
    with ExitStack() as _es:
        lgT = _es.enter_context(nc.sbuf_tensor(nm("lgT"), [128, 8], F32))
        msk = _es.enter_context(nc.sbuf_tensor(nm("msk"), [128, 8], F32))
        wfT = _es.enter_context(nc.sbuf_tensor(nm("wfT"), [128, 8], F32))
        aa = _es.enter_context(nc.sbuf_tensor(nm("aa"), [128, 8], F32))
        m1 = _es.enter_context(nc.sbuf_tensor(nm("m1"), [128, 8], F32))
        ps_t = _es.enter_context(nc.psum_tensor(nm("ps_t"), [128, 128], F32))
        for t in range(16):
            em.te.transpose(ps_t[0:128, 0:8],
                            st["logits_sb"][0:8, t * 128:(t + 1) * 128],
                            C["ident_f32"][0:8, 0:8])
            em.barrier()
            em.ve.tensor_copy(lgT[:, 0:8], ps_t[0:128, 0:8])
            em.barrier()
            em.ve.reduce_max(m1[:, 0:1], lgT[:, 0:8], axis=AX.X)
            em.barrier()
            em.ve.tensor_scalar(msk[:, 0:8], lgT[:, 0:8], m1[:, 0:1], None,
                                ALU.is_equal)
            em.barrier()
            em.ve.tensor_scalar(aa[:, 0:8], msk[:, 0:8], 2.0e30, None, ALU.mult)
            em.barrier()
            em.ve.tensor_tensor(aa[:, 0:8], lgT[:, 0:8], aa[:, 0:8], ALU.subtract)
            em.barrier()
            em.ve.reduce_max(m1[:, 3:4], aa[:, 0:8], axis=AX.X)
            em.barrier()
            em.ve.tensor_scalar(m1[:, 4:5], m1[:, 0:1], -1.0, None, ALU.mult)
            em.barrier()
            em.se.activation(m1[:, 5:6], m1[:, 3:4], AF.Exp, bias=m1[:, 4:5])
            em.barrier()
            em.ve.tensor_scalar(m1[:, 6:7], m1[:, 5:6], 1.0, None, ALU.add)
            em.barrier()
            em.ve.reciprocal(m1[:, 6:7], m1[:, 6:7])
            em.barrier()
            em.ve.tensor_tensor(m1[:, 7:8], m1[:, 5:6], m1[:, 6:7], ALU.mult)
            em.barrier()
            em.ve.tensor_scalar(wfT[:, 0:8], msk[:, 0:8], m1[:, 6:7], None,
                                ALU.mult)
            em.barrier()
            em.ve.tensor_scalar(msk[:, 0:8], aa[:, 0:8], m1[:, 3:4], None,
                                ALU.is_equal, )
            em.barrier()
            em.ve.tensor_scalar(msk[:, 0:8], msk[:, 0:8], m1[:, 7:8], None,
                                ALU.mult)
            em.barrier()
            em.ve.tensor_tensor(wfT[:, 0:8], wfT[:, 0:8], msk[:, 0:8], ALU.add)
            em.barrier()
            em.te.transpose(ps_t[0:8, 0:128], wfT[:, 0:8], C["ident_f32"][:, :])
            em.barrier()
            em.ve.tensor_copy(st["wfull_bf"][0:8, t * 128:(t + 1) * 128],
                              ps_t[0:8, 0:128])
            em.barrier()


def emit_ffn_l3(em, C, st, xn_sb, io, acc_out):
    nc = em.nc
    NO = 44
    with ExitStack() as _es:
        a1T = _es.enter_context(nc.sbuf_tensor(nm("a1T"), [128, NF * 128], BF16))
        a3T = _es.enter_context(nc.sbuf_tensor(nm("a3T"), [128, NF * 128], BF16))
        ust = _es.enter_context(nc.sbuf_tensor(nm("ust"), [16, 512], BF16))
        ps_u = _es.enter_context(nc.psum_tensor(nm("ps_u"), [16, 512], F32))
        em.dma(em.gp, a1T[:].rearrange("p (f c) -> p f c", f=NF),
               io["a1T"].rearrange("(f p) c -> p f c", p=128), "d_w")
        em.dma(em.gp, a3T[:].rearrange("p (f c) -> p f c", f=NF),
               io["a3T"].rearrange("(f p) c -> p f c", p=128), "d_w")
        em.hard()
        for e in range(8):
            for src, dst in ((a1T, "u1_hbm"), (a3T, "u3_hbm")):
                for c in range(4):
                    mm_kseries(em, ps_u[0:16, :],
                               [(src[:, f * 128 + 16 * e: f * 128 + 16 * e + 16],
                                 xn_sb[:, f * S + c * 512: f * S + (c + 1) * 512])
                                for f in range(NF)])
                    em.barrier()
                    em.se.activation(ust[0:16, :], ps_u[0:16, :], AF.Identity,
                                     scale=2.0)
                    em.barrier()
                    em.dma(em.sy, io[dst][e * 16:(e + 1) * 16, c * 512:(c + 1) * 512],
                           ust[0:16, :], "d_sc")
                    em.hard()
    with ExitStack() as _es:
        w1c = _es.enter_context(nc.sbuf_tensor(nm("w1c"), [128, NF * 128], BF16))
        w3c = _es.enter_context(nc.sbuf_tensor(nm("w3c"), [128, NF * 128], BF16))
        stg = _es.enter_context(nc.sbuf_tensor(nm("stg"), [128, 512], BF16))
        ps = _es.enter_context(nc.psum_tensor(nm("ps"), [128, 512], F32))
        for f in range(NO):
            em.dma(em.gp, w1c[:].rearrange("p (k c) -> p k c", k=NF),
                   io["w1T"][:, f * 128:(f + 1) * 128].rearrange("(k p) c -> p k c", p=128),
                   "d_w")
            em.dma(em.gp, w3c[:].rearrange("p (k c) -> p k c", k=NF),
                   io["w3T"][:, f * 128:(f + 1) * 128].rearrange("(k p) c -> p k c", p=128),
                   "d_w")
            em.hard()
            for src, dst in ((w1c, "base1_hbm"), (w3c, "base3_hbm")):
                for c in range(4):
                    mm_kseries(em, ps[:, :],
                               [(src[:, k * 128:(k + 1) * 128],
                                 xn_sb[:, k * S + c * 512: k * S + (c + 1) * 512])
                                for k in range(NF)])
                    em.barrier()
                    em.se.activation(stg[:, :], ps[:, :], AF.Identity)
                    em.barrier()
                    em.dma(em.sy, io[dst][f * 128:(f + 1) * 128, c * 512:(c + 1) * 512],
                           stg[:, :], "d_sc")
                    em.hard()
    smm = em.sem("f3_mm")
    with ExitStack() as _es:
        b1e = _es.enter_context(nc.sbuf_tensor(nm("b1e"), [16, 5632], BF16))
        b3e = _es.enter_context(nc.sbuf_tensor(nm("b3e"), [16, 5632], BF16))
        u1e = _es.enter_context(nc.sbuf_tensor(nm("u1e"), [16, S], BF16))
        u3e = _es.enter_context(nc.sbuf_tensor(nm("u3e"), [16, S], BF16))
        a2T = _es.enter_context(nc.sbuf_tensor(nm("a2T"), [128, NO * 128], BF16))
        base1 = _es.enter_context(nc.sbuf_tensor(nm("base1"), [128, S], BF16))
        base3 = _es.enter_context(nc.sbuf_tensor(nm("base3"), [128, S], BF16))
        web = _es.enter_context(nc.sbuf_tensor(nm("web"), [128, S], F32))
        w16 = _es.enter_context(nc.sbuf_tensor(nm("w16"), [16, S], F32))
        t1 = _es.enter_context(nc.sbuf_tensor(nm("t1"), [128, 2 * 512], F32))
        s1 = _es.enter_context(nc.sbuf_tensor(nm("s1"), [128, 2 * 512], BF16))
        t3 = _es.enter_context(nc.sbuf_tensor(nm("t3"), [128, 2 * 512], BF16))
        actb = _es.enter_context(nc.sbuf_tensor(nm("actb"), [128, 2 * 512], BF16))
        actw = _es.enter_context(nc.sbuf_tensor(nm("actw"), [128, S], BF16))
        u2st = _es.enter_context(nc.sbuf_tensor(nm("u2st"), [16, 512], BF16))
        ps_d1 = _es.enter_context(nc.psum_tensor(nm("ps_d1"), [128, 512], F32))
        ps_d3 = _es.enter_context(nc.psum_tensor(nm("ps_d3"), [128, 512], F32))
        ps_u2 = [
            _es.enter_context(nc.psum_tensor(nm("ps_u2"), [16, 512], F32))
            for _ in range(4)]
        ps_w = _es.enter_context(nc.psum_tensor(nm("ps_w"), [128, 512], F32))
        em.dma(em.gp, a2T[:].rearrange("p (f c) -> p f c", f=NO),
               io["a2T"].rearrange("(f p) c -> p f c", p=128), "d_w")
        em.hard()
        for e in range(8):
            em.dma(em.gp, b1e[0:16, :], io["b1T"][e, :, :], "d_w")
            em.dma(em.gp, b3e[0:16, :], io["b3T"][e, :, :], "d_w")
            em.dma(em.sy, u1e[0:16, :], io["u1_hbm"][e * 16:(e + 1) * 16, :], "d_w")
            em.dma(em.sy, u3e[0:16, :], io["u3_hbm"][e * 16:(e + 1) * 16, :], "d_w")
            em.hard()
            for c in range(4):
                em.te.matmul(ps_w[:, :], st["sel8_sb"][:, e * 128:(e + 1) * 128],
                             st["wfull_bf"][0:8, c * 512:(c + 1) * 512],
                             start=True, stop=True)
                em.barrier()
                em.ve.tensor_copy(web[:, c * 512:(c + 1) * 512], ps_w[:, :])
                em.barrier()
                em.te.matmul(ps_w[0:16, :],
                             st["sel8_sb"][:, e * 128:e * 128 + 16],
                             st["wfull_bf"][0:8, c * 512:(c + 1) * 512],
                             start=True, stop=True)
                em.barrier()
                em.ve.tensor_copy(w16[0:16, c * 512:(c + 1) * 512], ps_w[0:16, :])
                em.barrier()
            gpdone = []
            for f in range(NO):
                em.wait(em.sy, "d_act")
                em.wait(em.sy, "q_ve")
                em.dma(em.sy, base1[:].rearrange("p (k c) -> p k c", k=4),
                       io["base1_hbm"][f * 128:(f + 1) * 128, :].rearrange("p (k c) -> p k c", k=4),
                       "d_act")
                em.dma(em.sy, base3[:].rearrange("p (k c) -> p k c", k=4),
                       io["base3_hbm"][f * 128:(f + 1) * 128, :].rearrange("p (k c) -> p k c", k=4),
                       "d_act")
                for eng in (em.te, em.ve, em.gp):
                    em.wait(eng, "d_act")
                for c in range(4):
                    csl = slice(c * 512, (c + 1) * 512)
                    pd1, pd3 = (ps_d1, ps_d3)
                    em.te.matmul(pd1[:, :], b1e[0:16, f * 128:(f + 1) * 128],
                                 u1e[0:16, csl], start=True, stop=True).then_inc(smm, 1)
                    em.bump("f3_mm")
                    em.te.matmul(pd3[:, :], b3e[0:16, f * 128:(f + 1) * 128],
                                 u3e[0:16, csl], start=True,
                                 stop=True).then_inc(smm, 1)
                    em.bump("f3_mm")
                    tb = t1[:, (c % 2) * 512:(c % 2 + 1) * 512]
                    sb1 = s1[:, (c % 2) * 512:(c % 2 + 1) * 512]
                    tb3 = t3[:, (c % 2) * 512:(c % 2 + 1) * 512]
                    ab = actb[:, (c % 2) * 512:(c % 2 + 1) * 512]
                    if gpdone:
                        em.wait(em.ve, "q_gp", gpdone[-1])
                    em.wait(em.ve, "f3_mm")
                    em.ve.tensor_tensor(tb, pd1[:, :], base1[:, csl], ALU.add)
                    em.ve.tensor_tensor(tb3, pd3[:, :], base3[:, csl], ALU.add)
                    em.wait(em.se, "q_ve")
                    em.se.activation(sb1, tb, AF.Silu)
                    em.wait(em.ve, "q_se")
                    em.ve.tensor_tensor(ab, sb1, tb3, ALU.mult)
                    em.wait(em.gp, "q_ve")
                    em.gp.tensor_tensor(actw[:, csl], ab, web[:, csl], ALU.mult)
                    gpdone.append(em.cnt("q_gp"))
                    em.wait(em.te, "q_gp")
                    em.te.matmul(ps_u2[c][0:16, :],
                                 a2T[:, f * 128 + 16 * e: f * 128 + 16 * e + 16],
                                 actw[:, csl], start=(f == 0), stop=(f == NO - 1))
                em.wait(em.gp, "q_gp")
                if e == 0:
                    em.dma(em.gp, io["ACT_hbm"][f * 128:(f + 1) * 128, :],
                           actw[:, :], "d_act")
                else:
                    h = em.sem("d_act_g")
                    em.gp._eng.dma_start(io["ACT_hbm"][f * 128:(f + 1) * 128, :],
                                         actw[:, :], accum_op=ALU.add).then_inc(h, 16)
                    em.bump("d_act_g", 16)
            em.barrier()
            em.barrier()
            for c in range(4):
                em.ve.scalar_tensor_tensor(u2st[0:16, :], ps_u2[c][0:16, :], 2.0,
                                           w16[0:16, c * 512:(c + 1) * 512],
                                           ALU.mult, ALU.mult)
                em.barrier()
                em.dma(em.sy, io["u2w_hbm"][e * 16:(e + 1) * 16,
                                            c * 512:(c + 1) * 512], u2st[0:16, :],
                       "d_sc")
                em.hard()
    with ExitStack() as _es:
        rhs = _es.enter_context(nc.sbuf_tensor(nm("rhs"), [128, NO * 512], BF16))
        dc = _es.enter_context(nc.sbuf_tensor(nm("dc"), [128, NO * 128], BF16))
        u2w = _es.enter_context(nc.sbuf_tensor(nm("u2w"), [16, 8 * 512], BF16))
        b2o = _es.enter_context(nc.sbuf_tensor(nm("b2o"), [16, 8 * 128], BF16))
        stg = _es.enter_context(nc.sbuf_tensor(nm("stg"), [128, 512], F32))
        ps = _es.enter_context(nc.psum_tensor(nm("ps"), [128, 512], F32))
        for c in range(4):
            em.dma(em.gp, rhs[:].rearrange("p (f c) -> p f c", f=NO),
                   io["ACT_hbm"][:, c * 512:(c + 1) * 512].rearrange("(f p) c -> p f c", p=128),
                   "d_w")
            em.dma(em.sy, u2w[0:16, :].rearrange("p (e c) -> p e c", e=8),
                   io["u2w_hbm"][:, c * 512:(c + 1) * 512].rearrange("(e p) c -> p e c", p=16),
                   "d_w")
            em.hard()
            for o in range(NF):
                em.dma(em.gp, dc[:].rearrange("p (f c) -> p f c", f=NO),
                       io["w2T"][:, o * 128:(o + 1) * 128].rearrange("(f p) c -> p f c", p=128),
                       "d_w")
                em.dma(em.gp, b2o[0:16, :].rearrange("p (e c) -> p e c", e=8),
                       io["b2T"].rearrange("r (e n) -> r e n", e=8)[:, :, o * 128:(o + 1) * 128],
                       "d_w")
                em.hard()
                pairs = [(dc[:, f * 128:(f + 1) * 128], rhs[:, f * 512:(f + 1) * 512])
                         for f in range(NO)]
                pairs += [(b2o[0:16, e * 128:(e + 1) * 128],
                           u2w[0:16, e * 512:(e + 1) * 512]) for e in range(8)]
                mm_kseries(em, ps[:, :], pairs)
                em.barrier()
                em.se.activation(stg[:, :], ps[:, :], AF.Identity)
                em.barrier()
                em.dma(em.sy, acc_out[o * 128:(o + 1) * 128, c * 512:(c + 1) * 512],
                       stg[:, :], "d_sc")
                em.hard()


def build_layer3(consts, debug=False):
    nc = bass.Bass("TRN2", target_bir_lowering=False)
    em = Em(nc)
    dk = "ExternalOutput" if debug else "Internal"
    specs = [
        ("xin", (D, S), F32, "ExternalInput"),
        ("xout", (D, S), F32, "ExternalOutput"),
        ("wq", (D, D), F32, "ExternalInput"),
        ("wk", (D, 512), F32, "ExternalInput"),
        ("wv", (D, 512), F32, "ExternalInput"),
        ("wo", (D, D), F32, "ExternalInput"),
        ("bo", (128, NF), F32, "ExternalInput"),
        ("gwT", (D, 8), F32, "ExternalInput"),
        ("w1T", (D, 5632), BF16, "ExternalInput"),
        ("w3T", (D, 5632), BF16, "ExternalInput"),
        ("w2T", (5632, D), BF16, "ExternalInput"),
        ("a1T", (D, 128), F32, "ExternalInput"),
        ("a3T", (D, 128), F32, "ExternalInput"),
        ("a2T", (5632, 128), F32, "ExternalInput"),
        ("b1T", (8, 16, 5632), F32, "ExternalInput"),
        ("b3T", (8, 16, 5632), F32, "ExternalInput"),
        ("b2T", (16, 8 * D), F32, "ExternalInput"),
        ("accA", (D, S), F32, dk),
        ("accF", (D, S), F32, dk),
        ("q_hbm", (D, S), BF16, "Internal"),
        ("k_hbm", (512, S), BF16, "Internal"),
        ("v_hbm", (S, 512), BF16, "Internal"),
        ("f_hbm", (D, S), BF16, "Internal"),
        ("base1_hbm", (5632, S), BF16, "Internal"),
        ("base3_hbm", (5632, S), BF16, "Internal"),
        ("ACT_hbm", (5632, S), BF16, dk),
        ("u1_hbm", (128, S), BF16, "Internal"),
        ("u3_hbm", (128, S), BF16, "Internal"),
        ("u2w_hbm", (128, S), BF16, dk),
    ]
    io = _io_tensors(nc, specs)
    cfg = ATTN_CFG[3]
    C = Consts(em, ["onescol_f", "onesrow_f", "onesrow_b", "ident_f32", "sel8",
                    "mask512_r0", "mask512_r1", "mask512_r2", "mask512_r3"], consts)
    wd_attn = dict(wq=io["wq"], wk=io["wk"], wv=io["wv"], wo=io["wo"], bo=io["bo"])
    ast = attn_alloc(em, cfg)
    st = {}
    st["gwT_sb"] = nc.alloc_sbuf_tensor(nm("gwT"), [128, NF * 8], F32)
    st["gsum_sb"] = nc.alloc_sbuf_tensor(nm("gsum"), [8, 1], F32)
    st["logits_sb"] = nc.alloc_sbuf_tensor(nm("logits"), [8, S], F32)
    st["wfull_bf"] = nc.alloc_sbuf_tensor(nm("wfull"), [8, S], BF16)
    st["t8"] = nc.alloc_sbuf_tensor(nm("t8"), [8, 512], F32)
    st["t8b"] = nc.alloc_sbuf_tensor(nm("t8b"), [8, 512], F32)
    st["sel8_sb"] = C["sel8"]
    em.dma(em.sy, st["gwT_sb"][:].rearrange("p (f c) -> p f c", f=NF),
           io["gwT"].rearrange("(f p) c -> p f c", p=128), "d_w")
    em.hard()
    em.barrier()
    mean_sb = nc.alloc_sbuf_tensor(nm("p3mean"), [1, S], F32)
    rstd_sb = nc.alloc_sbuf_tensor(nm("p3rstd"), [1, S], F32)
    with nc.sbuf_tensor(nm("xn"), [128, NF * S], BF16) as xn_sb:
        with ExitStack() as _es:
            ps_l = _es.enter_context(nc.psum_tensor(nm("ps_l"), [8, 512], F32))
            ps_b8 = _es.enter_context(nc.psum_tensor(nm("ps_b8"), [8, 512], F32))
            st["ps_l"] = ps_l
            st["ps_b8"] = ps_b8
            mm_kseries(em, ps_b8[0:8, 0:1],
                       [(st["gwT_sb"][:, f * 8:(f + 1) * 8], C["onescol_f"][:, :])
                        for f in range(NF)])
            em.barrier()
            em.ve.tensor_copy(st["gsum_sb"][0:8, :], ps_b8[0:8, 0:1])
            em.barrier()
            rows_holder = {}

            def router(em_, C_, c, xch, post=False):
                if not post:
                    mm_kseries(em_, ps_l[0:8, :],
                               [(st["gwT_sb"][:, f * 8:(f + 1) * 8],
                                 xch[:, f * 512:(f + 1) * 512]) for f in range(NF)])
                    return
                r_mean, r_rstd = rows_holder["mean"], rows_holder["rstd"]
                em_.te.matmul(ps_b8[0:8, :], C_["onesrow_f"][0:1, 0:8], r_mean,
                              start=True, stop=True)
                em_.barrier()
                em_.ve.tensor_scalar(st["t8"][0:8, :], ps_b8[0:8, :],
                                     st["gsum_sb"][0:8, 0:1], None, ALU.mult)
                em_.barrier()
                em_.ve.tensor_tensor(st["t8b"][0:8, :], ps_l[0:8, :],
                                     st["t8"][0:8, :], ALU.subtract)
                em_.barrier()
                em_.te.matmul(ps_b8[0:8, :], C_["onesrow_f"][0:1, 0:8], r_rstd,
                              start=True, stop=True)
                em_.barrier()
                em_.ve.tensor_tensor(st["logits_sb"][0:8, c * 512:(c + 1) * 512],
                                     st["t8b"][0:8, :], ps_b8[0:8, :], ALU.mult)
                em_.barrier()

            def router3(em_, C_, c, xch, post=False):
                if post:
                    rows_holder["mean"] = mean_sb[:, c * 512:(c + 1) * 512]
                    rows_holder["rstd"] = rstd_sb[:, c * 512:(c + 1) * 512]
                router(em_, C_, c, xch, post=post)

            emit_phase0_l3(em, C, {"x_in": io["xin"]}, xn_sb, router3,
                           mean_sb, rstd_sb)
        emit_l3_top2(em, C, st)
        emit_ffn_l3(em, C, st, xn_sb, io, io["accF"])
        emit_attn_proj(em, C, ast, xn_sb, cfg, wd_attn, io["q_hbm"],
                       io["v_hbm"], io["k_hbm"])
    emit_attn_core(em, C, ast, cfg, io["q_hbm"], io["v_hbm"], io["k_hbm"],
                   io["f_hbm"], io["accA"], wd_attn)
    emit_phase0(em, C, {"x_in": io["xin"]}, adds=(io["accA"], io["accF"]),
                out_x=io["xout"], combine_only=True)
    em.hard()
    return nc


def emit_phase0_l3(em, C, io, xn_sb, router3, mean_sb, rstd_sb):
    emit_phase0(em, C, io, xn_sb=xn_sb, router=router3,
                save_rows={"mean": mean_sb, "rstd": rstd_sb})


def run_layer3(inp, prep, xT, debug=False):
    nc = build_layer3(_consts(), debug=debug)
    m = dict(
        wq=_t(inp["L3_attn_Wq"]), wk=_t(inp["L3_attn_Wk"]),
        wv=_t(inp["L3_attn_Wv"]), wo=_t(inp["L3_attn_Wo"]),
        bo=_pcol(inp["L3_attn_bo"]),
        gwT=_t(inp["L3_ffn_gate_w"]),
        w1T=prep["l3w1_bf"], w3T=prep["l3w3_bf"], w2T=prep["l3w2_bf"],
        a1T=np.ascontiguousarray(np.transpose(inp["L3_ffn_a1"], (2, 0, 1)).reshape(D, 128)),
        a3T=np.ascontiguousarray(np.transpose(inp["L3_ffn_a3"], (2, 0, 1)).reshape(D, 128)),
        a2T=np.ascontiguousarray(np.transpose(inp["L3_ffn_a2"], (2, 0, 1)).reshape(5632, 128)),
        b1T=np.ascontiguousarray(np.transpose(inp["L3_ffn_b1"], (0, 2, 1))),
        b3T=np.ascontiguousarray(np.transpose(inp["L3_ffn_b3"], (0, 2, 1))),
        b2T=np.ascontiguousarray(np.transpose(inp["L3_ffn_b2"], (2, 0, 1)).reshape(16, 8 * D)),
    )
    m = {k: np.ascontiguousarray(v).astype(np.float32)
         if np.asarray(v).dtype == np.float64 or np.asarray(v).dtype == np.float32
         else np.ascontiguousarray(v) for k, v in m.items()}
    in_maps = []
    for c in range(8):
        mc = dict(m)
        mc["xin"] = xT[c]
        in_maps.append(mc)
    res = run_bass_kernel_spmd(nc, in_maps, core_ids=list(range(8)))
    if debug:
        return res.results
    return [res.results[c]["xout"] for c in range(8)]




def build_layer2(consts, debug=False):
    nc = bass.Bass("TRN2", target_bir_lowering=False)
    em = Em(nc)
    dk = "ExternalOutput" if debug else "Internal"
    specs = [
        ("xin", (D, S), F32, "ExternalInput"),
        ("xout", (D, S), F32, "ExternalOutput"),
        ("wq", (D, D), F32, "ExternalInput"),
        ("wk", (D, D), F32, "ExternalInput"),
        ("wv", (D, D), F32, "ExternalInput"),
        ("wz", (D, D), F32, "ExternalInput"),
        ("wab", (D, 32), F32, "ExternalInput"),
        ("convw", (D, 12), F32, "ExternalInput"),
        ("normw", (128, NF), F32, "ExternalInput"),
        ("wout", (D, D), F32, "ExternalInput"),
        ("wr", (128, NF), F32, "ExternalInput"),
        ("br", (1, 1), F32, "ExternalInput"),
        ("w1", (D, 8192), BF16, "ExternalInput"),
        ("b1", (128, 64), F32, "ExternalInput"),
        ("w2", (8192, D), BF16, "ExternalInput"),
        ("b2", (128, NF), F32, "ExternalInput"),
        ("accA", (D, S), F32, dk),
        ("accF", (D, S), F32, dk),
        ("qn_hbm", (D, S), BF16, dk),
        ("kn_hbm", (D, S), BF16, dk),
        ("v_hbm", (D, S), BF16, dk),
        ("z_hbm", (D, S), BF16, dk),
        ("wrow_dbg", (1, S), F32, dk),
        ("sel_dbg", (1, S), F32, dk),
    ]
    io = _io_tensors(nc, specs)
    C = Consts(em, ["onescol_f", "onesrow_f", "onesrow_b", "ident_f32", "selE",
                    "selB", "ropecos", "ropesin", "rope_R"], consts)
    ab_sb = nc.alloc_sbuf_tensor(nm("ab"), [16, 2 * (S + 3)], F32)
    wrow_sb = nc.alloc_sbuf_tensor(nm("wrow"), [1, S], F32)
    selw_sb = nc.alloc_sbuf_tensor(nm("selw"), [1, S], F32)
    comb_sb = nc.alloc_sbuf_tensor(nm("comb"), [1, 2 * S], F32)
    mean_sb = nc.alloc_sbuf_tensor(nm("mean2"), [1, S], F32)
    rstd_sb = nc.alloc_sbuf_tensor(nm("rstd2"), [1, S], F32)
    misc_sb = nc.alloc_sbuf_tensor(nm("misc2"), [128, 8], F32)
    wsum_sb = nc.alloc_sbuf_tensor(nm("wsum2"), [1, 1], F32)
    wr_sb = nc.alloc_sbuf_tensor(nm("wr2"), [128, NF], F32)
    normw_sb = nc.alloc_sbuf_tensor(nm("nw2"), [128, NF], F32)
    wab_sb = nc.alloc_sbuf_tensor(nm("wab2"), [128, NF * 32], BF16)
    convw_sb = nc.alloc_sbuf_tensor(nm("cw2"), [128, NF * 12], F32)
    em.dma(em.sy, wr_sb[:, :], io["wr"][:, :], "d_w")
    em.dma(em.sy, normw_sb[:, :], io["normw"][:, :], "d_w")
    em.dma(em.gp, wab_sb[:].rearrange("p (f c) -> p f c", f=NF),
           io["wab"].rearrange("(f p) c -> p f c", p=128), "d_w")
    em.dma(em.sy, convw_sb[:].rearrange("p (f c) -> p f c", f=NF),
           io["convw"].rearrange("(f p) c -> p f c", p=128), "d_w")
    em.dma(em.sy, wsum_sb[0:1, 0:1], io["br"][:, :], "d_w")
    em.hard()
    with nc.psum_tensor(nm("psg"), [1, 512], F32) as psg:
        mm_kseries(em, psg[0:1, 0:1],
                   [(wr_sb[:, f:f + 1], C["onescol_f"][:, :]) for f in range(NF)])
        em.barrier()
        em.ve.tensor_copy(misc_sb[0:1, 0:1], psg[0:1, 0:1])
        em.barrier()
    br_sb = wsum_sb
    rows_holder = {}

    def router(em_, C_, c, xch, post=False):
        if not post:
            mm_kseries(em_, rt["ps_w"][0:1, :],
                       [(wr_sb[:, f:f + 1], xch[:, f * 256:(f + 1) * 256])
                        for f in range(NF)])
            return
        tsl = slice(c * 256, (c + 1) * 256)
        r_mean, r_rstd = rows_holder["mean"], rows_holder["rstd"]
        em_.ve.tensor_scalar(rt["t1"][0:1, :], r_mean, misc_sb[0:1, 0:1], None,
                             ALU.mult)
        em_.barrier()
        em_.ve.tensor_tensor(rt["t1"][0:1, :], rt["ps_w"][0:1, :], rt["t1"][0:1, :],
                             ALU.subtract)
        em_.barrier()
        em_.ve.tensor_tensor(rt["t1"][0:1, :], rt["t1"][0:1, :], r_rstd, ALU.mult)
        em_.barrier()
        em_.ve.tensor_scalar(wrow_sb[0:1, tsl], rt["t1"][0:1, :],
                             br_sb[0:1, 0:1], None, ALU.add)
        em_.barrier()

    def router3(em_, C_, c, xch, post=False):
        if post:
            rows_holder["mean"] = mean_sb[:, c * 256:(c + 1) * 256]
            rows_holder["rstd"] = rstd_sb[:, c * 256:(c + 1) * 256]
        router(em_, C_, c, xch, post=post)

    rt = {}
    with nc.sbuf_tensor(nm("xn"), [128, NF * S], BF16) as xn_sb:
        with ExitStack() as _es:
            rt["ps_w"] = _es.enter_context(nc.psum_tensor(nm("ps_w"), [1, 256], F32))
            rt["t1"] = _es.enter_context(nc.sbuf_tensor(nm("rt_t1"), [1, 256], F32))
            emit_phase0(em, C, {"x_in": io["xin"]}, xn_sb=xn_sb, router=router3,
                        save_rows={"mean": mean_sb, "rstd": rstd_sb}, ct=256)
        emit_l2_select(em, C, io, wrow_sb, selw_sb, comb_sb, mean_sb, rstd_sb)
        emit_l2_ffn(em, C, io, xn_sb, selw_sb)
        emit_l2_attnproj(em, C, io, xn_sb, ab_sb, wab_sb, convw_sb)
    emit_l2_attncore(em, C, io, ab_sb, normw_sb)
    def l2fix(em_, C_, c, xch):
        with ExitStack() as _es:
            ps_a = _es.enter_context(nc.psum_tensor(nm("psfa"), [128, 256], F32))
            ps_b = _es.enter_context(nc.psum_tensor(nm("psfb"), [128, 256], F32))
            em_.te.matmul(ps_a[:, :], C_["onesrow_f"][:, :], comb_sb[0:1, c * 256:(c + 1) * 256],
                          start=True, stop=True)
            em_.te.matmul(ps_b[:, :], C_["onesrow_f"][:, :], comb_sb[0:1, S + c * 256:S + (c + 1) * 256],
                          start=True, stop=True)
            em_.barrier()
            for f in range(NF):
                em_.ve.tensor_tensor(xch[:, f * 256:(f + 1) * 256],
                                     xch[:, f * 256:(f + 1) * 256], ps_a[:, :],
                                     ALU.mult)
            em_.barrier()
            for f in range(NF):
                em_.ve.tensor_tensor(xch[:, f * 256:(f + 1) * 256],
                                     xch[:, f * 256:(f + 1) * 256], ps_b[:, :],
                                     ALU.add)
            em_.barrier()
    emit_phase0(em, C, {"x_in": io["xin"]}, adds=(io["accA"], io["accF"]),
                out_x=io["xout"], combine_only=True, l2fix=l2fix, ct=256)
    if debug:
        em.ve.tensor_copy(wrow_sb[0:1, 0:1], wrow_sb[0:1, 0:1])
        em.dma(em.sy, io["wrow_dbg"][:, :], wrow_sb[0:1, :], "d_sc")
        em.dma(em.sy, io["sel_dbg"][:, :], selw_sb[0:1, :], "d_sc")
    em.hard()
    return nc


def emit_l2_select(em, C, io, wrow_sb, selw_sb, comb_sb, mean_sb, rstd_sb):
    nc = em.nc
    with ExitStack() as _es:
        wcol = _es.enter_context(nc.sbuf_tensor(nm("wcol"), [128, 16], F32))
        wb = _es.enter_context(nc.sbuf_tensor(nm("wb"), [128, S], F32))
        cmp = _es.enter_context(nc.sbuf_tensor(nm("cmp"), [128, 512], F32))
        rowt = _es.enter_context(nc.sbuf_tensor(nm("rowt"), [1, 2 * S], F32))
        ps_t = _es.enter_context(nc.psum_tensor(nm("ps_t2"), [128, 128], F32))
        ps_c = _es.enter_context(nc.psum_tensor(nm("ps_c2"), [1, 512], F32))
        ps_bx = _es.enter_context(nc.psum_tensor(nm("ps_bx"), [128, 512], F32))
        for i in range(16):
            em.te.transpose(ps_t[0:128, 0:1], wrow_sb[0:1, i * 128:(i + 1) * 128],
                            C["ident_f32"][0:1, 0:1])
            em.barrier()
            em.ve.tensor_copy(wcol[:, i:i + 1], ps_t[0:128, 0:1])
            em.barrier()
        for c in range(4):
            em.te.matmul(ps_bx[:, :], C["onesrow_f"][:, :],
                         wrow_sb[0:1, c * 512:(c + 1) * 512], start=True, stop=True)
            em.barrier()
            em.ve.tensor_copy(wb[:, c * 512:(c + 1) * 512], ps_bx[:, :])
            em.barrier()
        for c in range(4):
            tsl = slice(c * 512, (c + 1) * 512)
            for i in range(16):
                em.ve.tensor_scalar(cmp[:, :], wb[:, tsl], wcol[:, i:i + 1], None,
                                    ALU.is_lt)
                em.barrier()
                em.te.matmul(ps_c[0:1, :], C["onescol_f"][:, :], cmp[:, :],
                             start=(i == 0), stop=(i == 15))
                em.barrier()
            em.ve.tensor_scalar(rowt[0:1, 0:512], ps_c[0:1, :], 510.5, None,
                                ALU.is_lt)
            em.barrier()
            em.ve.tensor_tensor(selw_sb[0:1, tsl], rowt[0:1, 0:512],
                                wrow_sb[0:1, tsl], ALU.mult)
            em.ve.tensor_scalar(rowt[0:1, 512:1024], rowt[0:1, 0:512], -1.0, 1.0,
                                ALU.mult, ALU.add)
            em.barrier()
            em.ve.tensor_tensor(rowt[0:1, 1024:1536], rowt[0:1, 512:1024],
                                rstd_sb[0:1, tsl], ALU.mult)
            em.barrier()
            em.ve.tensor_scalar(comb_sb[0:1, tsl], rowt[0:1, 1024:1536], 1.0, None,
                                ALU.add)
            em.ve.tensor_tensor(rowt[0:1, 1536:2048], rowt[0:1, 1024:1536],
                                mean_sb[0:1, tsl], ALU.mult)
            em.barrier()
            em.ve.tensor_scalar(comb_sb[0:1, S + c * 512:S + (c + 1) * 512],
                                rowt[0:1, 1536:2048], -1.0, None, ALU.mult)
            em.barrier()


def emit_l2_ffn(em, C, io, xn_sb, selw_sb):
    nc = em.nc
    GA = 0.7978845608028654
    GB = 0.044715
    CT = 256
    with ExitStack() as _es:
        wc = _es.enter_context(nc.sbuf_tensor(nm("wc"), [128, NF * 128], BF16))
        b1_sb = _es.enter_context(nc.sbuf_tensor(nm("b1s"), [128, 64], F32))
        b2_sb = _es.enter_context(nc.sbuf_tensor(nm("b2s"), [128, NF], F32))
        hch = _es.enter_context(nc.sbuf_tensor(nm("hch"), [128, 64 * CT], BF16))
        dc = _es.enter_context(nc.sbuf_tensor(nm("dc"), [128, 64 * 128], BF16))
        ug = _es.enter_context(nc.sbuf_tensor(nm("ug"), [128, CT], F32))
        t_ = _es.enter_context(nc.sbuf_tensor(nm("t_"), [128, CT], F32))
        stg = _es.enter_context(nc.sbuf_tensor(nm("stg"), [128, CT], F32))
        ps = _es.enter_context(nc.psum_tensor(nm("ps"), [128, CT], F32))
        ps_b = _es.enter_context(nc.psum_tensor(nm("ps_b2b"), [128, CT], F32))
        em.dma(em.sy, b1_sb[:, :], io["b1"][:, :], "d_w")
        em.dma(em.sy, b2_sb[:, :], io["b2"][:, :], "d_w")
        em.hard()
        for c in range(S // CT):
            tsl = slice(c * CT, (c + 1) * CT)
            for o in range(64):
                em.dma(em.gp, wc[:].rearrange("p (f x) -> p f x", f=NF),
                       io["w1"][:, o * 128:(o + 1) * 128].rearrange("(f p) x -> p f x", p=128),
                       "d_w")
                em.hard()
                mm_kseries(em, ps[:, :],
                           [(wc[:, f * 128:(f + 1) * 128],
                             xn_sb[:, f * S + c * CT: f * S + (c + 1) * CT])
                            for f in range(NF)])
                em.barrier()
                em.se.activation(ug[:, :], ps[:, :], AF.Identity,
                                 bias=b1_sb[:, o:o + 1])
                em.barrier()
                em.se.activation(t_[:, :], ug[:, :], AF.Square)
                em.barrier()
                em.ve.tensor_scalar(t_[:, :], t_[:, :], GA * GB, GA, ALU.mult,
                                    ALU.add)
                em.barrier()
                em.ve.tensor_tensor(t_[:, :], ug[:, :], t_[:, :], ALU.mult)
                em.barrier()
                em.se.activation(t_[:, :], t_[:, :], AF.Tanh)
                em.barrier()
                em.ve.tensor_scalar(t_[:, :], t_[:, :], 0.5, 0.5, ALU.mult, ALU.add)
                em.barrier()
                em.ve.tensor_tensor(hch[:, o * CT:(o + 1) * CT], ug[:, :], t_[:, :],
                                    ALU.mult)
                em.barrier()
            em.te.matmul(ps_b[:, :], C["onesrow_f"][:, :], selw_sb[0:1, tsl],
                         start=True, stop=True)
            em.barrier()
            for o2 in range(NF):
                em.dma(em.gp, dc[:].rearrange("p (f x) -> p f x", f=64),
                       io["w2"][:, o2 * 128:(o2 + 1) * 128].rearrange("(f p) x -> p f x", p=128),
                       "d_w")
                em.hard()
                mm_kseries(em, ps[:, :],
                           [(dc[:, f * 128:(f + 1) * 128],
                             hch[:, f * CT:(f + 1) * CT]) for f in range(64)])
                em.barrier()
                em.se.activation(stg[:, :], ps[:, :], AF.Identity,
                                 bias=b2_sb[:, o2:o2 + 1])
                em.barrier()
                em.ve.tensor_tensor(stg[:, :], stg[:, :], ps_b[:, :], ALU.mult)
                em.barrier()
                em.dma(em.sy, io["accF"][o2 * 128:(o2 + 1) * 128, tsl], stg[:, :],
                       "d_sc")
                em.hard()


def emit_l2_attnproj(em, C, io, xn_sb, ab_sb, wab_sb, convw_sb):
    nc = em.nc
    with ExitStack() as _es:
        wc = _es.enter_context(nc.sbuf_tensor(nm("wc"), [128, NF * 128], BF16))
        raw = _es.enter_context(nc.sbuf_tensor(nm("raw"), [128, S + 3], F32))
        cvb = _es.enter_context(nc.sbuf_tensor(nm("cvb"), [128, S], BF16))
        qr = _es.enter_context(nc.sbuf_tensor(nm("qr"), [128, 512], F32))
        t_ = _es.enter_context(nc.sbuf_tensor(nm("t_2"), [128, 512], F32))
        rows = _es.enter_context(nc.sbuf_tensor(nm("rows2"), [1, 512], F32))
        stg = _es.enter_context(nc.sbuf_tensor(nm("stg2"), [128, 512], BF16))
        ps = _es.enter_context(nc.psum_tensor(nm("ps2"), [128, 512], F32))
        ps_r = _es.enter_context(nc.psum_tensor(nm("ps_r2"), [128, 512], F32))
        ps_s = _es.enter_context(nc.psum_tensor(nm("ps_s2"), [1, 512], F32))
        ps_bx = _es.enter_context(nc.psum_tensor(nm("ps_bx2"), [128, 512], F32))
        em.ve.memset(raw[:, 0:3], 0.0)
        em.barrier()
        for name, w_ap, dst, do_rope in (
                ("q", io["wq"], io["qn_hbm"], True),
                ("k", io["wk"], io["kn_hbm"], True),
                ("v", io["wv"], io["v_hbm"], False)):
            cvcol = {"q": 0, "k": 4, "v": 8}[name]
            for h in range(16):
                em.dma(em.gp, wc[:].rearrange("p (f x) -> p f x", f=NF),
                       w_ap[:, h * 128:(h + 1) * 128].rearrange("(f p) x -> p f x", p=128),
                       "d_w")
                em.hard()
                for c in range(4):
                    mm_kseries(em, ps[:, :],
                               [(wc[:, f * 128:(f + 1) * 128],
                                 xn_sb[:, f * S + c * 512: f * S + (c + 1) * 512])
                                for f in range(NF)])
                    em.barrier()
                    em.se.activation(raw[:, 3 + c * 512:3 + (c + 1) * 512], ps[:, :],
                                     AF.Silu)
                    em.barrier()
                cw = convw_sb
                for c in range(4):
                    em.ve.tensor_scalar(t_[:, :], raw[:, 3 + c * 512 - 3: 3 + (c + 1) * 512 - 3],
                                        cw[:, h * 12 + cvcol: h * 12 + cvcol + 1],
                                        None, ALU.mult)
                    em.barrier()
                    for l in (2, 1, 0):
                        j = 3 - l
                        em.ve.scalar_tensor_tensor(
                            t_[:, :],
                            raw[:, 3 + c * 512 - l: 3 + (c + 1) * 512 - l],
                            cw[:, h * 12 + cvcol + j: h * 12 + cvcol + j + 1],
                            t_[:, :], ALU.mult, ALU.add)
                        em.barrier()
                    em.ve.tensor_copy(cvb[:, c * 512:(c + 1) * 512], t_[:, :])
                    em.barrier()
                if not do_rope:
                    em.dma(em.sy, dst[h * 128:(h + 1) * 128, :], cvb[:, :], "d_sc")
                    em.hard()
                    continue
                for c in range(4):
                    csl = slice(c * 512, (c + 1) * 512)
                    em.te.matmul(ps_r[:, :], C["rope_R"][:, :], cvb[:, csl],
                                 start=True, stop=True)
                    em.barrier()
                    em.ve.tensor_tensor(qr[:, :], cvb[:, csl],
                                        C["ropecos"][:, csl], ALU.mult)
                    em.barrier()
                    em.ve.tensor_tensor(t_[:, :], ps_r[:, :], C["ropesin"][:, csl],
                                        ALU.mult)
                    em.barrier()
                    em.ve.tensor_tensor(qr[:, :], qr[:, :], t_[:, :], ALU.add)
                    em.barrier()
                    em.se.activation(t_[:, :], qr[:, :], AF.Square)
                    em.barrier()
                    em.te.matmul(ps_s[0:1, :], C["onescol_f"][:, :], t_[:, :],
                                 start=True, stop=True)
                    em.barrier()
                    em.se.activation(rows[0:1, :], ps_s[0:1, :], AF.Sqrt)
                    em.barrier()
                    em.ve.tensor_scalar(rows[0:1, :], rows[0:1, :], 1e-12, None,
                                        ALU.max)
                    em.barrier()
                    em.ve.reciprocal(rows[0:1, :], rows[0:1, :])
                    em.barrier()
                    em.te.matmul(ps_bx[:, :], C["onesrow_f"][:, :], rows[0:1, :],
                                 start=True, stop=True)
                    em.barrier()
                    em.ve.tensor_tensor(stg[:, :], qr[:, :], ps_bx[:, :], ALU.mult)
                    em.barrier()
                    em.dma(em.sy, dst[h * 128:(h + 1) * 128, csl], stg[:, :], "d_sc")
                    em.hard()
        for h in range(16):
            em.dma(em.gp, wc[:].rearrange("p (f x) -> p f x", f=NF),
                   io["wz"][:, h * 128:(h + 1) * 128].rearrange("(f p) x -> p f x", p=128),
                   "d_w")
            em.hard()
            for c in range(4):
                mm_kseries(em, ps[:, :],
                           [(wc[:, f * 128:(f + 1) * 128],
                             xn_sb[:, f * S + c * 512: f * S + (c + 1) * 512])
                            for f in range(NF)])
                em.barrier()
                em.se.activation(stg[:, :], ps[:, :], AF.Silu)
                em.barrier()
                em.dma(em.sy, io["z_hbm"][h * 128:(h + 1) * 128, c * 512:(c + 1) * 512],
                       stg[:, :], "d_sc")
                em.hard()
        em.ve.memset(ab_sb[0:16, :], 0.0)
        em.barrier()
        with nc.psum_tensor(nm("ps_ab"), [16, 512], F32) as ps_ab:
            for c in range(4):
                mm_kseries(em, ps_ab[0:16, :],
                           [(wab_sb[:, f * 32:f * 32 + 16],
                             xn_sb[:, f * S + c * 512: f * S + (c + 1) * 512])
                            for f in range(NF)])
                em.barrier()
                em.ve.tensor_copy(ab_sb[0:16, 3 + c * 512: 3 + (c + 1) * 512],
                                  ps_ab[0:16, :])
                em.barrier()
                mm_kseries(em, ps_ab[0:16, :],
                           [(wab_sb[:, f * 32 + 16:f * 32 + 32],
                             xn_sb[:, f * S + c * 512: f * S + (c + 1) * 512])
                            for f in range(NF)])
                em.barrier()
                em.ve.tensor_copy(ab_sb[0:16, (S + 3) + 3 + c * 512: (S + 3) + 3 + (c + 1) * 512],
                                  ps_ab[0:16, :])
                em.barrier()
        em.hard()


def emit_l2_attncore(em, C, io, ab_sb, normw_sb):
    nc = em.nc
    ESC = 1.0 / math.sqrt(128.0)
    with ExitStack() as _es:
        qnb = _es.enter_context(nc.sbuf_tensor(nm("qnb"), [128, S], BF16))
        knb = _es.enter_context(nc.sbuf_tensor(nm("knb"), [128, S + 3], BF16))
        vb = _es.enter_context(nc.sbuf_tensor(nm("vb"), [128, S + 3], BF16))
        m_ = _es.enter_context(nc.sbuf_tensor(nm("m_"), [128, 512], BF16))
        gt = _es.enter_context(nc.sbuf_tensor(nm("gt"), [16, 512], F32))
        pb = _es.enter_context(nc.sbuf_tensor(nm("pb"), [16, 4 * 512], BF16))
        oacc = _es.enter_context(nc.sbuf_tensor(nm("oacc"), [128, NF * 512], F32))
        t_ = _es.enter_context(nc.sbuf_tensor(nm("t_3"), [128, 512], F32))
        zb = _es.enter_context(nc.sbuf_tensor(nm("zb"), [128, 512], BF16))
        wor = _es.enter_context(nc.sbuf_tensor(nm("wor"), [128, NF * 512], BF16))
        rows = _es.enter_context(nc.sbuf_tensor(nm("rows3"), [1, 512], F32))
        wc = _es.enter_context(nc.sbuf_tensor(nm("wc3"), [128, NF * 128], BF16))
        stg = _es.enter_context(nc.sbuf_tensor(nm("stg3"), [128, 512], F32))
        ps_at = [_es.enter_context(nc.psum_tensor(nm("ps_at"), [16, 512], F32))
                 for _ in range(4)]
        ps_w = _es.enter_context(nc.psum_tensor(nm("ps_w3"), [128, 512], F32))
        ps_s = _es.enter_context(nc.psum_tensor(nm("ps_s3"), [1, 512], F32))
        ps_bx = _es.enter_context(nc.psum_tensor(nm("ps_bx3"), [128, 512], F32))
        em.ve.memset(knb[:, 0:3], 0.0)
        em.ve.memset(vb[:, 0:3], 0.0)
        em.barrier()
        for c in range(4):
            tsl = slice(c * 512, (c + 1) * 512)
            for h in range(16):
                em.dma(em.sy, qnb[:, tsl], io["qn_hbm"][h * 128:(h + 1) * 128, tsl],
                       "d_sc")
                em.dma(em.sy, knb[:, 3:3 + S], io["kn_hbm"][h * 128:(h + 1) * 128, :],
                       "d_sc")
                em.hard()
                for s_ in range(4):
                    em.ve.tensor_tensor(
                        m_[:, :], qnb[:, tsl],
                        knb[:, 3 + c * 512 - s_: 3 + (c + 1) * 512 - s_], ALU.mult)
                    em.barrier()
                    em.te.matmul(ps_at[s_][0:16, :], C["selE"][:, h * 16:(h + 1) * 16],
                                 m_[:, :], start=(h == 0), stop=(h == 15))
                    em.barrier()
            for s_ in range(4):
                a_sh = ab_sb[0:16, 3 + c * 512 - s_: 3 + (c + 1) * 512 - s_]
                b_r = ab_sb[0:16, (S + 3) + 3 + c * 512: (S + 3) + 3 + (c + 1) * 512]
                em.ve.tensor_tensor(gt[0:16, :], a_sh, b_r, ALU.mult)
                em.barrier()
                em.se.activation(gt[0:16, :], gt[0:16, :], AF.Sigmoid)
                em.barrier()
                em.ve.scalar_tensor_tensor(pb[0:16, s_ * 512:(s_ + 1) * 512],
                                           ps_at[s_][0:16, :], ESC, gt[0:16, :],
                                           ALU.mult, ALU.mult)
                em.barrier()
            em.ve.memset(oacc[:, :], 0.0)
            em.barrier()
            for h in range(16):
                em.dma(em.sy, vb[:, 3:3 + S], io["v_hbm"][h * 128:(h + 1) * 128, :],
                       "d_sc")
                em.hard()
                for s_ in range(4):
                    em.te.matmul(ps_w[:, :], C["selB"][0:16, h * 128:(h + 1) * 128],
                                 pb[0:16, s_ * 512:(s_ + 1) * 512], start=True,
                                 stop=True)
                    em.barrier()
                    em.ve.tensor_tensor(t_[:, :], ps_w[:, :],
                                        vb[:, 3 + c * 512 - s_: 3 + (c + 1) * 512 - s_],
                                        ALU.mult)
                    em.barrier()
                    em.ve.tensor_tensor(oacc[:, h * 512:(h + 1) * 512],
                                        oacc[:, h * 512:(h + 1) * 512], t_[:, :],
                                        ALU.add)
                    em.barrier()
            for h in range(16):
                em.se.activation(t_[:, :], oacc[:, h * 512:(h + 1) * 512], AF.Square)
                em.barrier()
                em.te.matmul(ps_s[0:1, :], C["onescol_f"][:, :], t_[:, :],
                             start=(h == 0), stop=(h == 15))
                em.barrier()
            em.ve.tensor_scalar(rows[0:1, :], ps_s[0:1, :], 1.0 / D, None, ALU.mult)
            em.barrier()
            em.ve.tensor_scalar(rows[0:1, :], rows[0:1, :], 1e-6, None, ALU.add)
            em.barrier()
            em.se.activation(rows[0:1, :], rows[0:1, :], AF.Sqrt)
            em.barrier()
            em.ve.reciprocal(rows[0:1, :], rows[0:1, :])
            em.barrier()
            em.te.matmul(ps_bx[:, :], C["onesrow_f"][:, :], rows[0:1, :],
                         start=True, stop=True)
            em.barrier()
            for h in range(16):
                em.wait(em.ve, "d_sc")
                em.dma(em.sy, zb[:, :], io["z_hbm"][h * 128:(h + 1) * 128, tsl],
                       "d_sc")
                em.wait(em.ve, "d_sc")
                em.ve.tensor_tensor(t_[:, :], oacc[:, h * 512:(h + 1) * 512],
                                    ps_bx[:, :], ALU.mult)
                em.barrier()
                em.ve.tensor_scalar(t_[:, :], t_[:, :], normw_sb[:, h:h + 1], None,
                                    ALU.mult)
                em.barrier()
                em.ve.tensor_tensor(wor[:, h * 512:(h + 1) * 512], t_[:, :], zb[:, :],
                                    ALU.mult)
                em.barrier()
            for o in range(NF):
                em.dma(em.gp, wc[:].rearrange("p (f x) -> p f x", f=NF),
                       io["wout"][:, o * 128:(o + 1) * 128].rearrange("(f p) x -> p f x", p=128),
                       "d_w")
                em.hard()
                mm_kseries(em, ps_w[:, :],
                           [(wc[:, f * 128:(f + 1) * 128],
                             wor[:, f * 512:(f + 1) * 512]) for f in range(NF)])
                em.barrier()
                em.se.activation(stg[:, :], ps_w[:, :], AF.Identity)
                em.barrier()
                em.dma(em.sy, io["accA"][o * 128:(o + 1) * 128, tsl], stg[:, :],
                       "d_sc")
                em.hard()


def run_layer2(inp, prep, xT, debug=False):
    nc = build_layer2(_consts(), debug=debug)
    cw = np.concatenate([inp["L2_attn_conv_q_w"], inp["L2_attn_conv_k_w"],
                         inp["L2_attn_conv_v_w"]], axis=1)
    m = dict(
        wq=inp["L2_attn_Wq"], wk=inp["L2_attn_Wk"], wv=inp["L2_attn_Wv"],
        wz=inp["L2_attn_Wz"],
        wab=np.concatenate([inp["L2_attn_Wa"], inp["L2_attn_Wb"]], axis=1),
        convw=cw, normw=_pcol(inp["L2_attn_norm_w"]), wout=inp["L2_attn_Wout"],
        wr=_pcol(inp["L2_ffn_W_r"][:, 0]), br=inp["L2_ffn_b_r"].reshape(1, 1),
        w1=prep["l2w1_bf"], b1=_pcol(inp["L2_ffn_b1"]), w2=prep["l2w2_bf"],
        b2=_pcol(inp["L2_ffn_b2"]),
    )
    m = {k: np.ascontiguousarray(v).astype(np.float32)
         if np.asarray(v).dtype in (np.float64, np.float32) else np.ascontiguousarray(v)
         for k, v in m.items()}
    in_maps = []
    for c in range(8):
        mc = dict(m)
        mc["xin"] = xT[c]
        in_maps.append(mc)
    res = run_bass_kernel_spmd(nc, in_maps, core_ids=list(range(8)))
    if debug:
        return res.results
    return [res.results[c]["xout"] for c in range(8)]


# version 09
# speedup vs baseline: 1.3036x; 1.1413x over previous
import math
from contextlib import ExitStack

import numpy as np
import ml_dtypes

import concourse.bass as bass
import concourse.mybir as mybir
from concourse.bass_utils import run_bass_kernel_spmd

F32 = mybir.dt.float32
BF16 = mybir.dt.bfloat16
I32 = mybir.dt.int32
AF = mybir.ActivationFunctionType
ALU = mybir.AluOpType
AX = mybir.AxisListType
BFNP = ml_dtypes.bfloat16

_NMC = [0]


def nm(base):
    _NMC[0] += 1
    return f"{base}_{_NMC[0]}"


S = 2048
D = 2048
NF = D // 128
NEG = -1.0e30



_ENG_OPS = {
    "tensor_tensor", "tensor_scalar", "tensor_copy", "tensor_add", "tensor_sub",
    "tensor_mul", "tensor_max", "tensor_scalar_add", "tensor_scalar_sub",
    "tensor_scalar_mul", "tensor_scalar_max", "tensor_scalar_min", "tensor_relu",
    "tensor_reduce", "reduce_max", "reduce_sum", "scalar_tensor_tensor",
    "tensor_tensor_scan", "memset", "memzero", "reciprocal", "select",
    "copy_predicated", "activation", "copy", "add", "mul", "sqrt", "square",
    "sign", "iota", "affine_select", "tensor_single_scalar",
}


class _EngWrap:

    def __init__(self, em, eng, sem_name):
        self._em = em
        self._eng = eng
        self._sem_name = sem_name

    def __getattr__(self, name):
        eng = self._eng
        attr = getattr(eng, name)
        if name not in _ENG_OPS:
            return attr
        em = self._em
        sem_name = self._sem_name

        def wrapped(*a, **k):
            h = em.sem(sem_name)
            eng.wait_ge(h, em.cnt(sem_name))
            inst = attr(*a, **k)
            inst.then_inc(h, 1)
            em.bump(sem_name)
            return inst
        return wrapped


class Em:

    def __init__(self, nc):
        self.nc = nc
        self.te = nc.tensor
        self.ve = _EngWrap(self, nc.vector, "q_ve")
        self.se = _EngWrap(self, nc.scalar, "q_se")
        self.gp = _EngWrap(self, nc.gpsimd, "q_gp")
        self.sy = nc.sync
        self._sems = {}
        self._cnt = {}
        self._ctx = []

    def sem(self, name):
        if name not in self._sems:
            h = self.nc.alloc_semaphore(name=name)
            self._sems[name] = h
            self._cnt[name] = 0
        return self._sems[name]

    def cnt(self, name):
        return self._cnt[name]

    def bump(self, name, by=1):
        self._cnt[name] += by
        return self._cnt[name]

    def _qsem(self, q, sem_name):
        return sem_name + ("_g" if q is self.gp else "")

    def dma(self, q, out, in_, sem_name):
        name = self._qsem(q, sem_name)
        h = self.sem(name)
        q.dma_start(out, in_).then_inc(h, 16)
        self.bump(name, 16)

    def wait(self, eng, sem_name, val=None):
        for name in (sem_name, sem_name + "_g"):
            if name in self._sems:
                eng.wait_ge(self._sems[name],
                            self.cnt(name) if val is None else val)

    def barrier(self):
        self.nc.all_engine_barrier()

    def hard(self):
        if not hasattr(self, "_hard_seen"):
            self._hard_seen = {}
        for name in list(self._sems):
            if name.startswith("d_") and self._hard_seen.get(name) != self._cnt[name]:
                self._hard_seen[name] = self._cnt[name]
                for eng in (self.te, self.ve, self.se, self.gp, self.sy):
                    eng.wait_ge(self._sems[name], self._cnt[name])
        self.nc.all_engine_barrier()


class ColStream:

    def __init__(self, em, streams, n_o, sem="d_w"):
        self.em = em
        self.streams = streams
        self.n_o = n_o
        self.sem = sem
        self.loadc = []
        self.consumed = []
        self.tsem = em.sem("t_mm")

    def _load(self, o):
        em = self.em
        if o >= 2:
            em.gp.wait_ge(self.tsem, self.consumed[o - 2])
        name = self.sem + "AB"[o % 2]
        for (w_ap, cols, kt, bufs) in self.streams:
            em.dma(em.gp, bufs[o % 2][:].rearrange("p (k c) -> p k c", k=kt),
                   w_ap[:, o * cols:(o + 1) * cols].rearrange(
                       "(k p) c -> p k c", p=128), name)
        self.loadc.append((name, em.cnt(em._qsem(em.gp, name))))

    def tag(self, inst):
        inst.then_inc(self.tsem, 1)
        self.em.bump("t_mm")

    def run(self, body):
        em = self.em
        self._load(0)
        for o in range(self.n_o):
            if o + 1 < self.n_o:
                self._load(o + 1)
            nmo, cnto = self.loadc[o]
            em.wait(em.te, em._qsem(em.gp, nmo), cnto)
            n0 = self.em.cnt("t_mm")
            body(o, [bufs[o % 2] for (_, _, _, bufs) in self.streams])
            assert self.em.cnt("t_mm") == n0 + 1, "body must cs.tag one matmul"
            self.consumed.append(self.em.cnt("t_mm"))
        em.barrier()


def mm_kseries(em, ps_ap, pairs, then_sem=None, inc=1):
    n = len(pairs)
    last = None
    for i, (l, r) in enumerate(pairs):
        last = em.te.matmul(ps_ap, l, r, start=(i == 0), stop=(i == n - 1))
    if then_sem is not None:
        last.then_inc(then_sem, inc)
    return last




def _t(a):
    return np.ascontiguousarray(np.asarray(a).T)


def _pcol(b):
    b = np.asarray(b)
    return np.ascontiguousarray(b.reshape(-1, 128).T)


def _row(b):
    return np.ascontiguousarray(np.asarray(b).reshape(1, -1))




def build_consts():
    c = {}
    c["ident_bf"] = np.eye(128, dtype=BFNP)
    c["ident_f32"] = np.eye(128, dtype=np.float32)
    onescol = np.ones((128, 1), np.float32)
    c["onescol_f"] = onescol
    c["onescol_b"] = onescol.astype(BFNP)
    onesrow = np.ones((1, 128), np.float32)
    c["onesrow_f"] = onesrow
    c["onesrow_b"] = onesrow.astype(BFNP)
    for r in range(2):
        p = np.arange(128)[:, None]
        j = np.arange(256)[None, :]
        c[f"mask256_r{r}"] = np.where(p <= j - 128 * r, 0.0, NEG).astype(np.float32)
    for r in range(4):
        p = np.arange(128)[:, None]
        j = np.arange(512)[None, :]
        c[f"mask512_r{r}"] = np.where(p <= j - 128 * r, 0.0, NEG).astype(np.float32)
    p = np.arange(128)[:, None]
    j = np.arange(512)[None, :]
    c["mask4mul"] = np.concatenate(
        [np.where(p <= j - 128 * r, 1.0, 0.0) for r in range(4)],
        axis=1).astype(BFNP)
    Dh = 128
    inv = 1.0 / (1e6 ** (np.arange(0, Dh, 2) / Dh))
    fr = np.arange(S)[:, None] * inv[None, :]
    cos = np.concatenate([np.cos(fr), np.cos(fr)], 1)
    sin = np.concatenate([np.sin(fr), np.sin(fr)], 1)
    c["ropecos"] = np.ascontiguousarray(cos.T).astype(np.float32)
    c["ropesin"] = np.ascontiguousarray(sin.T).astype(np.float32)
    R = np.zeros((128, 128), np.float32)
    for m in range(64):
        R[m + 64, m] = -1.0
    for m in range(64, 128):
        R[m - 64, m] = 1.0
    c["rope_R"] = R.astype(BFNP)
    E = np.zeros((128, 16 * 16), np.float32)
    for h in range(16):
        E[:, 16 * h + h] = 1.0
    c["selE"] = E.astype(BFNP)
    B = np.zeros((16, 16 * 128), np.float32)
    for h in range(16):
        B[h, 128 * h:128 * (h + 1)] = 1.0
    c["selB"] = B.astype(BFNP)
    B8 = np.zeros((8, 8 * 128), np.float32)
    for e in range(8):
        B8[e, 128 * e:128 * (e + 1)] = 1.0
    c["sel8"] = B8.astype(BFNP)
    return c


class Consts:

    def __init__(self, em, names, consts):
        self.handles = {}
        self.sb = {}
        nc = em.nc
        self._stack = []
        for n in names:
            a = consts[n]
            h = nc.inline_tensor(np.ascontiguousarray(a), f"c_{n}")
            dt = BF16 if a.dtype == BFNP else (F32 if a.dtype == np.float32 else I32)
            t = nc.alloc_sbuf_tensor(f"cs_{n}", list(a.shape), dt)
            em.dma(em.sy, t[tuple(slice(None) for _ in a.shape)], h.ap()[tuple(slice(None) for _ in a.shape)], "d_const")
            self.sb[n] = t
        em.hard()

    def __getitem__(self, n):
        return self.sb[n]




def emit_phase0(em, C, io, xn_sb=None, adds=(), out_x=None, router=None,
                l2fix=None, save_rows=None, combine_only=False, ct=512):
    nc = em.nc
    x_in = io["x_in"]
    with (
        nc.sbuf_tensor([128, NF * ct], F32) as xch,
        nc.sbuf_tensor([128, NF * ct], F32) as ach,
        nc.sbuf_tensor([1, ct * 8], F32) as rows,
        nc.psum_tensor([1, ct], F32) as ps_sum,
        nc.psum_tensor([1, ct], F32) as ps_sq,
        nc.psum_tensor([128, ct], F32) as ps_b1,
        nc.psum_tensor([128, ct], F32) as ps_b2,
    ):
        sqch = ach
        r_mean = rows[:, 0:ct]
        r_msq = rows[:, ct:2 * ct]
        r_var = rows[:, 2 * ct:3 * ct]
        r_rstd = rows[:, 3 * ct:4 * ct]
        r_nm = rows[:, 4 * ct:5 * ct]
        r_t = rows[:, 5 * ct:6 * ct]
        for c in range(S // ct):
            tsl = slice(c * ct, (c + 1) * ct)
            if "d_xout" in em._sems:
                em.wait(em.sy, "d_xout")
            em.dma(em.sy, xch[:].rearrange("p (f t) -> p f t", f=NF),
                   x_in[:, tsl].rearrange("(f p) t -> p f t", p=128), "d_act")
            em.wait(em.ve, "d_act")
            em.barrier()
            if l2fix is not None:
                l2fix(em, C, c, xch)
            for a_ap in adds:
                em.dma(em.sy, ach[:].rearrange("p (f t) -> p f t", f=NF),
                       a_ap[:, tsl].rearrange("(f p) t -> p f t", p=128), "d_act")
                em.wait(em.ve, "d_act")
                for f in range(NF):
                    em.ve.tensor_tensor(xch[:, f * ct:(f + 1) * ct],
                                        xch[:, f * ct:(f + 1) * ct],
                                        ach[:, f * ct:(f + 1) * ct], ALU.add)
                em.barrier()
            if out_x is not None:
                em.dma(em.sy, out_x[:, tsl].rearrange("(f p) t -> p f t", p=128),
                       xch[:].rearrange("p (f t) -> p f t", f=NF), "d_xout")
            if combine_only:
                em.barrier()
                continue
            for f in range(NF):
                em.se.activation(sqch[:, f * ct:(f + 1) * ct],
                                 xch[:, f * ct:(f + 1) * ct], AF.Square)
            em.barrier()
            mm_kseries(em, ps_sum[:, :],
                       [(C["onescol_f"][:, :], xch[:, f * ct:(f + 1) * ct]) for f in range(NF)])
            mm_kseries(em, ps_sq[:, :],
                       [(C["onescol_f"][:, :], sqch[:, f * ct:(f + 1) * ct]) for f in range(NF)])
            if router is not None:
                router(em, C, c, xch)
            em.barrier()
            em.ve.tensor_scalar(r_mean, ps_sum[:, :], 1.0 / D, None, ALU.mult)
            em.ve.tensor_scalar(r_msq, ps_sq[:, :], 1.0 / D, None, ALU.mult)
            em.barrier()
            em.se.activation(r_t, r_mean, AF.Square)
            em.barrier()
            em.ve.tensor_tensor(r_var, r_msq, r_t, ALU.subtract)
            em.barrier()
            em.ve.tensor_scalar(r_var, r_var, 1e-5, None, ALU.add)
            em.barrier()
            em.se.activation(r_t, r_var, AF.Sqrt)
            em.barrier()
            em.ve.reciprocal(r_rstd, r_t)
            em.barrier()
            em.ve.tensor_tensor(r_nm, r_mean, r_rstd, ALU.mult)
            em.barrier()
            em.ve.tensor_scalar(r_nm, r_nm, -1.0, None, ALU.mult)
            em.barrier()
            if save_rows is not None:
                em.ve.tensor_copy(save_rows["mean"][:, tsl], r_mean)
                em.ve.tensor_copy(save_rows["rstd"][:, tsl], r_rstd)
            em.te.matmul(ps_b1[:, :], C["onesrow_f"][:, :], r_rstd, start=True, stop=True)
            em.te.matmul(ps_b2[:, :], C["onesrow_f"][:, :], r_nm, start=True, stop=True)
            em.barrier()
            for f in range(NF):
                em.ve.tensor_tensor(sqch[:, f * ct:(f + 1) * ct],
                                    xch[:, f * ct:(f + 1) * ct], ps_b1[:, :], ALU.mult)
            em.barrier()
            for f in range(NF):
                em.ve.tensor_tensor(xn_sb[:, f * S + c * ct: f * S + (c + 1) * ct],
                                    sqch[:, f * ct:(f + 1) * ct], ps_b2[:, :], ALU.add)
            em.barrier()
            if router is not None:
                router(em, C, c, xch, post=True)
        em.hard()




def attn_alloc(em, cfg):
    nc = em.nc
    st = {}
    st["bias_sb"] = nc.alloc_sbuf_tensor(nm("at_bias"), [128, 3 * NF + 1], F32)
    Dkv = cfg["kvH"] * cfg["dh"]
    st["bvrow_sb"] = nc.alloc_sbuf_tensor(nm("at_bvrow"), [1, Dkv], BF16)
    return st


def emit_attn_proj(em, C, st, xn_sb, cfg, wd, q_hbm, v_hbm, k_hbm):
    nc = em.nc
    H, dh, kvH = cfg["H"], cfg["dh"], cfg["kvH"]
    Dq = H * dh
    Dkv = kvH * dh
    bias_sb = st["bias_sb"]
    bvrow_sb = st["bvrow_sb"]
    if cfg.get("has_bias", True):
        em.dma(em.sy, bias_sb[:, 0:Dq // 128], wd["bq"][:, :], "d_w")
        em.dma(em.sy, bias_sb[:, NF:NF + Dkv // 128], wd["bk"][:, :], "d_w")
        em.dma(em.gp, bvrow_sb[:, :], wd["bv"][:, :], "d_w")
    em.dma(em.sy, bias_sb[:, 2 * NF:3 * NF], wd["bo"][:, :], "d_w")
    em.hard()
    em.barrier()

    def proj_featmajor(w_ap, ncols, bias_col, dest_cb):
        smm = em.sem("a_mm")
        with ExitStack() as _es:
            wb2 = [_es.enter_context(nc.sbuf_tensor(nm("pjw"), [128, NF * 128], BF16))
                   for _ in range(2)]
            st4 = _es.enter_context(nc.sbuf_tensor(nm("pjst"), [128, 4 * 512], BF16))
            pss = [_es.enter_context(nc.psum_tensor(nm("pjps"), [128, 512], F32))
                   for _ in range(2)]
            psfree = []
            stgd = []

            def body(o, bufs):
                wcb = bufs[0]
                for tq in range(4):
                    gi = o * 4 + tq
                    p = pss[gi % 2]
                    if gi >= 2:
                        em.wait(em.te, "q_se", psfree[gi - 2])
                    last = mm_kseries(em, p[:, :],
                                      [(wcb[:, f * 128:(f + 1) * 128],
                                        xn_sb[:, f * S + tq * 512: f * S + (tq + 1) * 512])
                                       for f in range(NF)])
                    if tq == 3:
                        cs.tag(last)
                        em.wait(em.se, "t_mm")
                    else:
                        last.then_inc(smm, 1)
                        em.bump("a_mm")
                        em.wait(em.se, "a_mm")
                    em.wait(em.se, "d_sc")
                    stg = st4[:, (gi % 4) * 512:(gi % 4 + 1) * 512]
                    if bias_col is not None:
                        em.se.activation(stg, p[:, :], AF.Identity,
                                         bias=bias_col(o))
                    else:
                        em.se.activation(stg, p[:, :], AF.Identity)
                    psfree.append(em.cnt("q_se"))
                    em.wait(em.sy, "q_se")
                    dest_cb(o, tq, stg)
                    stgd.append(em.cnt("d_sc"))

            cs = ColStream(em, [(w_ap, 128, NF, (wb2[0], wb2[1]))], ncols // 128)
            cs.run(body)
            em.hard()

    qb = (lambda o: bias_sb[:, o:o + 1]) if cfg.get("has_bias", True) else None
    proj_featmajor(wd["wq"], Dq, qb,
                   lambda o, tq, stg: em.dma(em.sy, q_hbm[o * 128:(o + 1) * 128, tq * 512:(tq + 1) * 512], stg, "d_sc"))
    kb = (lambda o: bias_sb[:, NF + o:NF + o + 1]) if cfg.get("has_bias", True) else None
    proj_featmajor(wd["wk"], Dkv, kb,
                   lambda o, tq, stg: em.dma(em.sy, k_hbm[o * 128:(o + 1) * 128, tq * 512:(tq + 1) * 512], stg, "d_sc"))

    smmv = em.sem("a_mm")
    with ExitStack() as _es:
        wv2 = [_es.enter_context(nc.sbuf_tensor(nm("pvw"), [128, NF * 512], BF16))
               for _ in range(2)]
        st4 = _es.enter_context(nc.sbuf_tensor(nm("pvst"), [128, 4 * 512], BF16))
        pss = [_es.enter_context(nc.psum_tensor(nm("pvps"), [128, 512], F32))
               for _ in range(2)]
        psfree = []
        stgd = []

        def vbody(oc, bufs):
            wvc = bufs[0]
            osl = slice(oc * 512, (oc + 1) * 512)
            for t in range(16):
                gi = oc * 16 + t
                p = pss[gi % 2]
                if gi >= 2:
                    em.wait(em.te, "q_se", psfree[gi - 2])
                pairs = [(xn_sb[:, f * S + t * 128: f * S + (t + 1) * 128],
                          wvc[:, f * 512:(f + 1) * 512]) for f in range(NF)]
                if cfg.get("has_bias", True):
                    pairs.append((C["onesrow_b"][:, :], bvrow_sb[:, osl]))
                last = mm_kseries(em, p[:, :], pairs)
                if t == 15:
                    cs.tag(last)
                    em.wait(em.se, "t_mm")
                else:
                    last.then_inc(smmv, 1)
                    em.bump("a_mm")
                    em.wait(em.se, "a_mm")
                em.wait(em.se, "d_sc")
                stg = st4[:, (gi % 4) * 512:(gi % 4 + 1) * 512]
                em.se.activation(stg, p[:, :], AF.Identity)
                psfree.append(em.cnt("q_se"))
                em.wait(em.sy, "q_se")
                em.dma(em.sy, v_hbm[t * 128:(t + 1) * 128, osl], stg, "d_sc")
                stgd.append(em.cnt("d_sc"))

        cs = ColStream(em, [(wd["wv"], 512, NF, (wv2[0], wv2[1]))],
                       max(1, Dkv // 512))
        cs.run(vbody)
        em.hard()


def emit_attn_core(em, C, st, cfg, q_hbm, v_hbm, k_hbm, feats_hbm, acc_out, wd):
    nc = em.nc
    H, dh, kvH = cfg["H"], cfg["dh"], cfg["kvH"]
    Dq = H * dh
    causal = cfg["causal"]
    escale = cfg["escale"]
    use_max = cfg["use_max"]
    TQ = 512
    bias_sb = st["bias_sb"]
    NTQ = S // TQ
    smm = em.sem("a_mm")
    with ExitStack() as _es:
        expP = _es.enter_context(nc.sbuf_tensor(nm("expP"), [128, 16 * TQ], BF16))
        vh = _es.enter_context(nc.sbuf_tensor(nm("vh"), [128, 16 * dh], BF16))
        kh = _es.enter_context(nc.sbuf_tensor(nm("kh"), [128, S], BF16))
        qh = _es.enter_context(nc.sbuf_tensor(nm("qh"), [128, TQ], BF16))
        eacc = _es.enter_context(nc.sbuf_tensor(nm("eacc"), [128, 1024], F32))
        esum = _es.enter_context(nc.sbuf_tensor(nm("esum"), [128, TQ], F32))
        rows = _es.enter_context(nc.sbuf_tensor(nm("rows"), [1, 2 * TQ], F32))
        ftmp = _es.enter_context(nc.sbuf_tensor(nm("ftmp"), [128, TQ], F32))
        fbf = _es.enter_context(nc.sbuf_tensor(nm("fbf"), [128, TQ], BF16))
        if use_max:
            sraw = _es.enter_context(nc.sbuf_tensor(nm("sraw"), [128, 16 * TQ], F32))
            mrun = _es.enter_context(nc.sbuf_tensor(nm("mrun"), [128, TQ], F32))
            cm = _es.enter_context(nc.sbuf_tensor(nm("cm"), [128, 4], F32))
        psA = _es.enter_context(nc.psum_tensor(nm("psA"), [128, 1024], F32))
        psB = _es.enter_context(nc.psum_tensor(nm("psB"), [128, 1024], F32))
        ps_f = _es.enter_context(nc.psum_tensor(nm("ps_f"), [128, TQ], F32))
        ps_b = _es.enter_context(nc.psum_tensor(nm("ps_b"), [128, TQ], F32))
        ps_t = None
        if use_max:
            ps_t = _es.enter_context(nc.psum_tensor(nm("ps_t"), [128, 128], F32))
        pss = [psA, psB]
        mrow = rows[:, 0:TQ]
        rrow = rows[:, TQ:2 * TQ]
        for g in range(kvH):
            em.wait(em.gp, "a_mm")
            em.wait(em.sy, "a_mm")
            em.dma(em.gp, vh[:, 0:16 * dh].rearrange("p (f c) -> p f c", f=16),
                   v_hbm[:, g * dh:(g + 1) * dh].rearrange("(f p) c -> p f c", p=128),
                   "d_sc")
            em.dma(em.sy, kh[0:dh, :], k_hbm[g * dh:(g + 1) * dh, :], "d_sc")
            em.hard()
            for h in range(g * (H // kvH), (g + 1) * (H // kvH)):
                for tq in range(NTQ):
                    ntk = min(16, (tq + 1) * (TQ // 128)) if causal else 16
                    nb = ntk // 2
                    em.wait(em.sy, "d_sc")
                    em.dma(em.sy, qh[0:dh, :],
                           q_hbm[h * dh:(h + 1) * dh, tq * TQ:(tq + 1) * TQ], "d_sc")
                    em.wait(em.te, "d_sc")
                    free_at = []
                    for b in range(nb):
                        ps = pss[b % 2]
                        if b >= 2:
                            em.wait(em.te, "q_se", free_at[b - 2])
                        for i in (2 * b, 2 * b + 1):
                            last = em.te.matmul(
                                ps[:, (i % 2) * 512:(i % 2 + 1) * 512],
                                kh[0:dh, i * 128:(i + 1) * 128], qh[0:dh, :],
                                start=True, stop=True)
                        last.then_inc(smm, 1)
                        em.bump("a_mm")
                        em.wait(em.se, "a_mm")
                        if use_max:
                            em.se.activation(sraw[:, 2 * b * 512:(2 * b + 2) * 512],
                                             ps[:, :], AF.Identity)
                        else:
                            em.se.activation(expP[:, 2 * b * 512:(2 * b + 2) * 512],
                                             ps[:, :], AF.Exp, scale=escale)
                        free_at.append(em.cnt("q_se"))
                    if use_max:
                        em.wait(em.ve, "q_se")
                        wv = ntk * 512
                        em.ve.tensor_tensor(mrun[:, :], sraw[:, 0:512],
                                            sraw[:, 512:1024], ALU.max)
                        for i in range(2, ntk):
                            em.ve.tensor_tensor(mrun[:, :], mrun[:, :],
                                                sraw[:, i * 512:(i + 1) * 512],
                                                ALU.max)
                        for j in range(4):
                            em.wait(em.te, "q_ve")
                            em.te.transpose(ps_t[:, :], mrun[:, j * 128:(j + 1) * 128],
                                            C["ident_f32"][:, :]).then_inc(smm, 1)
                            em.bump("a_mm")
                            em.wait(em.ve, "a_mm")
                            em.ve.reduce_max(cm[:, j:j + 1], ps_t[:, :], axis=AX.X)
                        for j in range(4):
                            em.wait(em.te, "q_ve")
                            em.te.transpose(ps_t[0:1, 0:128], cm[:, j:j + 1],
                                            C["ident_f32"][:, :]).then_inc(smm, 1)
                            em.bump("a_mm")
                            em.wait(em.ve, "a_mm")
                            em.ve.tensor_copy(mrow[:, j * 128:(j + 1) * 128],
                                              ps_t[0:1, 0:128])
                        em.wait(em.te, "q_ve")
                        em.te.matmul(ps_b[:, :], C["onesrow_f"][:, :], mrow,
                                     start=True, stop=True).then_inc(smm, 1)
                        em.bump("a_mm")
                        em.wait(em.ve, "a_mm")
                        for i in range(ntk):
                            em.ve.tensor_tensor(sraw[:, i * 512:(i + 1) * 512],
                                                sraw[:, i * 512:(i + 1) * 512],
                                                ps_b[:, :], ALU.subtract)
                        em.wait(em.se, "q_ve")
                        for b in range(nb):
                            em.se.activation(expP[:, 2 * b * 512:(2 * b + 2) * 512],
                                             sraw[:, 2 * b * 512:(2 * b + 2) * 512],
                                             AF.Exp, scale=escale)
                    em.wait(em.ve, "q_se")
                    if causal:
                        dg = 4 * tq * 512
                        em.ve.tensor_tensor(expP[:, dg:dg + 2048],
                                            expP[:, dg:dg + 2048],
                                            C["mask4mul"][:, :], ALU.mult)
                    nblk = ntk // 2
                    em.ve.tensor_tensor(eacc[:, :], expP[:, 0:1024],
                                        expP[:, 1024:2048], ALU.add)
                    for i in range(2, nblk):
                        em.ve.tensor_tensor(eacc[:, :], eacc[:, :],
                                            expP[:, i * 1024:(i + 1) * 1024], ALU.add)
                    em.ve.tensor_tensor(esum[:, :], eacc[:, 0:512], eacc[:, 512:1024],
                                        ALU.add)
                    em.wait(em.te, "q_ve")
                    em.te.matmul(ps_b[0:1, :], C["onescol_f"][:, :], esum[:, :],
                                 start=True, stop=True).then_inc(smm, 1)
                    em.bump("a_mm")
                    em.wait(em.ve, "a_mm")
                    em.ve.reciprocal(rrow, ps_b[0:1, :])
                    em.wait(em.te, "q_ve")
                    em.te.matmul(ps_b[:, :], C["onesrow_f"][:, :], rrow,
                                 start=True, stop=True).then_inc(smm, 1)
                    em.bump("a_mm")
                    em.wait(em.te, "q_se")
                    for i in range(ntk):
                        lastav = em.te.matmul(ps_f[0:dh, :], vh[:, i * dh:(i + 1) * dh],
                                              expP[:, i * TQ:(i + 1) * TQ],
                                              start=(i == 0), stop=(i == ntk - 1))
                    lastav.then_inc(smm, 1)
                    em.bump("a_mm")
                    em.wait(em.se, "a_mm")
                    em.se.activation(ftmp[0:dh, :], ps_f[0:dh, :], AF.Identity)
                    em.wait(em.ve, "q_se")
                    em.wait(em.ve, "d_sc")
                    em.ve.tensor_tensor(fbf[0:dh, :], ftmp[0:dh, :], ps_b[0:dh, :],
                                        ALU.mult)
                    em.wait(em.sy, "q_ve")
                    em.dma(em.sy, feats_hbm[h * dh:(h + 1) * dh,
                                            tq * TQ:(tq + 1) * TQ], fbf[0:dh, :],
                           "d_sc")
        em.hard()

    smm2 = em.sem("a_mm")
    with ExitStack() as _es:
        wo2 = [_es.enter_context(nc.sbuf_tensor(nm("opw"), [128, (Dq // 128) * 128], BF16))
               for _ in range(2)]
        rhs = _es.enter_context(nc.sbuf_tensor(nm("oprhs"), [128, (Dq // 128) * 512], BF16))
        st4 = _es.enter_context(nc.sbuf_tensor(nm("opst"), [128, 4 * 512], F32))
        pss = [_es.enter_context(nc.psum_tensor(nm("opps"), [128, 512], F32))
               for _ in range(2)]
        for tq in range(4):
            em.dma(em.gp, rhs[:].rearrange("p (f c) -> p f c", f=Dq // 128),
                   feats_hbm[:, tq * 512:(tq + 1) * 512].rearrange("(f p) c -> p f c", p=128),
                   "d_sc")
            em.hard()
            psfree = []
            stgd = []

            def obody(o, bufs):
                wcb = bufs[0]
                p = pss[o % 2]
                if o >= 2:
                    em.wait(em.te, "q_se", psfree[o - 2])
                cs.tag(mm_kseries(em, p[:, :],
                                  [(wcb[:, f * 128:(f + 1) * 128],
                                    rhs[:, f * 512:(f + 1) * 512])
                                   for f in range(Dq // 128)]))
                em.wait(em.se, "t_mm")
                em.wait(em.se, "d_sc")
                stg = st4[:, (o % 4) * 512:(o % 4 + 1) * 512]
                em.se.activation(stg, p[:, :], AF.Identity,
                                 bias=bias_sb[:, 2 * NF + o:2 * NF + o + 1])
                psfree.append(em.cnt("q_se"))
                em.wait(em.sy, "q_se")
                em.dma(em.sy, acc_out[o * 128:(o + 1) * 128, tq * 512:(tq + 1) * 512],
                       stg, "d_sc")
                stgd.append(em.cnt("d_sc"))

            cs = ColStream(em, [(wd["wo"], 128, Dq // 128, (wo2[0], wo2[1]))], NF)
            cs.run(obody)
        em.hard()





def emit_ffn_swiglu(em, C, xn_sb, cfg, wd, acc_out):
    nc = em.nc
    NI = cfg["NI"]
    NO = NI // 128
    has_s = cfg.get("scales", False)
    scal_sb = cfg.get("scal_sb")
    if has_s:
        em.dma(em.sy, scal_sb[:, 0:NO], wd["gate_s"][:, :], "d_w")
        em.dma(em.sy, scal_sb[:, NO:2 * NO], wd["up_s"][:, :], "d_w")
        em.dma(em.sy, scal_sb[:, 2 * NO:2 * NO + NF], wd["down_s"][:, :], "d_w")
        em.hard()
    with ExitStack() as _es:
        wg = [_es.enter_context(nc.sbuf_tensor(nm("wg"), [128, NF * 128], BF16))
              for _ in range(2)]
        wu = [_es.enter_context(nc.sbuf_tensor(nm("wu"), [128, NF * 128], BF16))
              for _ in range(2)]
        dcb = [_es.enter_context(nc.sbuf_tensor(nm("dcb"), [128, NO * 128], BF16))
               for _ in range(2)]
        gsb = _es.enter_context(nc.sbuf_tensor(nm("gsb"), [128, 2 * 512], BF16))
        usb = _es.enter_context(nc.sbuf_tensor(nm("usb"), [128, 2 * 512], BF16))
        hch = _es.enter_context(nc.sbuf_tensor(nm("hch"), [128, NO * 512], BF16))
        stg = _es.enter_context(nc.sbuf_tensor(nm("stg"), [128, 2 * 512], F32))
        psg = [_es.enter_context(nc.psum_tensor(nm("psg"), [128, 512], F32))
               for _ in range(2)]
        psu = [_es.enter_context(nc.psum_tensor(nm("psu"), [128, 512], F32))
               for _ in range(2)]
        for tq in range(4):
            tsl = slice(tq * 512, (tq + 1) * 512)
            psfree = []
            mufree = []

            def body(o, bufs):
                wgb, wub = bufs
                if o >= 2:
                    em.wait(em.te, "q_se", psfree[o - 2])
                mm_kseries(em, psg[o % 2][:, :],
                           [(wgb[:, f * 128:(f + 1) * 128],
                             xn_sb[:, f * S + tq * 512: f * S + (tq + 1) * 512])
                            for f in range(NF)])
                cs.tag(mm_kseries(em, psu[o % 2][:, :],
                                  [(wub[:, f * 128:(f + 1) * 128],
                                    xn_sb[:, f * S + tq * 512: f * S + (tq + 1) * 512])
                                   for f in range(NF)]))
                em.wait(em.se, "t_mm")
                if o >= 2:
                    em.wait(em.se, "q_ve", mufree[o - 2])
                gs = gsb[:, (o % 2) * 512:(o % 2 + 1) * 512]
                us = usb[:, (o % 2) * 512:(o % 2 + 1) * 512]
                if has_s:
                    em.se.activation(gs, psg[o % 2][:, :], AF.Silu,
                                     scale=scal_sb[:, o:o + 1])
                    em.se.activation(us, psu[o % 2][:, :], AF.Identity,
                                     scale=scal_sb[:, NO + o:NO + o + 1])
                else:
                    em.se.activation(gs, psg[o % 2][:, :], AF.Silu)
                    em.se.activation(us, psu[o % 2][:, :], AF.Identity)
                psfree.append(em.cnt("q_se"))
                em.wait(em.ve, "q_se")
                em.ve.tensor_tensor(hch[:, o * 512:(o + 1) * 512], gs, us, ALU.mult)
                mufree.append(em.cnt("q_ve"))

            cs = ColStream(em, [(wd["gate"], 128, NF, (wg[0], wg[1])),
                                (wd["up"], 128, NF, (wu[0], wu[1]))], NO)
            cs.run(body)
            dfree = []
            stgd = []

            def dbody(o2, bufs):
                dcb_ = bufs[0]
                if o2 >= 2:
                    em.wait(em.te, "q_se", dfree[o2 - 2])
                cs2.tag(mm_kseries(em, psg[o2 % 2][:, :],
                                   [(dcb_[:, f * 128:(f + 1) * 128],
                                     hch[:, f * 512:(f + 1) * 512])
                                    for f in range(NO)]))
                em.wait(em.se, "t_mm")
                em.wait(em.se, "d_sc")
                sg = stg[:, (o2 % 2) * 512:(o2 % 2 + 1) * 512]
                if has_s:
                    em.se.activation(sg, psg[o2 % 2][:, :], AF.Identity,
                                     scale=scal_sb[:, 2 * NO + o2:2 * NO + o2 + 1])
                else:
                    em.se.activation(sg, psg[o2 % 2][:, :], AF.Identity)
                dfree.append(em.cnt("q_se"))
                em.wait(em.sy, "q_se")
                em.dma(em.sy, acc_out[o2 * 128:(o2 + 1) * 128, tsl], sg, "d_sc")
                stgd.append(em.cnt("d_sc"))

            cs2 = ColStream(em, [(wd["down"], 128, NO, (dcb[0], dcb[1]))], NF)
            cs2.run(dbody)
        em.hard()




def _io_tensors(nc, names_shapes_dtypes):
    out = {}
    for name, shape, dt, kind in names_shapes_dtypes:
        out[name] = nc.dram_tensor(name, list(shape), dt, kind=kind).ap()
    return out


def build_prep(consts):
    nc = bass.Bass("TRN2", target_bir_lowering=False)
    em = Em(nc)
    specs = [
        ("lut", (1, 16), F32, "ExternalInput"),
        ("gidxT", (256, 5504), I32, "ExternalInput"),
        ("uidxT", (256, 5504), I32, "ExternalInput"),
        ("didxT", (688, 2048), I32, "ExternalInput"),
        ("gw_bf", (256, 5504), BF16, "ExternalOutput"),
        ("uw_bf", (256, 5504), BF16, "ExternalOutput"),
        ("dw_bf", (688, 2048), BF16, "ExternalOutput"),
    ]
    casts = [
        ("l1g", (256, 5632)), ("l1u", (256, 5632)), ("l1d", (704, 2048)),
        ("l2w1", (256, 8192)), ("l2w2", (1024, 2048)),
        ("l3w1", (256, 5632)), ("l3w3", (256, 5632)), ("l3w2", (704, 2048)),
    ]
    for n, shp in casts:
        specs.append((n, shp, F32, "ExternalInput"))
        specs.append((n + "_bf", shp, BF16, "ExternalOutput"))
    io = _io_tensors(nc, specs)
    C = Consts(em, ["onesrow_f"], consts)
    lut_sb = nc.alloc_sbuf_tensor([1, 16], F32)
    lutb_sb = nc.alloc_sbuf_tensor([128, 16], F32)
    em.dma(em.sy, lut_sb[:, :], io["lut"][:, :], "d_w")
    em.hard()
    with nc.psum_tensor([128, 16], F32) as psl:
        em.te.matmul(psl[:, :], C["onesrow_f"][:, :], lut_sb[:, :], start=True, stop=True)
        em.barrier()
        em.ve.tensor_copy(lutb_sb[:, :], psl[:, :])
        em.barrier()
    with (
        nc.sbuf_tensor([128, 5504], I32) as idxi,
        nc.sbuf_tensor([128, 5504], BF16) as idxb,
        nc.sbuf_tensor([128, 5504], BF16) as tmp,
        nc.sbuf_tensor([128, 5504], BF16) as acc,
    ):
        for mat, cols, rows in (("gidxT", 5504, 256), ("uidxT", 5504, 256),
                                ("didxT", 2048, 688)):
            for r0 in range(0, rows, 128):
                rr = min(128, rows - r0)
                em.dma(em.sy, idxi[0:rr, 0:cols], io[mat][r0:r0 + rr, :], "d_w")
                em.hard()
                em.ve.tensor_copy(idxb[0:rr, 0:cols], idxi[0:rr, 0:cols])
                em.barrier()
                for v in range(16):
                    em.ve.tensor_scalar(tmp[0:rr, 0:cols], idxb[0:rr, 0:cols], float(v),
                                        lutb_sb[0:rr, v:v + 1], ALU.is_equal, ALU.mult)
                    em.barrier()
                    if v == 0:
                        em.ve.tensor_copy(acc[0:rr, 0:cols], tmp[0:rr, 0:cols])
                    else:
                        em.ve.tensor_tensor(acc[0:rr, 0:cols], acc[0:rr, 0:cols],
                                            tmp[0:rr, 0:cols], ALU.add)
                    em.barrier()
                out = {"gidxT": "gw_bf", "uidxT": "uw_bf", "didxT": "dw_bf"}[mat]
                em.dma(em.sy, io[out][r0:r0 + rr, :], acc[0:rr, 0:cols], "d_sc")
                em.hard()
    with nc.sbuf_tensor([128, 8192], BF16) as cst:
        for n, shp in casts:
            rows, cols = shp
            for r0 in range(0, rows, 128):
                rr = min(128, rows - r0)
                em.dma(em.gp, cst[0:rr, 0:cols], io[n][r0:r0 + rr, :], "d_w")
                em.hard()
                em.dma(em.sy, io[n + "_bf"][r0:r0 + rr, :], cst[0:rr, 0:cols], "d_sc")
                em.hard()
    em.hard()
    return nc


ATTN_CFG = {
    0: dict(H=32, dh=64, kvH=32, causal=False, escale=1.0 / math.sqrt(32.0),
            has_bias=True, Dq=2048, Dkv=2048, use_max=False),
    1: dict(H=16, dh=128, kvH=16, causal=True, escale=1.0 / math.sqrt(128.0),
            has_bias=True, Dq=2048, Dkv=2048, use_max=False),
    3: dict(H=16, dh=128, kvH=4, causal=True, escale=1.0,
            has_bias=False, Dq=2048, Dkv=512, use_max=True),
}


def build_layer01(layer, consts, debug=False):
    nc = bass.Bass("TRN2", target_bir_lowering=False)
    em = Em(nc)
    NI = 5504 if layer == 0 else 5632
    dk = "ExternalOutput" if debug else "Internal"
    specs = [
        ("xin", (D, S), F32, "ExternalInput"),
        ("xout", (D, S), F32, "ExternalOutput"),
        ("wq", (D, D), F32, "ExternalInput"),
        ("wk", (D, D), F32, "ExternalInput"),
        ("wv", (D, D), F32, "ExternalInput"),
        ("wo", (D, D), F32, "ExternalInput"),
        ("bq", (128, NF), F32, "ExternalInput"),
        ("bk", (128, NF), F32, "ExternalInput"),
        ("bv", (1, D), F32, "ExternalInput"),
        ("bo", (128, NF), F32, "ExternalInput"),
        ("gate", (D, NI), BF16, "ExternalInput"),
        ("up", (D, NI), BF16, "ExternalInput"),
        ("down", (NI, D), BF16, "ExternalInput"),
        ("accA", (D, S), F32, dk),
        ("accF", (D, S), F32, dk),
        ("q_hbm", (D, S), BF16, dk),
        ("k_hbm", (D, S), BF16, dk),
        ("v_hbm", (S, D), BF16, dk),
        ("f_hbm", (D, S), BF16, dk),
    ]
    if layer == 1:
        specs += [("gate_s", (128, NI // 128), F32, "ExternalInput"),
                  ("up_s", (128, NI // 128), F32, "ExternalInput"),
                  ("down_s", (128, NF), F32, "ExternalInput")]
    io = _io_tensors(nc, specs)
    cfg = ATTN_CFG[layer]
    mask_names = ["mask4mul"] if cfg["causal"] else []
    C = Consts(em, ["onescol_f", "onesrow_f", "onesrow_b", "ident_f32"] + mask_names,
               consts)
    ffn_cfg = dict(NI=NI, scales=(layer == 1))
    if layer == 1:
        ffn_cfg["scal_sb"] = nc.alloc_sbuf_tensor(nm("ff_scal"),
                                                  [128, 2 * (NI // 128) + NF], F32)
    wd_attn = dict(wq=io["wq"], wk=io["wk"], wv=io["wv"], wo=io["wo"],
                   bq=io["bq"], bk=io["bk"], bv=io["bv"], bo=io["bo"])
    wd_ffn = dict(gate=io["gate"], up=io["up"], down=io["down"])
    if layer == 1:
        wd_ffn.update(gate_s=io["gate_s"], up_s=io["up_s"], down_s=io["down_s"])

    ast = attn_alloc(em, cfg)
    with nc.sbuf_tensor([128, NF * S], BF16) as xn_sb:
        emit_phase0(em, C, {"x_in": io["xin"]}, xn_sb=xn_sb)
        emit_ffn_swiglu(em, C, xn_sb, ffn_cfg, wd_ffn, io["accF"])
        emit_attn_proj(em, C, ast, xn_sb, cfg, wd_attn, io["q_hbm"], io["v_hbm"],
                       io["k_hbm"])
    emit_attn_core(em, C, ast, cfg, io["q_hbm"], io["v_hbm"], io["k_hbm"],
                   io["f_hbm"], io["accA"], wd_attn)
    emit_phase0(em, C, {"x_in": io["xin"]}, adds=(io["accA"], io["accF"]),
                out_x=io["xout"], combine_only=True)
    em.hard()
    return nc



_CONSTS = None


def _consts():
    global _CONSTS
    if _CONSTS is None:
        _CONSTS = build_consts()
    return _CONSTS


def _bf(a):
    return np.ascontiguousarray(np.asarray(a)).astype(BFNP)


def run_prep(inp):
    consts = _consts()
    nc = build_prep(consts)
    g_idxT = _t(inp["L0_ffn_gate_idx"])
    u_idxT = _t(inp["L0_ffn_up_idx"])
    d_idxT = _t(inp["L0_ffn_down_idx"])
    casts_full = {
        "l1g": _t(inp["L1_ffn_gate_w"]), "l1u": _t(inp["L1_ffn_up_w"]),
        "l1d": _t(inp["L1_ffn_down_w"]),
        "l2w1": np.ascontiguousarray(inp["L2_ffn_W1"]),
        "l2w2": np.ascontiguousarray(inp["L2_ffn_W2"]),
        "l3w1": _t(inp["L3_ffn_w1"]), "l3w3": _t(inp["L3_ffn_w3"]),
        "l3w2": _t(inp["L3_ffn_w2"]),
    }
    in_maps = []
    for c in range(8):
        m = {"lut": _row(inp["L0_ffn_lut"]).astype(np.float32),
             "gidxT": np.ascontiguousarray(g_idxT[c * 256:(c + 1) * 256]),
             "uidxT": np.ascontiguousarray(u_idxT[c * 256:(c + 1) * 256]),
             "didxT": np.ascontiguousarray(d_idxT[c * 688:(c + 1) * 688])}
        for n, a in casts_full.items():
            rows = a.shape[0] // 8
            m[n] = np.ascontiguousarray(a[c * rows:(c + 1) * rows]).astype(np.float32)
        in_maps.append(m)
    res = run_bass_kernel_spmd(nc, in_maps, core_ids=list(range(8)))
    out = {}
    for n in ["gw_bf", "uw_bf", "dw_bf"] + [k + "_bf" for k in casts_full]:
        out[n] = np.concatenate([res.results[c][n] for c in range(8)], axis=0)
    return out


def layer01_inmap(layer, inp, prep, xT_c):
    L = f"L{layer}_attn_"
    if layer == 0:
        m = dict(wq=inp[L + "Wq"], wk=inp[L + "Wk"], wv=inp[L + "Wv"],
                 wo=inp[L + "Wo"])
        m["gate"], m["up"], m["down"] = prep["gw_bf"], prep["uw_bf"], prep["dw_bf"]
    else:
        m = dict(wq=_t(inp[L + "Wq"]), wk=_t(inp[L + "Wk"]), wv=_t(inp[L + "Wv"]),
                 wo=_t(inp[L + "Wo"]))
        m["gate"], m["up"], m["down"] = prep["l1g_bf"], prep["l1u_bf"], prep["l1d_bf"]
        m["gate_s"] = _pcol(inp["L1_ffn_gate_s"]).astype(np.float32)
        m["up_s"] = _pcol(inp["L1_ffn_up_s"]).astype(np.float32)
        m["down_s"] = _pcol(inp["L1_ffn_down_s"]).astype(np.float32)
    m["bq"] = _pcol(inp[L + "bq"]).astype(np.float32)
    m["bk"] = _pcol(inp[L + "bk"]).astype(np.float32)
    m["bv"] = _row(inp[L + "bv"]).astype(np.float32)
    m["bo"] = _pcol(inp[L + "bo"]).astype(np.float32)
    m = {k: (np.ascontiguousarray(v) if v.dtype != np.float64 else
             np.ascontiguousarray(v).astype(np.float32)) for k, v in m.items()}
    m["xin"] = xT_c
    return m


def run_layer01(layer, inp, prep, xT, debug=False):
    nc = build_layer01(layer, _consts(), debug=debug)
    in_maps = [layer01_inmap(layer, inp, prep, xT[c]) for c in range(8)]
    res = run_bass_kernel_spmd(nc, in_maps, core_ids=list(range(8)))
    if debug:
        return res.results
    return [res.results[c]["xout"] for c in range(8)]


def kernel(**inputs):
    inp = {k: np.asarray(v) for k, v in inputs.items()}
    prep = run_prep(inp)
    x = inp["L0_attn_x"]
    xT = [np.ascontiguousarray(x[c].T).astype(np.float32) for c in range(8)]
    xT = run_layer01(0, inp, prep, xT)
    xT = run_layer01(1, inp, prep, xT)
    xT = run_layer2(inp, prep, xT)
    xT = run_layer3(inp, prep, xT)
    out = np.stack([np.ascontiguousarray(np.asarray(t).T) for t in xT])
    return out.astype(np.float32)




def l3_router_hook(em, C, st, io):
    def router(em, C, c, xch, post=False):
        nc = em.nc
        if not post:
            mm_kseries(em, st["ps_l"][:, :],
                       [(st["gwT_sb"][:, f * 8:(f + 1) * 8],
                         xch[:, f * 512:(f + 1) * 512]) for f in range(NF)])
            return
        r_mean = st["r_mean"]
        r_rstd = st["r_rstd"]
        em.te.matmul(st["ps_b8"][0:8, :], C["onesrow_f"][0:1, 0:8], r_mean,
                     start=True, stop=True)
        em.barrier()
        em.ve.tensor_scalar(st["t8"][0:8, :], st["ps_b8"][0:8, :],
                            st["gsum_sb"][0:8, 0:1], None, ALU.mult)
        em.barrier()
        em.ve.tensor_tensor(st["t8b"][0:8, :], st["ps_l"][0:8, :], st["t8"][0:8, :],
                            ALU.subtract)
        em.barrier()
        em.te.matmul(st["ps_b8"][0:8, :], C["onesrow_f"][0:1, 0:8], r_rstd,
                     start=True, stop=True)
        em.barrier()
        em.ve.tensor_tensor(st["logits_sb"][0:8, c * 512:(c + 1) * 512],
                            st["t8b"][0:8, :], st["ps_b8"][0:8, :], ALU.mult)
        em.barrier()
    return router


def emit_l3_top2(em, C, st):
    nc = em.nc
    with ExitStack() as _es:
        lgT = _es.enter_context(nc.sbuf_tensor(nm("lgT"), [128, 8], F32))
        msk = _es.enter_context(nc.sbuf_tensor(nm("msk"), [128, 8], F32))
        wfT = _es.enter_context(nc.sbuf_tensor(nm("wfT"), [128, 8], F32))
        aa = _es.enter_context(nc.sbuf_tensor(nm("aa"), [128, 8], F32))
        m1 = _es.enter_context(nc.sbuf_tensor(nm("m1"), [128, 8], F32))
        ps_t = _es.enter_context(nc.psum_tensor(nm("ps_t"), [128, 128], F32))
        for t in range(16):
            em.te.transpose(ps_t[0:128, 0:8],
                            st["logits_sb"][0:8, t * 128:(t + 1) * 128],
                            C["ident_f32"][0:8, 0:8])
            em.barrier()
            em.ve.tensor_copy(lgT[:, 0:8], ps_t[0:128, 0:8])
            em.barrier()
            em.ve.reduce_max(m1[:, 0:1], lgT[:, 0:8], axis=AX.X)
            em.barrier()
            em.ve.tensor_scalar(msk[:, 0:8], lgT[:, 0:8], m1[:, 0:1], None,
                                ALU.is_equal)
            em.barrier()
            em.ve.tensor_scalar(aa[:, 0:8], msk[:, 0:8], 2.0e30, None, ALU.mult)
            em.barrier()
            em.ve.tensor_tensor(aa[:, 0:8], lgT[:, 0:8], aa[:, 0:8], ALU.subtract)
            em.barrier()
            em.ve.reduce_max(m1[:, 3:4], aa[:, 0:8], axis=AX.X)
            em.barrier()
            em.ve.tensor_scalar(m1[:, 4:5], m1[:, 0:1], -1.0, None, ALU.mult)
            em.barrier()
            em.se.activation(m1[:, 5:6], m1[:, 3:4], AF.Exp, bias=m1[:, 4:5])
            em.barrier()
            em.ve.tensor_scalar(m1[:, 6:7], m1[:, 5:6], 1.0, None, ALU.add)
            em.barrier()
            em.ve.reciprocal(m1[:, 6:7], m1[:, 6:7])
            em.barrier()
            em.ve.tensor_tensor(m1[:, 7:8], m1[:, 5:6], m1[:, 6:7], ALU.mult)
            em.barrier()
            em.ve.tensor_scalar(wfT[:, 0:8], msk[:, 0:8], m1[:, 6:7], None,
                                ALU.mult)
            em.barrier()
            em.ve.tensor_scalar(msk[:, 0:8], aa[:, 0:8], m1[:, 3:4], None,
                                ALU.is_equal, )
            em.barrier()
            em.ve.tensor_scalar(msk[:, 0:8], msk[:, 0:8], m1[:, 7:8], None,
                                ALU.mult)
            em.barrier()
            em.ve.tensor_tensor(wfT[:, 0:8], wfT[:, 0:8], msk[:, 0:8], ALU.add)
            em.barrier()
            em.te.transpose(ps_t[0:8, 0:128], wfT[:, 0:8], C["ident_f32"][:, :])
            em.barrier()
            em.ve.tensor_copy(st["wfull_bf"][0:8, t * 128:(t + 1) * 128],
                              ps_t[0:8, 0:128])
            em.barrier()


def emit_ffn_l3(em, C, st, xn_sb, io, acc_out):
    nc = em.nc
    NO = 44
    with ExitStack() as _es:
        a1T = _es.enter_context(nc.sbuf_tensor(nm("a1T"), [128, NF * 128], BF16))
        a3T = _es.enter_context(nc.sbuf_tensor(nm("a3T"), [128, NF * 128], BF16))
        ust = _es.enter_context(nc.sbuf_tensor(nm("ust"), [16, 512], BF16))
        ps_u = _es.enter_context(nc.psum_tensor(nm("ps_u"), [16, 512], F32))
        em.dma(em.gp, a1T[:].rearrange("p (f c) -> p f c", f=NF),
               io["a1T"].rearrange("(f p) c -> p f c", p=128), "d_w")
        em.dma(em.gp, a3T[:].rearrange("p (f c) -> p f c", f=NF),
               io["a3T"].rearrange("(f p) c -> p f c", p=128), "d_w")
        em.hard()
        for e in range(8):
            for src, dst in ((a1T, "u1_hbm"), (a3T, "u3_hbm")):
                for c in range(4):
                    mm_kseries(em, ps_u[0:16, :],
                               [(src[:, f * 128 + 16 * e: f * 128 + 16 * e + 16],
                                 xn_sb[:, f * S + c * 512: f * S + (c + 1) * 512])
                                for f in range(NF)])
                    em.barrier()
                    em.se.activation(ust[0:16, :], ps_u[0:16, :], AF.Identity,
                                     scale=2.0)
                    em.barrier()
                    em.dma(em.sy, io[dst][e * 16:(e + 1) * 16, c * 512:(c + 1) * 512],
                           ust[0:16, :], "d_sc")
                    em.hard()
    with ExitStack() as _es:
        w1c = _es.enter_context(nc.sbuf_tensor(nm("w1c"), [128, NF * 128], BF16))
        w3c = _es.enter_context(nc.sbuf_tensor(nm("w3c"), [128, NF * 128], BF16))
        stg = _es.enter_context(nc.sbuf_tensor(nm("stg"), [128, 512], BF16))
        ps = _es.enter_context(nc.psum_tensor(nm("ps"), [128, 512], F32))
        for f in range(NO):
            em.dma(em.gp, w1c[:].rearrange("p (k c) -> p k c", k=NF),
                   io["w1T"][:, f * 128:(f + 1) * 128].rearrange("(k p) c -> p k c", p=128),
                   "d_w")
            em.dma(em.gp, w3c[:].rearrange("p (k c) -> p k c", k=NF),
                   io["w3T"][:, f * 128:(f + 1) * 128].rearrange("(k p) c -> p k c", p=128),
                   "d_w")
            em.hard()
            for src, dst in ((w1c, "base1_hbm"), (w3c, "base3_hbm")):
                for c in range(4):
                    mm_kseries(em, ps[:, :],
                               [(src[:, k * 128:(k + 1) * 128],
                                 xn_sb[:, k * S + c * 512: k * S + (c + 1) * 512])
                                for k in range(NF)])
                    em.barrier()
                    em.se.activation(stg[:, :], ps[:, :], AF.Identity)
                    em.barrier()
                    em.dma(em.sy, io[dst][f * 128:(f + 1) * 128, c * 512:(c + 1) * 512],
                           stg[:, :], "d_sc")
                    em.hard()
    smm = em.sem("f3_mm")
    with ExitStack() as _es:
        b1e = _es.enter_context(nc.sbuf_tensor(nm("b1e"), [16, 5632], BF16))
        b3e = _es.enter_context(nc.sbuf_tensor(nm("b3e"), [16, 5632], BF16))
        u1e = _es.enter_context(nc.sbuf_tensor(nm("u1e"), [16, S], BF16))
        u3e = _es.enter_context(nc.sbuf_tensor(nm("u3e"), [16, S], BF16))
        a2T = _es.enter_context(nc.sbuf_tensor(nm("a2T"), [128, NO * 128], BF16))
        base1 = _es.enter_context(nc.sbuf_tensor(nm("base1"), [128, S], BF16))
        base3 = _es.enter_context(nc.sbuf_tensor(nm("base3"), [128, S], BF16))
        web = _es.enter_context(nc.sbuf_tensor(nm("web"), [128, S], F32))
        w16 = _es.enter_context(nc.sbuf_tensor(nm("w16"), [16, S], F32))
        t1 = _es.enter_context(nc.sbuf_tensor(nm("t1"), [128, 2 * 512], F32))
        s1 = _es.enter_context(nc.sbuf_tensor(nm("s1"), [128, 2 * 512], BF16))
        t3 = _es.enter_context(nc.sbuf_tensor(nm("t3"), [128, 2 * 512], BF16))
        actb = _es.enter_context(nc.sbuf_tensor(nm("actb"), [128, 2 * 512], BF16))
        actw = _es.enter_context(nc.sbuf_tensor(nm("actw"), [128, S], BF16))
        u2st = _es.enter_context(nc.sbuf_tensor(nm("u2st"), [16, 512], BF16))
        ps_d1 = _es.enter_context(nc.psum_tensor(nm("ps_d1"), [128, 512], F32))
        ps_d3 = _es.enter_context(nc.psum_tensor(nm("ps_d3"), [128, 512], F32))
        ps_u2 = [
            _es.enter_context(nc.psum_tensor(nm("ps_u2"), [16, 512], F32))
            for _ in range(4)]
        ps_w = _es.enter_context(nc.psum_tensor(nm("ps_w"), [128, 512], F32))
        em.dma(em.gp, a2T[:].rearrange("p (f c) -> p f c", f=NO),
               io["a2T"].rearrange("(f p) c -> p f c", p=128), "d_w")
        em.hard()
        for e in range(8):
            em.dma(em.gp, b1e[0:16, :], io["b1T"][e, :, :], "d_w")
            em.dma(em.gp, b3e[0:16, :], io["b3T"][e, :, :], "d_w")
            em.dma(em.sy, u1e[0:16, :], io["u1_hbm"][e * 16:(e + 1) * 16, :], "d_w")
            em.dma(em.sy, u3e[0:16, :], io["u3_hbm"][e * 16:(e + 1) * 16, :], "d_w")
            em.hard()
            for c in range(4):
                em.te.matmul(ps_w[:, :], st["sel8_sb"][:, e * 128:(e + 1) * 128],
                             st["wfull_bf"][0:8, c * 512:(c + 1) * 512],
                             start=True, stop=True)
                em.barrier()
                em.ve.tensor_copy(web[:, c * 512:(c + 1) * 512], ps_w[:, :])
                em.barrier()
                em.te.matmul(ps_w[0:16, :],
                             st["sel8_sb"][:, e * 128:e * 128 + 16],
                             st["wfull_bf"][0:8, c * 512:(c + 1) * 512],
                             start=True, stop=True)
                em.barrier()
                em.ve.tensor_copy(w16[0:16, c * 512:(c + 1) * 512], ps_w[0:16, :])
                em.barrier()
            gpdone = []
            for f in range(NO):
                em.wait(em.sy, "d_act")
                em.wait(em.sy, "q_ve")
                em.dma(em.sy, base1[:].rearrange("p (k c) -> p k c", k=4),
                       io["base1_hbm"][f * 128:(f + 1) * 128, :].rearrange("p (k c) -> p k c", k=4),
                       "d_act")
                em.dma(em.sy, base3[:].rearrange("p (k c) -> p k c", k=4),
                       io["base3_hbm"][f * 128:(f + 1) * 128, :].rearrange("p (k c) -> p k c", k=4),
                       "d_act")
                for eng in (em.te, em.ve, em.gp):
                    em.wait(eng, "d_act")
                for c in range(4):
                    csl = slice(c * 512, (c + 1) * 512)
                    pd1, pd3 = (ps_d1, ps_d3)
                    em.te.matmul(pd1[:, :], b1e[0:16, f * 128:(f + 1) * 128],
                                 u1e[0:16, csl], start=True, stop=True).then_inc(smm, 1)
                    em.bump("f3_mm")
                    em.te.matmul(pd3[:, :], b3e[0:16, f * 128:(f + 1) * 128],
                                 u3e[0:16, csl], start=True,
                                 stop=True).then_inc(smm, 1)
                    em.bump("f3_mm")
                    tb = t1[:, (c % 2) * 512:(c % 2 + 1) * 512]
                    sb1 = s1[:, (c % 2) * 512:(c % 2 + 1) * 512]
                    tb3 = t3[:, (c % 2) * 512:(c % 2 + 1) * 512]
                    ab = actb[:, (c % 2) * 512:(c % 2 + 1) * 512]
                    if gpdone:
                        em.wait(em.ve, "q_gp", gpdone[-1])
                    em.wait(em.ve, "f3_mm")
                    em.ve.tensor_tensor(tb, pd1[:, :], base1[:, csl], ALU.add)
                    em.ve.tensor_tensor(tb3, pd3[:, :], base3[:, csl], ALU.add)
                    em.wait(em.se, "q_ve")
                    em.se.activation(sb1, tb, AF.Silu)
                    em.wait(em.ve, "q_se")
                    em.ve.tensor_tensor(ab, sb1, tb3, ALU.mult)
                    em.wait(em.gp, "q_ve")
                    em.gp.tensor_tensor(actw[:, csl], ab, web[:, csl], ALU.mult)
                    gpdone.append(em.cnt("q_gp"))
                    em.wait(em.te, "q_gp")
                    em.te.matmul(ps_u2[c][0:16, :],
                                 a2T[:, f * 128 + 16 * e: f * 128 + 16 * e + 16],
                                 actw[:, csl], start=(f == 0), stop=(f == NO - 1))
                em.wait(em.gp, "q_gp")
                if e == 0:
                    em.dma(em.gp, io["ACT_hbm"][f * 128:(f + 1) * 128, :],
                           actw[:, :], "d_act")
                else:
                    h = em.sem("d_act_g")
                    em.gp._eng.dma_start(io["ACT_hbm"][f * 128:(f + 1) * 128, :],
                                         actw[:, :], accum_op=ALU.add).then_inc(h, 16)
                    em.bump("d_act_g", 16)
            em.barrier()
            em.barrier()
            for c in range(4):
                em.ve.scalar_tensor_tensor(u2st[0:16, :], ps_u2[c][0:16, :], 2.0,
                                           w16[0:16, c * 512:(c + 1) * 512],
                                           ALU.mult, ALU.mult)
                em.barrier()
                em.dma(em.sy, io["u2w_hbm"][e * 16:(e + 1) * 16,
                                            c * 512:(c + 1) * 512], u2st[0:16, :],
                       "d_sc")
                em.hard()
    with ExitStack() as _es:
        rhs = _es.enter_context(nc.sbuf_tensor(nm("rhs"), [128, NO * 512], BF16))
        dc = _es.enter_context(nc.sbuf_tensor(nm("dc"), [128, NO * 128], BF16))
        u2w = _es.enter_context(nc.sbuf_tensor(nm("u2w"), [16, 8 * 512], BF16))
        b2o = _es.enter_context(nc.sbuf_tensor(nm("b2o"), [16, 8 * 128], BF16))
        stg = _es.enter_context(nc.sbuf_tensor(nm("stg"), [128, 512], F32))
        ps = _es.enter_context(nc.psum_tensor(nm("ps"), [128, 512], F32))
        for c in range(4):
            em.dma(em.gp, rhs[:].rearrange("p (f c) -> p f c", f=NO),
                   io["ACT_hbm"][:, c * 512:(c + 1) * 512].rearrange("(f p) c -> p f c", p=128),
                   "d_w")
            em.dma(em.sy, u2w[0:16, :].rearrange("p (e c) -> p e c", e=8),
                   io["u2w_hbm"][:, c * 512:(c + 1) * 512].rearrange("(e p) c -> p e c", p=16),
                   "d_w")
            em.hard()
            for o in range(NF):
                em.dma(em.gp, dc[:].rearrange("p (f c) -> p f c", f=NO),
                       io["w2T"][:, o * 128:(o + 1) * 128].rearrange("(f p) c -> p f c", p=128),
                       "d_w")
                em.dma(em.gp, b2o[0:16, :].rearrange("p (e c) -> p e c", e=8),
                       io["b2T"].rearrange("r (e n) -> r e n", e=8)[:, :, o * 128:(o + 1) * 128],
                       "d_w")
                em.hard()
                pairs = [(dc[:, f * 128:(f + 1) * 128], rhs[:, f * 512:(f + 1) * 512])
                         for f in range(NO)]
                pairs += [(b2o[0:16, e * 128:(e + 1) * 128],
                           u2w[0:16, e * 512:(e + 1) * 512]) for e in range(8)]
                mm_kseries(em, ps[:, :], pairs)
                em.barrier()
                em.se.activation(stg[:, :], ps[:, :], AF.Identity)
                em.barrier()
                em.dma(em.sy, acc_out[o * 128:(o + 1) * 128, c * 512:(c + 1) * 512],
                       stg[:, :], "d_sc")
                em.hard()


def build_layer3(consts, debug=False):
    nc = bass.Bass("TRN2", target_bir_lowering=False)
    em = Em(nc)
    dk = "ExternalOutput" if debug else "Internal"
    specs = [
        ("xin", (D, S), F32, "ExternalInput"),
        ("xout", (D, S), F32, "ExternalOutput"),
        ("wq", (D, D), F32, "ExternalInput"),
        ("wk", (D, 512), F32, "ExternalInput"),
        ("wv", (D, 512), F32, "ExternalInput"),
        ("wo", (D, D), F32, "ExternalInput"),
        ("bo", (128, NF), F32, "ExternalInput"),
        ("gwT", (D, 8), F32, "ExternalInput"),
        ("w1T", (D, 5632), BF16, "ExternalInput"),
        ("w3T", (D, 5632), BF16, "ExternalInput"),
        ("w2T", (5632, D), BF16, "ExternalInput"),
        ("a1T", (D, 128), F32, "ExternalInput"),
        ("a3T", (D, 128), F32, "ExternalInput"),
        ("a2T", (5632, 128), F32, "ExternalInput"),
        ("b1T", (8, 16, 5632), F32, "ExternalInput"),
        ("b3T", (8, 16, 5632), F32, "ExternalInput"),
        ("b2T", (16, 8 * D), F32, "ExternalInput"),
        ("accA", (D, S), F32, dk),
        ("accF", (D, S), F32, dk),
        ("q_hbm", (D, S), BF16, "Internal"),
        ("k_hbm", (512, S), BF16, "Internal"),
        ("v_hbm", (S, 512), BF16, "Internal"),
        ("f_hbm", (D, S), BF16, "Internal"),
        ("base1_hbm", (5632, S), BF16, "Internal"),
        ("base3_hbm", (5632, S), BF16, "Internal"),
        ("ACT_hbm", (5632, S), BF16, dk),
        ("u1_hbm", (128, S), BF16, "Internal"),
        ("u3_hbm", (128, S), BF16, "Internal"),
        ("u2w_hbm", (128, S), BF16, dk),
    ]
    io = _io_tensors(nc, specs)
    cfg = ATTN_CFG[3]
    C = Consts(em, ["onescol_f", "onesrow_f", "onesrow_b", "ident_f32", "sel8",
                    "mask4mul"], consts)
    wd_attn = dict(wq=io["wq"], wk=io["wk"], wv=io["wv"], wo=io["wo"], bo=io["bo"])
    ast = attn_alloc(em, cfg)
    st = {}
    st["gwT_sb"] = nc.alloc_sbuf_tensor(nm("gwT"), [128, NF * 8], F32)
    st["gsum_sb"] = nc.alloc_sbuf_tensor(nm("gsum"), [8, 1], F32)
    st["logits_sb"] = nc.alloc_sbuf_tensor(nm("logits"), [8, S], F32)
    st["wfull_bf"] = nc.alloc_sbuf_tensor(nm("wfull"), [8, S], BF16)
    st["t8"] = nc.alloc_sbuf_tensor(nm("t8"), [8, 512], F32)
    st["t8b"] = nc.alloc_sbuf_tensor(nm("t8b"), [8, 512], F32)
    st["sel8_sb"] = C["sel8"]
    em.dma(em.sy, st["gwT_sb"][:].rearrange("p (f c) -> p f c", f=NF),
           io["gwT"].rearrange("(f p) c -> p f c", p=128), "d_w")
    em.hard()
    em.barrier()
    mean_sb = nc.alloc_sbuf_tensor(nm("p3mean"), [1, S], F32)
    rstd_sb = nc.alloc_sbuf_tensor(nm("p3rstd"), [1, S], F32)
    with nc.sbuf_tensor(nm("xn"), [128, NF * S], BF16) as xn_sb:
        with ExitStack() as _es:
            ps_l = _es.enter_context(nc.psum_tensor(nm("ps_l"), [8, 512], F32))
            ps_b8 = _es.enter_context(nc.psum_tensor(nm("ps_b8"), [8, 512], F32))
            st["ps_l"] = ps_l
            st["ps_b8"] = ps_b8
            mm_kseries(em, ps_b8[0:8, 0:1],
                       [(st["gwT_sb"][:, f * 8:(f + 1) * 8], C["onescol_f"][:, :])
                        for f in range(NF)])
            em.barrier()
            em.ve.tensor_copy(st["gsum_sb"][0:8, :], ps_b8[0:8, 0:1])
            em.barrier()
            rows_holder = {}

            def router(em_, C_, c, xch, post=False):
                if not post:
                    mm_kseries(em_, ps_l[0:8, :],
                               [(st["gwT_sb"][:, f * 8:(f + 1) * 8],
                                 xch[:, f * 512:(f + 1) * 512]) for f in range(NF)])
                    return
                r_mean, r_rstd = rows_holder["mean"], rows_holder["rstd"]
                em_.te.matmul(ps_b8[0:8, :], C_["onesrow_f"][0:1, 0:8], r_mean,
                              start=True, stop=True)
                em_.barrier()
                em_.ve.tensor_scalar(st["t8"][0:8, :], ps_b8[0:8, :],
                                     st["gsum_sb"][0:8, 0:1], None, ALU.mult)
                em_.barrier()
                em_.ve.tensor_tensor(st["t8b"][0:8, :], ps_l[0:8, :],
                                     st["t8"][0:8, :], ALU.subtract)
                em_.barrier()
                em_.te.matmul(ps_b8[0:8, :], C_["onesrow_f"][0:1, 0:8], r_rstd,
                              start=True, stop=True)
                em_.barrier()
                em_.ve.tensor_tensor(st["logits_sb"][0:8, c * 512:(c + 1) * 512],
                                     st["t8b"][0:8, :], ps_b8[0:8, :], ALU.mult)
                em_.barrier()

            def router3(em_, C_, c, xch, post=False):
                if post:
                    rows_holder["mean"] = mean_sb[:, c * 512:(c + 1) * 512]
                    rows_holder["rstd"] = rstd_sb[:, c * 512:(c + 1) * 512]
                router(em_, C_, c, xch, post=post)

            emit_phase0_l3(em, C, {"x_in": io["xin"]}, xn_sb, router3,
                           mean_sb, rstd_sb)
        emit_l3_top2(em, C, st)
        emit_ffn_l3(em, C, st, xn_sb, io, io["accF"])
        emit_attn_proj(em, C, ast, xn_sb, cfg, wd_attn, io["q_hbm"],
                       io["v_hbm"], io["k_hbm"])
    emit_attn_core(em, C, ast, cfg, io["q_hbm"], io["v_hbm"], io["k_hbm"],
                   io["f_hbm"], io["accA"], wd_attn)
    emit_phase0(em, C, {"x_in": io["xin"]}, adds=(io["accA"], io["accF"]),
                out_x=io["xout"], combine_only=True)
    em.hard()
    return nc


def emit_phase0_l3(em, C, io, xn_sb, router3, mean_sb, rstd_sb):
    emit_phase0(em, C, io, xn_sb=xn_sb, router=router3,
                save_rows={"mean": mean_sb, "rstd": rstd_sb})


def run_layer3(inp, prep, xT, debug=False):
    nc = build_layer3(_consts(), debug=debug)
    m = dict(
        wq=_t(inp["L3_attn_Wq"]), wk=_t(inp["L3_attn_Wk"]),
        wv=_t(inp["L3_attn_Wv"]), wo=_t(inp["L3_attn_Wo"]),
        bo=_pcol(inp["L3_attn_bo"]),
        gwT=_t(inp["L3_ffn_gate_w"]),
        w1T=prep["l3w1_bf"], w3T=prep["l3w3_bf"], w2T=prep["l3w2_bf"],
        a1T=np.ascontiguousarray(np.transpose(inp["L3_ffn_a1"], (2, 0, 1)).reshape(D, 128)),
        a3T=np.ascontiguousarray(np.transpose(inp["L3_ffn_a3"], (2, 0, 1)).reshape(D, 128)),
        a2T=np.ascontiguousarray(np.transpose(inp["L3_ffn_a2"], (2, 0, 1)).reshape(5632, 128)),
        b1T=np.ascontiguousarray(np.transpose(inp["L3_ffn_b1"], (0, 2, 1))),
        b3T=np.ascontiguousarray(np.transpose(inp["L3_ffn_b3"], (0, 2, 1))),
        b2T=np.ascontiguousarray(np.transpose(inp["L3_ffn_b2"], (2, 0, 1)).reshape(16, 8 * D)),
    )
    m = {k: np.ascontiguousarray(v).astype(np.float32)
         if np.asarray(v).dtype == np.float64 or np.asarray(v).dtype == np.float32
         else np.ascontiguousarray(v) for k, v in m.items()}
    in_maps = []
    for c in range(8):
        mc = dict(m)
        mc["xin"] = xT[c]
        in_maps.append(mc)
    res = run_bass_kernel_spmd(nc, in_maps, core_ids=list(range(8)))
    if debug:
        return res.results
    return [res.results[c]["xout"] for c in range(8)]




def build_layer2(consts, debug=False):
    nc = bass.Bass("TRN2", target_bir_lowering=False)
    em = Em(nc)
    dk = "ExternalOutput" if debug else "Internal"
    specs = [
        ("xin", (D, S), F32, "ExternalInput"),
        ("xout", (D, S), F32, "ExternalOutput"),
        ("wq", (D, D), F32, "ExternalInput"),
        ("wk", (D, D), F32, "ExternalInput"),
        ("wv", (D, D), F32, "ExternalInput"),
        ("wz", (D, D), F32, "ExternalInput"),
        ("wab", (D, 32), F32, "ExternalInput"),
        ("convw", (D, 12), F32, "ExternalInput"),
        ("normw", (128, NF), F32, "ExternalInput"),
        ("wout", (D, D), F32, "ExternalInput"),
        ("wr", (128, NF), F32, "ExternalInput"),
        ("br", (1, 1), F32, "ExternalInput"),
        ("w1", (D, 8192), BF16, "ExternalInput"),
        ("b1", (128, 64), F32, "ExternalInput"),
        ("w2", (8192, D), BF16, "ExternalInput"),
        ("b2", (128, NF), F32, "ExternalInput"),
        ("accA", (D, S), F32, dk),
        ("accF", (D, S), F32, dk),
        ("qn_hbm", (D, S), BF16, dk),
        ("kn_hbm", (D, S), BF16, dk),
        ("v_hbm", (D, S), BF16, dk),
        ("z_hbm", (D, S), BF16, dk),
        ("wrow_dbg", (1, S), F32, dk),
        ("sel_dbg", (1, S), F32, dk),
    ]
    io = _io_tensors(nc, specs)
    C = Consts(em, ["onescol_f", "onesrow_f", "onesrow_b", "ident_f32", "selE",
                    "selB", "ropecos", "ropesin", "rope_R"], consts)
    ab_sb = nc.alloc_sbuf_tensor(nm("ab"), [16, 2 * (S + 3)], F32)
    wrow_sb = nc.alloc_sbuf_tensor(nm("wrow"), [1, S], F32)
    selw_sb = nc.alloc_sbuf_tensor(nm("selw"), [1, S], F32)
    comb_sb = nc.alloc_sbuf_tensor(nm("comb"), [1, 2 * S], F32)
    mean_sb = nc.alloc_sbuf_tensor(nm("mean2"), [1, S], F32)
    rstd_sb = nc.alloc_sbuf_tensor(nm("rstd2"), [1, S], F32)
    misc_sb = nc.alloc_sbuf_tensor(nm("misc2"), [128, 8], F32)
    wsum_sb = nc.alloc_sbuf_tensor(nm("wsum2"), [1, 1], F32)
    wr_sb = nc.alloc_sbuf_tensor(nm("wr2"), [128, NF], F32)
    normw_sb = nc.alloc_sbuf_tensor(nm("nw2"), [128, NF], F32)
    wab_sb = nc.alloc_sbuf_tensor(nm("wab2"), [128, NF * 32], BF16)
    convw_sb = nc.alloc_sbuf_tensor(nm("cw2"), [128, NF * 12], F32)
    em.dma(em.sy, wr_sb[:, :], io["wr"][:, :], "d_w")
    em.dma(em.sy, normw_sb[:, :], io["normw"][:, :], "d_w")
    em.dma(em.gp, wab_sb[:].rearrange("p (f c) -> p f c", f=NF),
           io["wab"].rearrange("(f p) c -> p f c", p=128), "d_w")
    em.dma(em.sy, convw_sb[:].rearrange("p (f c) -> p f c", f=NF),
           io["convw"].rearrange("(f p) c -> p f c", p=128), "d_w")
    em.dma(em.sy, wsum_sb[0:1, 0:1], io["br"][:, :], "d_w")
    em.hard()
    with nc.psum_tensor(nm("psg"), [1, 512], F32) as psg:
        mm_kseries(em, psg[0:1, 0:1],
                   [(wr_sb[:, f:f + 1], C["onescol_f"][:, :]) for f in range(NF)])
        em.barrier()
        em.ve.tensor_copy(misc_sb[0:1, 0:1], psg[0:1, 0:1])
        em.barrier()
    br_sb = wsum_sb
    rows_holder = {}

    def router(em_, C_, c, xch, post=False):
        if not post:
            mm_kseries(em_, rt["ps_w"][0:1, :],
                       [(wr_sb[:, f:f + 1], xch[:, f * 256:(f + 1) * 256])
                        for f in range(NF)])
            return
        tsl = slice(c * 256, (c + 1) * 256)
        r_mean, r_rstd = rows_holder["mean"], rows_holder["rstd"]
        em_.ve.tensor_scalar(rt["t1"][0:1, :], r_mean, misc_sb[0:1, 0:1], None,
                             ALU.mult)
        em_.barrier()
        em_.ve.tensor_tensor(rt["t1"][0:1, :], rt["ps_w"][0:1, :], rt["t1"][0:1, :],
                             ALU.subtract)
        em_.barrier()
        em_.ve.tensor_tensor(rt["t1"][0:1, :], rt["t1"][0:1, :], r_rstd, ALU.mult)
        em_.barrier()
        em_.ve.tensor_scalar(wrow_sb[0:1, tsl], rt["t1"][0:1, :],
                             br_sb[0:1, 0:1], None, ALU.add)
        em_.barrier()

    def router3(em_, C_, c, xch, post=False):
        if post:
            rows_holder["mean"] = mean_sb[:, c * 256:(c + 1) * 256]
            rows_holder["rstd"] = rstd_sb[:, c * 256:(c + 1) * 256]
        router(em_, C_, c, xch, post=post)

    rt = {}
    with nc.sbuf_tensor(nm("xn"), [128, NF * S], BF16) as xn_sb:
        with ExitStack() as _es:
            rt["ps_w"] = _es.enter_context(nc.psum_tensor(nm("ps_w"), [1, 256], F32))
            rt["t1"] = _es.enter_context(nc.sbuf_tensor(nm("rt_t1"), [1, 256], F32))
            emit_phase0(em, C, {"x_in": io["xin"]}, xn_sb=xn_sb, router=router3,
                        save_rows={"mean": mean_sb, "rstd": rstd_sb}, ct=256)
        emit_l2_select(em, C, io, wrow_sb, selw_sb, comb_sb, mean_sb, rstd_sb)
        emit_l2_ffn(em, C, io, xn_sb, selw_sb)
        emit_l2_attnproj(em, C, io, xn_sb, ab_sb, wab_sb, convw_sb)
    emit_l2_attncore(em, C, io, ab_sb, normw_sb)
    def l2fix(em_, C_, c, xch):
        with ExitStack() as _es:
            ps_a = _es.enter_context(nc.psum_tensor(nm("psfa"), [128, 256], F32))
            ps_b = _es.enter_context(nc.psum_tensor(nm("psfb"), [128, 256], F32))
            em_.te.matmul(ps_a[:, :], C_["onesrow_f"][:, :], comb_sb[0:1, c * 256:(c + 1) * 256],
                          start=True, stop=True)
            em_.te.matmul(ps_b[:, :], C_["onesrow_f"][:, :], comb_sb[0:1, S + c * 256:S + (c + 1) * 256],
                          start=True, stop=True)
            em_.barrier()
            for f in range(NF):
                em_.ve.tensor_tensor(xch[:, f * 256:(f + 1) * 256],
                                     xch[:, f * 256:(f + 1) * 256], ps_a[:, :],
                                     ALU.mult)
            em_.barrier()
            for f in range(NF):
                em_.ve.tensor_tensor(xch[:, f * 256:(f + 1) * 256],
                                     xch[:, f * 256:(f + 1) * 256], ps_b[:, :],
                                     ALU.add)
            em_.barrier()
    emit_phase0(em, C, {"x_in": io["xin"]}, adds=(io["accA"], io["accF"]),
                out_x=io["xout"], combine_only=True, l2fix=l2fix, ct=256)
    if debug:
        em.ve.tensor_copy(wrow_sb[0:1, 0:1], wrow_sb[0:1, 0:1])
        em.dma(em.sy, io["wrow_dbg"][:, :], wrow_sb[0:1, :], "d_sc")
        em.dma(em.sy, io["sel_dbg"][:, :], selw_sb[0:1, :], "d_sc")
    em.hard()
    return nc


def emit_l2_select(em, C, io, wrow_sb, selw_sb, comb_sb, mean_sb, rstd_sb):
    nc = em.nc
    with ExitStack() as _es:
        wcol = _es.enter_context(nc.sbuf_tensor(nm("wcol"), [128, 16], F32))
        wb = _es.enter_context(nc.sbuf_tensor(nm("wb"), [128, S], F32))
        cmp = _es.enter_context(nc.sbuf_tensor(nm("cmp"), [128, 512], F32))
        rowt = _es.enter_context(nc.sbuf_tensor(nm("rowt"), [1, 2 * S], F32))
        ps_t = _es.enter_context(nc.psum_tensor(nm("ps_t2"), [128, 128], F32))
        ps_c = _es.enter_context(nc.psum_tensor(nm("ps_c2"), [1, 512], F32))
        ps_bx = _es.enter_context(nc.psum_tensor(nm("ps_bx"), [128, 512], F32))
        for i in range(16):
            em.te.transpose(ps_t[0:128, 0:1], wrow_sb[0:1, i * 128:(i + 1) * 128],
                            C["ident_f32"][0:1, 0:1])
            em.barrier()
            em.ve.tensor_copy(wcol[:, i:i + 1], ps_t[0:128, 0:1])
            em.barrier()
        for c in range(4):
            em.te.matmul(ps_bx[:, :], C["onesrow_f"][:, :],
                         wrow_sb[0:1, c * 512:(c + 1) * 512], start=True, stop=True)
            em.barrier()
            em.ve.tensor_copy(wb[:, c * 512:(c + 1) * 512], ps_bx[:, :])
            em.barrier()
        for c in range(4):
            tsl = slice(c * 512, (c + 1) * 512)
            for i in range(16):
                em.ve.tensor_scalar(cmp[:, :], wb[:, tsl], wcol[:, i:i + 1], None,
                                    ALU.is_lt)
                em.barrier()
                em.te.matmul(ps_c[0:1, :], C["onescol_f"][:, :], cmp[:, :],
                             start=(i == 0), stop=(i == 15))
                em.barrier()
            em.ve.tensor_scalar(rowt[0:1, 0:512], ps_c[0:1, :], 510.5, None,
                                ALU.is_lt)
            em.barrier()
            em.ve.tensor_tensor(selw_sb[0:1, tsl], rowt[0:1, 0:512],
                                wrow_sb[0:1, tsl], ALU.mult)
            em.ve.tensor_scalar(rowt[0:1, 512:1024], rowt[0:1, 0:512], -1.0, 1.0,
                                ALU.mult, ALU.add)
            em.barrier()
            em.ve.tensor_tensor(rowt[0:1, 1024:1536], rowt[0:1, 512:1024],
                                rstd_sb[0:1, tsl], ALU.mult)
            em.barrier()
            em.ve.tensor_scalar(comb_sb[0:1, tsl], rowt[0:1, 1024:1536], 1.0, None,
                                ALU.add)
            em.ve.tensor_tensor(rowt[0:1, 1536:2048], rowt[0:1, 1024:1536],
                                mean_sb[0:1, tsl], ALU.mult)
            em.barrier()
            em.ve.tensor_scalar(comb_sb[0:1, S + c * 512:S + (c + 1) * 512],
                                rowt[0:1, 1536:2048], -1.0, None, ALU.mult)
            em.barrier()


def emit_l2_ffn(em, C, io, xn_sb, selw_sb):
    nc = em.nc
    GA = 0.7978845608028654
    GB = 0.044715
    CT = 256
    with ExitStack() as _es:
        wc = _es.enter_context(nc.sbuf_tensor(nm("wc"), [128, NF * 128], BF16))
        b1_sb = _es.enter_context(nc.sbuf_tensor(nm("b1s"), [128, 64], F32))
        b2_sb = _es.enter_context(nc.sbuf_tensor(nm("b2s"), [128, NF], F32))
        hch = _es.enter_context(nc.sbuf_tensor(nm("hch"), [128, 64 * CT], BF16))
        dc = _es.enter_context(nc.sbuf_tensor(nm("dc"), [128, 64 * 128], BF16))
        ug = _es.enter_context(nc.sbuf_tensor(nm("ug"), [128, CT], F32))
        t_ = _es.enter_context(nc.sbuf_tensor(nm("t_"), [128, CT], F32))
        stg = _es.enter_context(nc.sbuf_tensor(nm("stg"), [128, CT], F32))
        ps = _es.enter_context(nc.psum_tensor(nm("ps"), [128, CT], F32))
        ps_b = _es.enter_context(nc.psum_tensor(nm("ps_b2b"), [128, CT], F32))
        em.dma(em.sy, b1_sb[:, :], io["b1"][:, :], "d_w")
        em.dma(em.sy, b2_sb[:, :], io["b2"][:, :], "d_w")
        em.hard()
        for c in range(S // CT):
            tsl = slice(c * CT, (c + 1) * CT)
            for o in range(64):
                em.dma(em.gp, wc[:].rearrange("p (f x) -> p f x", f=NF),
                       io["w1"][:, o * 128:(o + 1) * 128].rearrange("(f p) x -> p f x", p=128),
                       "d_w")
                em.hard()
                mm_kseries(em, ps[:, :],
                           [(wc[:, f * 128:(f + 1) * 128],
                             xn_sb[:, f * S + c * CT: f * S + (c + 1) * CT])
                            for f in range(NF)])
                em.barrier()
                em.se.activation(ug[:, :], ps[:, :], AF.Identity,
                                 bias=b1_sb[:, o:o + 1])
                em.barrier()
                em.se.activation(t_[:, :], ug[:, :], AF.Square)
                em.barrier()
                em.ve.tensor_scalar(t_[:, :], t_[:, :], GA * GB, GA, ALU.mult,
                                    ALU.add)
                em.barrier()
                em.ve.tensor_tensor(t_[:, :], ug[:, :], t_[:, :], ALU.mult)
                em.barrier()
                em.se.activation(t_[:, :], t_[:, :], AF.Tanh)
                em.barrier()
                em.ve.tensor_scalar(t_[:, :], t_[:, :], 0.5, 0.5, ALU.mult, ALU.add)
                em.barrier()
                em.ve.tensor_tensor(hch[:, o * CT:(o + 1) * CT], ug[:, :], t_[:, :],
                                    ALU.mult)
                em.barrier()
            em.te.matmul(ps_b[:, :], C["onesrow_f"][:, :], selw_sb[0:1, tsl],
                         start=True, stop=True)
            em.barrier()
            for o2 in range(NF):
                em.dma(em.gp, dc[:].rearrange("p (f x) -> p f x", f=64),
                       io["w2"][:, o2 * 128:(o2 + 1) * 128].rearrange("(f p) x -> p f x", p=128),
                       "d_w")
                em.hard()
                mm_kseries(em, ps[:, :],
                           [(dc[:, f * 128:(f + 1) * 128],
                             hch[:, f * CT:(f + 1) * CT]) for f in range(64)])
                em.barrier()
                em.se.activation(stg[:, :], ps[:, :], AF.Identity,
                                 bias=b2_sb[:, o2:o2 + 1])
                em.barrier()
                em.ve.tensor_tensor(stg[:, :], stg[:, :], ps_b[:, :], ALU.mult)
                em.barrier()
                em.dma(em.sy, io["accF"][o2 * 128:(o2 + 1) * 128, tsl], stg[:, :],
                       "d_sc")
                em.hard()


def emit_l2_attnproj(em, C, io, xn_sb, ab_sb, wab_sb, convw_sb):
    nc = em.nc
    with ExitStack() as _es:
        wc = _es.enter_context(nc.sbuf_tensor(nm("wc"), [128, NF * 128], BF16))
        raw = _es.enter_context(nc.sbuf_tensor(nm("raw"), [128, S + 3], F32))
        cvb = _es.enter_context(nc.sbuf_tensor(nm("cvb"), [128, S], BF16))
        qr = _es.enter_context(nc.sbuf_tensor(nm("qr"), [128, 512], F32))
        t_ = _es.enter_context(nc.sbuf_tensor(nm("t_2"), [128, 512], F32))
        rows = _es.enter_context(nc.sbuf_tensor(nm("rows2"), [1, 512], F32))
        stg = _es.enter_context(nc.sbuf_tensor(nm("stg2"), [128, 512], BF16))
        ps = _es.enter_context(nc.psum_tensor(nm("ps2"), [128, 512], F32))
        ps_r = _es.enter_context(nc.psum_tensor(nm("ps_r2"), [128, 512], F32))
        ps_s = _es.enter_context(nc.psum_tensor(nm("ps_s2"), [1, 512], F32))
        ps_bx = _es.enter_context(nc.psum_tensor(nm("ps_bx2"), [128, 512], F32))
        em.ve.memset(raw[:, 0:3], 0.0)
        em.barrier()
        for name, w_ap, dst, do_rope in (
                ("q", io["wq"], io["qn_hbm"], True),
                ("k", io["wk"], io["kn_hbm"], True),
                ("v", io["wv"], io["v_hbm"], False)):
            cvcol = {"q": 0, "k": 4, "v": 8}[name]
            for h in range(16):
                em.dma(em.gp, wc[:].rearrange("p (f x) -> p f x", f=NF),
                       w_ap[:, h * 128:(h + 1) * 128].rearrange("(f p) x -> p f x", p=128),
                       "d_w")
                em.hard()
                for c in range(4):
                    mm_kseries(em, ps[:, :],
                               [(wc[:, f * 128:(f + 1) * 128],
                                 xn_sb[:, f * S + c * 512: f * S + (c + 1) * 512])
                                for f in range(NF)])
                    em.barrier()
                    em.se.activation(raw[:, 3 + c * 512:3 + (c + 1) * 512], ps[:, :],
                                     AF.Silu)
                    em.barrier()
                cw = convw_sb
                for c in range(4):
                    em.ve.tensor_scalar(t_[:, :], raw[:, 3 + c * 512 - 3: 3 + (c + 1) * 512 - 3],
                                        cw[:, h * 12 + cvcol: h * 12 + cvcol + 1],
                                        None, ALU.mult)
                    em.barrier()
                    for l in (2, 1, 0):
                        j = 3 - l
                        em.ve.scalar_tensor_tensor(
                            t_[:, :],
                            raw[:, 3 + c * 512 - l: 3 + (c + 1) * 512 - l],
                            cw[:, h * 12 + cvcol + j: h * 12 + cvcol + j + 1],
                            t_[:, :], ALU.mult, ALU.add)
                        em.barrier()
                    em.ve.tensor_copy(cvb[:, c * 512:(c + 1) * 512], t_[:, :])
                    em.barrier()
                if not do_rope:
                    em.dma(em.sy, dst[h * 128:(h + 1) * 128, :], cvb[:, :], "d_sc")
                    em.hard()
                    continue
                for c in range(4):
                    csl = slice(c * 512, (c + 1) * 512)
                    em.te.matmul(ps_r[:, :], C["rope_R"][:, :], cvb[:, csl],
                                 start=True, stop=True)
                    em.barrier()
                    em.ve.tensor_tensor(qr[:, :], cvb[:, csl],
                                        C["ropecos"][:, csl], ALU.mult)
                    em.barrier()
                    em.ve.tensor_tensor(t_[:, :], ps_r[:, :], C["ropesin"][:, csl],
                                        ALU.mult)
                    em.barrier()
                    em.ve.tensor_tensor(qr[:, :], qr[:, :], t_[:, :], ALU.add)
                    em.barrier()
                    em.se.activation(t_[:, :], qr[:, :], AF.Square)
                    em.barrier()
                    em.te.matmul(ps_s[0:1, :], C["onescol_f"][:, :], t_[:, :],
                                 start=True, stop=True)
                    em.barrier()
                    em.se.activation(rows[0:1, :], ps_s[0:1, :], AF.Sqrt)
                    em.barrier()
                    em.ve.tensor_scalar(rows[0:1, :], rows[0:1, :], 1e-12, None,
                                        ALU.max)
                    em.barrier()
                    em.ve.reciprocal(rows[0:1, :], rows[0:1, :])
                    em.barrier()
                    em.te.matmul(ps_bx[:, :], C["onesrow_f"][:, :], rows[0:1, :],
                                 start=True, stop=True)
                    em.barrier()
                    em.ve.tensor_tensor(stg[:, :], qr[:, :], ps_bx[:, :], ALU.mult)
                    em.barrier()
                    em.dma(em.sy, dst[h * 128:(h + 1) * 128, csl], stg[:, :], "d_sc")
                    em.hard()
        for h in range(16):
            em.dma(em.gp, wc[:].rearrange("p (f x) -> p f x", f=NF),
                   io["wz"][:, h * 128:(h + 1) * 128].rearrange("(f p) x -> p f x", p=128),
                   "d_w")
            em.hard()
            for c in range(4):
                mm_kseries(em, ps[:, :],
                           [(wc[:, f * 128:(f + 1) * 128],
                             xn_sb[:, f * S + c * 512: f * S + (c + 1) * 512])
                            for f in range(NF)])
                em.barrier()
                em.se.activation(stg[:, :], ps[:, :], AF.Silu)
                em.barrier()
                em.dma(em.sy, io["z_hbm"][h * 128:(h + 1) * 128, c * 512:(c + 1) * 512],
                       stg[:, :], "d_sc")
                em.hard()
        em.ve.memset(ab_sb[0:16, :], 0.0)
        em.barrier()
        with nc.psum_tensor(nm("ps_ab"), [16, 512], F32) as ps_ab:
            for c in range(4):
                mm_kseries(em, ps_ab[0:16, :],
                           [(wab_sb[:, f * 32:f * 32 + 16],
                             xn_sb[:, f * S + c * 512: f * S + (c + 1) * 512])
                            for f in range(NF)])
                em.barrier()
                em.ve.tensor_copy(ab_sb[0:16, 3 + c * 512: 3 + (c + 1) * 512],
                                  ps_ab[0:16, :])
                em.barrier()
                mm_kseries(em, ps_ab[0:16, :],
                           [(wab_sb[:, f * 32 + 16:f * 32 + 32],
                             xn_sb[:, f * S + c * 512: f * S + (c + 1) * 512])
                            for f in range(NF)])
                em.barrier()
                em.ve.tensor_copy(ab_sb[0:16, (S + 3) + 3 + c * 512: (S + 3) + 3 + (c + 1) * 512],
                                  ps_ab[0:16, :])
                em.barrier()
        em.hard()


def emit_l2_attncore(em, C, io, ab_sb, normw_sb):
    nc = em.nc
    ESC = 1.0 / math.sqrt(128.0)
    with ExitStack() as _es:
        qnb = _es.enter_context(nc.sbuf_tensor(nm("qnb"), [128, S], BF16))
        knb = _es.enter_context(nc.sbuf_tensor(nm("knb"), [128, S + 3], BF16))
        vb = _es.enter_context(nc.sbuf_tensor(nm("vb"), [128, S + 3], BF16))
        m_ = _es.enter_context(nc.sbuf_tensor(nm("m_"), [128, 512], BF16))
        gt = _es.enter_context(nc.sbuf_tensor(nm("gt"), [16, 512], F32))
        pb = _es.enter_context(nc.sbuf_tensor(nm("pb"), [16, 4 * 512], BF16))
        oacc = _es.enter_context(nc.sbuf_tensor(nm("oacc"), [128, NF * 512], F32))
        t_ = _es.enter_context(nc.sbuf_tensor(nm("t_3"), [128, 512], F32))
        zb = _es.enter_context(nc.sbuf_tensor(nm("zb"), [128, 512], BF16))
        wor = _es.enter_context(nc.sbuf_tensor(nm("wor"), [128, NF * 512], BF16))
        rows = _es.enter_context(nc.sbuf_tensor(nm("rows3"), [1, 512], F32))
        wc = _es.enter_context(nc.sbuf_tensor(nm("wc3"), [128, NF * 128], BF16))
        stg = _es.enter_context(nc.sbuf_tensor(nm("stg3"), [128, 512], F32))
        ps_at = [_es.enter_context(nc.psum_tensor(nm("ps_at"), [16, 512], F32))
                 for _ in range(4)]
        ps_w = _es.enter_context(nc.psum_tensor(nm("ps_w3"), [128, 512], F32))
        ps_s = _es.enter_context(nc.psum_tensor(nm("ps_s3"), [1, 512], F32))
        ps_bx = _es.enter_context(nc.psum_tensor(nm("ps_bx3"), [128, 512], F32))
        em.ve.memset(knb[:, 0:3], 0.0)
        em.ve.memset(vb[:, 0:3], 0.0)
        em.barrier()
        for c in range(4):
            tsl = slice(c * 512, (c + 1) * 512)
            for h in range(16):
                em.dma(em.sy, qnb[:, tsl], io["qn_hbm"][h * 128:(h + 1) * 128, tsl],
                       "d_sc")
                em.dma(em.sy, knb[:, 3:3 + S], io["kn_hbm"][h * 128:(h + 1) * 128, :],
                       "d_sc")
                em.hard()
                for s_ in range(4):
                    em.ve.tensor_tensor(
                        m_[:, :], qnb[:, tsl],
                        knb[:, 3 + c * 512 - s_: 3 + (c + 1) * 512 - s_], ALU.mult)
                    em.barrier()
                    em.te.matmul(ps_at[s_][0:16, :], C["selE"][:, h * 16:(h + 1) * 16],
                                 m_[:, :], start=(h == 0), stop=(h == 15))
                    em.barrier()
            for s_ in range(4):
                a_sh = ab_sb[0:16, 3 + c * 512 - s_: 3 + (c + 1) * 512 - s_]
                b_r = ab_sb[0:16, (S + 3) + 3 + c * 512: (S + 3) + 3 + (c + 1) * 512]
                em.ve.tensor_tensor(gt[0:16, :], a_sh, b_r, ALU.mult)
                em.barrier()
                em.se.activation(gt[0:16, :], gt[0:16, :], AF.Sigmoid)
                em.barrier()
                em.ve.scalar_tensor_tensor(pb[0:16, s_ * 512:(s_ + 1) * 512],
                                           ps_at[s_][0:16, :], ESC, gt[0:16, :],
                                           ALU.mult, ALU.mult)
                em.barrier()
            em.ve.memset(oacc[:, :], 0.0)
            em.barrier()
            for h in range(16):
                em.dma(em.sy, vb[:, 3:3 + S], io["v_hbm"][h * 128:(h + 1) * 128, :],
                       "d_sc")
                em.hard()
                for s_ in range(4):
                    em.te.matmul(ps_w[:, :], C["selB"][0:16, h * 128:(h + 1) * 128],
                                 pb[0:16, s_ * 512:(s_ + 1) * 512], start=True,
                                 stop=True)
                    em.barrier()
                    em.ve.tensor_tensor(t_[:, :], ps_w[:, :],
                                        vb[:, 3 + c * 512 - s_: 3 + (c + 1) * 512 - s_],
                                        ALU.mult)
                    em.barrier()
                    em.ve.tensor_tensor(oacc[:, h * 512:(h + 1) * 512],
                                        oacc[:, h * 512:(h + 1) * 512], t_[:, :],
                                        ALU.add)
                    em.barrier()
            for h in range(16):
                em.se.activation(t_[:, :], oacc[:, h * 512:(h + 1) * 512], AF.Square)
                em.barrier()
                em.te.matmul(ps_s[0:1, :], C["onescol_f"][:, :], t_[:, :],
                             start=(h == 0), stop=(h == 15))
                em.barrier()
            em.ve.tensor_scalar(rows[0:1, :], ps_s[0:1, :], 1.0 / D, None, ALU.mult)
            em.barrier()
            em.ve.tensor_scalar(rows[0:1, :], rows[0:1, :], 1e-6, None, ALU.add)
            em.barrier()
            em.se.activation(rows[0:1, :], rows[0:1, :], AF.Sqrt)
            em.barrier()
            em.ve.reciprocal(rows[0:1, :], rows[0:1, :])
            em.barrier()
            em.te.matmul(ps_bx[:, :], C["onesrow_f"][:, :], rows[0:1, :],
                         start=True, stop=True)
            em.barrier()
            for h in range(16):
                em.wait(em.ve, "d_sc")
                em.dma(em.sy, zb[:, :], io["z_hbm"][h * 128:(h + 1) * 128, tsl],
                       "d_sc")
                em.wait(em.ve, "d_sc")
                em.ve.tensor_tensor(t_[:, :], oacc[:, h * 512:(h + 1) * 512],
                                    ps_bx[:, :], ALU.mult)
                em.barrier()
                em.ve.tensor_scalar(t_[:, :], t_[:, :], normw_sb[:, h:h + 1], None,
                                    ALU.mult)
                em.barrier()
                em.ve.tensor_tensor(wor[:, h * 512:(h + 1) * 512], t_[:, :], zb[:, :],
                                    ALU.mult)
                em.barrier()
            for o in range(NF):
                em.dma(em.gp, wc[:].rearrange("p (f x) -> p f x", f=NF),
                       io["wout"][:, o * 128:(o + 1) * 128].rearrange("(f p) x -> p f x", p=128),
                       "d_w")
                em.hard()
                mm_kseries(em, ps_w[:, :],
                           [(wc[:, f * 128:(f + 1) * 128],
                             wor[:, f * 512:(f + 1) * 512]) for f in range(NF)])
                em.barrier()
                em.se.activation(stg[:, :], ps_w[:, :], AF.Identity)
                em.barrier()
                em.dma(em.sy, io["accA"][o * 128:(o + 1) * 128, tsl], stg[:, :],
                       "d_sc")
                em.hard()


def run_layer2(inp, prep, xT, debug=False):
    nc = build_layer2(_consts(), debug=debug)
    cw = np.concatenate([inp["L2_attn_conv_q_w"], inp["L2_attn_conv_k_w"],
                         inp["L2_attn_conv_v_w"]], axis=1)
    m = dict(
        wq=inp["L2_attn_Wq"], wk=inp["L2_attn_Wk"], wv=inp["L2_attn_Wv"],
        wz=inp["L2_attn_Wz"],
        wab=np.concatenate([inp["L2_attn_Wa"], inp["L2_attn_Wb"]], axis=1),
        convw=cw, normw=_pcol(inp["L2_attn_norm_w"]), wout=inp["L2_attn_Wout"],
        wr=_pcol(inp["L2_ffn_W_r"][:, 0]), br=inp["L2_ffn_b_r"].reshape(1, 1),
        w1=prep["l2w1_bf"], b1=_pcol(inp["L2_ffn_b1"]), w2=prep["l2w2_bf"],
        b2=_pcol(inp["L2_ffn_b2"]),
    )
    m = {k: np.ascontiguousarray(v).astype(np.float32)
         if np.asarray(v).dtype in (np.float64, np.float32) else np.ascontiguousarray(v)
         for k, v in m.items()}
    in_maps = []
    for c in range(8):
        mc = dict(m)
        mc["xin"] = xT[c]
        in_maps.append(mc)
    res = run_bass_kernel_spmd(nc, in_maps, core_ids=list(range(8)))
    if debug:
        return res.results
    return [res.results[c]["xout"] for c in range(8)]


# version 10
# speedup vs baseline: 1.5565x; 1.1940x over previous
import math
from contextlib import ExitStack

import numpy as np
import ml_dtypes

import concourse.bass as bass
import concourse.mybir as mybir
from concourse.bass_utils import run_bass_kernel_spmd

F32 = mybir.dt.float32
BF16 = mybir.dt.bfloat16
I32 = mybir.dt.int32
AF = mybir.ActivationFunctionType
ALU = mybir.AluOpType
AX = mybir.AxisListType
BFNP = ml_dtypes.bfloat16

_NMC = [0]


def nm(base):
    _NMC[0] += 1
    return f"{base}_{_NMC[0]}"


S = 2048
D = 2048
NF = D // 128
NEG = -1.0e30



_ENG_OPS = {
    "tensor_tensor", "tensor_scalar", "tensor_copy", "tensor_add", "tensor_sub",
    "tensor_mul", "tensor_max", "tensor_scalar_add", "tensor_scalar_sub",
    "tensor_scalar_mul", "tensor_scalar_max", "tensor_scalar_min", "tensor_relu",
    "tensor_reduce", "reduce_max", "reduce_sum", "scalar_tensor_tensor",
    "tensor_tensor_scan", "memset", "memzero", "reciprocal", "select",
    "copy_predicated", "activation", "copy", "add", "mul", "sqrt", "square",
    "sign", "iota", "affine_select", "tensor_single_scalar",
}


class _EngWrap:

    def __init__(self, em, eng, sem_name):
        self._em = em
        self._eng = eng
        self._sem_name = sem_name

    def __getattr__(self, name):
        eng = self._eng
        attr = getattr(eng, name)
        if name not in _ENG_OPS:
            return attr
        em = self._em
        sem_name = self._sem_name

        def wrapped(*a, **k):
            h = em.sem(sem_name)
            eng.wait_ge(h, em.cnt(sem_name))
            inst = attr(*a, **k)
            inst.then_inc(h, 1)
            em.bump(sem_name)
            return inst
        return wrapped


class Em:

    def __init__(self, nc):
        self.nc = nc
        self.te = nc.tensor
        self.ve = _EngWrap(self, nc.vector, "q_ve")
        self.se = _EngWrap(self, nc.scalar, "q_se")
        self.gp = _EngWrap(self, nc.gpsimd, "q_gp")
        self.sy = nc.sync
        self._sems = {}
        self._cnt = {}
        self._ctx = []

    def sem(self, name):
        if name not in self._sems:
            h = self.nc.alloc_semaphore(name=name)
            self._sems[name] = h
            self._cnt[name] = 0
        return self._sems[name]

    def cnt(self, name):
        return self._cnt[name]

    def bump(self, name, by=1):
        self._cnt[name] += by
        return self._cnt[name]

    def _qsem(self, q, sem_name):
        return sem_name + ("_g" if q is self.gp else "")

    def dma(self, q, out, in_, sem_name):
        name = self._qsem(q, sem_name)
        h = self.sem(name)
        q.dma_start(out, in_).then_inc(h, 16)
        self.bump(name, 16)

    def wait(self, eng, sem_name, val=None):
        for name in (sem_name, sem_name + "_g"):
            if name in self._sems:
                eng.wait_ge(self._sems[name],
                            self.cnt(name) if val is None else val)

    def barrier(self):
        self.nc.all_engine_barrier()

    def hard(self):
        if not hasattr(self, "_hard_seen"):
            self._hard_seen = {}
        for name in list(self._sems):
            if name.startswith("d_") and self._hard_seen.get(name) != self._cnt[name]:
                self._hard_seen[name] = self._cnt[name]
                for eng in (self.te, self.ve, self.se, self.gp, self.sy):
                    eng.wait_ge(self._sems[name], self._cnt[name])
        self.nc.all_engine_barrier()


class ColStream:

    def __init__(self, em, streams, n_o, sem="d_w"):
        self.em = em
        self.streams = streams
        self.n_o = n_o
        self.sem = sem
        self.loadc = []
        self.consumed = []
        self.tsem = em.sem("t_mm")

    def _load(self, o):
        em = self.em
        if o >= 2:
            em.gp.wait_ge(self.tsem, self.consumed[o - 2])
        name = self.sem + "AB"[o % 2]
        for (w_ap, cols, kt, bufs) in self.streams:
            em.dma(em.gp, bufs[o % 2][:].rearrange("p (k c) -> p k c", k=kt),
                   w_ap[:, o * cols:(o + 1) * cols].rearrange(
                       "(k p) c -> p k c", p=128), name)
        self.loadc.append((name, em.cnt(em._qsem(em.gp, name))))

    def tag(self, inst):
        inst.then_inc(self.tsem, 1)
        self.em.bump("t_mm")

    def run(self, body):
        em = self.em
        self._load(0)
        for o in range(self.n_o):
            if o + 1 < self.n_o:
                self._load(o + 1)
            nmo, cnto = self.loadc[o]
            em.wait(em.te, em._qsem(em.gp, nmo), cnto)
            n0 = self.em.cnt("t_mm")
            body(o, [bufs[o % 2] for (_, _, _, bufs) in self.streams])
            assert self.em.cnt("t_mm") == n0 + 1, "body must cs.tag one matmul"
            self.consumed.append(self.em.cnt("t_mm"))
        em.barrier()


def mm_kseries(em, ps_ap, pairs, then_sem=None, inc=1):
    n = len(pairs)
    last = None
    for i, (l, r) in enumerate(pairs):
        last = em.te.matmul(ps_ap, l, r, start=(i == 0), stop=(i == n - 1))
    if then_sem is not None:
        last.then_inc(then_sem, inc)
    return last




def _t(a):
    return np.ascontiguousarray(np.asarray(a).T)


def _pcol(b):
    b = np.asarray(b)
    return np.ascontiguousarray(b.reshape(-1, 128).T)


def _row(b):
    return np.ascontiguousarray(np.asarray(b).reshape(1, -1))




def build_consts():
    c = {}
    c["ident_bf"] = np.eye(128, dtype=BFNP)
    c["ident_f32"] = np.eye(128, dtype=np.float32)
    onescol = np.ones((128, 1), np.float32)
    c["onescol_f"] = onescol
    c["onescol_b"] = onescol.astype(BFNP)
    onesrow = np.ones((1, 128), np.float32)
    c["onesrow_f"] = onesrow
    c["onesrow_b"] = onesrow.astype(BFNP)
    for r in range(2):
        p = np.arange(128)[:, None]
        j = np.arange(256)[None, :]
        c[f"mask256_r{r}"] = np.where(p <= j - 128 * r, 0.0, NEG).astype(np.float32)
    for r in range(4):
        p = np.arange(128)[:, None]
        j = np.arange(512)[None, :]
        c[f"mask512_r{r}"] = np.where(p <= j - 128 * r, 0.0, NEG).astype(np.float32)
    p = np.arange(128)[:, None]
    j = np.arange(512)[None, :]
    c["mask4mul"] = np.concatenate(
        [np.where(p <= j - 128 * r, 1.0, 0.0) for r in range(4)],
        axis=1).astype(BFNP)
    Dh = 128
    inv = 1.0 / (1e6 ** (np.arange(0, Dh, 2) / Dh))
    fr = np.arange(S)[:, None] * inv[None, :]
    cos = np.concatenate([np.cos(fr), np.cos(fr)], 1)
    sin = np.concatenate([np.sin(fr), np.sin(fr)], 1)
    c["ropecos"] = np.ascontiguousarray(cos.T).astype(np.float32)
    c["ropesin"] = np.ascontiguousarray(sin.T).astype(np.float32)
    R = np.zeros((128, 128), np.float32)
    for m in range(64):
        R[m + 64, m] = -1.0
    for m in range(64, 128):
        R[m - 64, m] = 1.0
    c["rope_R"] = R.astype(BFNP)
    E = np.zeros((128, 16 * 16), np.float32)
    for h in range(16):
        E[:, 16 * h + h] = 1.0
    c["selE"] = E.astype(BFNP)
    B = np.zeros((16, 16 * 128), np.float32)
    for h in range(16):
        B[h, 128 * h:128 * (h + 1)] = 1.0
    c["selB"] = B.astype(BFNP)
    B8 = np.zeros((8, 8 * 128), np.float32)
    for e in range(8):
        B8[e, 128 * e:128 * (e + 1)] = 1.0
    c["sel8"] = B8.astype(BFNP)
    return c


class Consts:

    def __init__(self, em, names, consts, defer=()):
        self.handles = {}
        self.dtypes = {}
        self.sb = {}
        self.em = em
        nc = em.nc
        for n in list(names) + list(defer):
            a = consts[n]
            h = nc.inline_tensor(np.ascontiguousarray(a), f"c_{n}")
            dt = BF16 if a.dtype == BFNP else (F32 if a.dtype == np.float32 else I32)
            self.handles[n] = h
            self.dtypes[n] = (dt, list(a.shape))
            if n in defer:
                continue
            t = nc.alloc_sbuf_tensor(nm(f"cs_{n}"), list(a.shape), dt)
            em.dma(em.sy, t[tuple(slice(None) for _ in a.shape)], h.ap()[tuple(slice(None) for _ in a.shape)], "d_const")
            self.sb[n] = t
        em.hard()

    def load_scoped(self, es, names):
        nc = self.em.nc
        for n in names:
            dt, shape = self.dtypes[n]
            t = es.enter_context(nc.sbuf_tensor(nm(f"csx_{n}"), shape, dt))
            self.em.dma(self.em.sy, t[tuple(slice(None) for _ in shape)],
                        self.handles[n].ap()[tuple(slice(None) for _ in shape)],
                        "d_const")
            self.sb[n] = t

    def __getitem__(self, n):
        return self.sb[n]




def emit_phase0(em, C, io, xn_sb=None, adds=(), out_x=None, router=None,
                l2fix=None, save_rows=None, combine_only=False, ct=512):
    nc = em.nc
    x_in = io["x_in"]
    with (
        nc.sbuf_tensor([128, NF * ct], F32) as xch,
        nc.sbuf_tensor([128, NF * ct], F32) as ach,
        nc.sbuf_tensor([1, ct * 8], F32) as rows,
        nc.psum_tensor([1, ct], F32) as ps_sum,
        nc.psum_tensor([1, ct], F32) as ps_sq,
        nc.psum_tensor([128, ct], F32) as ps_b1,
        nc.psum_tensor([128, ct], F32) as ps_b2,
    ):
        sqch = ach
        r_mean = rows[:, 0:ct]
        r_msq = rows[:, ct:2 * ct]
        r_var = rows[:, 2 * ct:3 * ct]
        r_rstd = rows[:, 3 * ct:4 * ct]
        r_nm = rows[:, 4 * ct:5 * ct]
        r_t = rows[:, 5 * ct:6 * ct]
        for c in range(S // ct):
            tsl = slice(c * ct, (c + 1) * ct)
            if "d_xout" in em._sems:
                em.wait(em.sy, "d_xout")
            em.dma(em.sy, xch[:].rearrange("p (f t) -> p f t", f=NF),
                   x_in[:, tsl].rearrange("(f p) t -> p f t", p=128), "d_act")
            em.wait(em.ve, "d_act")
            em.barrier()
            if l2fix is not None:
                l2fix(em, C, c, xch)
            for a_ap in adds:
                em.dma(em.sy, ach[:].rearrange("p (f t) -> p f t", f=NF),
                       a_ap[:, tsl].rearrange("(f p) t -> p f t", p=128), "d_act")
                em.wait(em.ve, "d_act")
                for f in range(NF):
                    em.ve.tensor_tensor(xch[:, f * ct:(f + 1) * ct],
                                        xch[:, f * ct:(f + 1) * ct],
                                        ach[:, f * ct:(f + 1) * ct], ALU.add)
                em.barrier()
            if out_x is not None:
                em.dma(em.sy, out_x[:, tsl].rearrange("(f p) t -> p f t", p=128),
                       xch[:].rearrange("p (f t) -> p f t", f=NF), "d_xout")
            if combine_only:
                em.barrier()
                continue
            for f in range(NF):
                em.se.activation(sqch[:, f * ct:(f + 1) * ct],
                                 xch[:, f * ct:(f + 1) * ct], AF.Square)
            em.barrier()
            mm_kseries(em, ps_sum[:, :],
                       [(C["onescol_f"][:, :], xch[:, f * ct:(f + 1) * ct]) for f in range(NF)])
            mm_kseries(em, ps_sq[:, :],
                       [(C["onescol_f"][:, :], sqch[:, f * ct:(f + 1) * ct]) for f in range(NF)])
            if router is not None:
                router(em, C, c, xch)
            em.barrier()
            em.ve.tensor_scalar(r_mean, ps_sum[:, :], 1.0 / D, None, ALU.mult)
            em.ve.tensor_scalar(r_msq, ps_sq[:, :], 1.0 / D, None, ALU.mult)
            em.barrier()
            em.se.activation(r_t, r_mean, AF.Square)
            em.barrier()
            em.ve.tensor_tensor(r_var, r_msq, r_t, ALU.subtract)
            em.barrier()
            em.ve.tensor_scalar(r_var, r_var, 1e-5, None, ALU.add)
            em.barrier()
            em.se.activation(r_t, r_var, AF.Sqrt)
            em.barrier()
            em.ve.reciprocal(r_rstd, r_t)
            em.barrier()
            em.ve.tensor_tensor(r_nm, r_mean, r_rstd, ALU.mult)
            em.barrier()
            em.ve.tensor_scalar(r_nm, r_nm, -1.0, None, ALU.mult)
            em.barrier()
            if save_rows is not None:
                em.ve.tensor_copy(save_rows["mean"][:, tsl], r_mean)
                em.ve.tensor_copy(save_rows["rstd"][:, tsl], r_rstd)
            em.te.matmul(ps_b1[:, :], C["onesrow_f"][:, :], r_rstd, start=True, stop=True)
            em.te.matmul(ps_b2[:, :], C["onesrow_f"][:, :], r_nm, start=True, stop=True)
            em.barrier()
            for f in range(NF):
                em.ve.tensor_tensor(sqch[:, f * ct:(f + 1) * ct],
                                    xch[:, f * ct:(f + 1) * ct], ps_b1[:, :], ALU.mult)
            em.barrier()
            for f in range(NF):
                em.ve.tensor_tensor(xn_sb[:, f * S + c * ct: f * S + (c + 1) * ct],
                                    sqch[:, f * ct:(f + 1) * ct], ps_b2[:, :], ALU.add)
            em.barrier()
            if router is not None:
                router(em, C, c, xch, post=True)
        em.hard()




def attn_alloc(em, cfg):
    nc = em.nc
    st = {}
    st["bias_sb"] = nc.alloc_sbuf_tensor(nm("at_bias"), [128, 3 * NF + 1], F32)
    Dkv = cfg["kvH"] * cfg["dh"]
    st["bvrow_sb"] = nc.alloc_sbuf_tensor(nm("at_bvrow"), [1, Dkv], BF16)
    return st


def emit_attn_proj(em, C, st, xn_sb, cfg, wd, q_hbm, v_hbm, k_hbm):
    nc = em.nc
    H, dh, kvH = cfg["H"], cfg["dh"], cfg["kvH"]
    Dq = H * dh
    Dkv = kvH * dh
    bias_sb = st["bias_sb"]
    bvrow_sb = st["bvrow_sb"]
    if cfg.get("has_bias", True):
        em.dma(em.sy, bias_sb[:, 0:Dq // 128], wd["bq"][:, :], "d_w")
        em.dma(em.sy, bias_sb[:, NF:NF + Dkv // 128], wd["bk"][:, :], "d_w")
        em.dma(em.gp, bvrow_sb[:, :], wd["bv"][:, :], "d_w")
    em.dma(em.sy, bias_sb[:, 2 * NF:3 * NF], wd["bo"][:, :], "d_w")
    em.hard()
    em.barrier()

    def proj_featmajor(w_ap, ncols, bias_col, dest_cb):
        smm = em.sem("a_mm")
        with ExitStack() as _es:
            wb2 = [_es.enter_context(nc.sbuf_tensor(nm("pjw"), [128, NF * 128], BF16))
                   for _ in range(2)]
            st4 = _es.enter_context(nc.sbuf_tensor(nm("pjst"), [128, 4 * 512], BF16))
            pss = [_es.enter_context(nc.psum_tensor(nm("pjps"), [128, 512], F32))
                   for _ in range(2)]
            psfree = []
            stgd = []

            def body(o, bufs):
                wcb = bufs[0]
                for tq in range(4):
                    gi = o * 4 + tq
                    p = pss[gi % 2]
                    if gi >= 2:
                        em.wait(em.te, "q_se", psfree[gi - 2])
                    last = mm_kseries(em, p[:, :],
                                      [(wcb[:, f * 128:(f + 1) * 128],
                                        xn_sb[:, f * S + tq * 512: f * S + (tq + 1) * 512])
                                       for f in range(NF)])
                    if tq == 3:
                        cs.tag(last)
                        em.wait(em.se, "t_mm")
                    else:
                        last.then_inc(smm, 1)
                        em.bump("a_mm")
                        em.wait(em.se, "a_mm")
                    em.wait(em.se, "d_sc")
                    stg = st4[:, (gi % 4) * 512:(gi % 4 + 1) * 512]
                    if bias_col is not None:
                        em.se.activation(stg, p[:, :], AF.Identity,
                                         bias=bias_col(o))
                    else:
                        em.se.activation(stg, p[:, :], AF.Identity)
                    psfree.append(em.cnt("q_se"))
                    em.wait(em.sy, "q_se")
                    dest_cb(o, tq, stg)
                    stgd.append(em.cnt("d_sc"))

            cs = ColStream(em, [(w_ap, 128, NF, (wb2[0], wb2[1]))], ncols // 128)
            cs.run(body)
            em.hard()

    qb = (lambda o: bias_sb[:, o:o + 1]) if cfg.get("has_bias", True) else None
    proj_featmajor(wd["wq"], Dq, qb,
                   lambda o, tq, stg: em.dma(em.sy, q_hbm[o * 128:(o + 1) * 128, tq * 512:(tq + 1) * 512], stg, "d_sc"))
    kb = (lambda o: bias_sb[:, NF + o:NF + o + 1]) if cfg.get("has_bias", True) else None
    proj_featmajor(wd["wk"], Dkv, kb,
                   lambda o, tq, stg: em.dma(em.sy, k_hbm[o * 128:(o + 1) * 128, tq * 512:(tq + 1) * 512], stg, "d_sc"))

    smmv = em.sem("a_mm")
    with ExitStack() as _es:
        wv2 = [_es.enter_context(nc.sbuf_tensor(nm("pvw"), [128, NF * 512], BF16))
               for _ in range(2)]
        st4 = _es.enter_context(nc.sbuf_tensor(nm("pvst"), [128, 4 * 512], BF16))
        pss = [_es.enter_context(nc.psum_tensor(nm("pvps"), [128, 512], F32))
               for _ in range(2)]
        psfree = []
        stgd = []

        def vbody(oc, bufs):
            wvc = bufs[0]
            osl = slice(oc * 512, (oc + 1) * 512)
            for t in range(16):
                gi = oc * 16 + t
                p = pss[gi % 2]
                if gi >= 2:
                    em.wait(em.te, "q_se", psfree[gi - 2])
                pairs = [(xn_sb[:, f * S + t * 128: f * S + (t + 1) * 128],
                          wvc[:, f * 512:(f + 1) * 512]) for f in range(NF)]
                if cfg.get("has_bias", True):
                    pairs.append((C["onesrow_b"][:, :], bvrow_sb[:, osl]))
                last = mm_kseries(em, p[:, :], pairs)
                if t == 15:
                    cs.tag(last)
                    em.wait(em.se, "t_mm")
                else:
                    last.then_inc(smmv, 1)
                    em.bump("a_mm")
                    em.wait(em.se, "a_mm")
                em.wait(em.se, "d_sc")
                stg = st4[:, (gi % 4) * 512:(gi % 4 + 1) * 512]
                em.se.activation(stg, p[:, :], AF.Identity)
                psfree.append(em.cnt("q_se"))
                em.wait(em.sy, "q_se")
                em.dma(em.sy, v_hbm[t * 128:(t + 1) * 128, osl], stg, "d_sc")
                stgd.append(em.cnt("d_sc"))

        cs = ColStream(em, [(wd["wv"], 512, NF, (wv2[0], wv2[1]))],
                       max(1, Dkv // 512))
        cs.run(vbody)
        em.hard()


def emit_attn_core(em, C, st, cfg, q_hbm, v_hbm, k_hbm, feats_hbm, acc_out, wd):
    nc = em.nc
    H, dh, kvH = cfg["H"], cfg["dh"], cfg["kvH"]
    Dq = H * dh
    causal = cfg["causal"]
    escale = cfg["escale"]
    use_max = cfg["use_max"]
    TQ = 512
    bias_sb = st["bias_sb"]
    NTQ = S // TQ
    smm = em.sem("a_mm")
    with ExitStack() as _es:
        expP = _es.enter_context(nc.sbuf_tensor(nm("expP"), [128, 16 * TQ], BF16))
        vh = _es.enter_context(nc.sbuf_tensor(nm("vh"), [128, 16 * dh], BF16))
        kh = _es.enter_context(nc.sbuf_tensor(nm("kh"), [128, S], BF16))
        qh = _es.enter_context(nc.sbuf_tensor(nm("qh"), [128, TQ], BF16))
        eacc = _es.enter_context(nc.sbuf_tensor(nm("eacc"), [128, 1024], F32))
        esum = _es.enter_context(nc.sbuf_tensor(nm("esum"), [128, TQ], F32))
        rows = _es.enter_context(nc.sbuf_tensor(nm("rows"), [1, 2 * TQ], F32))
        ftmp = _es.enter_context(nc.sbuf_tensor(nm("ftmp"), [128, TQ], F32))
        fbf = _es.enter_context(nc.sbuf_tensor(nm("fbf"), [128, TQ], BF16))
        if use_max:
            sraw = _es.enter_context(nc.sbuf_tensor(nm("sraw"), [128, 16 * TQ], F32))
            mrun = _es.enter_context(nc.sbuf_tensor(nm("mrun"), [128, TQ], F32))
            cm = _es.enter_context(nc.sbuf_tensor(nm("cm"), [128, 4], F32))
        psA = _es.enter_context(nc.psum_tensor(nm("psA"), [128, 1024], F32))
        psB = _es.enter_context(nc.psum_tensor(nm("psB"), [128, 1024], F32))
        ps_f = _es.enter_context(nc.psum_tensor(nm("ps_f"), [128, TQ], F32))
        ps_b = _es.enter_context(nc.psum_tensor(nm("ps_b"), [128, TQ], F32))
        ps_t = None
        if use_max:
            ps_t = _es.enter_context(nc.psum_tensor(nm("ps_t"), [128, 128], F32))
        pss = [psA, psB]
        mrow = rows[:, 0:TQ]
        rrow = rows[:, TQ:2 * TQ]
        for g in range(kvH):
            em.wait(em.gp, "a_mm")
            em.wait(em.sy, "a_mm")
            em.dma(em.gp, vh[:, 0:16 * dh].rearrange("p (f c) -> p f c", f=16),
                   v_hbm[:, g * dh:(g + 1) * dh].rearrange("(f p) c -> p f c", p=128),
                   "d_sc")
            em.dma(em.sy, kh[0:dh, :], k_hbm[g * dh:(g + 1) * dh, :], "d_sc")
            em.hard()
            for h in range(g * (H // kvH), (g + 1) * (H // kvH)):
                for tq in range(NTQ):
                    ntk = min(16, (tq + 1) * (TQ // 128)) if causal else 16
                    nb = ntk // 2
                    em.wait(em.sy, "d_sc")
                    em.dma(em.sy, qh[0:dh, :],
                           q_hbm[h * dh:(h + 1) * dh, tq * TQ:(tq + 1) * TQ], "d_sc")
                    em.wait(em.te, "d_sc")
                    free_at = []
                    for b in range(nb):
                        ps = pss[b % 2]
                        if b >= 2:
                            em.wait(em.te, "q_se", free_at[b - 2])
                        for i in (2 * b, 2 * b + 1):
                            last = em.te.matmul(
                                ps[:, (i % 2) * 512:(i % 2 + 1) * 512],
                                kh[0:dh, i * 128:(i + 1) * 128], qh[0:dh, :],
                                start=True, stop=True)
                        last.then_inc(smm, 1)
                        em.bump("a_mm")
                        em.wait(em.se, "a_mm")
                        if use_max:
                            em.se.activation(sraw[:, 2 * b * 512:(2 * b + 2) * 512],
                                             ps[:, :], AF.Identity)
                        else:
                            em.se.activation(expP[:, 2 * b * 512:(2 * b + 2) * 512],
                                             ps[:, :], AF.Exp, scale=escale)
                        free_at.append(em.cnt("q_se"))
                    if use_max:
                        em.wait(em.ve, "q_se")
                        wv = ntk * 512
                        em.ve.tensor_tensor(mrun[:, :], sraw[:, 0:512],
                                            sraw[:, 512:1024], ALU.max)
                        for i in range(2, ntk):
                            em.ve.tensor_tensor(mrun[:, :], mrun[:, :],
                                                sraw[:, i * 512:(i + 1) * 512],
                                                ALU.max)
                        for j in range(4):
                            em.wait(em.te, "q_ve")
                            em.te.transpose(ps_t[:, :], mrun[:, j * 128:(j + 1) * 128],
                                            C["ident_f32"][:, :]).then_inc(smm, 1)
                            em.bump("a_mm")
                            em.wait(em.ve, "a_mm")
                            em.ve.reduce_max(cm[:, j:j + 1], ps_t[:, :], axis=AX.X)
                        for j in range(4):
                            em.wait(em.te, "q_ve")
                            em.te.transpose(ps_t[0:1, 0:128], cm[:, j:j + 1],
                                            C["ident_f32"][:, :]).then_inc(smm, 1)
                            em.bump("a_mm")
                            em.wait(em.ve, "a_mm")
                            em.ve.tensor_copy(mrow[:, j * 128:(j + 1) * 128],
                                              ps_t[0:1, 0:128])
                        em.wait(em.te, "q_ve")
                        em.te.matmul(ps_b[:, :], C["onesrow_f"][:, :], mrow,
                                     start=True, stop=True).then_inc(smm, 1)
                        em.bump("a_mm")
                        em.wait(em.ve, "a_mm")
                        for i in range(ntk):
                            em.ve.tensor_tensor(sraw[:, i * 512:(i + 1) * 512],
                                                sraw[:, i * 512:(i + 1) * 512],
                                                ps_b[:, :], ALU.subtract)
                        em.wait(em.se, "q_ve")
                        for b in range(nb):
                            em.se.activation(expP[:, 2 * b * 512:(2 * b + 2) * 512],
                                             sraw[:, 2 * b * 512:(2 * b + 2) * 512],
                                             AF.Exp, scale=escale)
                    em.wait(em.ve, "q_se")
                    if causal:
                        dg = 4 * tq * 512
                        em.ve.tensor_tensor(expP[:, dg:dg + 2048],
                                            expP[:, dg:dg + 2048],
                                            C["mask4mul"][:, :], ALU.mult)
                    nblk = ntk // 2
                    em.ve.tensor_tensor(eacc[:, :], expP[:, 0:1024],
                                        expP[:, 1024:2048], ALU.add)
                    for i in range(2, nblk):
                        em.ve.tensor_tensor(eacc[:, :], eacc[:, :],
                                            expP[:, i * 1024:(i + 1) * 1024], ALU.add)
                    em.ve.tensor_tensor(esum[:, :], eacc[:, 0:512], eacc[:, 512:1024],
                                        ALU.add)
                    em.wait(em.te, "q_ve")
                    em.te.matmul(ps_b[0:1, :], C["onescol_f"][:, :], esum[:, :],
                                 start=True, stop=True).then_inc(smm, 1)
                    em.bump("a_mm")
                    em.wait(em.ve, "a_mm")
                    em.ve.reciprocal(rrow, ps_b[0:1, :])
                    em.wait(em.te, "q_ve")
                    em.te.matmul(ps_b[:, :], C["onesrow_f"][:, :], rrow,
                                 start=True, stop=True).then_inc(smm, 1)
                    em.bump("a_mm")
                    em.wait(em.te, "q_se")
                    for i in range(ntk):
                        lastav = em.te.matmul(ps_f[0:dh, :], vh[:, i * dh:(i + 1) * dh],
                                              expP[:, i * TQ:(i + 1) * TQ],
                                              start=(i == 0), stop=(i == ntk - 1))
                    lastav.then_inc(smm, 1)
                    em.bump("a_mm")
                    em.wait(em.se, "a_mm")
                    em.se.activation(ftmp[0:dh, :], ps_f[0:dh, :], AF.Identity)
                    em.wait(em.ve, "q_se")
                    em.wait(em.ve, "d_sc")
                    em.ve.tensor_tensor(fbf[0:dh, :], ftmp[0:dh, :], ps_b[0:dh, :],
                                        ALU.mult)
                    em.wait(em.sy, "q_ve")
                    em.dma(em.sy, feats_hbm[h * dh:(h + 1) * dh,
                                            tq * TQ:(tq + 1) * TQ], fbf[0:dh, :],
                           "d_sc")
        em.hard()

    smm2 = em.sem("a_mm")
    with ExitStack() as _es:
        wo2 = [_es.enter_context(nc.sbuf_tensor(nm("opw"), [128, (Dq // 128) * 128], BF16))
               for _ in range(2)]
        rhs = _es.enter_context(nc.sbuf_tensor(nm("oprhs"), [128, (Dq // 128) * 512], BF16))
        st4 = _es.enter_context(nc.sbuf_tensor(nm("opst"), [128, 4 * 512], F32))
        pss = [_es.enter_context(nc.psum_tensor(nm("opps"), [128, 512], F32))
               for _ in range(2)]
        for tq in range(4):
            em.dma(em.gp, rhs[:].rearrange("p (f c) -> p f c", f=Dq // 128),
                   feats_hbm[:, tq * 512:(tq + 1) * 512].rearrange("(f p) c -> p f c", p=128),
                   "d_sc")
            em.hard()
            psfree = []
            stgd = []

            def obody(o, bufs):
                wcb = bufs[0]
                p = pss[o % 2]
                if o >= 2:
                    em.wait(em.te, "q_se", psfree[o - 2])
                cs.tag(mm_kseries(em, p[:, :],
                                  [(wcb[:, f * 128:(f + 1) * 128],
                                    rhs[:, f * 512:(f + 1) * 512])
                                   for f in range(Dq // 128)]))
                em.wait(em.se, "t_mm")
                em.wait(em.se, "d_sc")
                stg = st4[:, (o % 4) * 512:(o % 4 + 1) * 512]
                em.se.activation(stg, p[:, :], AF.Identity,
                                 bias=bias_sb[:, 2 * NF + o:2 * NF + o + 1])
                psfree.append(em.cnt("q_se"))
                em.wait(em.sy, "q_se")
                em.dma(em.sy, acc_out[o * 128:(o + 1) * 128, tq * 512:(tq + 1) * 512],
                       stg, "d_sc")
                stgd.append(em.cnt("d_sc"))

            cs = ColStream(em, [(wd["wo"], 128, Dq // 128, (wo2[0], wo2[1]))], NF)
            cs.run(obody)
        em.hard()





def emit_ffn_swiglu(em, C, xn_sb, cfg, wd, acc_out):
    nc = em.nc
    NI = cfg["NI"]
    NO = NI // 128
    has_s = cfg.get("scales", False)
    scal_sb = cfg.get("scal_sb")
    if has_s:
        em.dma(em.sy, scal_sb[:, 0:NO], wd["gate_s"][:, :], "d_w")
        em.dma(em.sy, scal_sb[:, NO:2 * NO], wd["up_s"][:, :], "d_w")
        em.dma(em.sy, scal_sb[:, 2 * NO:2 * NO + NF], wd["down_s"][:, :], "d_w")
        em.hard()
    with ExitStack() as _es:
        wg = [_es.enter_context(nc.sbuf_tensor(nm("wg"), [128, NF * 128], BF16))
              for _ in range(2)]
        wu = [_es.enter_context(nc.sbuf_tensor(nm("wu"), [128, NF * 128], BF16))
              for _ in range(2)]
        dcb = [_es.enter_context(nc.sbuf_tensor(nm("dcb"), [128, NO * 128], BF16))
               for _ in range(2)]
        gsb = _es.enter_context(nc.sbuf_tensor(nm("gsb"), [128, 2 * 512], BF16))
        usb = _es.enter_context(nc.sbuf_tensor(nm("usb"), [128, 2 * 512], BF16))
        hch = _es.enter_context(nc.sbuf_tensor(nm("hch"), [128, NO * 512], BF16))
        stg = _es.enter_context(nc.sbuf_tensor(nm("stg"), [128, 2 * 512], F32))
        psg = [_es.enter_context(nc.psum_tensor(nm("psg"), [128, 512], F32))
               for _ in range(2)]
        psu = [_es.enter_context(nc.psum_tensor(nm("psu"), [128, 512], F32))
               for _ in range(2)]
        for tq in range(4):
            tsl = slice(tq * 512, (tq + 1) * 512)
            psfree = []
            mufree = []

            def body(o, bufs):
                wgb, wub = bufs
                if o >= 2:
                    em.wait(em.te, "q_se", psfree[o - 2])
                mm_kseries(em, psg[o % 2][:, :],
                           [(wgb[:, f * 128:(f + 1) * 128],
                             xn_sb[:, f * S + tq * 512: f * S + (tq + 1) * 512])
                            for f in range(NF)])
                cs.tag(mm_kseries(em, psu[o % 2][:, :],
                                  [(wub[:, f * 128:(f + 1) * 128],
                                    xn_sb[:, f * S + tq * 512: f * S + (tq + 1) * 512])
                                   for f in range(NF)]))
                em.wait(em.se, "t_mm")
                if o >= 2:
                    em.wait(em.se, "q_ve", mufree[o - 2])
                gs = gsb[:, (o % 2) * 512:(o % 2 + 1) * 512]
                us = usb[:, (o % 2) * 512:(o % 2 + 1) * 512]
                if has_s:
                    em.se.activation(gs, psg[o % 2][:, :], AF.Silu,
                                     scale=scal_sb[:, o:o + 1])
                    em.se.activation(us, psu[o % 2][:, :], AF.Identity,
                                     scale=scal_sb[:, NO + o:NO + o + 1])
                else:
                    em.se.activation(gs, psg[o % 2][:, :], AF.Silu)
                    em.se.activation(us, psu[o % 2][:, :], AF.Identity)
                psfree.append(em.cnt("q_se"))
                em.wait(em.ve, "q_se")
                em.ve.tensor_tensor(hch[:, o * 512:(o + 1) * 512], gs, us, ALU.mult)
                mufree.append(em.cnt("q_ve"))

            cs = ColStream(em, [(wd["gate"], 128, NF, (wg[0], wg[1])),
                                (wd["up"], 128, NF, (wu[0], wu[1]))], NO)
            cs.run(body)
            dfree = []
            stgd = []

            def dbody(o2, bufs):
                dcb_ = bufs[0]
                if o2 >= 2:
                    em.wait(em.te, "q_se", dfree[o2 - 2])
                cs2.tag(mm_kseries(em, psg[o2 % 2][:, :],
                                   [(dcb_[:, f * 128:(f + 1) * 128],
                                     hch[:, f * 512:(f + 1) * 512])
                                    for f in range(NO)]))
                em.wait(em.se, "t_mm")
                em.wait(em.se, "d_sc")
                sg = stg[:, (o2 % 2) * 512:(o2 % 2 + 1) * 512]
                if has_s:
                    em.se.activation(sg, psg[o2 % 2][:, :], AF.Identity,
                                     scale=scal_sb[:, 2 * NO + o2:2 * NO + o2 + 1])
                else:
                    em.se.activation(sg, psg[o2 % 2][:, :], AF.Identity)
                dfree.append(em.cnt("q_se"))
                em.wait(em.sy, "q_se")
                em.dma(em.sy, acc_out[o2 * 128:(o2 + 1) * 128, tsl], sg, "d_sc")
                stgd.append(em.cnt("d_sc"))

            cs2 = ColStream(em, [(wd["down"], 128, NO, (dcb[0], dcb[1]))], NF)
            cs2.run(dbody)
        em.hard()




def _io_tensors(nc, names_shapes_dtypes):
    out = {}
    for name, shape, dt, kind in names_shapes_dtypes:
        out[name] = nc.dram_tensor(name, list(shape), dt, kind=kind).ap()
    return out


def build_prep(consts):
    nc = bass.Bass("TRN2", target_bir_lowering=False)
    em = Em(nc)
    specs = [
        ("lut", (1, 16), F32, "ExternalInput"),
        ("gidxT", (256, 5504), I32, "ExternalInput"),
        ("uidxT", (256, 5504), I32, "ExternalInput"),
        ("didxT", (688, 2048), I32, "ExternalInput"),
        ("gw_bf", (256, 5504), BF16, "ExternalOutput"),
        ("uw_bf", (256, 5504), BF16, "ExternalOutput"),
        ("dw_bf", (688, 2048), BF16, "ExternalOutput"),
    ]
    casts = [
        ("l1g", (256, 5632)), ("l1u", (256, 5632)), ("l1d", (704, 2048)),
        ("l2w1", (256, 8192)), ("l2w2", (1024, 2048)),
        ("l3w1", (256, 5632)), ("l3w3", (256, 5632)), ("l3w2", (704, 2048)),
    ]
    for n, shp in casts:
        specs.append((n, shp, F32, "ExternalInput"))
        specs.append((n + "_bf", shp, BF16, "ExternalOutput"))
    io = _io_tensors(nc, specs)
    C = Consts(em, ["onesrow_f"], consts)
    lut_sb = nc.alloc_sbuf_tensor([1, 16], F32)
    lutb_sb = nc.alloc_sbuf_tensor([128, 16], F32)
    em.dma(em.sy, lut_sb[:, :], io["lut"][:, :], "d_w")
    em.hard()
    with nc.psum_tensor([128, 16], F32) as psl:
        em.te.matmul(psl[:, :], C["onesrow_f"][:, :], lut_sb[:, :], start=True, stop=True)
        em.barrier()
        em.ve.tensor_copy(lutb_sb[:, :], psl[:, :])
        em.barrier()
    with (
        nc.sbuf_tensor([128, 5504], I32) as idxi,
        nc.sbuf_tensor([128, 5504], BF16) as idxb,
        nc.sbuf_tensor([128, 5504], BF16) as tmp,
        nc.sbuf_tensor([128, 5504], BF16) as acc,
    ):
        for mat, cols, rows in (("gidxT", 5504, 256), ("uidxT", 5504, 256),
                                ("didxT", 2048, 688)):
            for r0 in range(0, rows, 128):
                rr = min(128, rows - r0)
                em.dma(em.sy, idxi[0:rr, 0:cols], io[mat][r0:r0 + rr, :], "d_w")
                em.hard()
                em.ve.tensor_copy(idxb[0:rr, 0:cols], idxi[0:rr, 0:cols])
                em.barrier()
                for v in range(16):
                    em.ve.tensor_scalar(tmp[0:rr, 0:cols], idxb[0:rr, 0:cols], float(v),
                                        lutb_sb[0:rr, v:v + 1], ALU.is_equal, ALU.mult)
                    em.barrier()
                    if v == 0:
                        em.ve.tensor_copy(acc[0:rr, 0:cols], tmp[0:rr, 0:cols])
                    else:
                        em.ve.tensor_tensor(acc[0:rr, 0:cols], acc[0:rr, 0:cols],
                                            tmp[0:rr, 0:cols], ALU.add)
                    em.barrier()
                out = {"gidxT": "gw_bf", "uidxT": "uw_bf", "didxT": "dw_bf"}[mat]
                em.dma(em.sy, io[out][r0:r0 + rr, :], acc[0:rr, 0:cols], "d_sc")
                em.hard()
    with nc.sbuf_tensor([128, 8192], BF16) as cst:
        for n, shp in casts:
            rows, cols = shp
            for r0 in range(0, rows, 128):
                rr = min(128, rows - r0)
                em.dma(em.gp, cst[0:rr, 0:cols], io[n][r0:r0 + rr, :], "d_w")
                em.hard()
                em.dma(em.sy, io[n + "_bf"][r0:r0 + rr, :], cst[0:rr, 0:cols], "d_sc")
                em.hard()
    em.hard()
    return nc


ATTN_CFG = {
    0: dict(H=32, dh=64, kvH=32, causal=False, escale=1.0 / math.sqrt(32.0),
            has_bias=True, Dq=2048, Dkv=2048, use_max=False),
    1: dict(H=16, dh=128, kvH=16, causal=True, escale=1.0 / math.sqrt(128.0),
            has_bias=True, Dq=2048, Dkv=2048, use_max=False),
    3: dict(H=16, dh=128, kvH=4, causal=True, escale=1.0,
            has_bias=False, Dq=2048, Dkv=512, use_max=True),
}


def build_layer01(layer, consts, debug=False):
    nc = bass.Bass("TRN2", target_bir_lowering=False)
    em = Em(nc)
    NI = 5504 if layer == 0 else 5632
    dk = "ExternalOutput" if debug else "Internal"
    specs = [
        ("xin", (D, S), F32, "ExternalInput"),
        ("xout", (D, S), F32, "ExternalOutput"),
        ("wq", (D, D), F32, "ExternalInput"),
        ("wk", (D, D), F32, "ExternalInput"),
        ("wv", (D, D), F32, "ExternalInput"),
        ("wo", (D, D), F32, "ExternalInput"),
        ("bq", (128, NF), F32, "ExternalInput"),
        ("bk", (128, NF), F32, "ExternalInput"),
        ("bv", (1, D), F32, "ExternalInput"),
        ("bo", (128, NF), F32, "ExternalInput"),
        ("gate", (D, NI), BF16, "ExternalInput"),
        ("up", (D, NI), BF16, "ExternalInput"),
        ("down", (NI, D), BF16, "ExternalInput"),
        ("accA", (D, S), F32, dk),
        ("accF", (D, S), F32, dk),
        ("q_hbm", (D, S), BF16, dk),
        ("k_hbm", (D, S), BF16, dk),
        ("v_hbm", (S, D), BF16, dk),
        ("f_hbm", (D, S), BF16, dk),
    ]
    if layer == 1:
        specs += [("gate_s", (128, NI // 128), F32, "ExternalInput"),
                  ("up_s", (128, NI // 128), F32, "ExternalInput"),
                  ("down_s", (128, NF), F32, "ExternalInput")]
    io = _io_tensors(nc, specs)
    cfg = ATTN_CFG[layer]
    mask_names = ["mask4mul"] if cfg["causal"] else []
    C = Consts(em, ["onescol_f", "onesrow_f", "onesrow_b", "ident_f32"] + mask_names,
               consts)
    ffn_cfg = dict(NI=NI, scales=(layer == 1))
    if layer == 1:
        ffn_cfg["scal_sb"] = nc.alloc_sbuf_tensor(nm("ff_scal"),
                                                  [128, 2 * (NI // 128) + NF], F32)
    wd_attn = dict(wq=io["wq"], wk=io["wk"], wv=io["wv"], wo=io["wo"],
                   bq=io["bq"], bk=io["bk"], bv=io["bv"], bo=io["bo"])
    wd_ffn = dict(gate=io["gate"], up=io["up"], down=io["down"])
    if layer == 1:
        wd_ffn.update(gate_s=io["gate_s"], up_s=io["up_s"], down_s=io["down_s"])

    ast = attn_alloc(em, cfg)
    with nc.sbuf_tensor([128, NF * S], BF16) as xn_sb:
        emit_phase0(em, C, {"x_in": io["xin"]}, xn_sb=xn_sb)
        emit_ffn_swiglu(em, C, xn_sb, ffn_cfg, wd_ffn, io["accF"])
        emit_attn_proj(em, C, ast, xn_sb, cfg, wd_attn, io["q_hbm"], io["v_hbm"],
                       io["k_hbm"])
    emit_attn_core(em, C, ast, cfg, io["q_hbm"], io["v_hbm"], io["k_hbm"],
                   io["f_hbm"], io["accA"], wd_attn)
    emit_phase0(em, C, {"x_in": io["xin"]}, adds=(io["accA"], io["accF"]),
                out_x=io["xout"], combine_only=True)
    em.hard()
    return nc



_CONSTS = None


def _consts():
    global _CONSTS
    if _CONSTS is None:
        _CONSTS = build_consts()
    return _CONSTS


def _bf(a):
    return np.ascontiguousarray(np.asarray(a)).astype(BFNP)


def run_prep(inp):
    consts = _consts()
    nc = build_prep(consts)
    g_idxT = _t(inp["L0_ffn_gate_idx"])
    u_idxT = _t(inp["L0_ffn_up_idx"])
    d_idxT = _t(inp["L0_ffn_down_idx"])
    casts_full = {
        "l1g": _t(inp["L1_ffn_gate_w"]), "l1u": _t(inp["L1_ffn_up_w"]),
        "l1d": _t(inp["L1_ffn_down_w"]),
        "l2w1": np.ascontiguousarray(inp["L2_ffn_W1"]),
        "l2w2": np.ascontiguousarray(inp["L2_ffn_W2"]),
        "l3w1": _t(inp["L3_ffn_w1"]), "l3w3": _t(inp["L3_ffn_w3"]),
        "l3w2": _t(inp["L3_ffn_w2"]),
    }
    in_maps = []
    for c in range(8):
        m = {"lut": _row(inp["L0_ffn_lut"]).astype(np.float32),
             "gidxT": np.ascontiguousarray(g_idxT[c * 256:(c + 1) * 256]),
             "uidxT": np.ascontiguousarray(u_idxT[c * 256:(c + 1) * 256]),
             "didxT": np.ascontiguousarray(d_idxT[c * 688:(c + 1) * 688])}
        for n, a in casts_full.items():
            rows = a.shape[0] // 8
            m[n] = np.ascontiguousarray(a[c * rows:(c + 1) * rows]).astype(np.float32)
        in_maps.append(m)
    res = run_bass_kernel_spmd(nc, in_maps, core_ids=list(range(8)))
    out = {}
    for n in ["gw_bf", "uw_bf", "dw_bf"] + [k + "_bf" for k in casts_full]:
        out[n] = np.concatenate([res.results[c][n] for c in range(8)], axis=0)
    return out


def layer01_inmap(layer, inp, prep, xT_c):
    L = f"L{layer}_attn_"
    if layer == 0:
        m = dict(wq=inp[L + "Wq"], wk=inp[L + "Wk"], wv=inp[L + "Wv"],
                 wo=inp[L + "Wo"])
        m["gate"], m["up"], m["down"] = prep["gw_bf"], prep["uw_bf"], prep["dw_bf"]
    else:
        m = dict(wq=_t(inp[L + "Wq"]), wk=_t(inp[L + "Wk"]), wv=_t(inp[L + "Wv"]),
                 wo=_t(inp[L + "Wo"]))
        m["gate"], m["up"], m["down"] = prep["l1g_bf"], prep["l1u_bf"], prep["l1d_bf"]
        m["gate_s"] = _pcol(inp["L1_ffn_gate_s"]).astype(np.float32)
        m["up_s"] = _pcol(inp["L1_ffn_up_s"]).astype(np.float32)
        m["down_s"] = _pcol(inp["L1_ffn_down_s"]).astype(np.float32)
    m["bq"] = _pcol(inp[L + "bq"]).astype(np.float32)
    m["bk"] = _pcol(inp[L + "bk"]).astype(np.float32)
    m["bv"] = _row(inp[L + "bv"]).astype(np.float32)
    m["bo"] = _pcol(inp[L + "bo"]).astype(np.float32)
    m = {k: (np.ascontiguousarray(v) if v.dtype != np.float64 else
             np.ascontiguousarray(v).astype(np.float32)) for k, v in m.items()}
    m["xin"] = xT_c
    return m


def run_layer01(layer, inp, prep, xT, debug=False):
    nc = build_layer01(layer, _consts(), debug=debug)
    in_maps = [layer01_inmap(layer, inp, prep, xT[c]) for c in range(8)]
    res = run_bass_kernel_spmd(nc, in_maps, core_ids=list(range(8)))
    if debug:
        return res.results
    return [res.results[c]["xout"] for c in range(8)]


def kernel(**inputs):
    inp = {k: np.asarray(v) for k, v in inputs.items()}
    prep = run_prep(inp)
    x = inp["L0_attn_x"]
    xT = [np.ascontiguousarray(x[c].T).astype(np.float32) for c in range(8)]
    xT = run_layer01(0, inp, prep, xT)
    xT = run_layer01(1, inp, prep, xT)
    xT = run_layer2(inp, prep, xT)
    xT = run_layer3(inp, prep, xT)
    out = np.stack([np.ascontiguousarray(np.asarray(t).T) for t in xT])
    return out.astype(np.float32)




def l3_router_hook(em, C, st, io):
    def router(em, C, c, xch, post=False):
        nc = em.nc
        if not post:
            mm_kseries(em, st["ps_l"][:, :],
                       [(st["gwT_sb"][:, f * 8:(f + 1) * 8],
                         xch[:, f * 512:(f + 1) * 512]) for f in range(NF)])
            return
        r_mean = st["r_mean"]
        r_rstd = st["r_rstd"]
        em.te.matmul(st["ps_b8"][0:8, :], C["onesrow_f"][0:1, 0:8], r_mean,
                     start=True, stop=True)
        em.barrier()
        em.ve.tensor_scalar(st["t8"][0:8, :], st["ps_b8"][0:8, :],
                            st["gsum_sb"][0:8, 0:1], None, ALU.mult)
        em.barrier()
        em.ve.tensor_tensor(st["t8b"][0:8, :], st["ps_l"][0:8, :], st["t8"][0:8, :],
                            ALU.subtract)
        em.barrier()
        em.te.matmul(st["ps_b8"][0:8, :], C["onesrow_f"][0:1, 0:8], r_rstd,
                     start=True, stop=True)
        em.barrier()
        em.ve.tensor_tensor(st["logits_sb"][0:8, c * 512:(c + 1) * 512],
                            st["t8b"][0:8, :], st["ps_b8"][0:8, :], ALU.mult)
        em.barrier()
    return router


def emit_l3_top2(em, C, st):
    nc = em.nc
    with ExitStack() as _es:
        lgT = _es.enter_context(nc.sbuf_tensor(nm("lgT"), [128, 8], F32))
        msk = _es.enter_context(nc.sbuf_tensor(nm("msk"), [128, 8], F32))
        wfT = _es.enter_context(nc.sbuf_tensor(nm("wfT"), [128, 8], F32))
        aa = _es.enter_context(nc.sbuf_tensor(nm("aa"), [128, 8], F32))
        m1 = _es.enter_context(nc.sbuf_tensor(nm("m1"), [128, 8], F32))
        ps_t = _es.enter_context(nc.psum_tensor(nm("ps_t"), [128, 128], F32))
        for t in range(16):
            em.te.transpose(ps_t[0:128, 0:8],
                            st["logits_sb"][0:8, t * 128:(t + 1) * 128],
                            C["ident_f32"][0:8, 0:8])
            em.barrier()
            em.ve.tensor_copy(lgT[:, 0:8], ps_t[0:128, 0:8])
            em.barrier()
            em.ve.reduce_max(m1[:, 0:1], lgT[:, 0:8], axis=AX.X)
            em.barrier()
            em.ve.tensor_scalar(msk[:, 0:8], lgT[:, 0:8], m1[:, 0:1], None,
                                ALU.is_equal)
            em.barrier()
            em.ve.tensor_scalar(aa[:, 0:8], msk[:, 0:8], 2.0e30, None, ALU.mult)
            em.barrier()
            em.ve.tensor_tensor(aa[:, 0:8], lgT[:, 0:8], aa[:, 0:8], ALU.subtract)
            em.barrier()
            em.ve.reduce_max(m1[:, 3:4], aa[:, 0:8], axis=AX.X)
            em.barrier()
            em.ve.tensor_scalar(m1[:, 4:5], m1[:, 0:1], -1.0, None, ALU.mult)
            em.barrier()
            em.se.activation(m1[:, 5:6], m1[:, 3:4], AF.Exp, bias=m1[:, 4:5])
            em.barrier()
            em.ve.tensor_scalar(m1[:, 6:7], m1[:, 5:6], 1.0, None, ALU.add)
            em.barrier()
            em.ve.reciprocal(m1[:, 6:7], m1[:, 6:7])
            em.barrier()
            em.ve.tensor_tensor(m1[:, 7:8], m1[:, 5:6], m1[:, 6:7], ALU.mult)
            em.barrier()
            em.ve.tensor_scalar(wfT[:, 0:8], msk[:, 0:8], m1[:, 6:7], None,
                                ALU.mult)
            em.barrier()
            em.ve.tensor_scalar(msk[:, 0:8], aa[:, 0:8], m1[:, 3:4], None,
                                ALU.is_equal, )
            em.barrier()
            em.ve.tensor_scalar(msk[:, 0:8], msk[:, 0:8], m1[:, 7:8], None,
                                ALU.mult)
            em.barrier()
            em.ve.tensor_tensor(wfT[:, 0:8], wfT[:, 0:8], msk[:, 0:8], ALU.add)
            em.barrier()
            em.te.transpose(ps_t[0:8, 0:128], wfT[:, 0:8], C["ident_f32"][:, :])
            em.barrier()
            em.ve.tensor_copy(st["wfull_bf"][0:8, t * 128:(t + 1) * 128],
                              ps_t[0:8, 0:128])
            em.barrier()


def emit_ffn_l3(em, C, st, xn_sb, io, acc_out):
    nc = em.nc
    NO = 44
    with ExitStack() as _es:
        a1T = _es.enter_context(nc.sbuf_tensor(nm("a1T"), [128, NF * 128], BF16))
        a3T = _es.enter_context(nc.sbuf_tensor(nm("a3T"), [128, NF * 128], BF16))
        ust = _es.enter_context(nc.sbuf_tensor(nm("ust"), [16, 512], BF16))
        ps_u = _es.enter_context(nc.psum_tensor(nm("ps_u"), [16, 512], F32))
        em.dma(em.gp, a1T[:].rearrange("p (f c) -> p f c", f=NF),
               io["a1T"].rearrange("(f p) c -> p f c", p=128), "d_w")
        em.dma(em.gp, a3T[:].rearrange("p (f c) -> p f c", f=NF),
               io["a3T"].rearrange("(f p) c -> p f c", p=128), "d_w")
        em.hard()
        for e in range(8):
            for src, dst in ((a1T, "u1_hbm"), (a3T, "u3_hbm")):
                for c in range(4):
                    mm_kseries(em, ps_u[0:16, :],
                               [(src[:, f * 128 + 16 * e: f * 128 + 16 * e + 16],
                                 xn_sb[:, f * S + c * 512: f * S + (c + 1) * 512])
                                for f in range(NF)])
                    em.barrier()
                    em.se.activation(ust[0:16, :], ps_u[0:16, :], AF.Identity,
                                     scale=2.0)
                    em.barrier()
                    em.dma(em.sy, io[dst][e * 16:(e + 1) * 16, c * 512:(c + 1) * 512],
                           ust[0:16, :], "d_sc")
                    em.hard()
    with ExitStack() as _es:
        w1c = _es.enter_context(nc.sbuf_tensor(nm("w1c"), [128, NF * 128], BF16))
        w3c = _es.enter_context(nc.sbuf_tensor(nm("w3c"), [128, NF * 128], BF16))
        stg = _es.enter_context(nc.sbuf_tensor(nm("stg"), [128, 512], BF16))
        ps = _es.enter_context(nc.psum_tensor(nm("ps"), [128, 512], F32))
        for f in range(NO):
            em.dma(em.gp, w1c[:].rearrange("p (k c) -> p k c", k=NF),
                   io["w1T"][:, f * 128:(f + 1) * 128].rearrange("(k p) c -> p k c", p=128),
                   "d_w")
            em.dma(em.gp, w3c[:].rearrange("p (k c) -> p k c", k=NF),
                   io["w3T"][:, f * 128:(f + 1) * 128].rearrange("(k p) c -> p k c", p=128),
                   "d_w")
            em.hard()
            for src, dst in ((w1c, "base1_hbm"), (w3c, "base3_hbm")):
                for c in range(4):
                    mm_kseries(em, ps[:, :],
                               [(src[:, k * 128:(k + 1) * 128],
                                 xn_sb[:, k * S + c * 512: k * S + (c + 1) * 512])
                                for k in range(NF)])
                    em.barrier()
                    em.se.activation(stg[:, :], ps[:, :], AF.Identity)
                    em.barrier()
                    em.dma(em.sy, io[dst][f * 128:(f + 1) * 128, c * 512:(c + 1) * 512],
                           stg[:, :], "d_sc")
                    em.hard()
    smm = em.sem("f3_mm")
    with ExitStack() as _es:
        b1e = _es.enter_context(nc.sbuf_tensor(nm("b1e"), [16, 5632], BF16))
        b3e = _es.enter_context(nc.sbuf_tensor(nm("b3e"), [16, 5632], BF16))
        u1e = _es.enter_context(nc.sbuf_tensor(nm("u1e"), [16, S], BF16))
        u3e = _es.enter_context(nc.sbuf_tensor(nm("u3e"), [16, S], BF16))
        a2T = _es.enter_context(nc.sbuf_tensor(nm("a2T"), [128, NO * 128], BF16))
        base1 = _es.enter_context(nc.sbuf_tensor(nm("base1"), [128, S], BF16))
        base3 = _es.enter_context(nc.sbuf_tensor(nm("base3"), [128, S], BF16))
        web = _es.enter_context(nc.sbuf_tensor(nm("web"), [128, S], F32))
        w16 = _es.enter_context(nc.sbuf_tensor(nm("w16"), [16, S], F32))
        t1 = _es.enter_context(nc.sbuf_tensor(nm("t1"), [128, 2 * 512], F32))
        s1 = _es.enter_context(nc.sbuf_tensor(nm("s1"), [128, 2 * 512], BF16))
        t3 = _es.enter_context(nc.sbuf_tensor(nm("t3"), [128, 2 * 512], BF16))
        actb = _es.enter_context(nc.sbuf_tensor(nm("actb"), [128, 2 * 512], BF16))
        actw = _es.enter_context(nc.sbuf_tensor(nm("actw"), [128, S], BF16))
        u2st = _es.enter_context(nc.sbuf_tensor(nm("u2st"), [16, 512], BF16))
        ps_d1 = _es.enter_context(nc.psum_tensor(nm("ps_d1"), [128, 2 * 512], F32))
        ps_d3 = _es.enter_context(nc.psum_tensor(nm("ps_d3"), [128, 2 * 512], F32))
        ps_u2 = [
            _es.enter_context(nc.psum_tensor(nm("ps_u2"), [16, 512], F32))
            for _ in range(4)]
        ps_w = ps_d1
        em.dma(em.gp, a2T[:].rearrange("p (f c) -> p f c", f=NO),
               io["a2T"].rearrange("(f p) c -> p f c", p=128), "d_w")
        em.hard()
        for e in range(8):
            em.dma(em.gp, b1e[0:16, :], io["b1T"][e, :, :], "d_w")
            em.dma(em.gp, b3e[0:16, :], io["b3T"][e, :, :], "d_w")
            em.dma(em.sy, u1e[0:16, :], io["u1_hbm"][e * 16:(e + 1) * 16, :], "d_w")
            em.dma(em.sy, u3e[0:16, :], io["u3_hbm"][e * 16:(e + 1) * 16, :], "d_w")
            em.hard()
            for c in range(4):
                em.te.matmul(ps_w[:, 0:512], st["sel8_sb"][:, e * 128:(e + 1) * 128],
                             st["wfull_bf"][0:8, c * 512:(c + 1) * 512],
                             start=True, stop=True)
                em.barrier()
                em.ve.tensor_copy(web[:, c * 512:(c + 1) * 512], ps_w[:, 0:512])
                em.barrier()
                em.te.matmul(ps_w[0:16, 0:512],
                             st["sel8_sb"][:, e * 128:e * 128 + 16],
                             st["wfull_bf"][0:8, c * 512:(c + 1) * 512],
                             start=True, stop=True)
                em.barrier()
                em.ve.tensor_copy(w16[0:16, c * 512:(c + 1) * 512], ps_w[0:16, 0:512])
                em.barrier()
            gpdone = []
            dmm = []
            vadd = []
            u2mm = []
            for f in range(NO):
                em.wait(em.sy, "d_act")
                em.sy.wait_ge(em.sem("q_ve"), em.cnt("q_ve"))
                em.dma(em.sy, base1[:].rearrange("p (k c) -> p k c", k=4),
                       io["base1_hbm"][f * 128:(f + 1) * 128, :].rearrange("p (k c) -> p k c", k=4),
                       "d_act")
                em.dma(em.sy, base3[:].rearrange("p (k c) -> p k c", k=4),
                       io["base3_hbm"][f * 128:(f + 1) * 128, :].rearrange("p (k c) -> p k c", k=4),
                       "d_act")
                for eng in (em.te, em.ve, em.gp):
                    em.wait(eng, "d_act")
                for c in range(4):
                    gi = f * 4 + c
                    csl = slice(c * 512, (c + 1) * 512)
                    bsl = slice((c % 2) * 512, (c % 2 + 1) * 512)
                    if gi >= 2:
                        em.wait(em.te, "q_ve", vadd[gi - 2])
                    em.te.matmul(ps_d1[:, bsl], b1e[0:16, f * 128:(f + 1) * 128],
                                 u1e[0:16, csl], start=True, stop=True)
                    em.te.matmul(ps_d3[:, bsl], b3e[0:16, f * 128:(f + 1) * 128],
                                 u3e[0:16, csl], start=True,
                                 stop=True).then_inc(smm, 1)
                    em.bump("f3_mm")
                    dmm.append(em.cnt("f3_mm"))
                    tb = t1[:, bsl]
                    sb1 = s1[:, bsl]
                    tb3 = t3[:, bsl]
                    ab = actb[:, bsl]
                    if gi >= 2:
                        em.wait(em.ve, "q_gp", gpdone[gi - 2])
                    em.wait(em.ve, "f3_mm", dmm[gi])
                    em.ve.tensor_tensor(tb, ps_d1[:, bsl], base1[:, csl], ALU.add)
                    em.ve.tensor_tensor(tb3, ps_d3[:, bsl], base3[:, csl], ALU.add)
                    vadd.append(em.cnt("q_ve"))
                    em.wait(em.se, "q_ve", vadd[gi])
                    em.se.activation(sb1, tb, AF.Silu)
                    sse_c = em.cnt("q_se")
                    em.wait(em.ve, "q_se", sse_c)
                    em.ve.tensor_tensor(ab, sb1, tb3, ALU.mult)
                    vm_c = em.cnt("q_ve")
                    if gi >= 4:
                        em.wait(em.gp, "f3_mm", u2mm[gi - 4])
                    em.wait(em.gp, "q_ve", vm_c)
                    em.gp.tensor_tensor(actw[:, csl], ab, web[:, csl], ALU.mult)
                    gpdone.append(em.cnt("q_gp"))
                    em.wait(em.te, "q_gp", gpdone[gi])
                    em.te.matmul(ps_u2[c][0:16, :],
                                 a2T[:, f * 128 + 16 * e: f * 128 + 16 * e + 16],
                                 actw[:, csl], start=(f == 0),
                                 stop=(f == NO - 1)).then_inc(smm, 1)
                    em.bump("f3_mm")
                    u2mm.append(em.cnt("f3_mm"))
                em.wait(em.gp, "q_gp")
                em.gp.wait_ge(em.sem("t_mm" if False else "f3_mm"), u2mm[-1])
                if e == 0:
                    em.dma(em.gp, io["ACT_hbm"][f * 128:(f + 1) * 128, :],
                           actw[:, :], "d_act")
                else:
                    h = em.sem("d_act_g")
                    em.gp._eng.dma_start(io["ACT_hbm"][f * 128:(f + 1) * 128, :],
                                         actw[:, :], accum_op=ALU.add).then_inc(h, 16)
                    em.bump("d_act_g", 16)
            em.barrier()
            em.barrier()
            for c in range(4):
                em.ve.scalar_tensor_tensor(u2st[0:16, :], ps_u2[c][0:16, :], 2.0,
                                           w16[0:16, c * 512:(c + 1) * 512],
                                           ALU.mult, ALU.mult)
                em.barrier()
                em.dma(em.sy, io["u2w_hbm"][e * 16:(e + 1) * 16,
                                            c * 512:(c + 1) * 512], u2st[0:16, :],
                       "d_sc")
                em.hard()
    with ExitStack() as _es:
        rhs = _es.enter_context(nc.sbuf_tensor(nm("rhs"), [128, NO * 512], BF16))
        dc = _es.enter_context(nc.sbuf_tensor(nm("dc"), [128, NO * 128], BF16))
        u2w = _es.enter_context(nc.sbuf_tensor(nm("u2w"), [16, 8 * 512], BF16))
        b2o = _es.enter_context(nc.sbuf_tensor(nm("b2o"), [16, 8 * 128], BF16))
        stg = _es.enter_context(nc.sbuf_tensor(nm("stg"), [128, 512], F32))
        ps = _es.enter_context(nc.psum_tensor(nm("ps"), [128, 512], F32))
        for c in range(4):
            em.dma(em.gp, rhs[:].rearrange("p (f c) -> p f c", f=NO),
                   io["ACT_hbm"][:, c * 512:(c + 1) * 512].rearrange("(f p) c -> p f c", p=128),
                   "d_w")
            em.dma(em.sy, u2w[0:16, :].rearrange("p (e c) -> p e c", e=8),
                   io["u2w_hbm"][:, c * 512:(c + 1) * 512].rearrange("(e p) c -> p e c", p=16),
                   "d_w")
            em.hard()
            for o in range(NF):
                em.dma(em.gp, dc[:].rearrange("p (f c) -> p f c", f=NO),
                       io["w2T"][:, o * 128:(o + 1) * 128].rearrange("(f p) c -> p f c", p=128),
                       "d_w")
                em.dma(em.gp, b2o[0:16, :].rearrange("p (e c) -> p e c", e=8),
                       io["b2T"].rearrange("r (e n) -> r e n", e=8)[:, :, o * 128:(o + 1) * 128],
                       "d_w")
                em.hard()
                pairs = [(dc[:, f * 128:(f + 1) * 128], rhs[:, f * 512:(f + 1) * 512])
                         for f in range(NO)]
                pairs += [(b2o[0:16, e * 128:(e + 1) * 128],
                           u2w[0:16, e * 512:(e + 1) * 512]) for e in range(8)]
                mm_kseries(em, ps[:, :], pairs)
                em.barrier()
                em.se.activation(stg[:, :], ps[:, :], AF.Identity)
                em.barrier()
                em.dma(em.sy, acc_out[o * 128:(o + 1) * 128, c * 512:(c + 1) * 512],
                       stg[:, :], "d_sc")
                em.hard()


def build_layer3(consts, debug=False):
    nc = bass.Bass("TRN2", target_bir_lowering=False)
    em = Em(nc)
    dk = "ExternalOutput" if debug else "Internal"
    specs = [
        ("xin", (D, S), F32, "ExternalInput"),
        ("xout", (D, S), F32, "ExternalOutput"),
        ("wq", (D, D), F32, "ExternalInput"),
        ("wk", (D, 512), F32, "ExternalInput"),
        ("wv", (D, 512), F32, "ExternalInput"),
        ("wo", (D, D), F32, "ExternalInput"),
        ("bo", (128, NF), F32, "ExternalInput"),
        ("gwT", (D, 8), F32, "ExternalInput"),
        ("w1T", (D, 5632), BF16, "ExternalInput"),
        ("w3T", (D, 5632), BF16, "ExternalInput"),
        ("w2T", (5632, D), BF16, "ExternalInput"),
        ("a1T", (D, 128), F32, "ExternalInput"),
        ("a3T", (D, 128), F32, "ExternalInput"),
        ("a2T", (5632, 128), F32, "ExternalInput"),
        ("b1T", (8, 16, 5632), F32, "ExternalInput"),
        ("b3T", (8, 16, 5632), F32, "ExternalInput"),
        ("b2T", (16, 8 * D), F32, "ExternalInput"),
        ("accA", (D, S), F32, dk),
        ("accF", (D, S), F32, dk),
        ("q_hbm", (D, S), BF16, "Internal"),
        ("k_hbm", (512, S), BF16, "Internal"),
        ("v_hbm", (S, 512), BF16, "Internal"),
        ("f_hbm", (D, S), BF16, "Internal"),
        ("base1_hbm", (5632, S), BF16, "Internal"),
        ("base3_hbm", (5632, S), BF16, "Internal"),
        ("ACT_hbm", (5632, S), BF16, dk),
        ("u1_hbm", (128, S), BF16, "Internal"),
        ("u3_hbm", (128, S), BF16, "Internal"),
        ("u2w_hbm", (128, S), BF16, dk),
    ]
    io = _io_tensors(nc, specs)
    cfg = ATTN_CFG[3]
    C = Consts(em, ["onescol_f", "onesrow_f", "onesrow_b", "ident_f32", "sel8",
                    "mask4mul"], consts)
    wd_attn = dict(wq=io["wq"], wk=io["wk"], wv=io["wv"], wo=io["wo"], bo=io["bo"])
    ast = attn_alloc(em, cfg)
    st = {}
    st["gwT_sb"] = nc.alloc_sbuf_tensor(nm("gwT"), [128, NF * 8], F32)
    st["gsum_sb"] = nc.alloc_sbuf_tensor(nm("gsum"), [8, 1], F32)
    st["logits_sb"] = nc.alloc_sbuf_tensor(nm("logits"), [8, S], F32)
    st["wfull_bf"] = nc.alloc_sbuf_tensor(nm("wfull"), [8, S], BF16)
    st["t8"] = nc.alloc_sbuf_tensor(nm("t8"), [8, 512], F32)
    st["t8b"] = nc.alloc_sbuf_tensor(nm("t8b"), [8, 512], F32)
    st["sel8_sb"] = C["sel8"]
    em.dma(em.sy, st["gwT_sb"][:].rearrange("p (f c) -> p f c", f=NF),
           io["gwT"].rearrange("(f p) c -> p f c", p=128), "d_w")
    em.hard()
    em.barrier()
    mean_sb = nc.alloc_sbuf_tensor(nm("p3mean"), [1, S], F32)
    rstd_sb = nc.alloc_sbuf_tensor(nm("p3rstd"), [1, S], F32)
    with nc.sbuf_tensor(nm("xn"), [128, NF * S], BF16) as xn_sb:
        with ExitStack() as _es:
            ps_l = _es.enter_context(nc.psum_tensor(nm("ps_l"), [8, 512], F32))
            ps_b8 = _es.enter_context(nc.psum_tensor(nm("ps_b8"), [8, 512], F32))
            st["ps_l"] = ps_l
            st["ps_b8"] = ps_b8
            mm_kseries(em, ps_b8[0:8, 0:1],
                       [(st["gwT_sb"][:, f * 8:(f + 1) * 8], C["onescol_f"][:, :])
                        for f in range(NF)])
            em.barrier()
            em.ve.tensor_copy(st["gsum_sb"][0:8, :], ps_b8[0:8, 0:1])
            em.barrier()
            rows_holder = {}

            def router(em_, C_, c, xch, post=False):
                if not post:
                    mm_kseries(em_, ps_l[0:8, :],
                               [(st["gwT_sb"][:, f * 8:(f + 1) * 8],
                                 xch[:, f * 512:(f + 1) * 512]) for f in range(NF)])
                    return
                r_mean, r_rstd = rows_holder["mean"], rows_holder["rstd"]
                em_.te.matmul(ps_b8[0:8, :], C_["onesrow_f"][0:1, 0:8], r_mean,
                              start=True, stop=True)
                em_.barrier()
                em_.ve.tensor_scalar(st["t8"][0:8, :], ps_b8[0:8, :],
                                     st["gsum_sb"][0:8, 0:1], None, ALU.mult)
                em_.barrier()
                em_.ve.tensor_tensor(st["t8b"][0:8, :], ps_l[0:8, :],
                                     st["t8"][0:8, :], ALU.subtract)
                em_.barrier()
                em_.te.matmul(ps_b8[0:8, :], C_["onesrow_f"][0:1, 0:8], r_rstd,
                              start=True, stop=True)
                em_.barrier()
                em_.ve.tensor_tensor(st["logits_sb"][0:8, c * 512:(c + 1) * 512],
                                     st["t8b"][0:8, :], ps_b8[0:8, :], ALU.mult)
                em_.barrier()

            def router3(em_, C_, c, xch, post=False):
                if post:
                    rows_holder["mean"] = mean_sb[:, c * 512:(c + 1) * 512]
                    rows_holder["rstd"] = rstd_sb[:, c * 512:(c + 1) * 512]
                router(em_, C_, c, xch, post=post)

            emit_phase0_l3(em, C, {"x_in": io["xin"]}, xn_sb, router3,
                           mean_sb, rstd_sb)
        emit_l3_top2(em, C, st)
        emit_ffn_l3(em, C, st, xn_sb, io, io["accF"])
        emit_attn_proj(em, C, ast, xn_sb, cfg, wd_attn, io["q_hbm"],
                       io["v_hbm"], io["k_hbm"])
    emit_attn_core(em, C, ast, cfg, io["q_hbm"], io["v_hbm"], io["k_hbm"],
                   io["f_hbm"], io["accA"], wd_attn)
    emit_phase0(em, C, {"x_in": io["xin"]}, adds=(io["accA"], io["accF"]),
                out_x=io["xout"], combine_only=True)
    em.hard()
    return nc


def emit_phase0_l3(em, C, io, xn_sb, router3, mean_sb, rstd_sb):
    emit_phase0(em, C, io, xn_sb=xn_sb, router=router3,
                save_rows={"mean": mean_sb, "rstd": rstd_sb})


def run_layer3(inp, prep, xT, debug=False):
    nc = build_layer3(_consts(), debug=debug)
    m = dict(
        wq=_t(inp["L3_attn_Wq"]), wk=_t(inp["L3_attn_Wk"]),
        wv=_t(inp["L3_attn_Wv"]), wo=_t(inp["L3_attn_Wo"]),
        bo=_pcol(inp["L3_attn_bo"]),
        gwT=_t(inp["L3_ffn_gate_w"]),
        w1T=prep["l3w1_bf"], w3T=prep["l3w3_bf"], w2T=prep["l3w2_bf"],
        a1T=np.ascontiguousarray(np.transpose(inp["L3_ffn_a1"], (2, 0, 1)).reshape(D, 128)),
        a3T=np.ascontiguousarray(np.transpose(inp["L3_ffn_a3"], (2, 0, 1)).reshape(D, 128)),
        a2T=np.ascontiguousarray(np.transpose(inp["L3_ffn_a2"], (2, 0, 1)).reshape(5632, 128)),
        b1T=np.ascontiguousarray(np.transpose(inp["L3_ffn_b1"], (0, 2, 1))),
        b3T=np.ascontiguousarray(np.transpose(inp["L3_ffn_b3"], (0, 2, 1))),
        b2T=np.ascontiguousarray(np.transpose(inp["L3_ffn_b2"], (2, 0, 1)).reshape(16, 8 * D)),
    )
    m = {k: np.ascontiguousarray(v).astype(np.float32)
         if np.asarray(v).dtype == np.float64 or np.asarray(v).dtype == np.float32
         else np.ascontiguousarray(v) for k, v in m.items()}
    in_maps = []
    for c in range(8):
        mc = dict(m)
        mc["xin"] = xT[c]
        in_maps.append(mc)
    res = run_bass_kernel_spmd(nc, in_maps, core_ids=list(range(8)))
    if debug:
        return res.results
    return [res.results[c]["xout"] for c in range(8)]




def build_layer2(consts, debug=False):
    nc = bass.Bass("TRN2", target_bir_lowering=False)
    em = Em(nc)
    dk = "ExternalOutput" if debug else "Internal"
    specs = [
        ("xin", (D, S), F32, "ExternalInput"),
        ("xout", (D, S), F32, "ExternalOutput"),
        ("wq", (D, D), F32, "ExternalInput"),
        ("wk", (D, D), F32, "ExternalInput"),
        ("wv", (D, D), F32, "ExternalInput"),
        ("wz", (D, D), F32, "ExternalInput"),
        ("wab", (D, 32), F32, "ExternalInput"),
        ("convw", (D, 12), F32, "ExternalInput"),
        ("normw", (128, NF), F32, "ExternalInput"),
        ("wout", (D, D), F32, "ExternalInput"),
        ("wr", (128, NF), F32, "ExternalInput"),
        ("br", (1, 1), F32, "ExternalInput"),
        ("w1", (D, 8192), BF16, "ExternalInput"),
        ("b1", (128, 64), F32, "ExternalInput"),
        ("w2", (8192, D), BF16, "ExternalInput"),
        ("b2", (128, NF), F32, "ExternalInput"),
        ("accA", (D, S), F32, dk),
        ("accF", (D, S), F32, dk),
        ("qn_hbm", (D, S), BF16, dk),
        ("kn_hbm", (D, S), BF16, dk),
        ("v_hbm", (D, S), BF16, dk),
        ("z_hbm", (D, S), BF16, dk),
        ("wrow_dbg", (1, S), F32, dk),
        ("sel_dbg", (1, S), F32, dk),
    ]
    io = _io_tensors(nc, specs)
    C = Consts(em, ["onescol_f", "onesrow_f", "onesrow_b", "ident_f32", "selE",
                    "selB", "rope_R"], consts, defer=("ropecos", "ropesin"))
    ab_sb = nc.alloc_sbuf_tensor(nm("ab"), [16, 2 * (S + 3)], F32)
    wrow_sb = nc.alloc_sbuf_tensor(nm("wrow"), [1, S], F32)
    selw_sb = nc.alloc_sbuf_tensor(nm("selw"), [1, S], F32)
    comb_sb = nc.alloc_sbuf_tensor(nm("comb"), [1, 2 * S], F32)
    mean_sb = nc.alloc_sbuf_tensor(nm("mean2"), [1, S], F32)
    rstd_sb = nc.alloc_sbuf_tensor(nm("rstd2"), [1, S], F32)
    misc_sb = nc.alloc_sbuf_tensor(nm("misc2"), [128, 8], F32)
    wsum_sb = nc.alloc_sbuf_tensor(nm("wsum2"), [1, 1], F32)
    wr_sb = nc.alloc_sbuf_tensor(nm("wr2"), [128, NF], F32)
    normw_sb = nc.alloc_sbuf_tensor(nm("nw2"), [128, NF], F32)
    wab_sb = nc.alloc_sbuf_tensor(nm("wab2"), [128, NF * 32], BF16)
    convw_sb = nc.alloc_sbuf_tensor(nm("cw2"), [128, NF * 12], F32)
    em.dma(em.sy, wr_sb[:, :], io["wr"][:, :], "d_w")
    em.dma(em.sy, normw_sb[:, :], io["normw"][:, :], "d_w")
    em.dma(em.gp, wab_sb[:].rearrange("p (f c) -> p f c", f=NF),
           io["wab"].rearrange("(f p) c -> p f c", p=128), "d_w")
    em.dma(em.sy, convw_sb[:].rearrange("p (f c) -> p f c", f=NF),
           io["convw"].rearrange("(f p) c -> p f c", p=128), "d_w")
    em.dma(em.sy, wsum_sb[0:1, 0:1], io["br"][:, :], "d_w")
    em.hard()
    with nc.psum_tensor(nm("psg"), [1, 512], F32) as psg:
        mm_kseries(em, psg[0:1, 0:1],
                   [(wr_sb[:, f:f + 1], C["onescol_f"][:, :]) for f in range(NF)])
        em.barrier()
        em.ve.tensor_copy(misc_sb[0:1, 0:1], psg[0:1, 0:1])
        em.barrier()
    br_sb = wsum_sb
    rows_holder = {}

    def router(em_, C_, c, xch, post=False):
        if not post:
            mm_kseries(em_, rt["ps_w"][0:1, :],
                       [(wr_sb[:, f:f + 1], xch[:, f * 256:(f + 1) * 256])
                        for f in range(NF)])
            return
        tsl = slice(c * 256, (c + 1) * 256)
        r_mean, r_rstd = rows_holder["mean"], rows_holder["rstd"]
        em_.ve.tensor_scalar(rt["t1"][0:1, :], r_mean, misc_sb[0:1, 0:1], None,
                             ALU.mult)
        em_.barrier()
        em_.ve.tensor_tensor(rt["t1"][0:1, :], rt["ps_w"][0:1, :], rt["t1"][0:1, :],
                             ALU.subtract)
        em_.barrier()
        em_.ve.tensor_tensor(rt["t1"][0:1, :], rt["t1"][0:1, :], r_rstd, ALU.mult)
        em_.barrier()
        em_.ve.tensor_scalar(wrow_sb[0:1, tsl], rt["t1"][0:1, :],
                             br_sb[0:1, 0:1], None, ALU.add)
        em_.barrier()

    def router3(em_, C_, c, xch, post=False):
        if post:
            rows_holder["mean"] = mean_sb[:, c * 256:(c + 1) * 256]
            rows_holder["rstd"] = rstd_sb[:, c * 256:(c + 1) * 256]
        router(em_, C_, c, xch, post=post)

    rt = {}
    with nc.sbuf_tensor(nm("xn"), [128, NF * S], BF16) as xn_sb:
        with ExitStack() as _es:
            rt["ps_w"] = _es.enter_context(nc.psum_tensor(nm("ps_w"), [1, 256], F32))
            rt["t1"] = _es.enter_context(nc.sbuf_tensor(nm("rt_t1"), [1, 256], F32))
            emit_phase0(em, C, {"x_in": io["xin"]}, xn_sb=xn_sb, router=router3,
                        save_rows={"mean": mean_sb, "rstd": rstd_sb}, ct=256)
        emit_l2_select(em, C, io, wrow_sb, selw_sb, comb_sb, mean_sb, rstd_sb)
        emit_l2_ffn(em, C, io, xn_sb, selw_sb)
        emit_l2_attnproj(em, C, io, xn_sb, ab_sb, wab_sb, convw_sb)
    emit_l2_attncore(em, C, io, ab_sb, normw_sb)
    def l2fix(em_, C_, c, xch):
        with ExitStack() as _es:
            ps_a = _es.enter_context(nc.psum_tensor(nm("psfa"), [128, 256], F32))
            ps_b = _es.enter_context(nc.psum_tensor(nm("psfb"), [128, 256], F32))
            em_.te.matmul(ps_a[:, :], C_["onesrow_f"][:, :], comb_sb[0:1, c * 256:(c + 1) * 256],
                          start=True, stop=True)
            em_.te.matmul(ps_b[:, :], C_["onesrow_f"][:, :], comb_sb[0:1, S + c * 256:S + (c + 1) * 256],
                          start=True, stop=True)
            em_.barrier()
            for f in range(NF):
                em_.ve.tensor_tensor(xch[:, f * 256:(f + 1) * 256],
                                     xch[:, f * 256:(f + 1) * 256], ps_a[:, :],
                                     ALU.mult)
            em_.barrier()
            for f in range(NF):
                em_.ve.tensor_tensor(xch[:, f * 256:(f + 1) * 256],
                                     xch[:, f * 256:(f + 1) * 256], ps_b[:, :],
                                     ALU.add)
            em_.barrier()
    emit_phase0(em, C, {"x_in": io["xin"]}, adds=(io["accA"], io["accF"]),
                out_x=io["xout"], combine_only=True, l2fix=l2fix, ct=256)
    if debug:
        em.ve.tensor_copy(wrow_sb[0:1, 0:1], wrow_sb[0:1, 0:1])
        em.dma(em.sy, io["wrow_dbg"][:, :], wrow_sb[0:1, :], "d_sc")
        em.dma(em.sy, io["sel_dbg"][:, :], selw_sb[0:1, :], "d_sc")
    em.hard()
    return nc


def emit_l2_select(em, C, io, wrow_sb, selw_sb, comb_sb, mean_sb, rstd_sb):
    nc = em.nc
    with ExitStack() as _es:
        wcol = _es.enter_context(nc.sbuf_tensor(nm("wcol"), [128, 16], F32))
        wb = _es.enter_context(nc.sbuf_tensor(nm("wb"), [128, S], F32))
        cmp = _es.enter_context(nc.sbuf_tensor(nm("cmp"), [128, 512], F32))
        rowt = _es.enter_context(nc.sbuf_tensor(nm("rowt"), [1, 2 * S], F32))
        ps_t = _es.enter_context(nc.psum_tensor(nm("ps_t2"), [128, 128], F32))
        ps_c = _es.enter_context(nc.psum_tensor(nm("ps_c2"), [1, 512], F32))
        ps_bx = _es.enter_context(nc.psum_tensor(nm("ps_bx"), [128, 512], F32))
        for i in range(16):
            em.te.transpose(ps_t[0:128, 0:1], wrow_sb[0:1, i * 128:(i + 1) * 128],
                            C["ident_f32"][0:1, 0:1])
            em.barrier()
            em.ve.tensor_copy(wcol[:, i:i + 1], ps_t[0:128, 0:1])
            em.barrier()
        for c in range(4):
            em.te.matmul(ps_bx[:, :], C["onesrow_f"][:, :],
                         wrow_sb[0:1, c * 512:(c + 1) * 512], start=True, stop=True)
            em.barrier()
            em.ve.tensor_copy(wb[:, c * 512:(c + 1) * 512], ps_bx[:, :])
            em.barrier()
        for c in range(4):
            tsl = slice(c * 512, (c + 1) * 512)
            for i in range(16):
                em.ve.tensor_scalar(cmp[:, :], wb[:, tsl], wcol[:, i:i + 1], None,
                                    ALU.is_lt)
                em.barrier()
                em.te.matmul(ps_c[0:1, :], C["onescol_f"][:, :], cmp[:, :],
                             start=(i == 0), stop=(i == 15))
                em.barrier()
            em.ve.tensor_scalar(rowt[0:1, 0:512], ps_c[0:1, :], 510.5, None,
                                ALU.is_lt)
            em.barrier()
            em.ve.tensor_tensor(selw_sb[0:1, tsl], rowt[0:1, 0:512],
                                wrow_sb[0:1, tsl], ALU.mult)
            em.ve.tensor_scalar(rowt[0:1, 512:1024], rowt[0:1, 0:512], -1.0, 1.0,
                                ALU.mult, ALU.add)
            em.barrier()
            em.ve.tensor_tensor(rowt[0:1, 1024:1536], rowt[0:1, 512:1024],
                                rstd_sb[0:1, tsl], ALU.mult)
            em.barrier()
            em.ve.tensor_scalar(comb_sb[0:1, tsl], rowt[0:1, 1024:1536], 1.0, None,
                                ALU.add)
            em.ve.tensor_tensor(rowt[0:1, 1536:2048], rowt[0:1, 1024:1536],
                                mean_sb[0:1, tsl], ALU.mult)
            em.barrier()
            em.ve.tensor_scalar(comb_sb[0:1, S + c * 512:S + (c + 1) * 512],
                                rowt[0:1, 1536:2048], -1.0, None, ALU.mult)
            em.barrier()


def emit_l2_ffn(em, C, io, xn_sb, selw_sb):
    nc = em.nc
    GA = 0.7978845608028654
    GB = 0.044715
    CT = 256
    smm = em.sem("l2f_mm")
    with ExitStack() as _es:
        b1_sb = _es.enter_context(nc.sbuf_tensor(nm("b1s"), [128, 64], F32))
        b2_sb = _es.enter_context(nc.sbuf_tensor(nm("b2s"), [128, NF], F32))
        hch = _es.enter_context(nc.sbuf_tensor(nm("hch"), [128, 64 * CT], BF16))
        ug = _es.enter_context(nc.sbuf_tensor(nm("ug"), [128, 2 * CT], F32))
        t_ = _es.enter_context(nc.sbuf_tensor(nm("t_"), [128, 2 * CT], F32))
        stg = _es.enter_context(nc.sbuf_tensor(nm("stg"), [128, 2 * CT], F32))
        pss = [_es.enter_context(nc.psum_tensor(nm("psl2"), [128, CT], F32))
               for _ in range(2)]
        ps_b = _es.enter_context(nc.psum_tensor(nm("ps_b2b"), [128, CT], F32))
        em.dma(em.sy, b1_sb[:, :], io["b1"][:, :], "d_w")
        em.dma(em.sy, b2_sb[:, :], io["b2"][:, :], "d_w")
        em.hard()
        for c in range(S // CT):
            tsl = slice(c * CT, (c + 1) * CT)
            sedone = []

            def body(o, bufs):
                wcb = bufs[0]
                sl = slice((o % 2) * CT, (o % 2 + 1) * CT)
                if o >= 2:
                    em.wait(em.te, "q_ve", sedone[o - 2])
                cs.tag(mm_kseries(em, pss[o % 2][:, :],
                                  [(wcb[:, f * 128:(f + 1) * 128],
                                    xn_sb[:, f * S + c * CT: f * S + (c + 1) * CT])
                                   for f in range(NF)]))
                em.wait(em.se, "t_mm")
                em.se.activation(ug[:, sl], pss[o % 2][:, :], AF.Identity,
                                 bias=b1_sb[:, o:o + 1])
                em.se.activation(t_[:, sl], ug[:, sl], AF.Square)
                c1 = em.cnt("q_se")
                em.wait(em.ve, "q_se", c1)
                em.ve.tensor_scalar(t_[:, sl], t_[:, sl], GA * GB, GA, ALU.mult,
                                    ALU.add)
                em.ve.tensor_tensor(t_[:, sl], ug[:, sl], t_[:, sl], ALU.mult)
                c2 = em.cnt("q_ve")
                em.wait(em.se, "q_ve", c2)
                em.se.activation(t_[:, sl], t_[:, sl], AF.Tanh)
                c3 = em.cnt("q_se")
                em.wait(em.ve, "q_se", c3)
                em.ve.tensor_scalar(t_[:, sl], t_[:, sl], 0.5, 0.5, ALU.mult, ALU.add)
                em.ve.tensor_tensor(hch[:, o * CT:(o + 1) * CT], ug[:, sl], t_[:, sl],
                                    ALU.mult)
                sedone.append(em.cnt("q_ve"))

            with ExitStack() as _es2:
                w12 = [_es2.enter_context(nc.sbuf_tensor(nm("w12"), [128, NF * 128], BF16))
                       for _ in range(2)]
                cs = ColStream(em, [(io["w1"], 128, NF, (w12[0], w12[1]))], 64)
                cs.run(body)
                em.hard()
            em.wait(em.te, "q_ve")
            em.te.matmul(ps_b[:, :], C["onesrow_f"][:, :], selw_sb[0:1, tsl],
                         start=True, stop=True).then_inc(smm, 1)
            em.bump("l2f_mm")
            dfree = []

            def dbody(o2, bufs):
                dcb_ = bufs[0]
                sl = slice((o2 % 2) * CT, (o2 % 2 + 1) * CT)
                if o2 >= 2:
                    em.wait(em.te, "q_ve", dfree[o2 - 2])
                cs2.tag(mm_kseries(em, pss[o2 % 2][:, :],
                                   [(dcb_[:, f * 128:(f + 1) * 128],
                                     hch[:, f * CT:(f + 1) * CT])
                                    for f in range(64)]))
                em.wait(em.se, "t_mm")
                em.wait(em.se, "d_sc")
                em.se.activation(stg[:, sl], pss[o2 % 2][:, :], AF.Identity,
                                 bias=b2_sb[:, o2:o2 + 1])
                c1 = em.cnt("q_se")
                em.wait(em.ve, "q_se", c1)
                em.ve.tensor_tensor(stg[:, sl], stg[:, sl], ps_b[:, :], ALU.mult)
                dfree.append(em.cnt("q_ve"))
                em.wait(em.sy, "q_ve", dfree[-1])
                em.dma(em.sy, io["accF"][o2 * 128:(o2 + 1) * 128, tsl], stg[:, sl],
                       "d_sc")

            with ExitStack() as _es2:
                dc2 = [_es2.enter_context(nc.sbuf_tensor(nm("dc2"), [128, 64 * 128], BF16))
                       for _ in range(2)]
                cs2 = ColStream(em, [(io["w2"], 128, 64, (dc2[0], dc2[1]))], NF)
                cs2.run(dbody)
                em.hard()
        em.hard()


def emit_l2_attnproj(em, C, io, xn_sb, ab_sb, wab_sb, convw_sb):
    nc = em.nc
    smm = em.sem("l2p_mm")
    with ExitStack() as _es:
        wc2 = [_es.enter_context(nc.sbuf_tensor(nm("wc2p"), [128, NF * 128], BF16))
               for _ in range(2)]
        raw = _es.enter_context(nc.sbuf_tensor(nm("raw"), [128, S + 3], F32))
        cvb = _es.enter_context(nc.sbuf_tensor(nm("cvb"), [128, S], BF16))
        qr = _es.enter_context(nc.sbuf_tensor(nm("qr"), [128, 512], F32))
        t_ = _es.enter_context(nc.sbuf_tensor(nm("t_2"), [128, 512], F32))
        rows = _es.enter_context(nc.sbuf_tensor(nm("rows2"), [1, 512], F32))
        stg = _es.enter_context(nc.sbuf_tensor(nm("stg2"), [128, 2 * 512], BF16))
        pss = [_es.enter_context(nc.psum_tensor(nm("ps2p"), [128, 512], F32))
               for _ in range(2)]
        ps_r = _es.enter_context(nc.psum_tensor(nm("ps_r2"), [128, 512], F32))
        ps_s = _es.enter_context(nc.psum_tensor(nm("ps_s2"), [1, 512], F32))
        ps_bx = _es.enter_context(nc.psum_tensor(nm("ps_bx2"), [128, 512], F32))
        C.load_scoped(_es, ["ropecos", "ropesin"])
        em.hard()
        em.ve.memset(raw[:, 0:3], 0.0)
        em.barrier()
        mats = [("q", io["wq"], io["qn_hbm"], True, 0),
                ("k", io["wk"], io["kn_hbm"], True, 4),
                ("v", io["wv"], io["v_hbm"], False, 8),
                ("z", io["wz"], io["z_hbm"], None, 0)]
        for name, w_ap, dst, do_rope, cvcol in mats:
            evfree = []
            rawfree = []
            cvfree = []

            def body(h, bufs, dst=dst, do_rope=do_rope, cvcol=cvcol,
                     evfree=evfree, rawfree=rawfree, cvfree=cvfree):
                wcb = bufs[0]
                if do_rope is not None and h >= 1:
                    em.wait(em.se, "q_ve", rawfree[h - 1])
                for cc in range(4):
                    gi = h * 4 + cc
                    p = pss[cc % 2]
                    if gi >= 2:
                        em.wait(em.te, "q_se", evfree[gi - 2])
                    last = mm_kseries(em, p[:, :],
                                      [(wcb[:, f * 128:(f + 1) * 128],
                                        xn_sb[:, f * S + cc * 512: f * S + (cc + 1) * 512])
                                       for f in range(NF)])
                    if cc == 3:
                        cs.tag(last)
                        em.wait(em.se, "t_mm")
                    else:
                        last.then_inc(smm, 1)
                        em.bump("l2p_mm")
                        em.wait(em.se, "l2p_mm")
                    if do_rope is None:
                        em.wait(em.se, "d_sc")
                        em.se.activation(stg[:, (cc % 2) * 512:(cc % 2 + 1) * 512],
                                         p[:, :], AF.Silu)
                        evfree.append(em.cnt("q_se"))
                        em.wait(em.sy, "q_se")
                        em.dma(em.sy, dst[h * 128:(h + 1) * 128, cc * 512:(cc + 1) * 512],
                               stg[:, (cc % 2) * 512:(cc % 2 + 1) * 512], "d_sc")
                    else:
                        em.se.activation(raw[:, 3 + cc * 512:3 + (cc + 1) * 512],
                                         p[:, :], AF.Silu)
                        evfree.append(em.cnt("q_se"))
                if do_rope is None:
                    return
                sev = em.cnt("q_se")
                em.wait(em.ve, "q_se", sev)
                if h >= 1:
                    if do_rope:
                        em.wait(em.ve, "l2p_mm", cvfree[h - 1])
                    else:
                        em.wait(em.ve, "d_sc")
                cw = convw_sb
                for cc in range(4):
                    em.ve.tensor_scalar(t_[:, :],
                                        raw[:, 3 + cc * 512 - 3: 3 + (cc + 1) * 512 - 3],
                                        cw[:, h * 12 + cvcol: h * 12 + cvcol + 1],
                                        None, ALU.mult)
                    for l in (2, 1, 0):
                        j = 3 - l
                        em.ve.scalar_tensor_tensor(
                            t_[:, :],
                            raw[:, 3 + cc * 512 - l: 3 + (cc + 1) * 512 - l],
                            cw[:, h * 12 + cvcol + j: h * 12 + cvcol + j + 1],
                            t_[:, :], ALU.mult, ALU.add)
                    em.ve.tensor_copy(cvb[:, cc * 512:(cc + 1) * 512], t_[:, :])
                rawfree.append(em.cnt("q_ve"))
                if not do_rope:
                    vc = em.cnt("q_ve")
                    em.wait(em.sy, "q_ve", vc)
                    em.wait(em.sy, "d_sc")
                    em.dma(em.sy, dst[h * 128:(h + 1) * 128, :], cvb[:, :], "d_sc")
                    return
                for cc in range(4):
                    csl = slice(cc * 512, (cc + 1) * 512)
                    vr = em.cnt("q_ve")
                    em.wait(em.te, "q_ve", vr)
                    em.te.matmul(ps_r[:, :], C["rope_R"][:, :], cvb[:, csl],
                                 start=True, stop=True).then_inc(smm, 1)
                    em.bump("l2p_mm")
                    em.wait(em.ve, "l2p_mm")
                    em.ve.tensor_tensor(qr[:, :], cvb[:, csl],
                                        C["ropecos"][:, csl], ALU.mult)
                    em.ve.tensor_tensor(t_[:, :], ps_r[:, :], C["ropesin"][:, csl],
                                        ALU.mult)
                    em.ve.tensor_tensor(qr[:, :], qr[:, :], t_[:, :], ALU.add)
                    vq = em.cnt("q_ve")
                    em.wait(em.se, "q_ve", vq)
                    em.se.activation(t_[:, :], qr[:, :], AF.Square)
                    sq = em.cnt("q_se")
                    em.wait(em.te, "q_se", sq)
                    em.te.matmul(ps_s[0:1, :], C["onescol_f"][:, :], t_[:, :],
                                 start=True, stop=True).then_inc(smm, 1)
                    em.bump("l2p_mm")
                    em.wait(em.se, "l2p_mm")
                    em.se.activation(rows[0:1, :], ps_s[0:1, :], AF.Sqrt)
                    s2 = em.cnt("q_se")
                    em.wait(em.ve, "q_se", s2)
                    em.ve.tensor_scalar(rows[0:1, :], rows[0:1, :], 1e-12, None,
                                        ALU.max)
                    em.ve.reciprocal(rows[0:1, :], rows[0:1, :])
                    v2 = em.cnt("q_ve")
                    em.wait(em.te, "q_ve", v2)
                    em.te.matmul(ps_bx[:, :], C["onesrow_f"][:, :], rows[0:1, :],
                                 start=True, stop=True).then_inc(smm, 1)
                    em.bump("l2p_mm")
                    em.wait(em.ve, "l2p_mm")
                    em.wait(em.ve, "d_sc")
                    em.ve.tensor_tensor(stg[:, (cc % 2) * 512:(cc % 2 + 1) * 512],
                                        qr[:, :], ps_bx[:, :], ALU.mult)
                    v3 = em.cnt("q_ve")
                    em.wait(em.sy, "q_ve", v3)
                    em.dma(em.sy, dst[h * 128:(h + 1) * 128, csl],
                           stg[:, (cc % 2) * 512:(cc % 2 + 1) * 512], "d_sc")
                cvfree.append(em.cnt("l2p_mm"))

            cs = ColStream(em, [(w_ap, 128, NF, (wc2[0], wc2[1]))], 16)
            cs.run(body)
            em.hard()
        em.ve.memset(ab_sb[0:16, :], 0.0)
        em.barrier()
        with nc.psum_tensor(nm("ps_ab"), [16, 512], F32) as ps_ab:
            for c in range(4):
                mm_kseries(em, ps_ab[0:16, :],
                           [(wab_sb[:, f * 32:f * 32 + 16],
                             xn_sb[:, f * S + c * 512: f * S + (c + 1) * 512])
                            for f in range(NF)])
                em.barrier()
                em.ve.tensor_copy(ab_sb[0:16, 3 + c * 512: 3 + (c + 1) * 512],
                                  ps_ab[0:16, :])
                em.barrier()
                mm_kseries(em, ps_ab[0:16, :],
                           [(wab_sb[:, f * 32 + 16:f * 32 + 32],
                             xn_sb[:, f * S + c * 512: f * S + (c + 1) * 512])
                            for f in range(NF)])
                em.barrier()
                em.ve.tensor_copy(ab_sb[0:16, (S + 3) + 3 + c * 512: (S + 3) + 3 + (c + 1) * 512],
                                  ps_ab[0:16, :])
                em.barrier()
        em.hard()


def emit_l2_attncore(em, C, io, ab_sb, normw_sb):
    nc = em.nc
    ESC = 1.0 / math.sqrt(128.0)
    smm = em.sem("l2c_mm")
    with ExitStack() as _es:
        qnb = [_es.enter_context(nc.sbuf_tensor(nm("qnb"), [128, 512], BF16))
               for _ in range(2)]
        knb = [_es.enter_context(nc.sbuf_tensor(nm("knb"), [128, S + 3], BF16))
               for _ in range(2)]
        vb = [_es.enter_context(nc.sbuf_tensor(nm("vb"), [128, S + 3], BF16))
              for _ in range(2)]
        m_ = _es.enter_context(nc.sbuf_tensor(nm("m_"), [128, 2 * 512], BF16))
        gt = _es.enter_context(nc.sbuf_tensor(nm("gt"), [16, 512], F32))
        pb = _es.enter_context(nc.sbuf_tensor(nm("pb"), [16, 4 * 512], BF16))
        oacc = _es.enter_context(nc.sbuf_tensor(nm("oacc"), [128, NF * 512], F32))
        t_ = _es.enter_context(nc.sbuf_tensor(nm("t_3"), [128, 2 * 512], F32))
        zb = _es.enter_context(nc.sbuf_tensor(nm("zb"), [128, 512], BF16))
        wor = _es.enter_context(nc.sbuf_tensor(nm("wor"), [128, NF * 512], BF16))
        rows = _es.enter_context(nc.sbuf_tensor(nm("rows3"), [1, 512], F32))
        wc2 = [_es.enter_context(nc.sbuf_tensor(nm("wc3"), [128, NF * 128], BF16))
               for _ in range(2)]
        stg = _es.enter_context(nc.sbuf_tensor(nm("stg3"), [128, 2 * 512], F32))
        ps_at = [_es.enter_context(nc.psum_tensor(nm("ps_at"), [16, 512], F32))
                 for _ in range(4)]
        ps_w = [_es.enter_context(nc.psum_tensor(nm("ps_w3"), [128, 512], F32))
                for _ in range(2)]
        ps_s = _es.enter_context(nc.psum_tensor(nm("ps_s3"), [1, 512], F32))
        ps_bx = _es.enter_context(nc.psum_tensor(nm("ps_bx3"), [128, 512], F32))
        em.ve.memset(knb[0][:, 0:3], 0.0)
        em.ve.memset(knb[1][:, 0:3], 0.0)
        em.ve.memset(vb[0][:, 0:3], 0.0)
        em.ve.memset(vb[1][:, 0:3], 0.0)
        em.barrier()
        for c in range(4):
            tsl = slice(c * 512, (c + 1) * 512)
            emm = []
            hvdone = []
            em.sy.wait_ge(em.sem("q_ve"), em.cnt("q_ve"))
            em.dma(em.sy, qnb[0][:, :], io["qn_hbm"][0:128, tsl], "d_kqA")
            em.dma(em.sy, knb[0][:, 3:3 + S], io["kn_hbm"][0:128, :], "d_kqA")
            ldc = [("d_kqA", em.cnt("d_kqA"))]
            for h in range(16):
                if h + 1 < 16:
                    if h >= 1:
                        em.sy.wait_ge(em.sem("q_ve"), hvdone[h - 1])
                    pn = "d_kqA" if (h + 1) % 2 == 0 else "d_kqB"
                    em.dma(em.sy, qnb[(h + 1) % 2][:, :],
                           io["qn_hbm"][(h + 1) * 128:(h + 2) * 128, tsl], pn)
                    em.dma(em.sy, knb[(h + 1) % 2][:, 3:3 + S],
                           io["kn_hbm"][(h + 1) * 128:(h + 2) * 128, :], pn)
                    ldc.append((pn, em.cnt(pn)))
                em.wait(em.ve, ldc[h][0], ldc[h][1])
                for s_ in range(4):
                    gi2 = h * 4 + s_
                    if gi2 >= 2:
                        em.wait(em.ve, "l2c_mm", emm[gi2 - 2])
                    em.ve.tensor_tensor(
                        m_[:, (s_ % 2) * 512:(s_ % 2 + 1) * 512], qnb[h % 2][:, :],
                        knb[h % 2][:, 3 + c * 512 - s_: 3 + (c + 1) * 512 - s_],
                        ALU.mult)
                    mv = em.cnt("q_ve")
                    em.wait(em.te, "q_ve", mv)
                    em.te.matmul(ps_at[s_][0:16, :], C["selE"][:, h * 16:(h + 1) * 16],
                                 m_[:, (s_ % 2) * 512:(s_ % 2 + 1) * 512],
                                 start=(h == 0),
                                 stop=(h == 15)).then_inc(smm, 1)
                    em.bump("l2c_mm")
                    emm.append(em.cnt("l2c_mm"))
                hvdone.append(em.cnt("q_ve"))
            em.wait(em.ve, "l2c_mm")
            for s_ in range(4):
                a_sh = ab_sb[0:16, 3 + c * 512 - s_: 3 + (c + 1) * 512 - s_]
                b_r = ab_sb[0:16, (S + 3) + 3 + c * 512: (S + 3) + 3 + (c + 1) * 512]
                em.ve.tensor_tensor(gt[0:16, :], a_sh, b_r, ALU.mult)
                gv = em.cnt("q_ve")
                em.wait(em.se, "q_ve", gv)
                em.se.activation(gt[0:16, :], gt[0:16, :], AF.Sigmoid)
                gs = em.cnt("q_se")
                em.wait(em.ve, "q_se", gs)
                em.ve.scalar_tensor_tensor(pb[0:16, s_ * 512:(s_ + 1) * 512],
                                           ps_at[s_][0:16, :], ESC, gt[0:16, :],
                                           ALU.mult, ALU.mult)
            p2m = []
            hv2done = []
            em.ve.memset(oacc[:, :], 0.0)
            em.gp.wait_ge(em.sem("q_ve"), em.cnt("q_ve"))
            em.dma(em.gp, vb[0][:, 3:3 + S], io["v_hbm"][0:128, :], "d_vbA")
            vldc = [("d_vbA_g", em.cnt("d_vbA_g"))]
            for h in range(16):
                if h + 1 < 16:
                    if h >= 1:
                        em.gp.wait_ge(em.sem("q_ve"), hv2done[h - 1])
                    pn = "d_vbA" if (h + 1) % 2 == 0 else "d_vbB"
                    em.dma(em.gp, vb[(h + 1) % 2][:, 3:3 + S],
                           io["v_hbm"][(h + 1) * 128:(h + 2) * 128, :], pn)
                    vldc.append((pn + "_g", em.cnt(pn + "_g")))
                em.wait(em.te, vldc[h][0], vldc[h][1])
                for s_ in range(4):
                    gi2 = h * 4 + s_
                    if gi2 >= 2:
                        em.wait(em.te, "q_ve", p2m[gi2 - 2])
                    em.te.matmul(ps_w[s_ % 2][:, :],
                                 C["selB"][0:16, h * 128:(h + 1) * 128],
                                 pb[0:16, s_ * 512:(s_ + 1) * 512], start=True,
                                 stop=True).then_inc(smm, 1)
                    em.bump("l2c_mm")
                    em.wait(em.ve, "l2c_mm")
                    em.ve.tensor_tensor(t_[:, (s_ % 2) * 512:(s_ % 2 + 1) * 512],
                                        ps_w[s_ % 2][:, :],
                                        vb[h % 2][:, 3 + c * 512 - s_: 3 + (c + 1) * 512 - s_],
                                        ALU.mult)
                    p2m.append(em.cnt("q_ve"))
                    em.ve.tensor_tensor(oacc[:, h * 512:(h + 1) * 512],
                                        oacc[:, h * 512:(h + 1) * 512],
                                        t_[:, (s_ % 2) * 512:(s_ % 2 + 1) * 512],
                                        ALU.add)
                hv2done.append(em.cnt("q_ve"))
            em.wait(em.se, "q_ve")
            sqmm = []
            for h in range(16):
                if h >= 2:
                    em.wait(em.se, "l2c_mm", sqmm[h - 2])
                em.se.activation(t_[:, (h % 2) * 512:(h % 2 + 1) * 512],
                                 oacc[:, h * 512:(h + 1) * 512], AF.Square)
                sq = em.cnt("q_se")
                em.wait(em.te, "q_se", sq)
                em.te.matmul(ps_s[0:1, :], C["onescol_f"][:, :],
                             t_[:, (h % 2) * 512:(h % 2 + 1) * 512],
                             start=(h == 0), stop=(h == 15)).then_inc(smm, 1)
                em.bump("l2c_mm")
                sqmm.append(em.cnt("l2c_mm"))
            em.wait(em.ve, "l2c_mm")
            em.ve.tensor_scalar(rows[0:1, :], ps_s[0:1, :], 1.0 / D, None, ALU.mult)
            em.ve.tensor_scalar(rows[0:1, :], rows[0:1, :], 1e-6, None, ALU.add)
            rv = em.cnt("q_ve")
            em.wait(em.se, "q_ve", rv)
            em.se.activation(rows[0:1, :], rows[0:1, :], AF.Sqrt)
            rs = em.cnt("q_se")
            em.wait(em.ve, "q_se", rs)
            em.ve.reciprocal(rows[0:1, :], rows[0:1, :])
            rr = em.cnt("q_ve")
            em.wait(em.te, "q_ve", rr)
            em.te.matmul(ps_bx[:, :], C["onesrow_f"][:, :], rows[0:1, :],
                         start=True, stop=True).then_inc(smm, 1)
            em.bump("l2c_mm")
            em.wait(em.ve, "l2c_mm")
            for h in range(16):
                em.sy.wait_ge(em.sem("q_ve"), em.cnt("q_ve"))
                em.dma(em.sy, zb[:, :], io["z_hbm"][h * 128:(h + 1) * 128, tsl],
                       "d_sc")
                em.wait(em.ve, "d_sc")
                em.ve.tensor_tensor(t_[:, 0:512], oacc[:, h * 512:(h + 1) * 512],
                                    ps_bx[:, :], ALU.mult)
                em.ve.tensor_scalar(t_[:, 0:512], t_[:, 0:512], normw_sb[:, h:h + 1],
                                    None, ALU.mult)
                em.ve.tensor_tensor(wor[:, h * 512:(h + 1) * 512], t_[:, 0:512],
                                    zb[:, :], ALU.mult)
            wodone = []

            def obody(o, bufs, tsl=tsl):
                wcb = bufs[0]
                p = ps_w[o % 2]
                if o >= 2:
                    em.wait(em.te, "q_se", wodone[o - 2])
                cs.tag(mm_kseries(em, p[:, :],
                                  [(wcb[:, f * 128:(f + 1) * 128],
                                    wor[:, f * 512:(f + 1) * 512]) for f in range(NF)]))
                em.wait(em.se, "t_mm")
                em.wait(em.se, "d_sc")
                em.se.activation(stg[:, (o % 2) * 512:(o % 2 + 1) * 512], p[:, :],
                                 AF.Identity)
                wodone.append(em.cnt("q_se"))
                em.wait(em.sy, "q_se")
                em.dma(em.sy, io["accA"][o * 128:(o + 1) * 128, tsl],
                       stg[:, (o % 2) * 512:(o % 2 + 1) * 512], "d_sc")

            em.wait(em.te, "q_ve")
            cs = ColStream(em, [(io["wout"], 128, NF, (wc2[0], wc2[1]))], NF)
            cs.run(obody)
        em.hard()


def run_layer2(inp, prep, xT, debug=False):
    nc = build_layer2(_consts(), debug=debug)
    cw = np.concatenate([inp["L2_attn_conv_q_w"], inp["L2_attn_conv_k_w"],
                         inp["L2_attn_conv_v_w"]], axis=1)
    m = dict(
        wq=inp["L2_attn_Wq"], wk=inp["L2_attn_Wk"], wv=inp["L2_attn_Wv"],
        wz=inp["L2_attn_Wz"],
        wab=np.concatenate([inp["L2_attn_Wa"], inp["L2_attn_Wb"]], axis=1),
        convw=cw, normw=_pcol(inp["L2_attn_norm_w"]), wout=inp["L2_attn_Wout"],
        wr=_pcol(inp["L2_ffn_W_r"][:, 0]), br=inp["L2_ffn_b_r"].reshape(1, 1),
        w1=prep["l2w1_bf"], b1=_pcol(inp["L2_ffn_b1"]), w2=prep["l2w2_bf"],
        b2=_pcol(inp["L2_ffn_b2"]),
    )
    m = {k: np.ascontiguousarray(v).astype(np.float32)
         if np.asarray(v).dtype in (np.float64, np.float32) else np.ascontiguousarray(v)
         for k, v in m.items()}
    in_maps = []
    for c in range(8):
        mc = dict(m)
        mc["xin"] = xT[c]
        in_maps.append(mc)
    res = run_bass_kernel_spmd(nc, in_maps, core_ids=list(range(8)))
    if debug:
        return res.results
    return [res.results[c]["xout"] for c in range(8)]
